# version 02
# baseline (speedup 1.0000x reference)
import contextlib
import math
import numpy as np
import ml_dtypes
import concourse.bass as bass
import concourse.mybir as mybir
from concourse.bass_utils import run_bass_kernel_spmd

F32 = mybir.dt.float32
BF16 = mybir.dt.bfloat16
AF = mybir.ActivationFunctionType
ALU = mybir.AluOpType
AX = mybir.AxisListType

D = 2048
T = 2048
TFULL = 4096
NCH = D // 128
PAIRS = [[0, 1], [2, 3], [4, 5], [6, 7]]


class _Op:
    __slots__ = ("idx", "eng", "fn", "deps", "is_dma", "tag", "marked",
                 "sem", "val", "dma_count", "hard")

    def __init__(self, idx, eng, fn, deps, is_dma=False, tag=None):
        self.hard = set()
        self.idx = idx
        self.eng = eng
        self.fn = fn
        self.deps = list(deps)
        self.is_dma = is_dma
        self.tag = tag
        self.marked = False
        self.sem = None
        self.val = None


class Plan:
    ENGINES = ("tensor", "vector", "scalar", "gpsimd", "sync")

    def __init__(self, nc, stack):
        self.nc = nc
        self.stack = stack
        self.ops = []
        self.eng_sem = {e: stack.enter_context(nc.semaphore(f"S_{e}"))
                        for e in self.ENGINES}
        self.dma_sems = {}
        self._fence = {e: None for e in self.ENGINES}

    def fence(self, deps):
        deps = [d for d in deps if d is not None]
        for e in self.ENGINES:
            cur = self._fence[e] or []
            self._fence[e] = cur + deps

    def _take_fence(self, eng):
        f = self._fence[eng]
        self._fence[eng] = None
        return f or []

    def op(self, eng, fn, deps=(), hard=()):
        deps = list(deps) + list(hard) + self._take_fence(eng)
        o = _Op(len(self.ops), eng, fn, deps)
        o.hard = set(id(h) for h in hard)
        self.ops.append(o)
        for d in deps:
            d.marked = True
        return o

    def dma(self, eng, tag, fn, deps=()):
        deps = list(deps) + self._take_fence(eng)
        o = _Op(len(self.ops), eng, fn, deps, is_dma=True, tag=tag)
        self.ops.append(o)
        for d in deps:
            d.marked = True
        return o

    def barrier_all(self, deps):
        return deps

    def emit(self):
        eng_count = {e: 0 for e in self.ENGINES}
        for o in self.ops:
            if o.is_dma:
                if o.tag not in self.dma_sems:
                    h = self.stack.enter_context(self.nc.semaphore(f"D_{o.tag}"))
                    self.dma_sems[o.tag] = [h, 0]
                ent = self.dma_sems[o.tag]
                ent[1] += 16
                o.sem, o.val = ent[0], ent[1]
            elif o.marked:
                eng_count[o.eng] += 1
                o.sem, o.val = self.eng_sem[o.eng], eng_count[o.eng]

        queues = {e: [] for e in self.ENGINES}
        waited = {e: {} for e in self.ENGINES}

        for o in self.ops:
            q = queues[o.eng]
            w = waited[o.eng]
            for d in o.deps:
                if d.eng == o.eng and not d.is_dma and id(d) not in o.hard:
                    continue
                assert d.sem is not None, "dep without completion signal"
                key = id(d.sem)
                if w.get(key, -1) >= d.val:
                    continue
                w[key] = d.val
                q.append(lambda e, s=d.sem, v=d.val: e.wait_ge(s, v))
            if o.is_dma:
                q.append(lambda e, o=o: o.fn(e).then_inc(o.sem, 16))
            elif o.marked:
                q.append(lambda e, o=o: o.fn(e).then_inc(o.sem, 1))
            else:
                q.append(lambda e, o=o: o.fn(e))

        nc = self.nc
        with nc.Block() as block:
            @block.tensor
            def _(t):
                for fn in queues["tensor"]:
                    fn(t)

            @block.vector
            def _(v):
                for fn in queues["vector"]:
                    fn(v)

            @block.scalar
            def _(s):
                for fn in queues["scalar"]:
                    fn(s)

            @block.gpsimd
            def _(g):
                for fn in queues["gpsimd"]:
                    fn(g)

            @block.sync
            def _(sy):
                for fn in queues["sync"]:
                    fn(sy)


class Ctx:

    def __init__(self, nc, stack):
        self.nc = nc
        self.stack = stack
        self.plan = Plan(nc, stack)
        self.ps = [stack.enter_context(nc.psum_tensor(f"ps{i}", [128, 512], F32))
                   for i in range(8)]

    def sb(self, name, shape, dtype):
        return self.stack.enter_context(self.nc.sbuf_tensor(name, shape, dtype))

    @contextlib.contextmanager
    def scope(self):
        old = self.stack
        try:
            with contextlib.ExitStack() as es:
                self.stack = es
                yield
        finally:
            self.stack = old



def _panels(wT, pw=128):
    K, O = wT.shape
    return np.ascontiguousarray(wT.reshape(K, O // pw, pw).transpose(1, 0, 2))


def _rowblocks(wT, rb=128):
    K, O = wT.shape
    return np.ascontiguousarray(wT.reshape(K // rb, rb, O))


def _sigma_perm():
    return np.array(list(range(0, 16)) + list(range(32, 48)) +
                    list(range(16, 32)) + list(range(48, 64)))


def _rope_tables():
    HD = 64
    inv_freq = 1.0 / (10000.0 ** (np.arange(0, HD, 2) / HD))
    pos = np.arange(TFULL, dtype=np.float64)
    ang = pos[None, :] * inv_freq[:, None]
    cos, sin = np.cos(ang), np.sin(ang)
    cos_q = np.concatenate([cos[0:16], cos[0:16], cos[16:32], cos[16:32]], 0)
    sin_q = np.concatenate([sin[0:16], sin[0:16], sin[16:32], sin[16:32]], 0)
    cos_k = np.concatenate([cos[0:16], cos[0:16]], 0)
    sin_k = np.concatenate([sin[0:16], sin[0:16]], 0)
    return (np.tile(cos_q, (2, 1)).astype(np.float32),
            np.tile(sin_q, (2, 1)).astype(np.float32),
            np.tile(cos_k, (4, 1)).astype(np.float32),
            np.tile(sin_k, (4, 1)).astype(np.float32))


def _perm_matrix():
    P = np.zeros((128, 128), np.float32)
    for blk in range(4):
        b = 32 * blk
        for p in range(16):
            P[b + p, b + p + 16] = -1.0
            P[b + p + 16, b + p] = 1.0
    return P


def prep_common_consts():
    c = {}
    mul_masks, add_masks = [], []
    for m in range(4):
        k = np.arange(128)[:, None] + 128 * m
        q = np.arange(512)[None, :]
        keep = (k <= q)
        mul_masks.append(keep.astype(np.float32))
        add_masks.append(np.where(keep, 0.0, -1e30).astype(np.float32))
    c["tri_mul"] = np.stack(mul_masks)
    c["tri_add"] = np.stack(add_masks)
    cq, sq, ck, sk = _rope_tables()
    c["cos_q"], c["sin_q"], c["cos_k"], c["sin_k"] = cq, sq, ck, sk
    c["permT"] = np.ascontiguousarray(_perm_matrix().T)
    Hq = np.zeros((8, 128, 16), np.float32)
    for pn in range(8):
        Hq[pn, 0:64, 2 * pn] = 1.0
        Hq[pn, 64:128, 2 * pn + 1] = 1.0
    c["Hq"] = Hq
    H4 = np.zeros((128, 4), np.float32)
    for g in range(4):
        H4[32 * g:32 * (g + 1), g] = 1.0
    c["H4"] = H4
    IND = np.zeros((4, 16), np.float32)
    for qh in range(16):
        IND[qh // 4, qh] = -1.0
    c["IND"] = IND
    SCALE1 = 1.0 / math.sqrt(128.0)
    c["iota8neg"] = np.tile(-np.arange(TFULL, dtype=np.float64)[None, :] / SCALE1,
                            (8, 1)).astype(np.float32)
    jc = np.arange(128, dtype=np.float32)[:, None] + 128.0 * np.arange(32)[None, :]
    c["jcols"] = np.ascontiguousarray(jc)
    e8T = np.zeros((8, 8, 128), np.float32)
    for hh in range(8):
        e8T[hh, hh, :] = 1.0
    c["e8T"] = e8T
    E8 = np.zeros((8, 128, 8), np.float32)
    for pn in range(8):
        E8[pn, :, pn] = 1.0
    c["E8"] = E8
    return c


def prep_core_inputs(inputs, core):
    b, h = core // 2, core % 2
    d = {}
    x = np.asarray(inputs["L0_attn_hidden_states"])
    d["x"] = np.ascontiguousarray(x[b, h * T:(h + 1) * T, :].T)

    d["wfcT"] = _panels(np.ascontiguousarray(np.asarray(inputs["L0_ffn_w_fc"]).T))
    d["wprojT"] = _panels(np.ascontiguousarray(np.asarray(inputs["L0_ffn_w_proj"]).T))

    sig = _sigma_perm()
    qw = np.asarray(inputs["L0_attn_q_w"])
    qw_my = np.concatenate([qw[qh * 64 + sig, :] for qh in range(16 * h, 16 * (h + 1))], 0)
    d["l0_qwT"] = _panels(np.ascontiguousarray(qw_my.T))
    kr = np.asarray(inputs["L0_attn_kr_w"])
    d["l0_krT"] = _panels(np.ascontiguousarray(kr[128 * h:128 * (h + 1), :].T))
    d["l0_dnT"] = _panels(np.ascontiguousarray(np.asarray(inputs["L0_attn_down_w"]).T))
    upk = np.asarray(inputs["L0_attn_upk_w"])
    d["l0_upkT"] = np.ascontiguousarray(upk[128 * h:128 * (h + 1), :].T)
    upv = np.asarray(inputs["L0_attn_upv_w"])
    d["l0_upvT"] = np.ascontiguousarray(upv[256 * h:256 * (h + 1), :].T)
    ow = np.asarray(inputs["L0_attn_o_w"])
    d["l0_owT"] = _panels(np.ascontiguousarray(ow[:, 1024 * h:1024 * (h + 1)].T))

    for nm, wkey, bkey in [("q", "L1_attn_q_w", "L1_attn_q_b"),
                           ("k", "L1_attn_k_w", "L1_attn_k_b")]:
        w = np.asarray(inputs[wkey])
        d[f"l1_{nm}wT"] = _panels(np.ascontiguousarray(w[1024 * h:1024 * (h + 1), :].T))
        bias = np.asarray(inputs[bkey])
        d[f"l1_{nm}b"] = np.ascontiguousarray(bias[1024 * h:1024 * (h + 1)].reshape(8, 128).T)
    vw = np.asarray(inputs["L1_attn_v_w"])
    d["l1_vwT"] = np.ascontiguousarray(vw[1024 * h:1024 * (h + 1), :].T)
    d["l1_vb"] = np.ascontiguousarray(
        np.asarray(inputs["L1_attn_v_b"])[1024 * h:1024 * (h + 1)][None, :])
    ow1 = np.asarray(inputs["L1_attn_o_w"])
    d["l1_owT"] = _panels(np.ascontiguousarray(ow1[:, 1024 * h:1024 * (h + 1)].T))
    d["l1_ob"] = np.ascontiguousarray(np.asarray(inputs["L1_attn_o_b"]).reshape(16, 128).T)

    d["guT"] = _panels(np.ascontiguousarray(np.asarray(inputs["L1_ffn_w_gate_up"]).T))
    d["downRB"] = _rowblocks(np.ascontiguousarray(np.asarray(inputs["L1_ffn_w_down"]).T))

    d.update({k: np.ascontiguousarray(v) for k, v in prep_common_consts().items()})
    return d



class Shared:

    def __init__(self, ctx):
        p = ctx.plan
        self.ones_f = ctx.sb("c_ones_f", [128, 1], F32)
        self.ones_b = ctx.sb("c_ones_b", [128, 1], BF16)
        self.ones_row_f = ctx.sb("c_onesr_f", [1, 128], F32)
        o1 = p.op("vector", lambda v: v.memset(self.ones_f[:], 1.0))
        o2 = p.op("vector", lambda v: v.memset(self.ones_b[:], 1.0))
        o3 = p.op("vector", lambda v: v.memset(self.ones_row_f[:], 1.0))
        self.ready = [o1, o2, o3]


def emit_ln(ctx, sh, xin_dram, out_sb, tag, compose_inner, eps=1e-5,
            out_dram=None, deps=()):
    p = ctx.plan
    ps = ctx.ps
    TT = 512
    NT = T // TT
    scope = ctx.scope()
    scope.__enter__()
    xtile = [ctx.sb(f"{tag}_xt{i}", [128, NCH * TT], F32) for i in range(2)]
    sq = [ctx.sb(f"{tag}_sq{i}", [128, TT], BF16) for i in range(2)]
    rows = ctx.sb(f"{tag}_rows", [1, 3 * TT], F32)
    srow = ctx.sb(f"{tag}_srow", [1, T], F32)
    msrow = ctx.sb(f"{tag}_msrow", [1, T], F32)
    tmpf = ctx.sb(f"{tag}_tmpf", [128, TT], F32)

    tile_done = []
    prev_apply0 = {}
    mm2_hist = {}
    fix1_hist = {}
    st_ops = []
    for tt in range(NT):
        xb = xtile[tt % 2]
        war = [tile_done[tt - 2]] if tt >= 2 else []
        lds = []
        for c in range(NCH):
            lds.append(p.dma("sync", f"{tag}x{tt % 2}",
                             lambda e, c=c, tt=tt, xb=xb: e.dma_start(
                                 out=xb[:, c * TT:(c + 1) * TT],
                                 in_=xin_dram[c * 128:(c + 1) * 128,
                                              tt * TT:(tt + 1) * TT]),
                             deps=list(deps) + war))
        mm1l = mm2l = None
        for c in range(NCH):
            sqd = [lds[c]]
            if (tt, c - 2) in mm2_hist:
                sqd.append(mm2_hist[(tt, c - 2)])
            elif tt > 0 and c < 2:
                sqd.append(mm2_hist[(tt - 1, NCH - 2 + c)])
            sqo = p.op("scalar", lambda e, c=c, xb=xb, s=sq[c % 2]: e.activation(
                s[:], xb[:, c * TT:(c + 1) * TT], AF.Square), deps=sqd)
            d1 = [lds[c], sh.ready[0]]
            if c == 0 and tt > 0:
                d1.append(fix1_hist[tt - 1])
            mm1l = p.op("tensor", lambda e, c=c, xb=xb: e.matmul(
                ps[0][0:1, :], sh.ones_f[:], xb[:, c * TT:(c + 1) * TT],
                start=(c == 0), stop=(c == NCH - 1)), deps=d1)
            d2 = [sqo, sh.ready[1]]
            if c == 0 and tt > 0:
                d2.append(fix1_hist[tt - 1])
            mm2l = p.op("tensor", lambda e, c=c, s=sq[c % 2]: e.matmul(
                ps[1][0:1, :], sh.ones_b[:], s[:],
                start=(c == 0), stop=(c == NCH - 1)), deps=d2)
            mm2_hist[(tt, c)] = mm2l

        m_ = rows[0:1, 0:TT]
        va_ = rows[0:1, TT:2 * TT]
        tp_ = rows[0:1, 2 * TT:3 * TT]
        fa = p.op("vector", lambda v: (
            v.tensor_scalar(m_, ps[0][0:1, :], 1.0 / D, None, ALU.mult),
            v.tensor_scalar(va_, ps[1][0:1, :], 1.0 / D, None, ALU.mult))[-1],
            deps=[mm1l, mm2l])
        fbb = p.op("vector", lambda v: v.tensor_tensor(tp_, m_, m_, ALU.mult),
                   hard=[fa])
        fcc = p.op("vector", lambda v: v.tensor_tensor(va_, va_, tp_, ALU.subtract),
                   hard=[fbb])
        if compose_inner:
            f1 = p.op("vector", lambda v: v.tensor_scalar(
                va_, va_, 1.0 + 1e-6, 1e-11, ALU.mult, ALU.add), hard=[fcc])
        else:
            f1 = p.op("vector", lambda v: v.tensor_scalar(
                va_, va_, eps, None, ALU.add), hard=[fcc])
        fix1_hist[tt] = f1
        sq1 = p.op("scalar", lambda e: e.activation(
            rows[0:1, 2 * TT:3 * TT], rows[0:1, TT:2 * TT], AF.Sqrt), deps=[f1])
        f2a = p.op("vector", lambda v, tt=tt: v.reciprocal(
            srow[0:1, tt * TT:(tt + 1) * TT], rows[0:1, 2 * TT:3 * TT]), deps=[sq1])
        f2 = p.op("vector", lambda v, tt=tt: v.tensor_tensor(
            msrow[0:1, tt * TT:(tt + 1) * TT], rows[0:1, 0:TT],
            srow[0:1, tt * TT:(tt + 1) * TT], ALU.mult), hard=[f2a])

        bwar = [tile_done[tt - 1]] if tt > 0 else []
        bs = p.op("tensor", lambda e, tt=tt: e.matmul(
            ps[2][:, :], sh.ones_row_f[:], srow[0:1, tt * TT:(tt + 1) * TT],
            start=True, stop=True), deps=[f2, sh.ready[2]] + bwar)
        bms = p.op("tensor", lambda e, tt=tt: e.matmul(
            ps[3][:, :], sh.ones_row_f[:], msrow[0:1, tt * TT:(tt + 1) * TT],
            start=True, stop=True), deps=[f2] + bwar)

        last_apply = None
        for c in range(NCH):
            def apply(v, c=c, tt=tt, xb=xb):
                v.tensor_tensor(tmpf[:], xb[:, c * TT:(c + 1) * TT], ps[2][:, :], ALU.mult)
                return v.tensor_tensor(
                    out_sb[:, c * T + tt * TT: c * T + (tt + 1) * TT],
                    tmpf[:], ps[3][:, :], ALU.subtract)
            ap_op = p.op("vector", apply, deps=([bs, bms] if c == 0 else []))
            if c == 0:
                prev_apply0[tt] = ap_op
            if out_dram is not None:
                st_ops.append(p.dma("sync", f"{tag}st",
                                    lambda e, c=c, tt=tt: e.dma_start(
                                        out=out_dram[c * 128:(c + 1) * 128,
                                                     tt * TT:(tt + 1) * TT],
                                        in_=out_sb[:, c * T + tt * TT: c * T + (tt + 1) * TT]),
                                    deps=[ap_op]))
            last_apply = ap_op
        tile_done.append(last_apply)
    scope.__exit__(None, None, None)
    tail = [tile_done[-1]] + st_ops[-2:]
    return dict(tile_done=tile_done, st_ops=st_ops, tail=tail)


def emit_l0ffn_prep(ctx, sh, wfcT, wprojT, tprojRB, tag):
    p = ctx.plan
    ps = ctx.ps
    thr_fc = ctx.sb(f"{tag}_thrfc", [128, 1], F32)
    thr_pj = ctx.sb(f"{tag}_thrpj", [128, 1], F32)
    gfc_col = ctx.sb(f"{tag}_gfccol", [128, 1], F32)
    gph = ctx.sb(f"{tag}_gph", [1, 1], F32)
    rs_row = ctx.sb(f"{tag}_rsrow", [1, D], F32)
    scope = ctx.scope()
    scope.__enter__()
    gbuf = [ctx.sb(f"{tag}_gb{i}", [128, 4096], F32) for i in range(2)]
    gcols = ctx.sb(f"{tag}_gcols", [128, 100], F32)
    rows = ctx.sb(f"{tag}_grows", [1, 16], F32)

    segs = []
    for i in range(64):
        segs.append((lambda i=i: wfcT[i].rearrange("(c p) o -> p c o", p=128), 16, "fc"))
    for i in range(16):
        segs.append((lambda i=i: wprojT[i].rearrange("(c p) o -> p c o", p=128)[:, 0:32, :], 32, "pj"))
        segs.append((lambda i=i: wprojT[i].rearrange("(c p) o -> p c o", p=128)[:, 32:64, :], 32, "pj"))
    red_ops = []
    fc_slots, pj_slots = [], []
    for si, (apf, nch, kind) in enumerate(segs):
        b = gbuf[si % 2]
        war = [red_ops[si - 2]] if si >= 2 else []
        ld = p.dma("sync", f"{tag}g{si % 2}",
                   lambda e, apf=apf, b=b, nch=nch: e.dma_start(
                       out=b.ap().rearrange("p (c o) -> p c o", o=128)[:, 0:nch, :],
                       in_=apf()),
                   deps=war)
        ro = p.op("vector", lambda v, b=b, nch=nch, sl=si: v.tensor_reduce(
            gcols[:, sl:sl + 1], b[:, 0:nch * 128], AX.X, ALU.add,
            apply_absolute_value=True), deps=[ld])
        red_ops.append(ro)
        (fc_slots if kind == "fc" else pj_slots).append(si)

    def red2(v):
        v.tensor_reduce(gcols[:, 96:97], gcols[:, fc_slots[0]:fc_slots[-1] + 1],
                        AX.X, ALU.add)
        return v.tensor_reduce(gcols[:, 97:98], gcols[:, pj_slots[0]:pj_slots[-1] + 1],
                               AX.X, ALU.add)
    r2 = p.op("vector", red2, hard=[red_ops[-1], red_ops[-2]])
    tot_fc = p.op("tensor", lambda e: e.matmul(
        ps[0][0:1, 0:1], gcols[:, 96:97], sh.ones_f[:], start=True, stop=True),
        deps=[r2, sh.ready[0]])
    tot_pj = p.op("tensor", lambda e: e.matmul(
        ps[4][0:1, 0:1], gcols[:, 97:98], sh.ones_f[:], start=True, stop=True),
        deps=[r2])

    gfa = p.op("vector", lambda v: (
        v.tensor_scalar(rows[0:1, 0:1], ps[0][0:1, 0:1], 1.0 / (8192.0 * 2048.0), None, ALU.mult),
        v.tensor_scalar(rows[0:1, 2:3], ps[4][0:1, 0:1], 1.0 / (8192.0 * 2048.0), None, ALU.mult))[-1],
        deps=[tot_fc, tot_pj])
    gf = p.op("vector", lambda v: (
        v.tensor_scalar(rows[0:1, 1:2], rows[0:1, 0:1], 1e-6, 0.5, ALU.add, ALU.mult),
        v.tensor_scalar(rows[0:1, 3:4], rows[0:1, 2:3], 1e-6, 0.5, ALU.add, ALU.mult),
        v.tensor_scalar(gph[0:1, :], rows[0:1, 2:3], 0.5, None, ALU.mult))[-1],
        hard=[gfa])
    bc = p.op("tensor", lambda e: e.matmul(
        ps[1][:, 0:4], sh.ones_row_f[:], rows[0:1, 0:4], start=True, stop=True),
        deps=[gf, sh.ready[2]])

    def copycols(v):
        v.tensor_copy(gfc_col[:], ps[1][:, 0:1])
        v.tensor_copy(thr_fc[:], ps[1][:, 1:2])
        return v.tensor_copy(thr_pj[:], ps[1][:, 3:4])
    cc = p.op("vector", copycols, deps=[bc])

    pfp = [ctx.sb(f"{tag}_pf{i}", [128, 4096], F32) for i in range(2)]
    ptr = [ctx.sb(f"{tag}_pt{i}", [128, 4096], BF16) for i in range(2)]
    psg = [ctx.sb(f"{tag}_psg{i}", [128, 4096], BF16) for i in range(2)]
    prev = {}
    rs_evacs = []
    last_spill = {0: None, 1: None}
    seg_idx = 0
    for pi in range(16):
        last_mm = None
        for hh in range(2):
            bi = seg_idx % 2
            pm2 = prev.get(seg_idx - 2)
            ld = p.dma("sync", f"{tag}p{bi}",
                       lambda e, pi=pi, hh=hh, bi=bi: e.dma_start(
                           out=pfp[bi].ap().rearrange("p (c o) -> p c o", o=128),
                           in_=wprojT[pi].rearrange("(c p) o -> p c o", p=128)[:, hh * 32:(hh + 1) * 32, :]),
                       deps=[pm2["tn"]] if pm2 else [])
            mkp = p.op("vector", lambda v, bi=bi: v.tensor_scalar(
                ptr[bi][:], pfp[bi][:], thr_pj[:], 0.0, ALU.subtract, ALU.is_gt),
                deps=[ld, cc] + ([pm2["last_t"], pm2["last_sp"]] if pm2 else []))
            mkn = p.op("vector", lambda v, bi=bi: v.tensor_scalar(
                psg[bi][:], pfp[bi][:], thr_pj[:], 0.0, ALU.add, ALU.is_lt),
                deps=[ld] + ([pm2["tn"]] if pm2 else []))
            tn = p.op("vector", lambda v, bi=bi: v.tensor_tensor(
                ptr[bi][:], ptr[bi][:], psg[bi][:], ALU.subtract), deps=[])
            sp = None
            for c in range(32):
                kc = hh * 32 + c
                mmd = []
                if c == 0:
                    mmd = [tn, sh.ready[1]]
                    if kc == 0 and rs_evacs:
                        mmd.append(rs_evacs[-1])
                last_mm = p.op("tensor", lambda e, bi=bi, c=c, kc=kc: e.matmul(
                    ps[2][0:1, 0:128], sh.ones_b[:], ptr[bi][:, c * 128:(c + 1) * 128],
                    start=(kc == 0), stop=(kc == 63)), deps=mmd)
                sp = p.dma("gpsimd", f"{tag}sp{bi}",
                           lambda e, bi=bi, c=c, pi=pi, kc=kc: e.dma_start(
                               out=tprojRB[kc, :, pi * 128:(pi + 1) * 128],
                               in_=ptr[bi][:, c * 128:(c + 1) * 128]),
                           deps=[tn])
            prev[seg_idx] = dict(tn=tn, last_t=last_mm, last_sp=sp)
            last_spill[bi] = sp
            seg_idx += 1
        ev = p.op("vector", lambda v, pi=pi: v.tensor_copy(
            rs_row[0:1, pi * 128:(pi + 1) * 128], ps[2][0:1, 0:128]),
            deps=[last_mm])
        rs_evacs.append(ev)

    scope.__exit__(None, None, None)
    tail = [cc, rs_evacs[-1], last_spill[0], last_spill[1],
            prev[30]["last_t"], prev[31]["last_t"],
            prev[30]["tn"], prev[31]["tn"]]
    return dict(thr_fc=thr_fc, thr_pj=thr_pj, gfc_col=gfc_col, gph=gph,
                rs_row=rs_row, cc=cc, rs_evacs=rs_evacs,
                spill_done=[last_spill[0], last_spill[1]], tail=tail)


def emit_l0ffn_main(ctx, sh, prep, xn_sb, xn_ready, xin, xout, wfcT, tprojRB,
                    h_dram, tag):
    p = ctx.plan
    ps = ctx.ps
    TT = 512
    NT = T // TT

    wfp = [ctx.sb(f"{tag}_wf{i}", [128, NCH * 128], F32) for i in range(2)]
    ftr = [ctx.sb(f"{tag}_ft{i}", [128, NCH * 128], BF16) for i in range(2)]
    fsg = [ctx.sb(f"{tag}_fs{i}", [128, NCH * 128], BF16) for i in range(2)]
    hst = [ctx.sb(f"{tag}_hs{i}", [128, TT], BF16) for i in range(2)]
    prevA = {}
    last_st_h = {}
    last_gelu_bank = {4: None, 5: None}
    for oc in range(64):
        bi = oc % 2
        pm2 = prevA.get(oc - 2)
        ld = p.dma("sync", f"{tag}wf{bi}",
                   lambda e, oc=oc, bi=bi: e.dma_start(
                       out=wfp[bi].ap().rearrange("p (c o) -> p c o", o=128),
                       in_=wfcT[oc].rearrange("(c p) o -> p c o", p=128)),
                   deps=[pm2["tn"]] if pm2 else [])
        mkp = p.op("vector", lambda v, bi=bi: v.tensor_scalar(
            ftr[bi][:], wfp[bi][:], prep["thr_fc"][:], 0.0, ALU.subtract, ALU.is_gt),
            deps=[ld, prep["cc"]] + ([pm2["last_mm"]] if pm2 else []))
        mkn = p.op("vector", lambda v, bi=bi: v.tensor_scalar(
            fsg[bi][:], wfp[bi][:], prep["thr_fc"][:], 0.0, ALU.add, ALU.is_lt),
            deps=[ld] + ([pm2["tn"]] if pm2 else []))
        tn = p.op("vector", lambda v, bi=bi: v.tensor_tensor(
            ftr[bi][:], ftr[bi][:], fsg[bi][:], ALU.subtract), deps=[])
        last_mm_all = None
        for tt in range(NT):
            pb = 4 + tt % 2
            mmd0 = [tn, xn_ready[tt]]
            if last_gelu_bank[pb] is not None:
                mmd0.append(last_gelu_bank[pb])
            last_mm = None
            for c in range(NCH):
                last_mm = p.op("tensor", lambda e, c=c, tt=tt, bi=bi, pb=pb: e.matmul(
                    ps[pb][:, :], ftr[bi][:, c * 128:(c + 1) * 128],
                    xn_sb[:, c * T + tt * TT: c * T + (tt + 1) * TT],
                    start=(c == 0), stop=(c == NCH - 1)),
                    deps=(mmd0 if c == 0 else []))
            gl_d = [last_mm, prep["cc"]]
            if (oc - 1, tt) in last_st_h:
                gl_d.append(last_st_h[(oc - 1, tt)])
            gl = p.op("scalar", lambda e, tt=tt, pb=pb: e.activation(
                hst[tt % 2][:], ps[pb][:, :], AF.Gelu, scale=prep["gfc_col"][:]),
                deps=gl_d)
            last_gelu_bank[pb] = gl
            sth = p.dma("sync", f"{tag}h{tt % 2}",
                        lambda e, oc=oc, tt=tt: e.dma_start(
                            out=h_dram[oc * 128:(oc + 1) * 128, tt * TT:(tt + 1) * TT],
                            in_=hst[tt % 2][:]),
                        deps=[gl])
            last_st_h[(oc, tt)] = sth
            last_mm_all = last_mm
        prevA[oc] = dict(last_mm=last_mm_all, tn=tn)
    h_done = [last_st_h[(63, tt)] for tt in range(NT)] + \
             [last_st_h[(62, tt)] for tt in range(NT)]

    hb = [ctx.sb(f"{tag}_hb{i}", [128, TT], BF16) for i in range(2)]
    hsc = [ctx.sb(f"{tag}_hc{i}", [128, TT], BF16) for i in range(2)]
    hsq = [ctx.sb(f"{tag}_hq{i}", [128, TT], BF16) for i in range(2)]
    wrb = [ctx.sb(f"{tag}_wr{i}", [128, 1024], BF16) for i in range(2)]
    bbf = ctx.sb(f"{tag}_bb", [128, TT], BF16)
    rowsB = ctx.sb(f"{tag}_rB", [1, 4 * TT], F32)
    augrow = ctx.sb(f"{tag}_aug", [1, TT], F32)
    xf = [ctx.sb(f"{tag}_xf{i}", [128, TT], F32) for i in range(2)]
    res = [ctx.sb(f"{tag}_rs{i}", [128, TT], F32) for i in range(2)]

    final_sts = []
    last_bank_reader = {j: None for j in range(8)}
    last_bb = None
    last_fix = None
    hb_reader = {0: None, 1: None}
    ep_hist = {}
    stx_hist = {}
    wrb_reader = {0: None, 1: None}
    for tb in range(NT):
        mm1 = mm2 = None
        sq_reader = {0: None, 1: None}
        for kc in range(64):
            bi = kc % 2
            lddeps = []
            if tb == 0 and kc < 2:
                lddeps += h_done
            if hb_reader[bi] is not None:
                lddeps += (hb_reader[bi] if isinstance(hb_reader[bi], list)
                           else [hb_reader[bi]])
            ldh = p.dma("sync", f"{tag}hb{bi}",
                        lambda e, kc=kc, tb=tb, bi=bi: e.dma_start(
                            out=hb[bi][:],
                            in_=h_dram[kc * 128:(kc + 1) * 128, tb * TT:(tb + 1) * TT]),
                        deps=lddeps)
            sqd = [ldh]
            if sq_reader[bi] is not None:
                sqd.append(sq_reader[bi])
            sqo = p.op("scalar", lambda e, bi=bi: e.activation(
                hsq[bi][:], hb[bi][:], AF.Square), deps=sqd)
            d1 = [ldh, sh.ready[1]]
            d2 = [sqo]
            if kc == 0 and last_fix is not None:
                d1.append(last_fix)
                d2.append(last_fix)
            mm1 = p.op("tensor", lambda e, kc=kc, bi=bi: e.matmul(
                ps[0][0:1, :], sh.ones_b[:], hb[bi][:],
                start=(kc == 0), stop=(kc == 63)), deps=d1)
            mm2 = p.op("tensor", lambda e, kc=kc, bi=bi: e.matmul(
                ps[1][0:1, :], sh.ones_b[:], hsq[bi][:],
                start=(kc == 0), stop=(kc == 63)), deps=d2)
            sq_reader[bi] = mm2
            hb_reader[bi] = [mm1, sqo]

        mB_ = rowsB[0:1, 0:TT]
        vaB_ = rowsB[0:1, TT:2 * TT]
        tpB_ = rowsB[0:1, 2 * TT:3 * TT]
        s2B_ = rowsB[0:1, 3 * TT:4 * TT]
        fba = p.op("vector", lambda v: (
            v.tensor_scalar(mB_, ps[0][0:1, :], 1.0 / 8192.0, None, ALU.mult),
            v.tensor_scalar(vaB_, ps[1][0:1, :], 1.0 / 8192.0, None, ALU.mult))[-1],
            deps=[mm1, mm2])
        fbb = p.op("vector", lambda v: v.tensor_tensor(tpB_, mB_, mB_, ALU.mult),
                   hard=[fba])
        fbc = p.op("vector", lambda v: v.tensor_tensor(vaB_, vaB_, tpB_, ALU.subtract),
                   hard=[fbb])
        fb = p.op("vector", lambda v: v.tensor_scalar(vaB_, vaB_, 1e-6, None, ALU.add),
                  hard=[fbc])
        sqB = p.op("scalar", lambda e: e.activation(
            tpB_, vaB_, AF.Sqrt), deps=[fb])
        f2b1 = p.op("vector", lambda v: v.reciprocal(s2B_, tpB_), deps=[sqB])
        f2b2 = p.op("vector", lambda v: v.tensor_scalar(
            s2B_, s2B_, prep["gph"][0:1, 0:1], None, ALU.mult), hard=[f2b1])
        f2b3 = p.op("vector", lambda v: v.tensor_tensor(
            augrow[0:1, :], s2B_, mB_, ALU.mult), hard=[f2b2])
        f2b = p.op("vector", lambda v: v.tensor_scalar(
            augrow[0:1, :], augrow[0:1, :], -1.0, None, ALU.mult), hard=[f2b3])
        last_fix = f2b
        bcb_d = [f2b, sh.ready[2]]
        if last_bb is not None:
            bcb_d.append(last_bb)
        if tb == 0:
            bcb_d += prep["rs_evacs"][-1:]
        bcb = p.op("tensor", lambda e: e.matmul(
            ps[2][:, :], sh.ones_row_f[:], rowsB[0:1, 3 * TT:4 * TT],
            start=True, stop=True), deps=bcb_d)
        bbo = p.op("vector", lambda v: v.tensor_copy(bbf[:], ps[2][:, :]), deps=[bcb])
        last_bb = bbo

        for og in range(2):
            scale_ops = {}
            last_grp_mm = None
            for kc in range(64):
                bi = kc % 2
                lddeps = [f2b] if (og == 0 and kc < 2) else []
                if hb_reader[bi] is not None:
                    lddeps += (hb_reader[bi] if isinstance(hb_reader[bi], list)
                               else [hb_reader[bi]])
                ldh = p.dma("sync", f"{tag}hb{bi}",
                            lambda e, kc=kc, tb=tb, bi=bi: e.dma_start(
                                out=hb[bi][:],
                                in_=h_dram[kc * 128:(kc + 1) * 128, tb * TT:(tb + 1) * TT]),
                            deps=lddeps)
                scd = [ldh, bbo]
                if kc >= 2 and scale_ops.get(kc - 2) is not None:
                    scd.append(scale_ops[kc - 2]["reader"])
                sc = p.op("vector", lambda v, bi=bi: v.tensor_tensor(
                    hsc[bi][:], hb[bi][:], bbf[:], ALU.mult), deps=scd)
                hb_reader[bi] = sc
                ldw_deps = []
                if wrb_reader[bi] is not None:
                    ldw_deps.append(wrb_reader[bi])
                if tb == 0 and og == 0 and kc < 2:
                    ldw_deps += prep["spill_done"]
                ldw = p.dma("gpsimd", f"{tag}wr{bi}",
                            lambda e, kc=kc, og=og, bi=bi: e.dma_start(
                                out=wrb[bi][:],
                                in_=tprojRB[kc, :, og * 1024:(og + 1) * 1024]),
                            deps=ldw_deps)
                lm = None
                for j in range(8):
                    oc = og * 8 + j
                    if kc == 0:
                        augd = [f2b, sh.ready[2]]
                        if last_bank_reader[j] is not None:
                            augd.append(last_bank_reader[j])
                        if tb == 0:
                            augd += h_done
                        p.op("tensor", lambda e, j=j, oc=oc: e.matmul(
                            ps[j][:, :], prep["rs_row"][0:1, oc * 128:(oc + 1) * 128],
                            augrow[0:1, :], start=True, stop=False), deps=augd)
                    lm = p.op("tensor", lambda e, j=j, kc=kc, bi=bi: e.matmul(
                        ps[j][:, :], wrb[bi][:, j * 128:(j + 1) * 128], hsc[bi][:],
                        start=False, stop=(kc == 63)),
                        deps=([sc, ldw] if j == 0 else []))
                scale_ops[kc] = dict(op=sc, reader=lm)
                wrb_reader[bi] = lm
                last_grp_mm = lm
            for j in range(8):
                oc = og * 8 + j
                bi = j % 2
                ldx_d = [ep_hist[bi]] if ep_hist.get(bi) is not None else []
                ldx = p.dma("sync", f"{tag}xi{bi}",
                            lambda e, oc=oc, tb=tb, bi=bi: e.dma_start(
                                out=xf[bi][:],
                                in_=xin[oc * 128:(oc + 1) * 128, tb * TT:(tb + 1) * TT]),
                            deps=ldx_d)
                ep_d = [last_grp_mm, ldx]
                if stx_hist.get(bi) is not None:
                    ep_d.append(stx_hist[bi])
                ep = p.op("vector", lambda v, j=j, bi=bi: v.tensor_tensor(
                    res[bi][:], ps[j][:, :], xf[bi][:], ALU.add), deps=ep_d)
                ep_hist[bi] = ep
                last_bank_reader[j] = ep
                stx = p.dma("sync", f"{tag}xo{bi}",
                            lambda e, oc=oc, tb=tb, bi=bi: e.dma_start(
                                out=xout[oc * 128:(oc + 1) * 128, tb * TT:(tb + 1) * TT],
                                in_=res[bi][:]),
                            deps=[ep])
                stx_hist[bi] = stx
                final_sts.append(stx)
    return final_sts



def emit_l1ffn(ctx, sh, u_sb, u_ready, xin, xout, guT, downRB, h_dram, tag):
    p = ctx.plan
    ps = ctx.ps
    TT = 512
    NT = T // TT

    gp = [ctx.sb(f"{tag}_gp{i}", [128, NCH * 128], BF16) for i in range(2)]
    up = [ctx.sb(f"{tag}_up{i}", [128, NCH * 128], BF16) for i in range(2)]
    sil = ctx.sb(f"{tag}_sil", [128, TT], F32)
    hst = [ctx.sb(f"{tag}_hs{i}", [128, TT], BF16) for i in range(2)]
    prevA = {}
    last_st_h = {}
    last_sil = None
    last_stt_bank = {4: None, 5: None, 6: None, 7: None}
    for oc in range(64):
        bi = oc % 2
        pm2 = prevA.get(oc - 2)
        ldg = p.dma("gpsimd", f"{tag}wg{bi}",
                    lambda e, oc=oc, bi=bi: e.dma_start(
                        out=gp[bi].ap().rearrange("p (c o) -> p c o", o=128),
                        in_=guT[oc].rearrange("(c p) o -> p c o", p=128)),
                    deps=[pm2["last_g"]] if pm2 else [])
        ldu = p.dma("gpsimd", f"{tag}wu{bi}",
                    lambda e, oc=oc, bi=bi: e.dma_start(
                        out=up[bi].ap().rearrange("p (c o) -> p c o", o=128),
                        in_=guT[64 + oc].rearrange("(c p) o -> p c o", p=128)),
                    deps=[pm2["last_u"]] if pm2 else [])
        lg = lu = None
        for tt in range(NT):
            pbg = 4 + tt % 2
            pbu = 6 + tt % 2
            mmdg = [ldg, u_ready[tt]]
            if last_stt_bank[pbg] is not None:
                mmdg.append(last_stt_bank[pbg])
            mmdu = [ldu, u_ready[tt]]
            if last_stt_bank[pbu] is not None:
                mmdu.append(last_stt_bank[pbu])
            for c in range(NCH):
                lg = p.op("tensor", lambda e, c=c, tt=tt, bi=bi, pbg=pbg: e.matmul(
                    ps[pbg][:, :], gp[bi][:, c * 128:(c + 1) * 128],
                    u_sb[:, c * T + tt * TT: c * T + (tt + 1) * TT],
                    start=(c == 0), stop=(c == NCH - 1)),
                    deps=(mmdg if c == 0 else []))
            for c in range(NCH):
                lu = p.op("tensor", lambda e, c=c, tt=tt, bi=bi, pbu=pbu: e.matmul(
                    ps[pbu][:, :], up[bi][:, c * 128:(c + 1) * 128],
                    u_sb[:, c * T + tt * TT: c * T + (tt + 1) * TT],
                    start=(c == 0), stop=(c == NCH - 1)),
                    deps=(mmdu if c == 0 else []))
            sl_d = [lg]
            if last_sil is not None:
                sl_d.append(last_sil)
            sl = p.op("scalar", lambda e, pbg=pbg: e.activation(
                sil[:], ps[pbg][:, :], AF.Silu), deps=sl_d)
            stt_d = [sl, lu]
            if (oc - 1, tt) in last_st_h:
                stt_d.append(last_st_h[(oc - 1, tt)])
            so = p.op("vector", lambda v, tt=tt, pbu=pbu: v.scalar_tensor_tensor(
                hst[tt % 2][:], sil[:], 0.5, ps[pbu][:, :], ALU.mult, ALU.mult),
                deps=stt_d)
            last_sil = so
            last_stt_bank[pbg] = so
            last_stt_bank[pbu] = so
            sth = p.dma("sync", f"{tag}h{tt % 2}",
                        lambda e, oc=oc, tt=tt: e.dma_start(
                            out=h_dram[oc * 128:(oc + 1) * 128, tt * TT:(tt + 1) * TT],
                            in_=hst[tt % 2][:]),
                        deps=[so])
            last_st_h[(oc, tt)] = sth
        prevA[oc] = dict(last_g=lg, last_u=lu)
    h_done = [last_st_h[(63, tt)] for tt in range(NT)] + \
             [last_st_h[(62, tt)] for tt in range(NT)]

    hb = [ctx.sb(f"{tag}_hb{i}", [128, TT], BF16) for i in range(2)]
    wrb = [ctx.sb(f"{tag}_wr{i}", [128, 1024], BF16) for i in range(2)]
    xf = [ctx.sb(f"{tag}_xf{i}", [128, TT], F32) for i in range(2)]
    res = [ctx.sb(f"{tag}_rs{i}", [128, TT], F32) for i in range(2)]

    final_sts = []
    last_bank_reader = {j: None for j in range(8)}
    hb_reader = {0: None, 1: None}
    wrb_reader = {0: None, 1: None}
    ep_hist = {}
    stx_hist = {}
    for tb in range(NT):
        for og in range(2):
            last_grp_mm = None
            for kc in range(64):
                bi = kc % 2
                lddeps = list(h_done) if (tb == 0 and og == 0 and kc < 2) else []
                if hb_reader[bi] is not None:
                    lddeps.append(hb_reader[bi])
                ldh = p.dma("sync", f"{tag}hb{bi}",
                            lambda e, kc=kc, tb=tb, bi=bi: e.dma_start(
                                out=hb[bi][:],
                                in_=h_dram[kc * 128:(kc + 1) * 128, tb * TT:(tb + 1) * TT]),
                            deps=lddeps)
                ldw_deps = []
                if wrb_reader[bi] is not None:
                    ldw_deps.append(wrb_reader[bi])
                ldw = p.dma("gpsimd", f"{tag}wr{bi}",
                            lambda e, kc=kc, og=og, bi=bi: e.dma_start(
                                out=wrb[bi][:],
                                in_=downRB[kc, :, og * 1024:(og + 1) * 1024]),
                            deps=ldw_deps)
                lm = None
                for j in range(8):
                    mmd = []
                    if kc == 0:
                        mmd = [ldh, ldw]
                        if last_bank_reader[j] is not None:
                            mmd.append(last_bank_reader[j])
                    elif j == 0:
                        mmd = [ldh, ldw]
                    lm = p.op("tensor", lambda e, j=j, kc=kc, bi=bi: e.matmul(
                        ps[j][:, :], wrb[bi][:, j * 128:(j + 1) * 128], hb[bi][:],
                        start=(kc == 0), stop=(kc == 63)), deps=mmd)
                hb_reader[bi] = lm
                wrb_reader[bi] = lm
                last_grp_mm = lm
            for j in range(8):
                oc = og * 8 + j
                bi = j % 2
                ldx_d = [ep_hist[bi]] if ep_hist.get(bi) is not None else []
                ldx = p.dma("sync", f"{tag}xi{bi}",
                            lambda e, oc=oc, tb=tb, bi=bi: e.dma_start(
                                out=xf[bi][:],
                                in_=xin[oc * 128:(oc + 1) * 128, tb * TT:(tb + 1) * TT]),
                            deps=ldx_d)
                ep_d = [last_grp_mm, ldx]
                if stx_hist.get(bi) is not None:
                    ep_d.append(stx_hist[bi])
                ep = p.op("vector", lambda v, j=j, bi=bi: v.tensor_tensor(
                    res[bi][:], ps[j][:, :], xf[bi][:], ALU.add), deps=ep_d)
                ep_hist[bi] = ep
                last_bank_reader[j] = ep
                stx = p.dma("sync", f"{tag}xo{bi}",
                            lambda e, oc=oc, tb=tb, bi=bi: e.dma_start(
                                out=xout[oc * 128:(oc + 1) * 128, tb * TT:(tb + 1) * TT],
                                in_=res[bi][:]),
                            deps=[ep])
                stx_hist[bi] = stx
                final_sts.append(stx)
    return final_sts


def build_l1ffn_launch():
    nc = bass.Bass("TRN2", target_bir_lowering=False)
    stack = contextlib.ExitStack()
    xin = nc.dram_tensor("x", [D, T], F32, kind="ExternalInput").ap()
    guT = nc.dram_tensor("guT", [128, 2048, 128], F32, kind="ExternalInput").ap()
    downRB = nc.dram_tensor("downRB", [64, 128, 2048], F32, kind="ExternalInput").ap()
    xout = nc.dram_tensor("xo", [D, T], F32, kind="ExternalOutput").ap()
    h_dram = nc.dram_tensor("h_scr", [8192, T], BF16, kind="Internal").ap()

    ctx = Ctx(nc, stack)
    sh = Shared(ctx)
    u_sb = ctx.sb("u", [128, NCH * T], BF16)
    ln = emit_ln(ctx, sh, xin, u_sb, "ln", compose_inner=False, eps=1e-5)
    ctx.plan.fence(ln["tail"])
    sts = emit_l1ffn(ctx, sh, u_sb, ln["tile_done"], xin, xout, guT, downRB,
                     h_dram, "sw")
    p = ctx.plan
    p.op("gpsimd", lambda g: g.nop(), deps=sts[-16:])
    p.emit()
    stack.close()
    return nc



def emit_ugather(ctx, sh, xin, u_all, cc_u, tag, deps=()):
    p = ctx.plan
    with ctx.scope():
        u_sb = ctx.sb(f"{tag}_usb", [128, NCH * T], BF16)
        ln = emit_ln(ctx, sh, xin, u_sb, f"{tag}ln", compose_inner=False,
                     eps=1e-5, out_dram=cc_u, deps=deps)
    gather = p.op("gpsimd", lambda g: g.collective_compute(
        "AllGather", ALU.bypass, replica_groups=PAIRS,
        ins=[cc_u.opt()], outs=[u_all.opt()]),
        deps=[ln["st_ops"][-1], ln["tile_done"][-1]])
    p.fence(ln["tail"])
    return dict(gather=gather)


def emit_oproj_exchange(ctx, sh, o_dram, o_ready, owT, cc2_in, cc2_out,
                        xin, xout, ob_cols, tag):
    p = ctx.plan
    ps = ctx.ps
    TT = 512
    final_sts = []
    with ctx.scope():
        own = [ctx.sb(f"{tag}_ow{i}", [128, 8 * 128], BF16) for i in range(16)]
        osb = [ctx.sb(f"{tag}_os{i}", [128, 8 * 512], BF16) for i in range(2)]
        cst = [ctx.sb(f"{tag}_cs{i}", [128, TT], F32) for i in range(2)]
        ow_ready = []
        for oc in range(16):
            ow_ready.append(p.dma("gpsimd", f"{tag}owp{oc % 2}",
                lambda e, oc=oc: e.dma_start(
                    out=own[oc].ap().rearrange("p (c o) -> p c o", o=128),
                    in_=owT[oc].rearrange("(c p) o -> p c o", p=128)),
                deps=()))
        osb_reader = {0: None, 1: None}
        cst_reader = {0: None, 1: None}
        evac_bank = {6: None, 7: None}
        cc_sts = []
        for rt in range(8):
            r, tt = rt // 4, rt % 4
            bi = rt % 2
            lds = []
            lddeps = list(o_ready) if rt < 2 else []
            if osb_reader[bi] is not None:
                lddeps.append(osb_reader[bi])
            for c in range(8):
                lds.append(p.dma("sync", f"{tag}ol{bi}",
                    lambda e, c=c, r=r, tt=tt, bi=bi: e.dma_start(
                        out=osb[bi][:, c * 512:(c + 1) * 512],
                        in_=o_dram[c * 128:(c + 1) * 128,
                                   r * 2048 + tt * TT: r * 2048 + (tt + 1) * TT]),
                    deps=lddeps if c < 2 else []))
            last_mm = None
            for oc in range(16):
                pb = 6 + oc % 2
                mmd = [lds[-1], ow_ready[oc]]
                if evac_bank[pb] is not None:
                    mmd.append(evac_bank[pb])
                for c in range(8):
                    last_mm = p.op("tensor", lambda e, oc=oc, c=c, bi=bi, pb=pb: e.matmul(
                        ps[pb][:, :], own[oc][:, c * 128:(c + 1) * 128],
                        osb[bi][:, c * 512:(c + 1) * 512],
                        start=(c == 0), stop=(c == 7)),
                        deps=(mmd if c == 0 else []))
                cb = oc % 2
                evd = [last_mm]
                if cst_reader[cb] is not None:
                    evd.append(cst_reader[cb])
                ev = p.op("vector", lambda v, cb=cb, pb=pb: v.tensor_copy(
                    cst[cb][:], ps[pb][:, :]), deps=evd)
                evac_bank[pb] = ev
                stc = p.dma("sync", f"{tag}cs{cb}",
                    lambda e, oc=oc, r=r, tt=tt, cb=cb: e.dma_start(
                        out=cc2_in[r * 2048 + oc * 128: r * 2048 + (oc + 1) * 128,
                                   tt * TT:(tt + 1) * TT],
                        in_=cst[cb][:]),
                    deps=[ev])
                cst_reader[cb] = stc
                cc_sts.append(stc)
            osb_reader[bi] = last_mm
        rs = p.op("gpsimd", lambda g: g.collective_compute(
            "ReduceScatter", ALU.add, replica_groups=PAIRS,
            ins=[cc2_in.opt()], outs=[cc2_out.opt()]),
            deps=[cc_sts[-1], cc_sts[-2]])
        xf = [ctx.sb(f"{tag}_xf{i}", [128, TT], F32) for i in range(2)]
        ccb = [ctx.sb(f"{tag}_cc{i}", [128, TT], F32) for i in range(2)]
        res = [ctx.sb(f"{tag}_rs{i}", [128, TT], F32) for i in range(2)]
        ep_hist = {0: None, 1: None}
        stx_hist = {0: None, 1: None}
        for oc in range(16):
            for tt in range(4):
                bi = (oc * 4 + tt) % 2
                lddeps = [rs] if oc == 0 and tt < 2 else []
                if ep_hist[bi] is not None:
                    lddeps.append(ep_hist[bi])
                ldx = p.dma("sync", f"{tag}xi{bi}",
                    lambda e, oc=oc, tt=tt, bi=bi: e.dma_start(
                        out=xf[bi][:],
                        in_=xin[oc * 128:(oc + 1) * 128, tt * TT:(tt + 1) * TT]),
                    deps=lddeps)
                ldc = p.dma("sync", f"{tag}ci{bi}",
                    lambda e, oc=oc, tt=tt, bi=bi: e.dma_start(
                        out=ccb[bi][:],
                        in_=cc2_out[oc * 128:(oc + 1) * 128, tt * TT:(tt + 1) * TT]),
                    deps=[rs] if oc == 0 and tt < 2 else [])
                epd = [ldx, ldc]
                if stx_hist[bi] is not None:
                    epd.append(stx_hist[bi])
                if ob_cols is not None:
                    ep = p.op("vector", lambda v, bi=bi, oc=oc: v.scalar_tensor_tensor(
                        res[bi][:], ccb[bi][:], ob_cols[:, oc:oc + 1], xf[bi][:],
                        ALU.add, ALU.add), deps=epd)
                else:
                    ep = p.op("vector", lambda v, bi=bi: v.tensor_tensor(
                        res[bi][:], ccb[bi][:], xf[bi][:], ALU.add), deps=epd)
                ep_hist[bi] = ep
                stx = p.dma("sync", f"{tag}xo{bi}",
                    lambda e, oc=oc, tt=tt, bi=bi: e.dma_start(
                        out=xout[oc * 128:(oc + 1) * 128, tt * TT:(tt + 1) * TT],
                        in_=res[bi][:]),
                    deps=[ep])
                stx_hist[bi] = stx
                final_sts.append(stx)
    tail = final_sts[-2:] + [ep_hist[0], ep_hist[1]]
    p.fence(tail)
    return final_sts



def emit_l0attn(ctx, sh, con, xin, xout, W, scratch, tag):
    p = ctx.plan
    ps = ctx.ps
    TT = 512
    u_all, qp, kp, vdr = scratch["u_all"], scratch["qp"], scratch["kp"], scratch["vdr"]
    gather = scratch["gather_op"]

    o_sts = []
    with ctx.scope():
        qw = [ctx.sb(f"{tag}_qw{i}", [128, NCH * 128], BF16) for i in range(8)]
        krw = ctx.sb(f"{tag}_krw", [128, NCH * 128], BF16)
        dnw = [ctx.sb(f"{tag}_dnw{i}", [128, NCH * 128], BF16) for i in range(2)]
        upkw = ctx.sb(f"{tag}_upkw", [128, 2 * 128], BF16)
        upvw = ctx.sb(f"{tag}_upvw", [128, 2 * 256], BF16)
        wlds = []
        for pn in range(8):
            wlds.append(p.dma("gpsimd", f"{tag}wq{pn % 2}", lambda e, pn=pn: e.dma_start(
                out=qw[pn].ap().rearrange("p (c o) -> p c o", o=128),
                in_=W["qwT"][pn].rearrange("(c p) o -> p c o", p=128)), deps=()))
        wlds.append(p.dma("gpsimd", f"{tag}wk", lambda e: e.dma_start(
            out=krw.ap().rearrange("p (c o) -> p c o", o=128),
            in_=W["krT"][0].rearrange("(c p) o -> p c o", p=128)), deps=()))
        for j in range(2):
            wlds.append(p.dma("gpsimd", f"{tag}wd{j}", lambda e, j=j: e.dma_start(
                out=dnw[j].ap().rearrange("p (c o) -> p c o", o=128),
                in_=W["dnT"][j].rearrange("(c p) o -> p c o", p=128)), deps=()))
        wlds.append(p.dma("gpsimd", f"{tag}wuk", lambda e: e.dma_start(
            out=upkw.ap().rearrange("p (c o) -> p c o", o=128),
            in_=W["upkT"].rearrange("(c p) o -> p c o", p=128)), deps=()))
        wlds.append(p.dma("gpsimd", f"{tag}wuv", lambda e: e.dma_start(
            out=upvw.ap().rearrange("p (c o) -> p c o", o=256),
            in_=W["upvT"].rearrange("(c p) o -> p c o", p=128)), deps=()))
        wld_done = wlds[-6:] + wlds[:2]

        ub = [ctx.sb(f"{tag}_ub{i}", [128, NCH * TT], BF16) for i in range(2)]
        qraw = ctx.sb(f"{tag}_qraw", [128, TT], BF16)
        qsq = ctx.sb(f"{tag}_qsq", [128, TT], BF16)
        rope1 = ctx.sb(f"{tag}_rp1", [128, TT], BF16)
        ropeo = ctx.sb(f"{tag}_rpo", [128, TT], BF16)
        ckv = ctx.sb(f"{tag}_ckv", [128, 2 * TT], BF16)
        vstage = [ctx.sb(f"{tag}_vst{i}", [128, 260], BF16) for i in range(2)]
        knorm = ctx.sb(f"{tag}_knorm", [4, TFULL], F32)
        maxkq = ctx.sb(f"{tag}_maxkq", [16, 1], F32)
        qn_sqrt = ctx.sb(f"{tag}_qns", [16, TT], F32)
        aug16 = ctx.sb(f"{tag}_aug16", [16, TT], BF16)
        onesrow_b = ctx.sb(f"{tag}_onerow", [1, TFULL], BF16)
        rowtmp = ctx.sb(f"{tag}_rowt", [4, 4], F32)

        oro = p.op("vector", lambda v: v.memset(onesrow_b[:], 1.0))
        ones_sts = [p.dma("sync", f"{tag}kon", lambda e, g=g: e.dma_start(
            out=kp[g, 64:65, :], in_=onesrow_b[:]), deps=[oro]) for g in range(4)]

        ub_reader = {0: None, 1: None}
        kr_sts, kc_sts, v_sts = [], [], []
        last_knmm = None
        kn_ev = []
        prev_rope = None
        prev_vst = {0: None, 1: None}
        prev_vev = None
        prev_ckv = None
        prev_kq = {}
        for rt in range(8):
            r, tt = rt // 4, rt % 4
            bi = rt % 2
            seg = slice(r * 2048 + tt * TT, r * 2048 + (tt + 1) * TT)
            lds = []
            for c in range(NCH):
                dd = [gather] if rt < 2 and c < 2 else []
                if ub_reader[bi] is not None and c == 0:
                    dd.append(ub_reader[bi])
                lds.append(p.dma("sync", f"{tag}u{bi}",
                    lambda e, c=c, r=r, tt=tt, bi=bi: e.dma_start(
                        out=ub[bi][:, c * TT:(c + 1) * TT],
                        in_=u_all[r * 2048 + c * 128: r * 2048 + (c + 1) * 128,
                                  tt * TT:(tt + 1) * TT]),
                    deps=dd))
            mmd = [lds[-1]] + (wld_done if rt == 0 else [])
            if prev_rope is not None:
                mmd.append(prev_rope)
            mm = None
            for c in range(NCH):
                mm = p.op("tensor", lambda e, c=c, bi=bi: e.matmul(
                    ps[0][:, :], krw[:, c * 128:(c + 1) * 128],
                    ub[bi][:, c * TT:(c + 1) * TT],
                    start=(c == 0), stop=(c == NCH - 1)), deps=(mmd if c == 0 else []))
            ev = p.op("vector", lambda v: v.tensor_copy(qraw[:], ps[0][:, :]),
                      deps=[mm, prev_kq.get("krperm")] if prev_kq.get("krperm") else [mm])
            pm = p.op("tensor", lambda e: e.matmul(
                ps[1][:, :], con["permT"][:], qraw[:], start=True, stop=True),
                deps=[ev] + ([prev_kq["krrope2"]] if "krrope2" in prev_kq else []))
            prev_kq["krperm"] = pm
            r1 = p.op("vector", lambda v, seg=seg: v.tensor_tensor(
                rope1[:], qraw[:], con["cos_k"][:, seg], ALU.mult), deps=[])
            r2 = p.op("vector", lambda v, seg=seg: v.tensor_tensor(
                ropeo[:], ps[1][:, :], con["sin_k"][:, seg], ALU.mult),
                deps=[pm] + kc_sts[-4:] + ([prev_kq["sq2"]] if "sq2" in prev_kq else []))
            r3 = p.op("vector", lambda v: v.tensor_tensor(
                ropeo[:], ropeo[:], rope1[:], ALU.add), deps=[])
            prev_kq["krrope2"] = r3
            sq1 = p.op("scalar", lambda e: e.activation(qsq[:], ropeo[:], AF.Square),
                       deps=[r3] + ([prev_kq["knmm1"]] if "knmm1" in prev_kq else []))
            knmm1 = p.op("tensor", lambda e: e.matmul(
                ps[2][0:4, 0:TT], con["H4"][:], qsq[:], start=True, stop=False),
                deps=[sq1] + (kn_ev[-1:] if kn_ev else []))
            prev_kq["knmm1"] = knmm1
            for g in range(4):
                kr_sts.append(p.dma("sync", f"{tag}krs{g % 2}",
                    lambda e, g=g, seg=seg: e.dma_start(
                        out=kp[g, 0:32, seg], in_=ropeo[32 * g:32 * (g + 1), :]),
                    deps=[r3]))
            prev_rope = r3

            mmc = None
            for j in range(2):
                for c in range(NCH):
                    mmc = p.op("tensor", lambda e, j=j, c=c, bi=bi: e.matmul(
                        ps[3][:, :],
                        dnw[j][:, c * 128:(c + 1) * 128],
                        ub[bi][:, c * TT:(c + 1) * TT],
                        start=(c == 0), stop=(c == NCH - 1)),
                        deps=([lds[-1], prev_ckv] if c == 0 and prev_ckv else ([lds[-1]] if c == 0 else [])))
                evc = p.op("vector", lambda v, j=j: v.tensor_copy(
                    ckv[:, j * TT:(j + 1) * TT], ps[3][:, :]), deps=[mmc])
                prev_ckv = evc
            mk = None
            for j in range(2):
                mk = p.op("tensor", lambda e, j=j: e.matmul(
                    ps[4][:, :], upkw[:, j * 128:(j + 1) * 128],
                    ckv[:, j * TT:(j + 1) * TT],
                    start=(j == 0), stop=(j == 1)),
                    deps=[prev_ckv] + ([prev_kq["kcev"]] if "kcev" in prev_kq else []))
            evk = p.op("vector", lambda v: v.tensor_copy(ropeo[:], ps[4][:, :]),
                       deps=[mk, sq1] + kr_sts[-4:])
            prev_kq["kcev"] = evk
            sq2 = p.op("scalar", lambda e: e.activation(qsq[:], ropeo[:], AF.Square),
                       deps=[evk, knmm1])
            prev_kq["sq2"] = sq2
            knmm2 = p.op("tensor", lambda e: e.matmul(
                ps[2][0:4, 0:TT], con["H4"][:], qsq[:], start=False, stop=True),
                deps=[sq2])
            kne = p.op("vector", lambda v, seg=seg: v.tensor_copy(
                knorm[:, seg], ps[2][0:4, 0:TT]), deps=[knmm2])
            kn_ev.append(kne)
            for g in range(4):
                kc_sts.append(p.dma("sync", f"{tag}kcs{g % 2}",
                    lambda e, g=g, seg=seg: e.dma_start(
                        out=kp[g, 32:64, seg], in_=ropeo[32 * g:32 * (g + 1), :]),
                    deps=[evk]))
            for sub in range(4):
                vb = sub % 2
                mv = None
                for j in range(2):
                    mv = p.op("tensor", lambda e, j=j, sub=sub: e.matmul(
                        ps[5][0:128, 0:256],
                        ckv[:, j * TT + sub * 128: j * TT + (sub + 1) * 128],
                        upvw[:, j * 256:(j + 1) * 256],
                        start=(j == 0), stop=(j == 1)),
                        deps=([prev_ckv] if sub == 0 and j == 0 else []) +
                             ([prev_vev] if j == 0 and prev_vev is not None else []) +
                             ([prev_vst[vb]] if j == 0 and prev_vst[vb] else []))
                def vevac(v, vb=vb):
                    for g in range(4):
                        v.tensor_copy(vstage[vb][:, 65 * g:65 * g + 64],
                                      ps[5][:, 64 * g:64 * (g + 1)])
                    return v.memset(vstage[vb][:, 64:260:65], 1.0)
                vev = p.op("vector", vevac, deps=[mv])
                prev_vst[vb] = vev
                prev_vev = vev
                tglob = (r * 2048 + tt * TT + sub * 128) // 128
                v_sts.append(p.dma("sync", f"{tag}vs{vb}",
                    lambda e, tglob=tglob, vb=vb: e.dma_start(
                        out=vdr[tglob, :, :], in_=vstage[vb][:]), deps=[vev]))
            ub_reader[bi] = mmc

        mx = p.op("vector", lambda v: v.tensor_reduce(
            rowtmp[0:4, 0:1], knorm[:, :], AX.X, ALU.max), deps=kn_ev)
        sqx = p.op("scalar", lambda e: e.activation(
            rowtmp[0:4, 1:2], rowtmp[0:4, 0:1], AF.Sqrt), deps=[mx])
        mi = p.op("tensor", lambda e: e.matmul(
            ps[2][0:16, 0:1], con["IND"][:], rowtmp[0:4, 1:2], start=True, stop=True),
            deps=[sqx, kn_ev[-1]])
        mie = p.op("vector", lambda v: v.tensor_copy(maxkq[:], ps[2][0:16, 0:1]),
                   deps=[mi])

        q_sts = []
        prev_q = {}
        qn_last = None
        for rt in range(8):
            r, tt = rt // 4, rt % 4
            bi = rt % 2
            seg = slice(r * 2048 + tt * TT, r * 2048 + (tt + 1) * TT)
            lds = []
            for c in range(NCH):
                dd = []
                if ub_reader[bi] is not None and c == 0:
                    dd.append(ub_reader[bi])
                lds.append(p.dma("sync", f"{tag}u{bi}",
                    lambda e, c=c, r=r, tt=tt, bi=bi: e.dma_start(
                        out=ub[bi][:, c * TT:(c + 1) * TT],
                        in_=u_all[r * 2048 + c * 128: r * 2048 + (c + 1) * 128,
                                  tt * TT:(tt + 1) * TT]),
                    deps=dd))
            last_pn_mm = None
            qnmm = None
            for pn in range(8):
                mmd = [lds[-1]]
                if "ev" in prev_q:
                    mmd.append(prev_q["ev"])
                mm = None
                for c in range(NCH):
                    mm = p.op("tensor", lambda e, c=c, bi=bi, pn=pn: e.matmul(
                        ps[0][:, :], qw[pn][:, c * 128:(c + 1) * 128],
                        ub[bi][:, c * TT:(c + 1) * TT],
                        start=(c == 0), stop=(c == NCH - 1)), deps=(mmd if c == 0 else []))
                ev = p.op("vector", lambda v: v.tensor_copy(qraw[:], ps[0][:, :]),
                          deps=[mm] + ([prev_q["pm"]] if "pm" in prev_q else []) +
                               ([prev_q["sq"]] if "sq" in prev_q else []))
                prev_q["ev"] = ev
                pm = p.op("tensor", lambda e: e.matmul(
                    ps[1][:, :], con["permT"][:], qraw[:], start=True, stop=True),
                    deps=[ev] + ([prev_q["r2"]] if "r2" in prev_q else []))
                prev_q["pm"] = pm
                r1 = p.op("vector", lambda v, seg=seg: v.tensor_tensor(
                    rope1[:], qraw[:], con["cos_q"][:, seg], ALU.mult), deps=[])
                r2 = p.op("vector", lambda v, seg=seg: v.tensor_tensor(
                    ropeo[:], ps[1][:, :], con["sin_q"][:, seg], ALU.mult),
                    deps=[pm] + q_sts[-2:])
                prev_q["r2"] = r2
                r3 = p.op("vector", lambda v: v.tensor_tensor(
                    ropeo[:], ropeo[:], rope1[:], ALU.add), deps=[])
                sq = p.op("scalar", lambda e: e.activation(qsq[:], qraw[:], AF.Square),
                          deps=[r1, ev] + ([qnmm] if qnmm else []))
                prev_q["sq"] = sq
                qnd = [sq]
                if pn == 0:
                    qnd.append(qn_last if qn_last is not None else mie)
                qnmm = p.op("tensor", lambda e, pn=pn: e.matmul(
                    ps[2][0:16, 0:TT], con["Hq"][pn][:], qsq[:],
                    start=(pn == 0), stop=(pn == 7)), deps=qnd)
                for half in range(2):
                    qh = 2 * pn + half
                    q_sts.append(p.dma("sync", f"{tag}qs{qh % 2}",
                        lambda e, qh=qh, half=half, seg=seg: e.dma_start(
                            out=qp[qh, 0:64, seg],
                            in_=ropeo[64 * half:64 * (half + 1), :]),
                        deps=[r3]))
            sqa = p.op("scalar", lambda e: e.activation(
                qn_sqrt[:], ps[2][0:16, 0:TT], AF.Sqrt),
                deps=[qnmm] + ([qn_last] if qn_last else []))
            aug = p.op("vector", lambda v: v.tensor_scalar(
                aug16[:], qn_sqrt[:], maxkq[:], None, ALU.mult),
                deps=[sqa, mie])
            qn_last = aug
            q_sts.append(p.dma("sync", f"{tag}qa",
                lambda e, seg=seg: e.dma_start(
                    out=qp[:, 64, seg], in_=aug16[:]), deps=[aug]))
            ub_reader[bi] = mm
        proj_done = [q_sts[-1], q_sts[-2], q_sts[-3],
                     kr_sts[-1], kr_sts[-2], kc_sts[-1], kc_sts[-2],
                     v_sts[-1], v_sts[-2]] + ones_sts

    p.fence([proj_done[0], proj_done[1]] + proj_done)
    o_dram = scratch["odr"]
    with ctx.scope():
        vsb = ctx.sb(f"{tag}_vsb", [128, 32 * 260], BF16)
        ksb = ctx.sb(f"{tag}_ksb", [65, TFULL], BF16)
        qsb = [ctx.sb(f"{tag}_qsb{i}", [65, TT], BF16) for i in range(2)]
        pbuf = [ctx.sb(f"{tag}_pb{i}", [128, TT], BF16) for i in range(3)]
        rsb = ctx.sb(f"{tag}_rsb", [1, TT], F32)
        rbs = ctx.sb(f"{tag}_rbs", [64, TT], F32)
        ostage = [ctx.sb(f"{tag}_ost{i}", [64, TT], BF16) for i in range(2)]
        vlds = [p.dma("sync", f"{tag}vl{ch % 2}", lambda e, ch=ch: e.dma_start(
            out=vsb[:, ch * 260:(ch + 1) * 260], in_=vdr[ch, :, :]),
            deps=proj_done if ch < 2 else []) for ch in range(32)]
        ksb_reader = None
        qsb_reader = {0: None, 1: None}
        pb_reader = {0: None, 1: None, 2: None}
        sband = {0: None, 1: None}
        avband = None
        rb_user = None
        prev_cp = None
        ost_reader = {0: None, 1: None}
        for g in range(4):
            kld = p.dma("sync", f"{tag}kl", lambda e, g=g: e.dma_start(
                out=ksb[:], in_=kp[g, :, :]),
                deps=([ksb_reader] if ksb_reader else []) +
                     (proj_done if g == 0 else []))
            for lq in range(4):
                qh = 4 * g + lq
                for qt in range(8):
                    qb = qt % 2
                    qld = p.dma("sync", f"{tag}ql{qb}", lambda e, qh=qh, qt=qt, qb=qb: e.dma_start(
                        out=qsb[qb][:], in_=qp[qh, :, qt * TT:(qt + 1) * TT]),
                        deps=[qsb_reader[qb]] if qsb_reader[qb] else [])
                    nkb = 4 * (qt + 1)
                    avmm = None
                    for kb in range(nkb):
                        sb = kb % 2
                        smd = [kld, qld]
                        if sband[sb] is not None:
                            smd.append(sband[sb])
                        smm = p.op("tensor", lambda e, kb=kb, sb=sb, qb=qb: e.matmul(
                            ps[sb][:, :], ksb[:, kb * 128:(kb + 1) * 128], qsb[qb][:],
                            start=True, stop=True), deps=smd)
                        pbN = kb % 3
                        exd = [smm]
                        if pb_reader[pbN] is not None:
                            exd.append(pb_reader[pbN])
                        ex = p.op("scalar", lambda e, sb=sb, pbN=pbN: e.activation(
                            pbuf[pbN][:], ps[sb][:, :], AF.Exp, scale=con["eighth"][:]),
                            deps=exd)
                        sband[sb] = ex
                        mdep = [ex]
                        if kb >= 4 * qt:
                            m = kb - 4 * qt
                            mk = p.op("vector", lambda v, m=m, pbN=pbN: v.tensor_tensor(
                                pbuf[pbN][:], pbuf[pbN][:], con["tri"][m][:], ALU.mult),
                                deps=[ex])
                            mdep = [mk]
                        avd = mdep + [vlds[kb]]
                        if kb == 0:
                            if avband is not None:
                                avd.append(avband)
                        avmm = p.op("tensor", lambda e, kb=kb, g=g, pbN=pbN: e.matmul(
                            ps[4][0:65, :], vsb[:, kb * 260 + 65 * g: kb * 260 + 65 * (g + 1)],
                            pbuf[pbN][:], start=(kb == 0), stop=(kb == nkb - 1)),
                            deps=avd)
                        pb_reader[pbN] = avmm
                    qsb_reader[qb] = avmm
                    rc = p.op("vector", lambda v: v.reciprocal(rsb[:], ps[4][64:65, :]),
                              deps=[avmm] + ([rb_user] if rb_user else []))
                    bc = p.op("tensor", lambda e: e.matmul(
                        ps[5][0:64, 0:TT], con["onesr64"][:], rsb[:],
                        start=True, stop=True), deps=[rc] + ([prev_cp] if prev_cp else []))
                    cp = p.op("vector", lambda v: v.tensor_copy(rbs[:], ps[5][0:64, 0:TT]),
                              deps=[bc])
                    prev_cp = cp
                    ob = qt % 2
                    mu_d = [cp]
                    if ost_reader[ob] is not None:
                        mu_d.append(ost_reader[ob])
                    mu = p.op("vector", lambda v, ob=ob: v.tensor_tensor(
                        ostage[ob][:], ps[4][0:64, 0:TT], rbs[:], ALU.mult), deps=mu_d)
                    rb_user = mu
                    avband = mu
                    st = p.dma("sync", f"{tag}os{ob}", lambda e, qh=qh, qt=qt, ob=ob: e.dma_start(
                        out=o_dram[qh * 64:(qh + 1) * 64, qt * TT:(qt + 1) * TT],
                        in_=ostage[ob][:]), deps=[mu])
                    ost_reader[ob] = st
                    o_sts.append(st)
            ksb_reader = avmm
    p.fence(o_sts[-2:])
    return o_sts



def setup_l0attn_consts(ctx, sh, nc):
    p = ctx.plan
    ins = {}
    for nm, shape in [("cos_q", [128, TFULL]), ("sin_q", [128, TFULL]),
                      ("cos_k", [128, TFULL]), ("sin_k", [128, TFULL]),
                      ("permT", [128, 128]), ("H4", [128, 4]),
                      ("Hq", [8, 128, 16]), ("IND", [4, 16]),
                      ("tri_mul", [4, 128, 512])]:
        ins[nm] = nc.dram_tensor(nm, shape, F32, kind="ExternalInput").ap()
    con = {}
    rdy = []
    for nm in ["cos_q", "sin_q", "cos_k", "sin_k"]:
        con[nm] = ctx.sb(f"c_{nm}", [128, TFULL], BF16)
        rdy.append(p.dma("gpsimd", "cst", lambda e, nm=nm: e.dma_start(
            out=con[nm][:], in_=ins[nm][:, :]), deps=()))
    con["permT"] = ctx.sb("c_permT", [128, 128], BF16)
    rdy.append(p.dma("gpsimd", "cst", lambda e: e.dma_start(
        out=con["permT"][:], in_=ins["permT"][:, :]), deps=()))
    con["H4"] = ctx.sb("c_H4", [128, 4], BF16)
    rdy.append(p.dma("gpsimd", "cst", lambda e: e.dma_start(
        out=con["H4"][:], in_=ins["H4"][:, :]), deps=()))
    con["Hq"] = []
    for pn in range(8):
        t = ctx.sb(f"c_Hq{pn}", [128, 16], BF16)
        con["Hq"].append(t)
        rdy.append(p.dma("gpsimd", "cst", lambda e, pn=pn, t=t: e.dma_start(
            out=t[:], in_=ins["Hq"][pn, :, :]), deps=()))
    con["IND"] = ctx.sb("c_IND", [4, 16], F32)
    rdy.append(p.dma("gpsimd", "cst", lambda e: e.dma_start(
        out=con["IND"][:], in_=ins["IND"][:, :]), deps=()))
    con["tri"] = []
    for m in range(4):
        t = ctx.sb(f"c_tri{m}", [128, 512], BF16)
        con["tri"].append(t)
        rdy.append(p.dma("gpsimd", "cst", lambda e, m=m, t=t: e.dma_start(
            out=t[:], in_=ins["tri_mul"][m, :, :]), deps=()))
    con["onesr64"] = ctx.sb("c_or64", [1, 64], F32)
    rdy.append(p.op("vector", lambda v: v.memset(con["onesr64"][:], 1.0)))
    con["eighth"] = ctx.sb("c_eighth", [128, 1], F32)
    rdy.append(p.op("vector", lambda v: v.memset(con["eighth"][:], 0.125)))
    return con, rdy


def build_l0attn_launch():
    nc = bass.Bass("TRN2", target_bir_lowering=False)
    stack = contextlib.ExitStack()
    xin = nc.dram_tensor("x", [D, T], F32, kind="ExternalInput").ap()
    W = dict(
        qwT=nc.dram_tensor("qwT", [8, 2048, 128], F32, kind="ExternalInput").ap(),
        krT=nc.dram_tensor("krT", [1, 2048, 128], F32, kind="ExternalInput").ap(),
        dnT=nc.dram_tensor("dnT", [2, 2048, 128], F32, kind="ExternalInput").ap(),
        upkT=nc.dram_tensor("upkT", [256, 128], F32, kind="ExternalInput").ap(),
        upvT=nc.dram_tensor("upvT", [256, 256], F32, kind="ExternalInput").ap(),
    )
    owT = nc.dram_tensor("owT", [16, 1024, 128], F32, kind="ExternalInput").ap()
    xout = nc.dram_tensor("xo", [D, T], F32, kind="ExternalOutput").ap()
    cc_u = nc.dram_tensor("cc_u", [D, T], BF16, kind="Internal").ap()
    u_all = nc.dram_tensor("u_all", [2 * D, T], BF16, kind="Internal").ap()
    dbg_kind = "ExternalOutput" if globals().get("_ATTN_DEBUG") else "Internal"
    qp = nc.dram_tensor("qp_scr", [16, 65, TFULL], BF16, kind=dbg_kind).ap()
    kp = nc.dram_tensor("kp_scr", [4, 65, TFULL], BF16, kind=dbg_kind).ap()
    vdr = nc.dram_tensor("v_scr", [32, 128, 260], BF16, kind=dbg_kind).ap()
    odr = nc.dram_tensor("o_scr", [1024, TFULL], BF16, kind=dbg_kind).ap()
    cc2_in = nc.dram_tensor("cc2_in", [2 * D, T], F32, kind="Internal").ap()
    cc2_out = nc.dram_tensor("cc2_out", [D, T], F32, kind="Internal").ap()

    ctx = Ctx(nc, stack)
    sh = Shared(ctx)
    con, crdy = setup_l0attn_consts(ctx, sh, nc)
    ug = emit_ugather(ctx, sh, xin, u_all, cc_u, "ug")
    ctx.plan.fence(crdy)
    scratch = dict(u_all=u_all, qp=qp, kp=kp, vdr=vdr, odr=odr,
                   gather_op=ug["gather"])
    o_sts = emit_l0attn(ctx, sh, con, xin, xout, W, scratch, "a0")
    sts = emit_oproj_exchange(ctx, sh, odr, o_sts[-2:], owT, cc2_in, cc2_out,
                              xin, xout, None, "a0x")
    p = ctx.plan
    p.op("gpsimd", lambda g: g.nop(), deps=sts[-4:])
    p.emit()
    stack.close()
    return nc


def setup_l1attn_consts(ctx, sh, nc):
    p = ctx.plan
    ins = {}
    for nm, shape in [("iota8neg", [8, TFULL]), ("jcols", [128, 32]),
                      ("tri_add", [4, 128, 512]), ("e8T", [8, 8, 128]),
                      ("E8", [8, 128, 8])]:
        ins[nm] = nc.dram_tensor(nm, shape, F32, kind="ExternalInput").ap()
    vbi = nc.dram_tensor("vb", [1, 1024], F32, kind="ExternalInput").ap()
    con = {}
    rdy = []
    con["iota8neg"] = ctx.sb("c_io8", [8, TFULL], BF16)
    rdy.append(p.dma("gpsimd", "cst1", lambda e: e.dma_start(
        out=con["iota8neg"][:], in_=ins["iota8neg"][:, :]), deps=()))
    con["jcols"] = ctx.sb("c_jc", [128, 32], F32)
    rdy.append(p.dma("gpsimd", "cst1", lambda e: e.dma_start(
        out=con["jcols"][:], in_=ins["jcols"][:, :]), deps=()))
    con["triA"] = []
    for m in range(4):
        t = ctx.sb(f"c_triA{m}", [128, 512], F32)
        con["triA"].append(t)
        rdy.append(p.dma("gpsimd", "cst1", lambda e, m=m, t=t: e.dma_start(
            out=t[:], in_=ins["tri_add"][m, :, :]), deps=()))
    con["e8T"] = []
    for hh in range(8):
        t = ctx.sb(f"c_e8T{hh}", [8, 128], BF16)
        con["e8T"].append(t)
        rdy.append(p.dma("gpsimd", "cst1", lambda e, hh=hh, t=t: e.dma_start(
            out=t[:], in_=ins["e8T"][hh, :, :]), deps=()))
    con["E8"] = []
    for pn in range(8):
        t = ctx.sb(f"c_E8{pn}", [128, 8], BF16)
        con["E8"].append(t)
        rdy.append(p.dma("gpsimd", "cst1", lambda e, pn=pn, t=t: e.dma_start(
            out=t[:], in_=ins["E8"][pn, :, :]), deps=()))
    con["vb_row"] = ctx.sb("c_vbr", [1, 1024], F32)
    rdy.append(p.dma("gpsimd", "cst1", lambda e: e.dma_start(
        out=con["vb_row"][:], in_=vbi[:, :]), deps=()))
    con["sc1"] = ctx.sb("c_sc1", [128, 1], F32)
    rdy.append(p.op("vector", lambda v: v.memset(con["sc1"][:], 1.0 / math.sqrt(128.0))))
    con["onesr64"] = ctx.sb("c_or64b", [1, 128], F32)
    rdy.append(p.op("vector", lambda v: v.memset(con["onesr64"][:], 1.0)))
    return con, rdy


def emit_l1attn(ctx, sh, con, W, scratch, tag):
    p = ctx.plan
    ps = ctx.ps
    TT = 512
    u_all = scratch["u_all"]
    qdr, kdr, vdr, odr = scratch["qdr"], scratch["kdr"], scratch["vdr1"], scratch["odr"]
    gather = scratch["gather_op"]
    o_sts = []
    qn_sb = ctx.sb(f"{tag}_qn", [8, TFULL], BF16)
    kn_sb = ctx.sb(f"{tag}_kn", [8, TFULL], BF16)
    maxq8 = ctx.sb(f"{tag}_mq8", [8, 2], F32)
    Mrows = ctx.sb(f"{tag}_Mr", [8, TFULL], BF16)
    Mtmp = ctx.sb(f"{tag}_Mt", [8, TT], F32)
    Mtmp2 = ctx.sb(f"{tag}_Mt2", [8, TT], BF16)
    scope1 = ctx.scope()
    scope1.__enter__()
    if True:
        qw = [ctx.sb(f"{tag}_qw{i}", [128, NCH * 128], BF16) for i in range(8)]
        kw = [ctx.sb(f"{tag}_kw{i}", [128, NCH * 128], BF16) for i in range(8)]
        vw = ctx.sb(f"{tag}_vw", [128, NCH * 1024], BF16)
        qb = ctx.sb(f"{tag}_qb", [128, 8], F32)
        kb = ctx.sb(f"{tag}_kb", [128, 8], F32)
        vbb = ctx.sb(f"{tag}_vbb", [128, 1024], F32)
        wlds = []
        for pn in range(8):
            wlds.append(p.dma("gpsimd", f"{tag}wq{pn % 2}", lambda e, pn=pn: e.dma_start(
                out=qw[pn].ap().rearrange("p (c o) -> p c o", o=128),
                in_=W["qwT"][pn].rearrange("(c p) o -> p c o", p=128)), deps=()))
            wlds.append(p.dma("gpsimd", f"{tag}wk{pn % 2}", lambda e, pn=pn: e.dma_start(
                out=kw[pn].ap().rearrange("p (c o) -> p c o", o=128),
                in_=W["kwT"][pn].rearrange("(c p) o -> p c o", p=128)), deps=()))
        wlds.append(p.dma("gpsimd", f"{tag}wv", lambda e: e.dma_start(
            out=vw.ap().rearrange("p (c o) -> p c o", o=1024),
            in_=W["vwT"].rearrange("(c p) o -> p c o", p=128)), deps=()))
        wlds.append(p.dma("gpsimd", f"{tag}wb", lambda e: e.dma_start(
            out=qb[:], in_=W["qb"][:, :]), deps=()))
        wlds.append(p.dma("gpsimd", f"{tag}wb", lambda e: e.dma_start(
            out=kb[:], in_=W["kb"][:, :]), deps=()))
        vb1 = p.op("tensor", lambda e: e.matmul(
            ps[6][:, :], con["onesr64"][0:1, :], con["vb_row"][0:1, 0:512],
            start=True, stop=True), deps=scratch["vb_deps"] + [sh.ready[2]])
        vb2 = p.op("tensor", lambda e: e.matmul(
            ps[7][:, :], con["onesr64"][0:1, :], con["vb_row"][0:1, 512:1024],
            start=True, stop=True), deps=scratch["vb_deps"])
        vbc = p.op("vector", lambda v: (
            v.tensor_copy(vbb[:, 0:512], ps[6][:, :]),
            v.tensor_copy(vbb[:, 512:1024], ps[7][:, :]))[-1], deps=[vb1, vb2])

        ub = [ctx.sb(f"{tag}_ub{i}", [128, NCH * TT], BF16) for i in range(2)]
        stage = [ctx.sb(f"{tag}_st{i}", [128, TT], BF16) for i in range(2)]
        sqb = [ctx.sb(f"{tag}_sq{i}", [128, TT], BF16) for i in range(2)]
        vstage = [ctx.sb(f"{tag}_vs{i}", [128, 1024], BF16) for i in range(2)]

        ub_reader = {0: None, 1: None}
        q_sts, k_sts, v_sts = [], [], []
        prev = {}
        nrm_ev = {"q": [], "k": []}
        for rt in range(8):
            r, tt = rt // 4, rt % 4
            bi = rt % 2
            seg = slice(r * 2048 + tt * TT, r * 2048 + (tt + 1) * TT)
            lds = []
            for c in range(NCH):
                dd = [gather] if rt < 2 and c < 2 else []
                if ub_reader[bi] is not None and c == 0:
                    dd.append(ub_reader[bi])
                lds.append(p.dma("sync", f"{tag}u{bi}",
                    lambda e, c=c, r=r, tt=tt, bi=bi: e.dma_start(
                        out=ub[bi][:, c * TT:(c + 1) * TT],
                        in_=u_all[r * 2048 + c * 128: r * 2048 + (c + 1) * 128,
                                  tt * TT:(tt + 1) * TT]),
                    deps=dd))
            last_mm = None
            for kind, wws, bcol, dst, sts, nbank in (
                    ("q", qw, qb, qdr, q_sts, 2), ("k", kw, kb, kdr, k_sts, 3)):
                pbk = 0 if kind == "q" else 1
                nmm = None
                for pn in range(8):
                    mmd = [lds[-1]] + (wlds[:2] if rt == 0 and pn == 0 else [])
                    if f"{kind}ev" in prev:
                        mmd.append(prev[f"{kind}ev"])
                    mm = None
                    for c in range(NCH):
                        mm = p.op("tensor", lambda e, c=c, bi=bi, pn=pn, wws=wws, pbk=pbk: e.matmul(
                            ps[pbk][:, :],
                            wws[pn][:, c * 128:(c + 1) * 128],
                            ub[bi][:, c * TT:(c + 1) * TT],
                            start=(c == 0), stop=(c == NCH - 1)),
                            deps=(mmd if c == 0 else []))
                    sb_ = stage[0 if kind == "q" else 1]
                    sq_ = sqb[0 if kind == "q" else 1]
                    evd = [mm]
                    if f"{kind}st" in prev:
                        evd.append(prev[f"{kind}st"])
                    if f"{kind}sq" in prev:
                        evd.append(prev[f"{kind}sq"])
                    ev = p.op("vector", lambda v, sb_=sb_, pn=pn, bcol=bcol, pbk=pbk: v.tensor_scalar(
                        sb_[:], ps[pbk][:, :],
                        bcol[:, pn:pn + 1], None, ALU.add), deps=evd)
                    prev[f"{kind}ev"] = ev
                    sqo = p.op("scalar", lambda e, sb_=sb_, sq_=sq_: e.activation(
                        sq_[:], sb_[:], AF.Square),
                        deps=[ev] + ([nmm] if nmm else []))
                    prev[f"{kind}sq"] = sqo
                    nd = [sqo]
                    if pn == 0:
                        if nrm_ev[kind]:
                            nd.append(nrm_ev[kind][-1])
                    nmm = p.op("tensor", lambda e, pn=pn, sq_=sq_, nbank=nbank: e.matmul(
                        ps[nbank][0:8, 0:TT], con["E8"][pn][:], sq_[:],
                        start=(pn == 0), stop=(pn == 7)), deps=nd)
                    st = p.dma("sync", f"{tag}{kind}s{pn % 2}",
                        lambda e, pn=pn, seg=seg, sb_=sb_, dst=dst: e.dma_start(
                            out=dst[pn * 128:(pn + 1) * 128, seg], in_=sb_[:]),
                        deps=[ev])
                    prev[f"{kind}st"] = st
                    sts.append(st)
                ne = p.op("vector", lambda v, seg=seg, nbank=nbank,
                          nsb=(qn_sb if kind == "q" else kn_sb): v.tensor_copy(
                    nsb[:, seg], ps[nbank][0:8, 0:TT]), deps=[nmm])
                nrm_ev[kind].append(ne)
            for sub in range(4):
                vbuf = sub % 2
                for half in range(2):
                    pv = 6 + half
                    mv = None
                    for c in range(NCH):
                        mvd = []
                        if c == 0:
                            mvd = [lds[-1], wlds[16]] if rt == 0 and sub == 0 else [lds[-1]]
                            if prev.get(f"vev{pv}") is not None:
                                mvd.append(prev[f"vev{pv}"])
                            if rt == 0 and sub == 0 and half == 0:
                                mvd.append(vbc)
                        mv = p.op("tensor", lambda e, c=c, sub=sub, half=half, bi=bi, pv=pv: e.matmul(
                            ps[pv][:, :],
                            ub[bi][:, c * TT + sub * 128: c * TT + (sub + 1) * 128],
                            vw[:, c * 1024 + half * 512: c * 1024 + (half + 1) * 512],
                            start=(c == 0), stop=(c == NCH - 1)),
                            deps=mvd)
                    vvd = [mv, vbc]
                    if prev.get(f"vst{vbuf}") is not None and half == 0:
                        vvd.append(prev[f"vst{vbuf}"])
                    vv = p.op("vector", lambda v, vbuf=vbuf, half=half, pv=pv: v.tensor_tensor(
                        vstage[vbuf][:, half * 512:(half + 1) * 512],
                        ps[pv][:, :], vbb[:, half * 512:(half + 1) * 512], ALU.add),
                        deps=vvd)
                    prev[f"vev{pv}"] = vv
                tglob = (r * 2048 + tt * TT + sub * 128) // 128
                vst = p.dma("sync", f"{tag}vs{vbuf}",
                    lambda e, tglob=tglob, vbuf=vbuf: e.dma_start(
                        out=vdr[tglob, :, :], in_=vstage[vbuf][:]),
                    deps=[prev["vev6"], prev["vev7"]])
                prev[f"vst{vbuf}"] = vst
                v_sts.append(vst)
            ub_reader[bi] = mv

        mx = p.op("vector", lambda v: v.tensor_reduce(
            maxq8[:, 0:1], qn_sb[:, :], AX.X, ALU.max), deps=nrm_ev["q"])
        sqx = p.op("scalar", lambda e: e.activation(
            maxq8[:, 1:2], maxq8[:, 0:1], AF.Sqrt), deps=[mx])
        ngx = p.op("vector", lambda v: v.tensor_scalar(
            maxq8[:, 1:2], maxq8[:, 1:2], -1.0, None, ALU.mult), hard=[sqx])
        mrows_done = []
        prev_mt = None
        for sgi in range(8):
            seg = slice(sgi * TT, (sgi + 1) * TT)
            s1 = p.op("scalar", lambda e, seg=seg: e.activation(
                Mtmp[:], kn_sb[:, seg], AF.Sqrt),
                deps=[ngx, nrm_ev["k"][-1]] + ([prev_mt] if prev_mt else []))
            s2 = p.op("vector", lambda v, seg=seg: v.tensor_scalar(
                Mtmp2[:], Mtmp[:], maxq8[:, 1:2], None, ALU.mult), deps=[s1])
            s3 = p.op("vector", lambda v, seg=seg: v.tensor_tensor(
                Mrows[:, seg], Mtmp2[:], con["iota8neg"][:, seg], ALU.add), hard=[s2])
            prev_mt = s3
            mrows_done.append(s3)

    scope1.__exit__(None, None, None)
    first_tail = [q_sts[-1], q_sts[-2], k_sts[-1], k_sts[-2],
                  v_sts[-1], v_sts[-2], mrows_done[-1],
                  ub_reader[0], ub_reader[1], nrm_ev["q"][-1], nrm_ev["k"][-1]]
    p.fence(first_tail)
    with ctx.scope():
        qtl = [ctx.sb(f"{tag}_qt{i}", [128, 128], BF16) for i in range(2)]
        ktl = [ctx.sb(f"{tag}_kt{i}", [128, TT], BF16) for i in range(2)]
        vtl = [ctx.sb(f"{tag}_vt{i}", [128, 128], BF16) for i in range(2)]
        pbuf = [ctx.sb(f"{tag}_pb{i}", [128, TT], BF16) for i in range(3)]
        rsb = ctx.sb(f"{tag}_rsb", [1, TT], F32)
        rbs = ctx.sb(f"{tag}_rbs", [128, TT], F32)
        ostage = [ctx.sb(f"{tag}_os{i}", [128, TT], BF16) for i in range(2)]
        qt_reader = {0: None, 1: None}
        kt_reader = {0: None, 1: None}
        vt_reader = {0: None, 1: None}
        pb_reader = {0: None, 1: None, 2: None}
        sband = {0: None, 1: None}
        av_user = None
        den_user = None
        prev_cp = None
        ost_reader = {0: None, 1: None}
        first_score_deps = [q_sts[-1], q_sts[-2], k_sts[-1], k_sts[-2],
                           v_sts[-1], v_sts[-2], mrows_done[-1]]
        for hh in range(8):
            for it in range(8):
                jb_lo = max(0, 4 * it - 6)
                jb_hi = 4 * it + 3
                kldd = list(first_score_deps) if hh == 0 and it == 0 else []
                if kt_reader[it % 2] is not None:
                    kldd.append(kt_reader[it % 2])
                kld = p.dma("sync", f"{tag}ktl{it % 2}",
                    lambda e, hh=hh, it=it: e.dma_start(
                        out=ktl[it % 2][:],
                        in_=kdr[hh * 128:(hh + 1) * 128, it * TT:(it + 1) * TT]),
                    deps=kldd)
                avmm = None
                denmm = None
                njb = jb_hi - jb_lo + 1
                for ji, jb in enumerate(range(jb_lo, jb_hi + 1)):
                    qbuf = ji % 2
                    qldd = [qt_reader[qbuf]] if qt_reader[qbuf] else []
                    if hh == 0 and it == 0 and ji < 2:
                        qldd += first_score_deps[:2]
                    qld = p.dma("sync", f"{tag}qtl{qbuf}",
                        lambda e, hh=hh, jb=jb, qbuf=qbuf: e.dma_start(
                            out=qtl[qbuf][:],
                            in_=qdr[hh * 128:(hh + 1) * 128, jb * 128:(jb + 1) * 128]),
                        deps=qldd)
                    vldd = [vt_reader[qbuf]] if vt_reader[qbuf] else []
                    vld = p.dma("sync", f"{tag}vtl{qbuf}",
                        lambda e, hh=hh, jb=jb, qbuf=qbuf: e.dma_start(
                            out=vtl[qbuf][:],
                            in_=vdr[jb, :, hh * 128:(hh + 1) * 128]),
                        deps=vldd)
                    sb_ = ji % 2
                    augd = [mrows_done[it]]
                    if sband[sb_] is not None:
                        augd.append(sband[sb_])
                    p.op("tensor", lambda e, hh=hh, it=it, sb_=sb_: e.matmul(
                        ps[sb_][:, :], con["e8T"][hh][:],
                        Mrows[0:8, it * TT:(it + 1) * TT],
                        start=True, stop=False), deps=augd)
                    smm = p.op("tensor", lambda e, sb_=sb_, qbuf=qbuf, it=it: e.matmul(
                        ps[sb_][:, :], qtl[qbuf][:], ktl[it % 2][:],
                        start=False, stop=True), deps=[qld, kld])
                    qt_reader[qbuf] = smm
                    exd_extra = []
                    if jb >= 4 * it:
                        m = jb - 4 * it
                        mk = p.op("vector", lambda v, m=m, sb_=sb_: v.tensor_tensor(
                            ps[sb_][:, :], ps[sb_][:, :], con["triA"][m][:], ALU.add),
                            deps=[smm])
                        exd_extra = [mk]
                    pbN = ji % 3
                    exd = [smm] + exd_extra
                    if pb_reader[pbN] is not None:
                        exd.append(pb_reader[pbN])
                    ex = p.op("scalar", lambda e, sb_=sb_, pbN=pbN, jb=jb: e.activation(
                        pbuf[pbN][:], ps[sb_][:, :], AF.Exp, scale=con["sc1"][:],
                        bias=con["jcols"][:, jb:jb + 1]), deps=exd)
                    sband[sb_] = ex
                    dend = [ex, sh.ready[1]]
                    if ji == 0 and den_user is not None:
                        dend.append(den_user)
                    denmm = p.op("tensor", lambda e, ji=ji, pbN=pbN, njb=njb: e.matmul(
                        ps[4][0:1, :], sh.ones_b[:], pbuf[pbN][:],
                        start=(ji == 0), stop=(ji == njb - 1)), deps=dend)
                    avd = [ex, vld]
                    if ji == 0 and av_user is not None:
                        avd.append(av_user)
                    avmm = p.op("tensor", lambda e, ji=ji, pbN=pbN, qbuf=qbuf, njb=njb: e.matmul(
                        ps[5][:, :], vtl[qbuf][:], pbuf[pbN][:],
                        start=(ji == 0), stop=(ji == njb - 1)), deps=avd)
                    vt_reader[qbuf] = avmm
                    pb_reader[pbN] = avmm
                kt_reader[it % 2] = smm
                rc = p.op("vector", lambda v: v.reciprocal(rsb[:], ps[4][0:1, :]),
                          deps=[denmm])
                den_user = rc
                bc = p.op("tensor", lambda e: e.matmul(
                    ps[3][:, :], con["onesr64"][0:1, :], rsb[:],
                    start=True, stop=True),
                    deps=[rc] + ([prev_cp] if prev_cp else []) +
                         (nrm_ev["k"][-1:] if hh == 0 and it == 0 else []))
                cp = p.op("vector", lambda v: v.tensor_copy(rbs[:], ps[3][:, :]),
                          deps=[bc])
                prev_cp = cp
                ob = it % 2
                mud = [cp]
                if ost_reader[ob] is not None:
                    mud.append(ost_reader[ob])
                mu = p.op("vector", lambda v, ob=ob: v.tensor_tensor(
                    ostage[ob][:], ps[5][:, :], rbs[:], ALU.mult), deps=mud)
                av_user = mu
                st = p.dma("sync", f"{tag}osx{ob}",
                    lambda e, hh=hh, it=it, ob=ob: e.dma_start(
                        out=odr[hh * 128:(hh + 1) * 128, it * TT:(it + 1) * TT],
                        in_=ostage[ob][:]), deps=[mu])
                ost_reader[ob] = st
                o_sts.append(st)
    p.fence(o_sts[-2:])
    return o_sts


def build_l1attn_launch():
    nc = bass.Bass("TRN2", target_bir_lowering=False)
    stack = contextlib.ExitStack()
    xin = nc.dram_tensor("x", [D, T], F32, kind="ExternalInput").ap()
    W = dict(
        qwT=nc.dram_tensor("qwT", [8, 2048, 128], F32, kind="ExternalInput").ap(),
        kwT=nc.dram_tensor("kwT", [8, 2048, 128], F32, kind="ExternalInput").ap(),
        vwT=nc.dram_tensor("vwT", [2048, 1024], F32, kind="ExternalInput").ap(),
        qb=nc.dram_tensor("qb", [128, 8], F32, kind="ExternalInput").ap(),
        kb=nc.dram_tensor("kb", [128, 8], F32, kind="ExternalInput").ap(),
    )
    owT = nc.dram_tensor("owT", [16, 1024, 128], F32, kind="ExternalInput").ap()
    obi = nc.dram_tensor("ob", [128, 16], F32, kind="ExternalInput").ap()
    xout = nc.dram_tensor("xo", [D, T], F32, kind="ExternalOutput").ap()
    cc_u = nc.dram_tensor("cc_u", [D, T], BF16, kind="Internal").ap()
    u_all = nc.dram_tensor("u_all", [2 * D, T], BF16, kind="Internal").ap()
    dbg_kind = "ExternalOutput" if globals().get("_ATTN_DEBUG") else "Internal"
    qdr = nc.dram_tensor("q_scr", [1024, TFULL], BF16, kind=dbg_kind).ap()
    kdr = nc.dram_tensor("k_scr", [1024, TFULL], BF16, kind=dbg_kind).ap()
    vdr1 = nc.dram_tensor("v_scr", [32, 128, 1024], BF16, kind=dbg_kind).ap()
    odr = nc.dram_tensor("o_scr", [1024, TFULL], BF16, kind=dbg_kind).ap()
    cc2_in = nc.dram_tensor("cc2_in", [2 * D, T], F32, kind="Internal").ap()
    cc2_out = nc.dram_tensor("cc2_out", [D, T], F32, kind="Internal").ap()

    ctx = Ctx(nc, stack)
    sh = Shared(ctx)
    con, crdy = setup_l1attn_consts(ctx, sh, nc)
    ob_cols = ctx.sb("ob_cols", [128, 16], F32)
    obl = ctx.plan.dma("gpsimd", "cst1", lambda e: e.dma_start(
        out=ob_cols[:], in_=obi[:, :]), deps=())
    ug = emit_ugather(ctx, sh, xin, u_all, cc_u, "ug")
    ctx.plan.fence(crdy + [obl])
    scratch = dict(u_all=u_all, qdr=qdr, kdr=kdr, vdr1=vdr1, odr=odr,
                   gather_op=ug["gather"], vb_deps=[crdy[-3]])
    o_sts = emit_l1attn(ctx, sh, con, W, scratch, "a1")
    sts = emit_oproj_exchange(ctx, sh, odr, o_sts[-2:], owT, cc2_in, cc2_out,
                              xin, xout, ob_cols, "a1x")
    p = ctx.plan
    p.op("gpsimd", lambda g: g.nop(), deps=sts[-4:])
    p.emit()
    stack.close()
    return nc


def build_l0ffn_launch(debug=False):
    nc = bass.Bass("TRN2", target_bir_lowering=False)
    stack = contextlib.ExitStack()
    xin = nc.dram_tensor("x", [D, T], F32, kind="ExternalInput").ap()
    wfcT = nc.dram_tensor("wfcT", [64, 2048, 128], F32, kind="ExternalInput").ap()
    wprojT = nc.dram_tensor("wprojT", [16, 8192, 128], F32, kind="ExternalInput").ap()
    xout = nc.dram_tensor("xo", [D, T], F32, kind="ExternalOutput").ap()
    kind = "ExternalOutput" if debug else "Internal"
    tprojRB = nc.dram_tensor("tprojRB", [64, 128, 2048], BF16, kind=kind).ap()
    h_dram = nc.dram_tensor("h_scr", [8192, T], BF16, kind=kind).ap()
    if debug:
        dbg = nc.dram_tensor("dbg", [128, NCH * T], BF16, kind="ExternalOutput").ap()
        dbg2 = nc.dram_tensor("dbg2", [128, 8], F32, kind="ExternalOutput").ap()

    ctx = Ctx(nc, stack)
    sh = Shared(ctx)
    xn_sb = ctx.sb("xn", [128, NCH * T], BF16)
    prep = emit_l0ffn_prep(ctx, sh, wfcT, wprojT, tprojRB, "pp")
    ctx.plan.fence(prep["tail"])
    ln = emit_ln(ctx, sh, xin, xn_sb, "ln", compose_inner=True)
    ctx.plan.fence(ln["tail"])
    sts = emit_l0ffn_main(ctx, sh, prep, xn_sb, ln["tile_done"], xin, xout,
                          wfcT, tprojRB, h_dram, "ff")
    p = ctx.plan
    if debug:
        sd = p.dma("sync", "dbgxn", lambda e: e.dma_start(out=dbg[:, :], in_=xn_sb[:]),
                   deps=ln["tile_done"])

        def dbgcols(v):
            v.tensor_copy(dbg2s[:, 0:1], prep["gfc_col"][:])
            v.tensor_copy(dbg2s[:, 1:2], prep["thr_fc"][:])
            v.tensor_copy(dbg2s[:, 2:3], prep["thr_pj"][:])
            return v.tensor_copy(dbg2s[0:1, 3:4], prep["gph"][0:1, :])
        dbg2s = ctx.sb("dbg2s", [128, 8], F32)
        z = p.op("vector", lambda v: v.memset(dbg2s[:], 0.0))
        dc = p.op("vector", dbgcols, deps=[prep["cc"], z])
        sd2 = p.dma("sync", "dbgc", lambda e: e.dma_start(out=dbg2[:, :], in_=dbg2s[:]),
                    deps=[dc])
        p.op("gpsimd", lambda g: g.nop(), deps=sts[-16:] + [sd, sd2])
    else:
        p.op("gpsimd", lambda g: g.nop(), deps=sts[-16:])
    p.emit()
    stack.close()
    return nc


_BUILD_CACHE = {}


def _get(name, fn):
    if name not in _BUILD_CACHE:
        _BUILD_CACHE[name] = fn()
    return _BUILD_CACHE[name]


def kernel(**inputs):
    cores = list(range(8))
    prep = [prep_core_inputs(inputs, c) for c in cores]
    x = [prep[c]["x"] for c in cores]

    nc_ffn0 = _get("ffn0", build_l0ffn_launch)
    nc_att0 = _get("att0", build_l0attn_launch)
    nc_ffn1 = _get("ffn1", build_l1ffn_launch)
    nc_att1 = _get("att1", build_l1attn_launch)

    def run(nc, maps):
        res = run_bass_kernel_spmd(nc, maps, core_ids=cores)
        return [res.results[c]["xo"] for c in cores]

    for step, nc in [("f0", nc_ffn0), ("a0", nc_att0), ("f0", nc_ffn0),
                     ("f1", nc_ffn1), ("a1", nc_att1), ("f1", nc_ffn1)]:
        maps = []
        for c in cores:
            d = prep[c]
            if step == "f0":
                maps.append(dict(x=x[c], wfcT=d["wfcT"], wprojT=d["wprojT"]))
            elif step == "a0":
                maps.append(dict(x=x[c], qwT=d["l0_qwT"], krT=d["l0_krT"],
                                 dnT=d["l0_dnT"], upkT=d["l0_upkT"],
                                 upvT=d["l0_upvT"], owT=d["l0_owT"],
                                 cos_q=d["cos_q"], sin_q=d["sin_q"],
                                 cos_k=d["cos_k"], sin_k=d["sin_k"],
                                 permT=d["permT"], H4=d["H4"], Hq=d["Hq"],
                                 IND=d["IND"], tri_mul=d["tri_mul"]))
            elif step == "f1":
                maps.append(dict(x=x[c], guT=d["guT"], downRB=d["downRB"]))
            else:
                maps.append(dict(x=x[c], qwT=d["l1_qwT"], kwT=d["l1_kwT"],
                                 vwT=d["l1_vwT"], qb=d["l1_qb"], kb=d["l1_kb"],
                                 vb=d["l1_vb"], owT=d["l1_owT"], ob=d["l1_ob"],
                                 iota8neg=d["iota8neg"], jcols=d["jcols"],
                                 tri_add=d["tri_add"], e8T=d["e8T"], E8=d["E8"]))
        x = run(nc, maps)

    out = np.zeros((4, 4096, 2048), np.float32)
    for c in cores:
        b, h = c // 2, c % 2
        out[b, h * T:(h + 1) * T, :] = np.asarray(x[c]).T
    return out


# version 04
# speedup vs baseline: 1.0131x; 1.0131x over previous
import contextlib
import math
import numpy as np
import ml_dtypes
import concourse.bass as bass
import concourse.mybir as mybir
from concourse.bass_utils import run_bass_kernel_spmd

F32 = mybir.dt.float32
BF16 = mybir.dt.bfloat16
AF = mybir.ActivationFunctionType
ALU = mybir.AluOpType
AX = mybir.AxisListType

D = 2048
T = 2048
TFULL = 4096
NCH = D // 128
PAIRS = [[0, 1], [2, 3], [4, 5], [6, 7]]


class _Op:
    __slots__ = ("idx", "eng", "fn", "deps", "is_dma", "tag", "marked",
                 "sem", "val", "dma_count", "hard")

    def __init__(self, idx, eng, fn, deps, is_dma=False, tag=None):
        self.hard = set()
        self.idx = idx
        self.eng = eng
        self.fn = fn
        self.deps = list(deps)
        self.is_dma = is_dma
        self.tag = tag
        self.marked = False
        self.sem = None
        self.val = None


class Plan:
    ENGINES = ("tensor", "vector", "scalar", "gpsimd", "sync")

    DMA_POOL = 80

    def __init__(self, nc, stack):
        self.nc = nc
        self.stack = stack
        self.ops = []
        self.eng_sem = {e: stack.enter_context(nc.semaphore(f"S_{e}"))
                        for e in self.ENGINES}
        self.dma_pool = []
        self.dma_sems = {}
        self.sem_counts = {}
        self._fence = {e: None for e in self.ENGINES}

    def fence(self, deps):
        deps = [d for d in deps if d is not None]
        for e in self.ENGINES:
            cur = self._fence[e] or []
            self._fence[e] = cur + deps

    def _take_fence(self, eng):
        f = self._fence[eng]
        self._fence[eng] = None
        return f or []

    def op(self, eng, fn, deps=(), hard=()):
        deps = list(deps) + list(hard) + self._take_fence(eng)
        o = _Op(len(self.ops), eng, fn, deps)
        o.hard = set(id(h) for h in hard)
        self.ops.append(o)
        for d in deps:
            d.marked = True
        return o

    def dma(self, eng, tag, fn, deps=()):
        deps = list(deps) + self._take_fence(eng)
        o = _Op(len(self.ops), eng, fn, deps, is_dma=True, tag=tag)
        self.ops.append(o)
        for d in deps:
            d.marked = True
        return o

    def barrier_all(self, deps):
        return deps

    def emit(self):
        eng_count = {e: 0 for e in self.ENGINES}
        pool_by_eng = {}
        tagcount_by_eng = {}
        PER_ENG = {"sync": 40, "gpsimd": 30, "scalar": 8, "vector": 8, "tensor": 8}
        for o in self.ops:
            if o.is_dma:
                key = (o.eng, o.tag)
                if key not in self.dma_sems:
                    pool = pool_by_eng.setdefault(o.eng, [])
                    n = tagcount_by_eng.get(o.eng, 0)
                    tagcount_by_eng[o.eng] = n + 1
                    idx = n % PER_ENG[o.eng]
                    while len(pool) <= idx:
                        pool.append(self.stack.enter_context(
                            self.nc.semaphore(f"Dp_{o.eng}{len(pool)}")))
                    self.dma_sems[key] = pool[idx]
                h = self.dma_sems[key]
                cnt = self.sem_counts.get(id(h), 0) + 16
                self.sem_counts[id(h)] = cnt
                o.sem, o.val = h, cnt
            elif o.marked:
                eng_count[o.eng] += 1
                o.sem, o.val = self.eng_sem[o.eng], eng_count[o.eng]

        queues = {e: [] for e in self.ENGINES}
        waited = {e: {} for e in self.ENGINES}

        for o in self.ops:
            q = queues[o.eng]
            w = waited[o.eng]
            for d in o.deps:
                if d.eng == o.eng and not d.is_dma and id(d) not in o.hard:
                    continue
                assert d.sem is not None, "dep without completion signal"
                key = id(d.sem)
                if w.get(key, -1) >= d.val:
                    continue
                w[key] = d.val
                q.append(lambda e, s=d.sem, v=d.val: e.wait_ge(s, v))
            if o.is_dma:
                q.append(lambda e, o=o: o.fn(e).then_inc(o.sem, 16))
            elif o.marked:
                q.append(lambda e, o=o: o.fn(e).then_inc(o.sem, 1))
            else:
                q.append(lambda e, o=o: o.fn(e))

        nc = self.nc
        with nc.Block() as block:
            @block.tensor
            def _(t):
                for fn in queues["tensor"]:
                    fn(t)

            @block.vector
            def _(v):
                for fn in queues["vector"]:
                    fn(v)

            @block.scalar
            def _(s):
                for fn in queues["scalar"]:
                    fn(s)

            @block.gpsimd
            def _(g):
                for fn in queues["gpsimd"]:
                    fn(g)

            @block.sync
            def _(sy):
                for fn in queues["sync"]:
                    fn(sy)


class Ctx:

    def __init__(self, nc, stack):
        self.nc = nc
        self.stack = stack
        self.plan = Plan(nc, stack)
        self.ps = [stack.enter_context(nc.psum_tensor(f"ps{i}", [128, 512], F32))
                   for i in range(8)]

    def sb(self, name, shape, dtype):
        return self.stack.enter_context(self.nc.sbuf_tensor(name, shape, dtype))

    @contextlib.contextmanager
    def scope(self):
        old = self.stack
        try:
            with contextlib.ExitStack() as es:
                self.stack = es
                yield
        finally:
            self.stack = old



def _panels(wT, pw=128):
    K, O = wT.shape
    return np.ascontiguousarray(wT.reshape(K, O // pw, pw).transpose(1, 0, 2))


def _rowblocks(wT, rb=128):
    K, O = wT.shape
    return np.ascontiguousarray(wT.reshape(K // rb, rb, O))


def _sigma_perm():
    return np.array(list(range(0, 16)) + list(range(32, 48)) +
                    list(range(16, 32)) + list(range(48, 64)))


def _rope_tables():
    HD = 64
    inv_freq = 1.0 / (10000.0 ** (np.arange(0, HD, 2) / HD))
    pos = np.arange(TFULL, dtype=np.float64)
    ang = pos[None, :] * inv_freq[:, None]
    cos, sin = np.cos(ang), np.sin(ang)
    cos_q = np.concatenate([cos[0:16], cos[0:16], cos[16:32], cos[16:32]], 0)
    sin_q = np.concatenate([sin[0:16], sin[0:16], sin[16:32], sin[16:32]], 0)
    cos_k = np.concatenate([cos[0:16], cos[0:16]], 0)
    sin_k = np.concatenate([sin[0:16], sin[0:16]], 0)
    return (np.tile(cos_q, (2, 1)).astype(np.float32),
            np.tile(sin_q, (2, 1)).astype(np.float32),
            np.tile(cos_k, (4, 1)).astype(np.float32),
            np.tile(sin_k, (4, 1)).astype(np.float32))


def _perm_matrix():
    P = np.zeros((128, 128), np.float32)
    for blk in range(4):
        b = 32 * blk
        for p in range(16):
            P[b + p, b + p + 16] = -1.0
            P[b + p + 16, b + p] = 1.0
    return P


def prep_common_consts():
    c = {}
    mul_masks, add_masks = [], []
    for m in range(4):
        k = np.arange(128)[:, None] + 128 * m
        q = np.arange(512)[None, :]
        keep = (k <= q)
        mul_masks.append(keep.astype(np.float32))
        add_masks.append(np.where(keep, 0.0, -1e30).astype(np.float32))
    c["tri_mul"] = np.stack(mul_masks)
    c["tri_add"] = np.stack(add_masks)
    cq, sq, ck, sk = _rope_tables()
    c["cos_q"], c["sin_q"], c["cos_k"], c["sin_k"] = cq, sq, ck, sk
    c["permT"] = np.ascontiguousarray(_perm_matrix().T)
    Hq = np.zeros((8, 128, 16), np.float32)
    for pn in range(8):
        Hq[pn, 0:64, 2 * pn] = 1.0
        Hq[pn, 64:128, 2 * pn + 1] = 1.0
    c["Hq"] = Hq
    H4 = np.zeros((128, 4), np.float32)
    for g in range(4):
        H4[32 * g:32 * (g + 1), g] = 1.0
    c["H4"] = H4
    IND = np.zeros((4, 16), np.float32)
    for qh in range(16):
        IND[qh // 4, qh] = -1.0
    c["IND"] = IND
    SCALE1 = 1.0 / math.sqrt(128.0)
    c["iota8neg"] = np.tile(-np.arange(TFULL, dtype=np.float64)[None, :] / SCALE1,
                            (8, 1)).astype(np.float32)
    jc = np.arange(128, dtype=np.float32)[:, None] + 128.0 * np.arange(32)[None, :]
    c["jcols"] = np.ascontiguousarray(jc)
    e8T = np.zeros((8, 8, 128), np.float32)
    for hh in range(8):
        e8T[hh, hh, :] = 1.0
    c["e8T"] = e8T
    E8 = np.zeros((8, 128, 8), np.float32)
    for pn in range(8):
        E8[pn, :, pn] = 1.0
    c["E8"] = E8
    return c


def prep_core_inputs(inputs, core):
    b, h = core // 2, core % 2
    d = {}
    x = np.asarray(inputs["L0_attn_hidden_states"])
    d["x"] = np.ascontiguousarray(x[b, h * T:(h + 1) * T, :].T)

    d["wfcT"] = _panels(np.ascontiguousarray(np.asarray(inputs["L0_ffn_w_fc"]).T))
    d["wprojT"] = _panels(np.ascontiguousarray(np.asarray(inputs["L0_ffn_w_proj"]).T))

    sig = _sigma_perm()
    qw = np.asarray(inputs["L0_attn_q_w"])
    qw_my = np.concatenate([qw[qh * 64 + sig, :] for qh in range(16 * h, 16 * (h + 1))], 0)
    d["l0_qwT"] = _panels(np.ascontiguousarray(qw_my.T))
    kr = np.asarray(inputs["L0_attn_kr_w"])
    d["l0_krT"] = _panels(np.ascontiguousarray(kr[128 * h:128 * (h + 1), :].T))
    d["l0_dnT"] = _panels(np.ascontiguousarray(np.asarray(inputs["L0_attn_down_w"]).T))
    upk = np.asarray(inputs["L0_attn_upk_w"])
    d["l0_upkT"] = np.ascontiguousarray(upk[128 * h:128 * (h + 1), :].T)
    upv = np.asarray(inputs["L0_attn_upv_w"])
    d["l0_upvT"] = np.ascontiguousarray(upv[256 * h:256 * (h + 1), :].T)
    ow = np.asarray(inputs["L0_attn_o_w"])
    d["l0_owT"] = _panels(np.ascontiguousarray(ow[:, 1024 * h:1024 * (h + 1)].T))

    for nm, wkey, bkey in [("q", "L1_attn_q_w", "L1_attn_q_b"),
                           ("k", "L1_attn_k_w", "L1_attn_k_b")]:
        w = np.asarray(inputs[wkey])
        d[f"l1_{nm}wT"] = _panels(np.ascontiguousarray(w[1024 * h:1024 * (h + 1), :].T))
        bias = np.asarray(inputs[bkey])
        d[f"l1_{nm}b"] = np.ascontiguousarray(bias[1024 * h:1024 * (h + 1)].reshape(8, 128).T)
    vw = np.asarray(inputs["L1_attn_v_w"])
    d["l1_vwT"] = np.ascontiguousarray(vw[1024 * h:1024 * (h + 1), :].T)
    d["l1_vb"] = np.ascontiguousarray(
        np.asarray(inputs["L1_attn_v_b"])[1024 * h:1024 * (h + 1)][None, :])
    ow1 = np.asarray(inputs["L1_attn_o_w"])
    d["l1_owT"] = _panels(np.ascontiguousarray(ow1[:, 1024 * h:1024 * (h + 1)].T))
    d["l1_ob"] = np.ascontiguousarray(np.asarray(inputs["L1_attn_o_b"]).reshape(16, 128).T)

    d["guT"] = _panels(np.ascontiguousarray(np.asarray(inputs["L1_ffn_w_gate_up"]).T))
    d["downRB"] = _rowblocks(np.ascontiguousarray(np.asarray(inputs["L1_ffn_w_down"]).T))

    d.update({k: np.ascontiguousarray(v) for k, v in prep_common_consts().items()})
    return d



class Shared:

    def __init__(self, ctx):
        p = ctx.plan
        self.ones_f = ctx.sb("c_ones_f", [128, 1], F32)
        self.ones_b = ctx.sb("c_ones_b", [128, 1], BF16)
        self.ones_row_f = ctx.sb("c_onesr_f", [1, 128], F32)
        o1 = p.op("vector", lambda v: v.memset(self.ones_f[:], 1.0))
        o2 = p.op("vector", lambda v: v.memset(self.ones_b[:], 1.0))
        o3 = p.op("vector", lambda v: v.memset(self.ones_row_f[:], 1.0))
        self.ready = [o1, o2, o3]


def emit_ln(ctx, sh, xin_dram, out_sb, tag, compose_inner, eps=1e-5,
            out_dram=None, deps=()):
    p = ctx.plan
    ps = ctx.ps
    TT = 512
    NT = T // TT
    scope = ctx.scope()
    scope.__enter__()
    xtile = [ctx.sb(f"{tag}_xt{i}", [128, NCH * TT], F32) for i in range(2)]
    sq = [ctx.sb(f"{tag}_sq{i}", [128, TT], BF16) for i in range(2)]
    rows = ctx.sb(f"{tag}_rows", [1, 3 * TT], F32)
    srow = ctx.sb(f"{tag}_srow", [1, T], F32)
    msrow = ctx.sb(f"{tag}_msrow", [1, T], F32)
    tmpf = ctx.sb(f"{tag}_tmpf", [128, TT], F32)

    tile_done = []
    prev_apply0 = {}
    mm2_hist = {}
    fix1_hist = {}
    st_ops = []
    for tt in range(NT):
        xb = xtile[tt % 2]
        war = [tile_done[tt - 2]] if tt >= 2 else []
        lds = []
        for c in range(NCH):
            lds.append(p.dma("sync", f"{tag}x{tt % 2}",
                             lambda e, c=c, tt=tt, xb=xb: e.dma_start(
                                 out=xb[:, c * TT:(c + 1) * TT],
                                 in_=xin_dram[c * 128:(c + 1) * 128,
                                              tt * TT:(tt + 1) * TT]),
                             deps=list(deps) + war))
        mm1l = mm2l = None
        for c in range(NCH):
            sqd = [lds[c]]
            if (tt, c - 2) in mm2_hist:
                sqd.append(mm2_hist[(tt, c - 2)])
            elif tt > 0 and c < 2:
                sqd.append(mm2_hist[(tt - 1, NCH - 2 + c)])
            sqo = p.op("scalar", lambda e, c=c, xb=xb, s=sq[c % 2]: e.activation(
                s[:], xb[:, c * TT:(c + 1) * TT], AF.Square), deps=sqd)
            d1 = [lds[c], sh.ready[0]]
            if c == 0 and tt > 0:
                d1.append(fix1_hist[tt - 1])
            mm1l = p.op("tensor", lambda e, c=c, xb=xb: e.matmul(
                ps[0][0:1, :], sh.ones_f[:], xb[:, c * TT:(c + 1) * TT],
                start=(c == 0), stop=(c == NCH - 1)), deps=d1)
            d2 = [sqo, sh.ready[1]]
            if c == 0 and tt > 0:
                d2.append(fix1_hist[tt - 1])
            mm2l = p.op("tensor", lambda e, c=c, s=sq[c % 2]: e.matmul(
                ps[1][0:1, :], sh.ones_b[:], s[:],
                start=(c == 0), stop=(c == NCH - 1)), deps=d2)
            mm2_hist[(tt, c)] = mm2l

        m_ = rows[0:1, 0:TT]
        va_ = rows[0:1, TT:2 * TT]
        tp_ = rows[0:1, 2 * TT:3 * TT]
        fa = p.op("vector", lambda v: (
            v.tensor_scalar(m_, ps[0][0:1, :], 1.0 / D, None, ALU.mult),
            v.tensor_scalar(va_, ps[1][0:1, :], 1.0 / D, None, ALU.mult))[-1],
            deps=[mm1l, mm2l])
        fbb = p.op("vector", lambda v: v.tensor_tensor(tp_, m_, m_, ALU.mult),
                   hard=[fa])
        fcc = p.op("vector", lambda v: v.tensor_tensor(va_, va_, tp_, ALU.subtract),
                   hard=[fbb])
        if compose_inner:
            f1 = p.op("vector", lambda v: v.tensor_scalar(
                va_, va_, 1.0 + 1e-6, 1e-11, ALU.mult, ALU.add), hard=[fcc])
        else:
            f1 = p.op("vector", lambda v: v.tensor_scalar(
                va_, va_, eps, None, ALU.add), hard=[fcc])
        fix1_hist[tt] = f1
        sq1 = p.op("scalar", lambda e: e.activation(
            rows[0:1, 2 * TT:3 * TT], rows[0:1, TT:2 * TT], AF.Sqrt), deps=[f1])
        f2a = p.op("vector", lambda v, tt=tt: v.reciprocal(
            srow[0:1, tt * TT:(tt + 1) * TT], rows[0:1, 2 * TT:3 * TT]), deps=[sq1])
        f2 = p.op("vector", lambda v, tt=tt: v.tensor_tensor(
            msrow[0:1, tt * TT:(tt + 1) * TT], rows[0:1, 0:TT],
            srow[0:1, tt * TT:(tt + 1) * TT], ALU.mult), hard=[f2a])

        bwar = [tile_done[tt - 1]] if tt > 0 else []
        bs = p.op("tensor", lambda e, tt=tt: e.matmul(
            ps[2][:, :], sh.ones_row_f[:], srow[0:1, tt * TT:(tt + 1) * TT],
            start=True, stop=True), deps=[f2, sh.ready[2]] + bwar)
        bms = p.op("tensor", lambda e, tt=tt: e.matmul(
            ps[3][:, :], sh.ones_row_f[:], msrow[0:1, tt * TT:(tt + 1) * TT],
            start=True, stop=True), deps=[f2] + bwar)

        last_apply = None
        for c in range(NCH):
            def apply(v, c=c, tt=tt, xb=xb):
                v.tensor_tensor(tmpf[:], xb[:, c * TT:(c + 1) * TT], ps[2][:, :], ALU.mult)
                return v.tensor_tensor(
                    out_sb[:, c * T + tt * TT: c * T + (tt + 1) * TT],
                    tmpf[:], ps[3][:, :], ALU.subtract)
            ap_op = p.op("vector", apply, deps=([bs, bms] if c == 0 else []))
            if c == 0:
                prev_apply0[tt] = ap_op
            if out_dram is not None:
                st_ops.append(p.dma("sync", f"{tag}st",
                                    lambda e, c=c, tt=tt: e.dma_start(
                                        out=out_dram[c * 128:(c + 1) * 128,
                                                     tt * TT:(tt + 1) * TT],
                                        in_=out_sb[:, c * T + tt * TT: c * T + (tt + 1) * TT]),
                                    deps=[ap_op]))
            last_apply = ap_op
        tile_done.append(last_apply)
    scope.__exit__(None, None, None)
    tail = [tile_done[-1]] + st_ops[-2:]
    return dict(tile_done=tile_done, st_ops=st_ops, tail=tail)


def emit_l0ffn_prep(ctx, sh, wfcT, wprojT, tprojRB, tag):
    p = ctx.plan
    ps = ctx.ps
    thr_fc = ctx.sb(f"{tag}_thrfc", [128, 1], F32)
    thr_pj = ctx.sb(f"{tag}_thrpj", [128, 1], F32)
    gfc_col = ctx.sb(f"{tag}_gfccol", [128, 1], F32)
    gph = ctx.sb(f"{tag}_gph", [1, 1], F32)
    rs_row = ctx.sb(f"{tag}_rsrow", [1, D], F32)
    scope = ctx.scope()
    scope.__enter__()
    gbuf = [ctx.sb(f"{tag}_gb{i}", [128, 4096], F32) for i in range(2)]
    gcols = ctx.sb(f"{tag}_gcols", [128, 100], F32)
    rows = ctx.sb(f"{tag}_grows", [1, 16], F32)

    segs = []
    for i in range(64):
        segs.append((lambda i=i: wfcT[i].rearrange("(c p) o -> p c o", p=128), 16, "fc"))
    for i in range(16):
        segs.append((lambda i=i: wprojT[i].rearrange("(c p) o -> p c o", p=128)[:, 0:32, :], 32, "pj"))
        segs.append((lambda i=i: wprojT[i].rearrange("(c p) o -> p c o", p=128)[:, 32:64, :], 32, "pj"))
    red_ops = []
    fc_slots, pj_slots = [], []
    for si, (apf, nch, kind) in enumerate(segs):
        b = gbuf[si % 2]
        war = [red_ops[si - 2]] if si >= 2 else []
        ld = p.dma("sync", f"{tag}g{si % 2}",
                   lambda e, apf=apf, b=b, nch=nch: e.dma_start(
                       out=b.ap().rearrange("p (c o) -> p c o", o=128)[:, 0:nch, :],
                       in_=apf()),
                   deps=war)
        ro = p.op("vector", lambda v, b=b, nch=nch, sl=si: v.tensor_reduce(
            gcols[:, sl:sl + 1], b[:, 0:nch * 128], AX.X, ALU.add,
            apply_absolute_value=True), deps=[ld])
        red_ops.append(ro)
        (fc_slots if kind == "fc" else pj_slots).append(si)

    def red2(v):
        v.tensor_reduce(gcols[:, 96:97], gcols[:, fc_slots[0]:fc_slots[-1] + 1],
                        AX.X, ALU.add)
        return v.tensor_reduce(gcols[:, 97:98], gcols[:, pj_slots[0]:pj_slots[-1] + 1],
                               AX.X, ALU.add)
    r2 = p.op("vector", red2, hard=[red_ops[-1], red_ops[-2]])
    tot_fc = p.op("tensor", lambda e: e.matmul(
        ps[0][0:1, 0:1], gcols[:, 96:97], sh.ones_f[:], start=True, stop=True),
        deps=[r2, sh.ready[0]])
    tot_pj = p.op("tensor", lambda e: e.matmul(
        ps[4][0:1, 0:1], gcols[:, 97:98], sh.ones_f[:], start=True, stop=True),
        deps=[r2])

    gfa = p.op("vector", lambda v: (
        v.tensor_scalar(rows[0:1, 0:1], ps[0][0:1, 0:1], 1.0 / (8192.0 * 2048.0), None, ALU.mult),
        v.tensor_scalar(rows[0:1, 2:3], ps[4][0:1, 0:1], 1.0 / (8192.0 * 2048.0), None, ALU.mult))[-1],
        deps=[tot_fc, tot_pj])
    gf = p.op("vector", lambda v: (
        v.tensor_scalar(rows[0:1, 1:2], rows[0:1, 0:1], 1e-6, 0.5, ALU.add, ALU.mult),
        v.tensor_scalar(rows[0:1, 3:4], rows[0:1, 2:3], 1e-6, 0.5, ALU.add, ALU.mult),
        v.tensor_scalar(gph[0:1, :], rows[0:1, 2:3], 0.5, None, ALU.mult))[-1],
        hard=[gfa])
    bc = p.op("tensor", lambda e: e.matmul(
        ps[1][:, 0:4], sh.ones_row_f[:], rows[0:1, 0:4], start=True, stop=True),
        deps=[gf, sh.ready[2]])

    def copycols(v):
        v.tensor_copy(gfc_col[:], ps[1][:, 0:1])
        v.tensor_copy(thr_fc[:], ps[1][:, 1:2])
        return v.tensor_copy(thr_pj[:], ps[1][:, 3:4])
    cc = p.op("vector", copycols, deps=[bc])

    pfp = [ctx.sb(f"{tag}_pf{i}", [128, 4096], F32) for i in range(2)]
    ptr = [ctx.sb(f"{tag}_pt{i}", [128, 4096], BF16) for i in range(2)]
    psg = [ctx.sb(f"{tag}_psg{i}", [128, 4096], BF16) for i in range(2)]
    prev = {}
    rs_evacs = []
    last_spill = {0: None, 1: None}
    seg_idx = 0
    for pi in range(16):
        last_mm = None
        for hh in range(2):
            bi = seg_idx % 2
            pm2 = prev.get(seg_idx - 2)
            ld = p.dma("sync", f"{tag}p{bi}",
                       lambda e, pi=pi, hh=hh, bi=bi: e.dma_start(
                           out=pfp[bi].ap().rearrange("p (c o) -> p c o", o=128),
                           in_=wprojT[pi].rearrange("(c p) o -> p c o", p=128)[:, hh * 32:(hh + 1) * 32, :]),
                       deps=[pm2["tn"]] if pm2 else [])
            mkp = p.op("vector", lambda v, bi=bi: v.tensor_scalar(
                ptr[bi][:], pfp[bi][:], thr_pj[:], 0.0, ALU.subtract, ALU.is_gt),
                deps=[ld, cc] + ([pm2["last_t"], pm2["last_sp"]] if pm2 else []))
            mkn = p.op("vector", lambda v, bi=bi: v.tensor_scalar(
                psg[bi][:], pfp[bi][:], thr_pj[:], 0.0, ALU.add, ALU.is_lt),
                deps=[ld] + ([pm2["tn"]] if pm2 else []))
            tn = p.op("vector", lambda v, bi=bi: v.tensor_tensor(
                ptr[bi][:], ptr[bi][:], psg[bi][:], ALU.subtract), deps=[])
            sp = None
            for c in range(32):
                kc = hh * 32 + c
                mmd = []
                if c == 0:
                    mmd = [tn, sh.ready[1]]
                    if kc == 0 and rs_evacs:
                        mmd.append(rs_evacs[-1])
                last_mm = p.op("tensor", lambda e, bi=bi, c=c, kc=kc: e.matmul(
                    ps[2][0:1, 0:128], sh.ones_b[:], ptr[bi][:, c * 128:(c + 1) * 128],
                    start=(kc == 0), stop=(kc == 63)), deps=mmd)
                sp = p.dma("gpsimd", f"{tag}sp{bi}",
                           lambda e, bi=bi, c=c, pi=pi, kc=kc: e.dma_start(
                               out=tprojRB[kc, :, pi * 128:(pi + 1) * 128],
                               in_=ptr[bi][:, c * 128:(c + 1) * 128]),
                           deps=[tn])
            prev[seg_idx] = dict(tn=tn, last_t=last_mm, last_sp=sp)
            last_spill[bi] = sp
            seg_idx += 1
        ev = p.op("vector", lambda v, pi=pi: v.tensor_copy(
            rs_row[0:1, pi * 128:(pi + 1) * 128], ps[2][0:1, 0:128]),
            deps=[last_mm])
        rs_evacs.append(ev)

    scope.__exit__(None, None, None)
    tail = [cc, rs_evacs[-1], last_spill[0], last_spill[1],
            prev[30]["last_t"], prev[31]["last_t"],
            prev[30]["tn"], prev[31]["tn"]]
    return dict(thr_fc=thr_fc, thr_pj=thr_pj, gfc_col=gfc_col, gph=gph,
                rs_row=rs_row, cc=cc, rs_evacs=rs_evacs,
                spill_done=[last_spill[0], last_spill[1]], tail=tail)


def emit_l0ffn_main(ctx, sh, prep, xn_sb, xn_ready, xin, xout, wfcT, tprojRB,
                    h_dram, tag):
    p = ctx.plan
    ps = ctx.ps
    TT = 512
    NT = T // TT

    wfp = [ctx.sb(f"{tag}_wf{i}", [128, NCH * 128], F32) for i in range(2)]
    ftr = [ctx.sb(f"{tag}_ft{i}", [128, NCH * 128], BF16) for i in range(2)]
    fsg = [ctx.sb(f"{tag}_fs{i}", [128, NCH * 128], BF16) for i in range(2)]
    hst = [ctx.sb(f"{tag}_hs{i}", [128, TT], BF16) for i in range(2)]
    prevA = {}
    last_st_h = {}
    last_gelu_bank = {4: None, 5: None}
    for oc in range(64):
        bi = oc % 2
        pm2 = prevA.get(oc - 2)
        ld = p.dma("sync", f"{tag}wf{bi}",
                   lambda e, oc=oc, bi=bi: e.dma_start(
                       out=wfp[bi].ap().rearrange("p (c o) -> p c o", o=128),
                       in_=wfcT[oc].rearrange("(c p) o -> p c o", p=128)),
                   deps=[pm2["tn"]] if pm2 else [])
        mkp = p.op("vector", lambda v, bi=bi: v.tensor_scalar(
            ftr[bi][:], wfp[bi][:], prep["thr_fc"][:], 0.0, ALU.subtract, ALU.is_gt),
            deps=[ld, prep["cc"]] + ([pm2["last_mm"]] if pm2 else []))
        mkn = p.op("vector", lambda v, bi=bi: v.tensor_scalar(
            fsg[bi][:], wfp[bi][:], prep["thr_fc"][:], 0.0, ALU.add, ALU.is_lt),
            deps=[ld] + ([pm2["tn"]] if pm2 else []))
        tn = p.op("vector", lambda v, bi=bi: v.tensor_tensor(
            ftr[bi][:], ftr[bi][:], fsg[bi][:], ALU.subtract), deps=[])
        last_mm_all = None
        for tt in range(NT):
            pb = 4 + tt % 2
            mmd0 = [tn, xn_ready[tt]]
            if last_gelu_bank[pb] is not None:
                mmd0.append(last_gelu_bank[pb])
            last_mm = None
            for c in range(NCH):
                last_mm = p.op("tensor", lambda e, c=c, tt=tt, bi=bi, pb=pb: e.matmul(
                    ps[pb][:, :], ftr[bi][:, c * 128:(c + 1) * 128],
                    xn_sb[:, c * T + tt * TT: c * T + (tt + 1) * TT],
                    start=(c == 0), stop=(c == NCH - 1)),
                    deps=(mmd0 if c == 0 else []))
            gl_d = [last_mm, prep["cc"]]
            if (oc - 1, tt) in last_st_h:
                gl_d.append(last_st_h[(oc - 1, tt)])
            gl = p.op("scalar", lambda e, tt=tt, pb=pb: e.activation(
                hst[tt % 2][:], ps[pb][:, :], AF.Gelu, scale=prep["gfc_col"][:]),
                deps=gl_d)
            last_gelu_bank[pb] = gl
            sth = p.dma("sync", f"{tag}h{tt % 2}",
                        lambda e, oc=oc, tt=tt: e.dma_start(
                            out=h_dram[oc * 128:(oc + 1) * 128, tt * TT:(tt + 1) * TT],
                            in_=hst[tt % 2][:]),
                        deps=[gl])
            last_st_h[(oc, tt)] = sth
            last_mm_all = last_mm
        prevA[oc] = dict(last_mm=last_mm_all, tn=tn)
    h_done = [last_st_h[(63, tt)] for tt in range(NT)] + \
             [last_st_h[(62, tt)] for tt in range(NT)]

    hb = [ctx.sb(f"{tag}_hb{i}", [128, TT], BF16) for i in range(2)]
    hsc = [ctx.sb(f"{tag}_hc{i}", [128, TT], BF16) for i in range(2)]
    hsq = [ctx.sb(f"{tag}_hq{i}", [128, TT], BF16) for i in range(2)]
    wrb = [ctx.sb(f"{tag}_wr{i}", [128, 1024], BF16) for i in range(2)]
    bbf = ctx.sb(f"{tag}_bb", [128, TT], BF16)
    rowsB = ctx.sb(f"{tag}_rB", [1, 4 * TT], F32)
    augrow = ctx.sb(f"{tag}_aug", [1, TT], F32)
    xf = [ctx.sb(f"{tag}_xf{i}", [128, TT], F32) for i in range(2)]
    res = [ctx.sb(f"{tag}_rs{i}", [128, TT], F32) for i in range(2)]

    final_sts = []
    last_bank_reader = {j: None for j in range(8)}
    last_bb = None
    last_fix = None
    hb_reader = {0: None, 1: None}
    ep_hist = {}
    stx_hist = {}
    wrb_reader = {0: None, 1: None}
    for tb in range(NT):
        mm1 = mm2 = None
        sq_reader = {0: None, 1: None}
        for kc in range(64):
            bi = kc % 2
            lddeps = []
            if tb == 0 and kc < 2:
                lddeps += h_done
            if hb_reader[bi] is not None:
                lddeps += (hb_reader[bi] if isinstance(hb_reader[bi], list)
                           else [hb_reader[bi]])
            ldh = p.dma("sync", f"{tag}hb{bi}",
                        lambda e, kc=kc, tb=tb, bi=bi: e.dma_start(
                            out=hb[bi][:],
                            in_=h_dram[kc * 128:(kc + 1) * 128, tb * TT:(tb + 1) * TT]),
                        deps=lddeps)
            sqd = [ldh]
            if sq_reader[bi] is not None:
                sqd.append(sq_reader[bi])
            sqo = p.op("scalar", lambda e, bi=bi: e.activation(
                hsq[bi][:], hb[bi][:], AF.Square), deps=sqd)
            d1 = [ldh, sh.ready[1]]
            d2 = [sqo]
            if kc == 0 and last_fix is not None:
                d1.append(last_fix)
                d2.append(last_fix)
            mm1 = p.op("tensor", lambda e, kc=kc, bi=bi: e.matmul(
                ps[0][0:1, :], sh.ones_b[:], hb[bi][:],
                start=(kc == 0), stop=(kc == 63)), deps=d1)
            mm2 = p.op("tensor", lambda e, kc=kc, bi=bi: e.matmul(
                ps[1][0:1, :], sh.ones_b[:], hsq[bi][:],
                start=(kc == 0), stop=(kc == 63)), deps=d2)
            sq_reader[bi] = mm2
            hb_reader[bi] = [mm1, sqo]

        mB_ = rowsB[0:1, 0:TT]
        vaB_ = rowsB[0:1, TT:2 * TT]
        tpB_ = rowsB[0:1, 2 * TT:3 * TT]
        s2B_ = rowsB[0:1, 3 * TT:4 * TT]
        fba = p.op("vector", lambda v: (
            v.tensor_scalar(mB_, ps[0][0:1, :], 1.0 / 8192.0, None, ALU.mult),
            v.tensor_scalar(vaB_, ps[1][0:1, :], 1.0 / 8192.0, None, ALU.mult))[-1],
            deps=[mm1, mm2])
        fbb = p.op("vector", lambda v: v.tensor_tensor(tpB_, mB_, mB_, ALU.mult),
                   hard=[fba])
        fbc = p.op("vector", lambda v: v.tensor_tensor(vaB_, vaB_, tpB_, ALU.subtract),
                   hard=[fbb])
        fb = p.op("vector", lambda v: v.tensor_scalar(vaB_, vaB_, 1e-6, None, ALU.add),
                  hard=[fbc])
        sqB = p.op("scalar", lambda e: e.activation(
            tpB_, vaB_, AF.Sqrt), deps=[fb])
        f2b1 = p.op("vector", lambda v: v.reciprocal(s2B_, tpB_), deps=[sqB])
        f2b2 = p.op("vector", lambda v: v.tensor_scalar(
            s2B_, s2B_, prep["gph"][0:1, 0:1], None, ALU.mult), hard=[f2b1])
        f2b3 = p.op("vector", lambda v: v.tensor_tensor(
            augrow[0:1, :], s2B_, mB_, ALU.mult), hard=[f2b2])
        f2b = p.op("vector", lambda v: v.tensor_scalar(
            augrow[0:1, :], augrow[0:1, :], -1.0, None, ALU.mult), hard=[f2b3])
        last_fix = f2b
        bcb_d = [f2b, sh.ready[2]]
        if last_bb is not None:
            bcb_d.append(last_bb)
        if tb == 0:
            bcb_d += prep["rs_evacs"][-1:]
        bcb = p.op("tensor", lambda e: e.matmul(
            ps[2][:, :], sh.ones_row_f[:], rowsB[0:1, 3 * TT:4 * TT],
            start=True, stop=True), deps=bcb_d)
        bbo = p.op("vector", lambda v: v.tensor_copy(bbf[:], ps[2][:, :]), deps=[bcb])
        last_bb = bbo

        for og in range(2):
            scale_ops = {}
            last_grp_mm = None
            for kc in range(64):
                bi = kc % 2
                lddeps = [f2b] if (og == 0 and kc < 2) else []
                if hb_reader[bi] is not None:
                    lddeps += (hb_reader[bi] if isinstance(hb_reader[bi], list)
                               else [hb_reader[bi]])
                ldh = p.dma("sync", f"{tag}hb{bi}",
                            lambda e, kc=kc, tb=tb, bi=bi: e.dma_start(
                                out=hb[bi][:],
                                in_=h_dram[kc * 128:(kc + 1) * 128, tb * TT:(tb + 1) * TT]),
                            deps=lddeps)
                scd = [ldh, bbo]
                if kc >= 2 and scale_ops.get(kc - 2) is not None:
                    scd.append(scale_ops[kc - 2]["reader"])
                sc = p.op("vector", lambda v, bi=bi: v.tensor_tensor(
                    hsc[bi][:], hb[bi][:], bbf[:], ALU.mult), deps=scd)
                hb_reader[bi] = sc
                ldw_deps = []
                if wrb_reader[bi] is not None:
                    ldw_deps.append(wrb_reader[bi])
                if tb == 0 and og == 0 and kc < 2:
                    ldw_deps += prep["spill_done"]
                ldw = p.dma("gpsimd", f"{tag}wr{bi}",
                            lambda e, kc=kc, og=og, bi=bi: e.dma_start(
                                out=wrb[bi][:],
                                in_=tprojRB[kc, :, og * 1024:(og + 1) * 1024]),
                            deps=ldw_deps)
                lm = None
                for j in range(8):
                    oc = og * 8 + j
                    if kc == 0:
                        augd = [f2b, sh.ready[2]]
                        if last_bank_reader[j] is not None:
                            augd.append(last_bank_reader[j])
                        if tb == 0:
                            augd += h_done
                        p.op("tensor", lambda e, j=j, oc=oc: e.matmul(
                            ps[j][:, :], prep["rs_row"][0:1, oc * 128:(oc + 1) * 128],
                            augrow[0:1, :], start=True, stop=False), deps=augd)
                    lm = p.op("tensor", lambda e, j=j, kc=kc, bi=bi: e.matmul(
                        ps[j][:, :], wrb[bi][:, j * 128:(j + 1) * 128], hsc[bi][:],
                        start=False, stop=(kc == 63)),
                        deps=([sc, ldw] if j == 0 else []))
                scale_ops[kc] = dict(op=sc, reader=lm)
                wrb_reader[bi] = lm
                last_grp_mm = lm
            for j in range(8):
                oc = og * 8 + j
                bi = j % 2
                ldx_d = [ep_hist[bi]] if ep_hist.get(bi) is not None else []
                ldx = p.dma("sync", f"{tag}xi{bi}",
                            lambda e, oc=oc, tb=tb, bi=bi: e.dma_start(
                                out=xf[bi][:],
                                in_=xin[oc * 128:(oc + 1) * 128, tb * TT:(tb + 1) * TT]),
                            deps=ldx_d)
                ep_d = [last_grp_mm, ldx]
                if stx_hist.get(bi) is not None:
                    ep_d.append(stx_hist[bi])
                ep = p.op("vector", lambda v, j=j, bi=bi: v.tensor_tensor(
                    res[bi][:], ps[j][:, :], xf[bi][:], ALU.add), deps=ep_d)
                ep_hist[bi] = ep
                last_bank_reader[j] = ep
                stx = p.dma("sync", f"{tag}xo{bi}",
                            lambda e, oc=oc, tb=tb, bi=bi: e.dma_start(
                                out=xout[oc * 128:(oc + 1) * 128, tb * TT:(tb + 1) * TT],
                                in_=res[bi][:]),
                            deps=[ep])
                stx_hist[bi] = stx
                final_sts.append(stx)
    return final_sts



def emit_l1ffn(ctx, sh, u_sb, u_ready, xin, xout, guT, downRB, h_dram, tag):
    p = ctx.plan
    ps = ctx.ps
    TT = 512
    NT = T // TT

    gp = [ctx.sb(f"{tag}_gp{i}", [128, NCH * 128], BF16) for i in range(2)]
    up = [ctx.sb(f"{tag}_up{i}", [128, NCH * 128], BF16) for i in range(2)]
    sil = ctx.sb(f"{tag}_sil", [128, TT], F32)
    hst = [ctx.sb(f"{tag}_hs{i}", [128, TT], BF16) for i in range(2)]
    prevA = {}
    last_st_h = {}
    last_sil = None
    last_stt_bank = {4: None, 5: None, 6: None, 7: None}
    for oc in range(64):
        bi = oc % 2
        pm2 = prevA.get(oc - 2)
        ldg = p.dma("gpsimd", f"{tag}wg{bi}",
                    lambda e, oc=oc, bi=bi: e.dma_start(
                        out=gp[bi].ap().rearrange("p (c o) -> p c o", o=128),
                        in_=guT[oc].rearrange("(c p) o -> p c o", p=128)),
                    deps=[pm2["last_g"]] if pm2 else [])
        ldu = p.dma("gpsimd", f"{tag}wu{bi}",
                    lambda e, oc=oc, bi=bi: e.dma_start(
                        out=up[bi].ap().rearrange("p (c o) -> p c o", o=128),
                        in_=guT[64 + oc].rearrange("(c p) o -> p c o", p=128)),
                    deps=[pm2["last_u"]] if pm2 else [])
        lg = lu = None
        for tt in range(NT):
            pbg = 4 + tt % 2
            pbu = 6 + tt % 2
            mmdg = [ldg, u_ready[tt]]
            if last_stt_bank[pbg] is not None:
                mmdg.append(last_stt_bank[pbg])
            mmdu = [ldu, u_ready[tt]]
            if last_stt_bank[pbu] is not None:
                mmdu.append(last_stt_bank[pbu])
            for c in range(NCH):
                lg = p.op("tensor", lambda e, c=c, tt=tt, bi=bi, pbg=pbg: e.matmul(
                    ps[pbg][:, :], gp[bi][:, c * 128:(c + 1) * 128],
                    u_sb[:, c * T + tt * TT: c * T + (tt + 1) * TT],
                    start=(c == 0), stop=(c == NCH - 1)),
                    deps=(mmdg if c == 0 else []))
            for c in range(NCH):
                lu = p.op("tensor", lambda e, c=c, tt=tt, bi=bi, pbu=pbu: e.matmul(
                    ps[pbu][:, :], up[bi][:, c * 128:(c + 1) * 128],
                    u_sb[:, c * T + tt * TT: c * T + (tt + 1) * TT],
                    start=(c == 0), stop=(c == NCH - 1)),
                    deps=(mmdu if c == 0 else []))
            sl_d = [lg]
            if last_sil is not None:
                sl_d.append(last_sil)
            sl = p.op("scalar", lambda e, pbg=pbg: e.activation(
                sil[:], ps[pbg][:, :], AF.Silu), deps=sl_d)
            stt_d = [sl, lu]
            if (oc - 1, tt) in last_st_h:
                stt_d.append(last_st_h[(oc - 1, tt)])
            so = p.op("vector", lambda v, tt=tt, pbu=pbu: v.scalar_tensor_tensor(
                hst[tt % 2][:], sil[:], 0.5, ps[pbu][:, :], ALU.mult, ALU.mult),
                deps=stt_d)
            last_sil = so
            last_stt_bank[pbg] = so
            last_stt_bank[pbu] = so
            sth = p.dma("sync", f"{tag}h{tt % 2}",
                        lambda e, oc=oc, tt=tt: e.dma_start(
                            out=h_dram[oc * 128:(oc + 1) * 128, tt * TT:(tt + 1) * TT],
                            in_=hst[tt % 2][:]),
                        deps=[so])
            last_st_h[(oc, tt)] = sth
        prevA[oc] = dict(last_g=lg, last_u=lu)
    h_done = [last_st_h[(63, tt)] for tt in range(NT)] + \
             [last_st_h[(62, tt)] for tt in range(NT)]

    hb = [ctx.sb(f"{tag}_hb{i}", [128, TT], BF16) for i in range(2)]
    wrb = [ctx.sb(f"{tag}_wr{i}", [128, 1024], BF16) for i in range(2)]
    xf = [ctx.sb(f"{tag}_xf{i}", [128, TT], F32) for i in range(2)]
    res = [ctx.sb(f"{tag}_rs{i}", [128, TT], F32) for i in range(2)]

    final_sts = []
    last_bank_reader = {j: None for j in range(8)}
    hb_reader = {0: None, 1: None}
    wrb_reader = {0: None, 1: None}
    ep_hist = {}
    stx_hist = {}
    for tb in range(NT):
        for og in range(2):
            last_grp_mm = None
            for kc in range(64):
                bi = kc % 2
                lddeps = list(h_done) if (tb == 0 and og == 0 and kc < 2) else []
                if hb_reader[bi] is not None:
                    lddeps.append(hb_reader[bi])
                ldh = p.dma("sync", f"{tag}hb{bi}",
                            lambda e, kc=kc, tb=tb, bi=bi: e.dma_start(
                                out=hb[bi][:],
                                in_=h_dram[kc * 128:(kc + 1) * 128, tb * TT:(tb + 1) * TT]),
                            deps=lddeps)
                ldw_deps = []
                if wrb_reader[bi] is not None:
                    ldw_deps.append(wrb_reader[bi])
                ldw = p.dma("gpsimd", f"{tag}wr{bi}",
                            lambda e, kc=kc, og=og, bi=bi: e.dma_start(
                                out=wrb[bi][:],
                                in_=downRB[kc, :, og * 1024:(og + 1) * 1024]),
                            deps=ldw_deps)
                lm = None
                for j in range(8):
                    mmd = []
                    if kc == 0:
                        mmd = [ldh, ldw]
                        if last_bank_reader[j] is not None:
                            mmd.append(last_bank_reader[j])
                    elif j == 0:
                        mmd = [ldh, ldw]
                    lm = p.op("tensor", lambda e, j=j, kc=kc, bi=bi: e.matmul(
                        ps[j][:, :], wrb[bi][:, j * 128:(j + 1) * 128], hb[bi][:],
                        start=(kc == 0), stop=(kc == 63)), deps=mmd)
                hb_reader[bi] = lm
                wrb_reader[bi] = lm
                last_grp_mm = lm
            for j in range(8):
                oc = og * 8 + j
                bi = j % 2
                ldx_d = [ep_hist[bi]] if ep_hist.get(bi) is not None else []
                ldx = p.dma("sync", f"{tag}xi{bi}",
                            lambda e, oc=oc, tb=tb, bi=bi: e.dma_start(
                                out=xf[bi][:],
                                in_=xin[oc * 128:(oc + 1) * 128, tb * TT:(tb + 1) * TT]),
                            deps=ldx_d)
                ep_d = [last_grp_mm, ldx]
                if stx_hist.get(bi) is not None:
                    ep_d.append(stx_hist[bi])
                ep = p.op("vector", lambda v, j=j, bi=bi: v.tensor_tensor(
                    res[bi][:], ps[j][:, :], xf[bi][:], ALU.add), deps=ep_d)
                ep_hist[bi] = ep
                last_bank_reader[j] = ep
                stx = p.dma("sync", f"{tag}xo{bi}",
                            lambda e, oc=oc, tb=tb, bi=bi: e.dma_start(
                                out=xout[oc * 128:(oc + 1) * 128, tb * TT:(tb + 1) * TT],
                                in_=res[bi][:]),
                            deps=[ep])
                stx_hist[bi] = stx
                final_sts.append(stx)
    return final_sts


def build_l1ffn_launch():
    nc = bass.Bass("TRN2", target_bir_lowering=False)
    stack = contextlib.ExitStack()
    xin = nc.dram_tensor("x", [D, T], F32, kind="ExternalInput").ap()
    guT = nc.dram_tensor("guT", [128, 2048, 128], F32, kind="ExternalInput").ap()
    downRB = nc.dram_tensor("downRB", [64, 128, 2048], F32, kind="ExternalInput").ap()
    xout = nc.dram_tensor("xo", [D, T], F32, kind="ExternalOutput").ap()
    h_dram = nc.dram_tensor("h_scr", [8192, T], BF16, kind="Internal").ap()

    ctx = Ctx(nc, stack)
    sh = Shared(ctx)
    u_sb = ctx.sb("u", [128, NCH * T], BF16)
    ln = emit_ln(ctx, sh, xin, u_sb, "ln", compose_inner=False, eps=1e-5)
    ctx.plan.fence(ln["tail"])
    sts = emit_l1ffn(ctx, sh, u_sb, ln["tile_done"], xin, xout, guT, downRB,
                     h_dram, "sw")
    p = ctx.plan
    p.op("gpsimd", lambda g: g.nop(), deps=sts[-16:])
    p.emit()
    stack.close()
    return nc



def emit_ugather(ctx, sh, xin, u_all, cc_u, tag, deps=()):
    p = ctx.plan
    with ctx.scope():
        u_sb = ctx.sb(f"{tag}_usb", [128, NCH * T], BF16)
        ln = emit_ln(ctx, sh, xin, u_sb, f"{tag}ln", compose_inner=False,
                     eps=1e-5, out_dram=cc_u, deps=deps)
    gather = p.op("gpsimd", lambda g: g.collective_compute(
        "AllGather", ALU.bypass, replica_groups=PAIRS,
        ins=[cc_u.opt()], outs=[u_all.opt()]),
        deps=[ln["st_ops"][-1], ln["tile_done"][-1]])
    p.fence(ln["tail"])
    return dict(gather=gather)


def emit_oproj_exchange(ctx, sh, o_dram, o_ready, owT, cc2_in, cc2_out,
                        xin, xout, ob_cols, tag):
    p = ctx.plan
    ps = ctx.ps
    TT = 512
    final_sts = []
    with ctx.scope():
        own = [ctx.sb(f"{tag}_ow{i}", [128, 8 * 128], BF16) for i in range(16)]
        osb = [ctx.sb(f"{tag}_os{i}", [128, 8 * 512], BF16) for i in range(2)]
        cst = [ctx.sb(f"{tag}_cs{i}", [128, TT], F32) for i in range(2)]
        ow_ready = []
        for oc in range(16):
            ow_ready.append(p.dma("gpsimd", f"{tag}owp{oc % 2}",
                lambda e, oc=oc: e.dma_start(
                    out=own[oc].ap().rearrange("p (c o) -> p c o", o=128),
                    in_=owT[oc].rearrange("(c p) o -> p c o", p=128)),
                deps=()))
        osb_reader = {0: None, 1: None}
        cst_reader = {0: None, 1: None}
        evac_bank = {6: None, 7: None}
        cc_sts = []
        for rt in range(8):
            r, tt = rt // 4, rt % 4
            bi = rt % 2
            lds = []
            lddeps = list(o_ready) if rt < 2 else []
            if osb_reader[bi] is not None:
                lddeps.append(osb_reader[bi])
            for c in range(8):
                lds.append(p.dma("sync", f"{tag}ol{bi}",
                    lambda e, c=c, r=r, tt=tt, bi=bi: e.dma_start(
                        out=osb[bi][:, c * 512:(c + 1) * 512],
                        in_=o_dram[c * 128:(c + 1) * 128,
                                   r * 2048 + tt * TT: r * 2048 + (tt + 1) * TT]),
                    deps=lddeps if c < 2 else []))
            last_mm = None
            for oc in range(16):
                pb = 6 + oc % 2
                mmd = [lds[-1], ow_ready[oc]]
                if evac_bank[pb] is not None:
                    mmd.append(evac_bank[pb])
                for c in range(8):
                    last_mm = p.op("tensor", lambda e, oc=oc, c=c, bi=bi, pb=pb: e.matmul(
                        ps[pb][:, :], own[oc][:, c * 128:(c + 1) * 128],
                        osb[bi][:, c * 512:(c + 1) * 512],
                        start=(c == 0), stop=(c == 7)),
                        deps=(mmd if c == 0 else []))
                cb = oc % 2
                evd = [last_mm]
                if cst_reader[cb] is not None:
                    evd.append(cst_reader[cb])
                ev = p.op("vector", lambda v, cb=cb, pb=pb: v.tensor_copy(
                    cst[cb][:], ps[pb][:, :]), deps=evd)
                evac_bank[pb] = ev
                stc = p.dma("sync", f"{tag}cs{cb}",
                    lambda e, oc=oc, r=r, tt=tt, cb=cb: e.dma_start(
                        out=cc2_in[r * 2048 + oc * 128: r * 2048 + (oc + 1) * 128,
                                   tt * TT:(tt + 1) * TT],
                        in_=cst[cb][:]),
                    deps=[ev])
                cst_reader[cb] = stc
                cc_sts.append(stc)
            osb_reader[bi] = last_mm
        rs = p.op("gpsimd", lambda g: g.collective_compute(
            "ReduceScatter", ALU.add, replica_groups=PAIRS,
            ins=[cc2_in.opt()], outs=[cc2_out.opt()]),
            deps=[cc_sts[-1], cc_sts[-2]])
        xf = [ctx.sb(f"{tag}_xf{i}", [128, TT], F32) for i in range(2)]
        ccb = [ctx.sb(f"{tag}_cc{i}", [128, TT], F32) for i in range(2)]
        res = [ctx.sb(f"{tag}_rs{i}", [128, TT], F32) for i in range(2)]
        ep_hist = {0: None, 1: None}
        stx_hist = {0: None, 1: None}
        for oc in range(16):
            for tt in range(4):
                bi = (oc * 4 + tt) % 2
                lddeps = [rs] if oc == 0 and tt < 2 else []
                if ep_hist[bi] is not None:
                    lddeps.append(ep_hist[bi])
                ldx = p.dma("sync", f"{tag}xi{bi}",
                    lambda e, oc=oc, tt=tt, bi=bi: e.dma_start(
                        out=xf[bi][:],
                        in_=xin[oc * 128:(oc + 1) * 128, tt * TT:(tt + 1) * TT]),
                    deps=lddeps)
                ldc = p.dma("sync", f"{tag}ci{bi}",
                    lambda e, oc=oc, tt=tt, bi=bi: e.dma_start(
                        out=ccb[bi][:],
                        in_=cc2_out[oc * 128:(oc + 1) * 128, tt * TT:(tt + 1) * TT]),
                    deps=[rs] if oc == 0 and tt < 2 else [])
                epd = [ldx, ldc]
                if stx_hist[bi] is not None:
                    epd.append(stx_hist[bi])
                if ob_cols is not None:
                    ep = p.op("vector", lambda v, bi=bi, oc=oc: v.scalar_tensor_tensor(
                        res[bi][:], ccb[bi][:], ob_cols[:, oc:oc + 1], xf[bi][:],
                        ALU.add, ALU.add), deps=epd)
                else:
                    ep = p.op("vector", lambda v, bi=bi: v.tensor_tensor(
                        res[bi][:], ccb[bi][:], xf[bi][:], ALU.add), deps=epd)
                ep_hist[bi] = ep
                stx = p.dma("sync", f"{tag}xo{bi}",
                    lambda e, oc=oc, tt=tt, bi=bi: e.dma_start(
                        out=xout[oc * 128:(oc + 1) * 128, tt * TT:(tt + 1) * TT],
                        in_=res[bi][:]),
                    deps=[ep])
                stx_hist[bi] = stx
                final_sts.append(stx)
    tail = final_sts[-2:] + [ep_hist[0], ep_hist[1]]
    p.fence(tail)
    return final_sts



def emit_l0attn(ctx, sh, con, xin, xout, W, scratch, tag):
    p = ctx.plan
    ps = ctx.ps
    TT = 512
    u_all, qp, kp, vdr = scratch["u_all"], scratch["qp"], scratch["kp"], scratch["vdr"]
    gather = scratch["gather_op"]

    o_sts = []
    with ctx.scope():
        qw = [ctx.sb(f"{tag}_qw{i}", [128, NCH * 128], BF16) for i in range(8)]
        krw = ctx.sb(f"{tag}_krw", [128, NCH * 128], BF16)
        dnw = [ctx.sb(f"{tag}_dnw{i}", [128, NCH * 128], BF16) for i in range(2)]
        upkw = ctx.sb(f"{tag}_upkw", [128, 2 * 128], BF16)
        upvw = ctx.sb(f"{tag}_upvw", [128, 2 * 256], BF16)
        wlds = []
        for pn in range(8):
            wlds.append(p.dma("gpsimd", f"{tag}wq{pn % 2}", lambda e, pn=pn: e.dma_start(
                out=qw[pn].ap().rearrange("p (c o) -> p c o", o=128),
                in_=W["qwT"][pn].rearrange("(c p) o -> p c o", p=128)), deps=()))
        wlds.append(p.dma("gpsimd", f"{tag}wk", lambda e: e.dma_start(
            out=krw.ap().rearrange("p (c o) -> p c o", o=128),
            in_=W["krT"][0].rearrange("(c p) o -> p c o", p=128)), deps=()))
        for j in range(2):
            wlds.append(p.dma("gpsimd", f"{tag}wd{j}", lambda e, j=j: e.dma_start(
                out=dnw[j].ap().rearrange("p (c o) -> p c o", o=128),
                in_=W["dnT"][j].rearrange("(c p) o -> p c o", p=128)), deps=()))
        wlds.append(p.dma("gpsimd", f"{tag}wuk", lambda e: e.dma_start(
            out=upkw.ap().rearrange("p (c o) -> p c o", o=128),
            in_=W["upkT"].rearrange("(c p) o -> p c o", p=128)), deps=()))
        wlds.append(p.dma("gpsimd", f"{tag}wuv", lambda e: e.dma_start(
            out=upvw.ap().rearrange("p (c o) -> p c o", o=256),
            in_=W["upvT"].rearrange("(c p) o -> p c o", p=128)), deps=()))
        wld_done = wlds[-6:] + wlds[:2]

        ub = [ctx.sb(f"{tag}_ub{i}", [128, NCH * TT], BF16) for i in range(2)]
        qraw = ctx.sb(f"{tag}_qraw", [128, TT], BF16)
        qsq = ctx.sb(f"{tag}_qsq", [128, TT], BF16)
        rope1 = ctx.sb(f"{tag}_rp1", [128, TT], BF16)
        ropeo = ctx.sb(f"{tag}_rpo", [128, TT], BF16)
        ckv = ctx.sb(f"{tag}_ckv", [128, 2 * TT], BF16)
        vstage = [ctx.sb(f"{tag}_vst{i}", [128, 260], BF16) for i in range(2)]
        knorm = ctx.sb(f"{tag}_knorm", [4, TFULL], F32)
        maxkq = ctx.sb(f"{tag}_maxkq", [16, 1], F32)
        qn_sqrt = ctx.sb(f"{tag}_qns", [16, TT], F32)
        aug16 = ctx.sb(f"{tag}_aug16", [16, TT], BF16)
        onesrow_b = ctx.sb(f"{tag}_onerow", [1, TFULL], BF16)
        rowtmp = ctx.sb(f"{tag}_rowt", [4, 4], F32)

        oro = p.op("vector", lambda v: v.memset(onesrow_b[:], 1.0))
        ones_sts = [p.dma("sync", f"{tag}kon", lambda e, g=g: e.dma_start(
            out=kp[g, 64:65, :], in_=onesrow_b[:]), deps=[oro]) for g in range(4)]

        ub_reader = {0: None, 1: None}
        kr_sts, kc_sts, v_sts = [], [], []
        last_knmm = None
        kn_ev = []
        prev_rope = None
        prev_vst = {0: None, 1: None}
        prev_vev = None
        prev_ckv = None
        prev_kq = {}
        for rt in range(8):
            r, tt = rt // 4, rt % 4
            bi = rt % 2
            seg = slice(r * 2048 + tt * TT, r * 2048 + (tt + 1) * TT)
            lds = []
            for c in range(NCH):
                dd = [gather] if rt < 2 and c < 2 else []
                if ub_reader[bi] is not None and c == 0:
                    dd.append(ub_reader[bi])
                lds.append(p.dma("sync", f"{tag}u{bi}",
                    lambda e, c=c, r=r, tt=tt, bi=bi: e.dma_start(
                        out=ub[bi][:, c * TT:(c + 1) * TT],
                        in_=u_all[r * 2048 + c * 128: r * 2048 + (c + 1) * 128,
                                  tt * TT:(tt + 1) * TT]),
                    deps=dd))
            mmd = [lds[-1]] + (wld_done if rt == 0 else [])
            if prev_rope is not None:
                mmd.append(prev_rope)
            mm = None
            for c in range(NCH):
                mm = p.op("tensor", lambda e, c=c, bi=bi: e.matmul(
                    ps[0][:, :], krw[:, c * 128:(c + 1) * 128],
                    ub[bi][:, c * TT:(c + 1) * TT],
                    start=(c == 0), stop=(c == NCH - 1)), deps=(mmd if c == 0 else []))
            ev = p.op("vector", lambda v: v.tensor_copy(qraw[:], ps[0][:, :]),
                      deps=[mm, prev_kq.get("krperm")] if prev_kq.get("krperm") else [mm])
            pm = p.op("tensor", lambda e: e.matmul(
                ps[1][:, :], con["permT"][:], qraw[:], start=True, stop=True),
                deps=[ev] + ([prev_kq["krrope2"]] if "krrope2" in prev_kq else []))
            prev_kq["krperm"] = pm
            r1 = p.op("vector", lambda v, seg=seg: v.tensor_tensor(
                rope1[:], qraw[:], con["cos_k"][:, seg], ALU.mult), deps=[])
            r2 = p.op("vector", lambda v, seg=seg: v.tensor_tensor(
                ropeo[:], ps[1][:, :], con["sin_k"][:, seg], ALU.mult),
                deps=[pm] + kc_sts[-4:] + ([prev_kq["sq2"]] if "sq2" in prev_kq else []))
            r3 = p.op("vector", lambda v: v.tensor_tensor(
                ropeo[:], ropeo[:], rope1[:], ALU.add), deps=[])
            prev_kq["krrope2"] = r3
            sq1 = p.op("scalar", lambda e: e.activation(qsq[:], ropeo[:], AF.Square),
                       deps=[r3] + ([prev_kq["knmm1"]] if "knmm1" in prev_kq else []))
            knmm1 = p.op("tensor", lambda e: e.matmul(
                ps[2][0:4, 0:TT], con["H4"][:], qsq[:], start=True, stop=False),
                deps=[sq1] + (kn_ev[-1:] if kn_ev else []))
            prev_kq["knmm1"] = knmm1
            for g in range(4):
                kr_sts.append(p.dma("sync", f"{tag}krs{g % 2}",
                    lambda e, g=g, seg=seg: e.dma_start(
                        out=kp[g, 0:32, seg], in_=ropeo[32 * g:32 * (g + 1), :]),
                    deps=[r3]))
            prev_rope = r3

            mmc = None
            for j in range(2):
                for c in range(NCH):
                    mmc = p.op("tensor", lambda e, j=j, c=c, bi=bi: e.matmul(
                        ps[3][:, :],
                        dnw[j][:, c * 128:(c + 1) * 128],
                        ub[bi][:, c * TT:(c + 1) * TT],
                        start=(c == 0), stop=(c == NCH - 1)),
                        deps=([lds[-1], prev_ckv] if c == 0 and prev_ckv else ([lds[-1]] if c == 0 else [])))
                evc = p.op("vector", lambda v, j=j: v.tensor_copy(
                    ckv[:, j * TT:(j + 1) * TT], ps[3][:, :]), deps=[mmc])
                prev_ckv = evc
            mk = None
            for j in range(2):
                mk = p.op("tensor", lambda e, j=j: e.matmul(
                    ps[4][:, :], upkw[:, j * 128:(j + 1) * 128],
                    ckv[:, j * TT:(j + 1) * TT],
                    start=(j == 0), stop=(j == 1)),
                    deps=[prev_ckv] + ([prev_kq["kcev"]] if "kcev" in prev_kq else []))
            evk = p.op("vector", lambda v: v.tensor_copy(ropeo[:], ps[4][:, :]),
                       deps=[mk, sq1] + kr_sts[-4:])
            prev_kq["kcev"] = evk
            sq2 = p.op("scalar", lambda e: e.activation(qsq[:], ropeo[:], AF.Square),
                       deps=[evk, knmm1])
            prev_kq["sq2"] = sq2
            knmm2 = p.op("tensor", lambda e: e.matmul(
                ps[2][0:4, 0:TT], con["H4"][:], qsq[:], start=False, stop=True),
                deps=[sq2])
            kne = p.op("vector", lambda v, seg=seg: v.tensor_copy(
                knorm[:, seg], ps[2][0:4, 0:TT]), deps=[knmm2])
            kn_ev.append(kne)
            for g in range(4):
                kc_sts.append(p.dma("sync", f"{tag}kcs{g % 2}",
                    lambda e, g=g, seg=seg: e.dma_start(
                        out=kp[g, 32:64, seg], in_=ropeo[32 * g:32 * (g + 1), :]),
                    deps=[evk]))
            for sub in range(4):
                vb = sub % 2
                mv = None
                for j in range(2):
                    mv = p.op("tensor", lambda e, j=j, sub=sub: e.matmul(
                        ps[5][0:128, 0:256],
                        ckv[:, j * TT + sub * 128: j * TT + (sub + 1) * 128],
                        upvw[:, j * 256:(j + 1) * 256],
                        start=(j == 0), stop=(j == 1)),
                        deps=([prev_ckv] if sub == 0 and j == 0 else []) +
                             ([prev_vev] if j == 0 and prev_vev is not None else []) +
                             ([prev_vst[vb]] if j == 0 and prev_vst[vb] else []))
                def vevac(v, vb=vb):
                    for g in range(4):
                        v.tensor_copy(vstage[vb][:, 65 * g:65 * g + 64],
                                      ps[5][:, 64 * g:64 * (g + 1)])
                    return v.memset(vstage[vb][:, 64:260:65], 1.0)
                vev = p.op("vector", vevac, deps=[mv])
                prev_vst[vb] = vev
                prev_vev = vev
                tglob = (r * 2048 + tt * TT + sub * 128) // 128
                v_sts.append(p.dma("sync", f"{tag}vs{vb}",
                    lambda e, tglob=tglob, vb=vb: e.dma_start(
                        out=vdr[tglob, :, :], in_=vstage[vb][:]), deps=[vev]))
            ub_reader[bi] = mmc

        mx = p.op("vector", lambda v: v.tensor_reduce(
            rowtmp[0:4, 0:1], knorm[:, :], AX.X, ALU.max), deps=kn_ev)
        sqx = p.op("scalar", lambda e: e.activation(
            rowtmp[0:4, 1:2], rowtmp[0:4, 0:1], AF.Sqrt), deps=[mx])
        mi = p.op("tensor", lambda e: e.matmul(
            ps[2][0:16, 0:1], con["IND"][:], rowtmp[0:4, 1:2], start=True, stop=True),
            deps=[sqx, kn_ev[-1]])
        mie = p.op("vector", lambda v: v.tensor_copy(maxkq[:], ps[2][0:16, 0:1]),
                   deps=[mi])

        q_sts = []
        prev_q = {}
        qn_last = None
        for rt in range(8):
            r, tt = rt // 4, rt % 4
            bi = rt % 2
            seg = slice(r * 2048 + tt * TT, r * 2048 + (tt + 1) * TT)
            lds = []
            for c in range(NCH):
                dd = []
                if ub_reader[bi] is not None and c == 0:
                    dd.append(ub_reader[bi])
                lds.append(p.dma("sync", f"{tag}u{bi}",
                    lambda e, c=c, r=r, tt=tt, bi=bi: e.dma_start(
                        out=ub[bi][:, c * TT:(c + 1) * TT],
                        in_=u_all[r * 2048 + c * 128: r * 2048 + (c + 1) * 128,
                                  tt * TT:(tt + 1) * TT]),
                    deps=dd))
            last_pn_mm = None
            qnmm = None
            for pn in range(8):
                mmd = [lds[-1]]
                if "ev" in prev_q:
                    mmd.append(prev_q["ev"])
                mm = None
                for c in range(NCH):
                    mm = p.op("tensor", lambda e, c=c, bi=bi, pn=pn: e.matmul(
                        ps[0][:, :], qw[pn][:, c * 128:(c + 1) * 128],
                        ub[bi][:, c * TT:(c + 1) * TT],
                        start=(c == 0), stop=(c == NCH - 1)), deps=(mmd if c == 0 else []))
                ev = p.op("vector", lambda v: v.tensor_copy(qraw[:], ps[0][:, :]),
                          deps=[mm] + ([prev_q["pm"]] if "pm" in prev_q else []) +
                               ([prev_q["sq"]] if "sq" in prev_q else []))
                prev_q["ev"] = ev
                pm = p.op("tensor", lambda e: e.matmul(
                    ps[1][:, :], con["permT"][:], qraw[:], start=True, stop=True),
                    deps=[ev] + ([prev_q["r2"]] if "r2" in prev_q else []))
                prev_q["pm"] = pm
                r1 = p.op("vector", lambda v, seg=seg: v.tensor_tensor(
                    rope1[:], qraw[:], con["cos_q"][:, seg], ALU.mult), deps=[])
                r2 = p.op("vector", lambda v, seg=seg: v.tensor_tensor(
                    ropeo[:], ps[1][:, :], con["sin_q"][:, seg], ALU.mult),
                    deps=[pm] + q_sts[-2:])
                prev_q["r2"] = r2
                r3 = p.op("vector", lambda v: v.tensor_tensor(
                    ropeo[:], ropeo[:], rope1[:], ALU.add), deps=[])
                sq = p.op("scalar", lambda e: e.activation(qsq[:], qraw[:], AF.Square),
                          deps=[r1, ev] + ([qnmm] if qnmm else []))
                prev_q["sq"] = sq
                qnd = [sq]
                if pn == 0:
                    qnd.append(qn_last if qn_last is not None else mie)
                qnmm = p.op("tensor", lambda e, pn=pn: e.matmul(
                    ps[2][0:16, 0:TT], con["Hq"][pn][:], qsq[:],
                    start=(pn == 0), stop=(pn == 7)), deps=qnd)
                for half in range(2):
                    qh = 2 * pn + half
                    q_sts.append(p.dma("sync", f"{tag}qs{qh % 2}",
                        lambda e, qh=qh, half=half, seg=seg: e.dma_start(
                            out=qp[qh, 0:64, seg],
                            in_=ropeo[64 * half:64 * (half + 1), :]),
                        deps=[r3]))
            sqa = p.op("scalar", lambda e: e.activation(
                qn_sqrt[:], ps[2][0:16, 0:TT], AF.Sqrt),
                deps=[qnmm] + ([qn_last] if qn_last else []))
            aug = p.op("vector", lambda v: v.tensor_scalar(
                aug16[:], qn_sqrt[:], maxkq[:], None, ALU.mult),
                deps=[sqa, mie])
            qn_last = aug
            q_sts.append(p.dma("sync", f"{tag}qa",
                lambda e, seg=seg: e.dma_start(
                    out=qp[:, 64, seg], in_=aug16[:]), deps=[aug]))
            ub_reader[bi] = mm
        proj_done = [q_sts[-1], q_sts[-2], q_sts[-3],
                     kr_sts[-1], kr_sts[-2], kc_sts[-1], kc_sts[-2],
                     v_sts[-1], v_sts[-2]] + ones_sts

    p.fence([proj_done[0], proj_done[1]] + proj_done)
    o_dram = scratch["odr"]
    with ctx.scope():
        vsb = ctx.sb(f"{tag}_vsb", [128, 32 * 260], BF16)
        ksb = ctx.sb(f"{tag}_ksb", [65, TFULL], BF16)
        qsb = [ctx.sb(f"{tag}_qsb{i}", [65, TT], BF16) for i in range(2)]
        pbuf = [ctx.sb(f"{tag}_pb{i}", [128, TT], BF16) for i in range(3)]
        rsb = ctx.sb(f"{tag}_rsb", [1, TT], F32)
        rbs = ctx.sb(f"{tag}_rbs", [64, TT], F32)
        ostage = [ctx.sb(f"{tag}_ost{i}", [64, TT], BF16) for i in range(2)]
        vlds = [p.dma("sync", f"{tag}vl{ch % 2}", lambda e, ch=ch: e.dma_start(
            out=vsb[:, ch * 260:(ch + 1) * 260], in_=vdr[ch, :, :]),
            deps=proj_done if ch < 2 else []) for ch in range(32)]
        ksb_reader = None
        qsb_reader = {0: None, 1: None}
        pb_reader = {0: None, 1: None, 2: None}
        sband = {0: None, 1: None}
        avband = None
        rb_user = None
        prev_cp = None
        ost_reader = {0: None, 1: None}
        for g in range(4):
            kld = p.dma("sync", f"{tag}kl", lambda e, g=g: e.dma_start(
                out=ksb[:], in_=kp[g, :, :]),
                deps=([ksb_reader] if ksb_reader else []) +
                     (proj_done if g == 0 else []))
            for lq in range(4):
                qh = 4 * g + lq
                for qt in range(8):
                    qb = qt % 2
                    qld = p.dma("sync", f"{tag}ql{qb}", lambda e, qh=qh, qt=qt, qb=qb: e.dma_start(
                        out=qsb[qb][:], in_=qp[qh, :, qt * TT:(qt + 1) * TT]),
                        deps=[qsb_reader[qb]] if qsb_reader[qb] else [])
                    nkb = 4 * (qt + 1)
                    avmm = None
                    for kb in range(nkb):
                        sb = kb % 2
                        smd = [kld, qld]
                        if sband[sb] is not None:
                            smd.append(sband[sb])
                        smm = p.op("tensor", lambda e, kb=kb, sb=sb, qb=qb: e.matmul(
                            ps[sb][:, :], ksb[:, kb * 128:(kb + 1) * 128], qsb[qb][:],
                            start=True, stop=True), deps=smd)
                        pbN = kb % 3
                        exd = [smm]
                        if pb_reader[pbN] is not None:
                            exd.append(pb_reader[pbN])
                        ex = p.op("scalar", lambda e, sb=sb, pbN=pbN: e.activation(
                            pbuf[pbN][:], ps[sb][:, :], AF.Exp, scale=con["eighth"][:]),
                            deps=exd)
                        sband[sb] = ex
                        mdep = [ex]
                        if kb >= 4 * qt:
                            m = kb - 4 * qt
                            mk = p.op("vector", lambda v, m=m, pbN=pbN: v.tensor_tensor(
                                pbuf[pbN][:], pbuf[pbN][:], con["tri"][m][:], ALU.mult),
                                deps=[ex])
                            mdep = [mk]
                        avd = mdep + [vlds[kb]]
                        if kb == 0:
                            if avband is not None:
                                avd.append(avband)
                        avmm = p.op("tensor", lambda e, kb=kb, g=g, pbN=pbN: e.matmul(
                            ps[4][0:65, :], vsb[:, kb * 260 + 65 * g: kb * 260 + 65 * (g + 1)],
                            pbuf[pbN][:], start=(kb == 0), stop=(kb == nkb - 1)),
                            deps=avd)
                        pb_reader[pbN] = avmm
                    qsb_reader[qb] = avmm
                    rc = p.op("vector", lambda v: v.reciprocal(rsb[:], ps[4][64:65, :]),
                              deps=[avmm] + ([rb_user] if rb_user else []))
                    bc = p.op("tensor", lambda e: e.matmul(
                        ps[5][0:64, 0:TT], con["onesr64"][:], rsb[:],
                        start=True, stop=True), deps=[rc] + ([prev_cp] if prev_cp else []))
                    cp = p.op("vector", lambda v: v.tensor_copy(rbs[:], ps[5][0:64, 0:TT]),
                              deps=[bc])
                    prev_cp = cp
                    ob = qt % 2
                    mu_d = [cp]
                    if ost_reader[ob] is not None:
                        mu_d.append(ost_reader[ob])
                    mu = p.op("vector", lambda v, ob=ob: v.tensor_tensor(
                        ostage[ob][:], ps[4][0:64, 0:TT], rbs[:], ALU.mult), deps=mu_d)
                    rb_user = mu
                    avband = mu
                    st = p.dma("sync", f"{tag}os{ob}", lambda e, qh=qh, qt=qt, ob=ob: e.dma_start(
                        out=o_dram[qh * 64:(qh + 1) * 64, qt * TT:(qt + 1) * TT],
                        in_=ostage[ob][:]), deps=[mu])
                    ost_reader[ob] = st
                    o_sts.append(st)
            ksb_reader = avmm
    p.fence(o_sts[-2:])
    return o_sts



def setup_l0attn_consts(ctx, sh, nc):
    p = ctx.plan
    ins = {}
    for nm, shape in [("cos_q", [128, TFULL]), ("sin_q", [128, TFULL]),
                      ("cos_k", [128, TFULL]), ("sin_k", [128, TFULL]),
                      ("permT", [128, 128]), ("H4", [128, 4]),
                      ("Hq", [8, 128, 16]), ("IND", [4, 16]),
                      ("tri_mul", [4, 128, 512])]:
        ins[nm] = nc.dram_tensor(nm, shape, F32, kind="ExternalInput").ap()
    con = {}
    rdy = []
    for nm in ["cos_q", "sin_q", "cos_k", "sin_k"]:
        con[nm] = ctx.sb(f"c_{nm}", [128, TFULL], BF16)
        rdy.append(p.dma("gpsimd", "cst", lambda e, nm=nm: e.dma_start(
            out=con[nm][:], in_=ins[nm][:, :]), deps=()))
    con["permT"] = ctx.sb("c_permT", [128, 128], BF16)
    rdy.append(p.dma("gpsimd", "cst", lambda e: e.dma_start(
        out=con["permT"][:], in_=ins["permT"][:, :]), deps=()))
    con["H4"] = ctx.sb("c_H4", [128, 4], BF16)
    rdy.append(p.dma("gpsimd", "cst", lambda e: e.dma_start(
        out=con["H4"][:], in_=ins["H4"][:, :]), deps=()))
    con["Hq"] = []
    for pn in range(8):
        t = ctx.sb(f"c_Hq{pn}", [128, 16], BF16)
        con["Hq"].append(t)
        rdy.append(p.dma("gpsimd", "cst", lambda e, pn=pn, t=t: e.dma_start(
            out=t[:], in_=ins["Hq"][pn, :, :]), deps=()))
    con["IND"] = ctx.sb("c_IND", [4, 16], F32)
    rdy.append(p.dma("gpsimd", "cst", lambda e: e.dma_start(
        out=con["IND"][:], in_=ins["IND"][:, :]), deps=()))
    con["tri"] = []
    for m in range(4):
        t = ctx.sb(f"c_tri{m}", [128, 512], BF16)
        con["tri"].append(t)
        rdy.append(p.dma("gpsimd", "cst", lambda e, m=m, t=t: e.dma_start(
            out=t[:], in_=ins["tri_mul"][m, :, :]), deps=()))
    con["onesr64"] = ctx.sb("c_or64", [1, 64], F32)
    rdy.append(p.op("vector", lambda v: v.memset(con["onesr64"][:], 1.0)))
    con["eighth"] = ctx.sb("c_eighth", [128, 1], F32)
    rdy.append(p.op("vector", lambda v: v.memset(con["eighth"][:], 0.125)))
    return con, rdy


def build_l0attn_launch():
    nc = bass.Bass("TRN2", target_bir_lowering=False)
    stack = contextlib.ExitStack()
    xin = nc.dram_tensor("x", [D, T], F32, kind="ExternalInput").ap()
    W = dict(
        qwT=nc.dram_tensor("qwT", [8, 2048, 128], F32, kind="ExternalInput").ap(),
        krT=nc.dram_tensor("krT", [1, 2048, 128], F32, kind="ExternalInput").ap(),
        dnT=nc.dram_tensor("dnT", [2, 2048, 128], F32, kind="ExternalInput").ap(),
        upkT=nc.dram_tensor("upkT", [256, 128], F32, kind="ExternalInput").ap(),
        upvT=nc.dram_tensor("upvT", [256, 256], F32, kind="ExternalInput").ap(),
    )
    owT = nc.dram_tensor("owT", [16, 1024, 128], F32, kind="ExternalInput").ap()
    xout = nc.dram_tensor("xo", [D, T], F32, kind="ExternalOutput").ap()
    cc_u = nc.dram_tensor("cc_u", [D, T], BF16, kind="Internal").ap()
    u_all = nc.dram_tensor("u_all", [2 * D, T], BF16, kind="Internal").ap()
    dbg_kind = "ExternalOutput" if globals().get("_ATTN_DEBUG") else "Internal"
    qp = nc.dram_tensor("qp_scr", [16, 65, TFULL], BF16, kind=dbg_kind).ap()
    kp = nc.dram_tensor("kp_scr", [4, 65, TFULL], BF16, kind=dbg_kind).ap()
    vdr = nc.dram_tensor("v_scr", [32, 128, 260], BF16, kind=dbg_kind).ap()
    odr = nc.dram_tensor("o_scr", [1024, TFULL], BF16, kind=dbg_kind).ap()
    cc2_in = nc.dram_tensor("cc2_in", [2 * D, T], F32, kind="Internal").ap()
    cc2_out = nc.dram_tensor("cc2_out", [D, T], F32, kind="Internal").ap()

    ctx = Ctx(nc, stack)
    sh = Shared(ctx)
    con, crdy = setup_l0attn_consts(ctx, sh, nc)
    ug = emit_ugather(ctx, sh, xin, u_all, cc_u, "ug")
    ctx.plan.fence(crdy)
    scratch = dict(u_all=u_all, qp=qp, kp=kp, vdr=vdr, odr=odr,
                   gather_op=ug["gather"])
    o_sts = emit_l0attn(ctx, sh, con, xin, xout, W, scratch, "a0")
    sts = emit_oproj_exchange(ctx, sh, odr, o_sts[-2:], owT, cc2_in, cc2_out,
                              xin, xout, None, "a0x")
    p = ctx.plan
    p.op("gpsimd", lambda g: g.nop(), deps=sts[-4:])
    p.emit()
    stack.close()
    return nc


def setup_l1attn_consts(ctx, sh, nc):
    p = ctx.plan
    ins = {}
    for nm, shape in [("iota8neg", [8, TFULL]), ("jcols", [128, 32]),
                      ("tri_add", [4, 128, 512]), ("e8T", [8, 8, 128]),
                      ("E8", [8, 128, 8])]:
        ins[nm] = nc.dram_tensor(nm, shape, F32, kind="ExternalInput").ap()
    vbi = nc.dram_tensor("vb", [1, 1024], F32, kind="ExternalInput").ap()
    con = {}
    rdy = []
    con["iota8neg"] = ctx.sb("c_io8", [8, TFULL], BF16)
    rdy.append(p.dma("gpsimd", "cst1", lambda e: e.dma_start(
        out=con["iota8neg"][:], in_=ins["iota8neg"][:, :]), deps=()))
    con["jcols"] = ctx.sb("c_jc", [128, 32], F32)
    rdy.append(p.dma("gpsimd", "cst1", lambda e: e.dma_start(
        out=con["jcols"][:], in_=ins["jcols"][:, :]), deps=()))
    con["triA"] = []
    for m in range(4):
        t = ctx.sb(f"c_triA{m}", [128, 512], F32)
        con["triA"].append(t)
        rdy.append(p.dma("gpsimd", "cst1", lambda e, m=m, t=t: e.dma_start(
            out=t[:], in_=ins["tri_add"][m, :, :]), deps=()))
    con["e8T"] = []
    for hh in range(8):
        t = ctx.sb(f"c_e8T{hh}", [8, 128], BF16)
        con["e8T"].append(t)
        rdy.append(p.dma("gpsimd", "cst1", lambda e, hh=hh, t=t: e.dma_start(
            out=t[:], in_=ins["e8T"][hh, :, :]), deps=()))
    con["E8"] = []
    for pn in range(8):
        t = ctx.sb(f"c_E8{pn}", [128, 8], BF16)
        con["E8"].append(t)
        rdy.append(p.dma("gpsimd", "cst1", lambda e, pn=pn, t=t: e.dma_start(
            out=t[:], in_=ins["E8"][pn, :, :]), deps=()))
    con["vb_row"] = ctx.sb("c_vbr", [1, 1024], F32)
    rdy.append(p.dma("gpsimd", "cst1", lambda e: e.dma_start(
        out=con["vb_row"][:], in_=vbi[:, :]), deps=()))
    con["sc1"] = ctx.sb("c_sc1", [128, 1], F32)
    rdy.append(p.op("vector", lambda v: v.memset(con["sc1"][:], 1.0 / math.sqrt(128.0))))
    con["onesr64"] = ctx.sb("c_or64b", [1, 128], F32)
    rdy.append(p.op("vector", lambda v: v.memset(con["onesr64"][:], 1.0)))
    return con, rdy


def emit_l1attn(ctx, sh, con, W, scratch, tag):
    p = ctx.plan
    ps = ctx.ps
    TT = 512
    u_all = scratch["u_all"]
    qdr, kdr, vdr, odr = scratch["qdr"], scratch["kdr"], scratch["vdr1"], scratch["odr"]
    gather = scratch["gather_op"]
    o_sts = []
    qn_sb = ctx.sb(f"{tag}_qn", [8, TFULL], BF16)
    kn_sb = ctx.sb(f"{tag}_kn", [8, TFULL], BF16)
    maxq8 = ctx.sb(f"{tag}_mq8", [8, 2], F32)
    Mrows = ctx.sb(f"{tag}_Mr", [8, TFULL], BF16)
    Mtmp = ctx.sb(f"{tag}_Mt", [8, TT], F32)
    Mtmp2 = ctx.sb(f"{tag}_Mt2", [8, TT], BF16)
    scope1 = ctx.scope()
    scope1.__enter__()
    if True:
        qw = [ctx.sb(f"{tag}_qw{i}", [128, NCH * 128], BF16) for i in range(8)]
        kw = [ctx.sb(f"{tag}_kw{i}", [128, NCH * 128], BF16) for i in range(8)]
        vw = ctx.sb(f"{tag}_vw", [128, NCH * 1024], BF16)
        qb = ctx.sb(f"{tag}_qb", [128, 8], F32)
        kb = ctx.sb(f"{tag}_kb", [128, 8], F32)
        vbb = ctx.sb(f"{tag}_vbb", [128, 1024], F32)
        wlds = []
        for pn in range(8):
            wlds.append(p.dma("gpsimd", f"{tag}wq{pn % 2}", lambda e, pn=pn: e.dma_start(
                out=qw[pn].ap().rearrange("p (c o) -> p c o", o=128),
                in_=W["qwT"][pn].rearrange("(c p) o -> p c o", p=128)), deps=()))
            wlds.append(p.dma("gpsimd", f"{tag}wk{pn % 2}", lambda e, pn=pn: e.dma_start(
                out=kw[pn].ap().rearrange("p (c o) -> p c o", o=128),
                in_=W["kwT"][pn].rearrange("(c p) o -> p c o", p=128)), deps=()))
        wlds.append(p.dma("gpsimd", f"{tag}wv", lambda e: e.dma_start(
            out=vw.ap().rearrange("p (c o) -> p c o", o=1024),
            in_=W["vwT"].rearrange("(c p) o -> p c o", p=128)), deps=()))
        wlds.append(p.dma("gpsimd", f"{tag}wb", lambda e: e.dma_start(
            out=qb[:], in_=W["qb"][:, :]), deps=()))
        wlds.append(p.dma("gpsimd", f"{tag}wb", lambda e: e.dma_start(
            out=kb[:], in_=W["kb"][:, :]), deps=()))
        vb1 = p.op("tensor", lambda e: e.matmul(
            ps[6][:, :], con["onesr64"][0:1, :], con["vb_row"][0:1, 0:512],
            start=True, stop=True), deps=scratch["vb_deps"] + [sh.ready[2]])
        vb2 = p.op("tensor", lambda e: e.matmul(
            ps[7][:, :], con["onesr64"][0:1, :], con["vb_row"][0:1, 512:1024],
            start=True, stop=True), deps=scratch["vb_deps"])
        vbc = p.op("vector", lambda v: (
            v.tensor_copy(vbb[:, 0:512], ps[6][:, :]),
            v.tensor_copy(vbb[:, 512:1024], ps[7][:, :]))[-1], deps=[vb1, vb2])

        ub = [ctx.sb(f"{tag}_ub{i}", [128, NCH * TT], BF16) for i in range(2)]
        stage = [ctx.sb(f"{tag}_st{i}", [128, TT], BF16) for i in range(2)]
        sqb = [ctx.sb(f"{tag}_sq{i}", [128, TT], BF16) for i in range(2)]
        vstage = [ctx.sb(f"{tag}_vs{i}", [128, 1024], BF16) for i in range(2)]

        ub_reader = {0: None, 1: None}
        q_sts, k_sts, v_sts = [], [], []
        prev = {}
        nrm_ev = {"q": [], "k": []}
        for rt in range(8):
            r, tt = rt // 4, rt % 4
            bi = rt % 2
            seg = slice(r * 2048 + tt * TT, r * 2048 + (tt + 1) * TT)
            lds = []
            for c in range(NCH):
                dd = [gather] if rt < 2 and c < 2 else []
                if ub_reader[bi] is not None and c == 0:
                    dd.append(ub_reader[bi])
                lds.append(p.dma("sync", f"{tag}u{bi}",
                    lambda e, c=c, r=r, tt=tt, bi=bi: e.dma_start(
                        out=ub[bi][:, c * TT:(c + 1) * TT],
                        in_=u_all[r * 2048 + c * 128: r * 2048 + (c + 1) * 128,
                                  tt * TT:(tt + 1) * TT]),
                    deps=dd))
            last_mm = None
            for kind, wws, bcol, dst, sts, nbank in (
                    ("q", qw, qb, qdr, q_sts, 2), ("k", kw, kb, kdr, k_sts, 3)):
                pbk = 0 if kind == "q" else 1
                nmm = None
                for pn in range(8):
                    mmd = [lds[-1]] + (wlds[:2] if rt == 0 and pn == 0 else [])
                    if f"{kind}ev" in prev:
                        mmd.append(prev[f"{kind}ev"])
                    mm = None
                    for c in range(NCH):
                        mm = p.op("tensor", lambda e, c=c, bi=bi, pn=pn, wws=wws, pbk=pbk: e.matmul(
                            ps[pbk][:, :],
                            wws[pn][:, c * 128:(c + 1) * 128],
                            ub[bi][:, c * TT:(c + 1) * TT],
                            start=(c == 0), stop=(c == NCH - 1)),
                            deps=(mmd if c == 0 else []))
                    sb_ = stage[0 if kind == "q" else 1]
                    sq_ = sqb[0 if kind == "q" else 1]
                    evd = [mm]
                    if f"{kind}st" in prev:
                        evd.append(prev[f"{kind}st"])
                    if f"{kind}sq" in prev:
                        evd.append(prev[f"{kind}sq"])
                    ev = p.op("vector", lambda v, sb_=sb_, pn=pn, bcol=bcol, pbk=pbk: v.tensor_scalar(
                        sb_[:], ps[pbk][:, :],
                        bcol[:, pn:pn + 1], None, ALU.add), deps=evd)
                    prev[f"{kind}ev"] = ev
                    sqo = p.op("scalar", lambda e, sb_=sb_, sq_=sq_: e.activation(
                        sq_[:], sb_[:], AF.Square),
                        deps=[ev] + ([nmm] if nmm else []))
                    prev[f"{kind}sq"] = sqo
                    nd = [sqo]
                    if pn == 0:
                        if nrm_ev[kind]:
                            nd.append(nrm_ev[kind][-1])
                    nmm = p.op("tensor", lambda e, pn=pn, sq_=sq_, nbank=nbank: e.matmul(
                        ps[nbank][0:8, 0:TT], con["E8"][pn][:], sq_[:],
                        start=(pn == 0), stop=(pn == 7)), deps=nd)
                    st = p.dma("sync", f"{tag}{kind}s{pn % 2}",
                        lambda e, pn=pn, seg=seg, sb_=sb_, dst=dst: e.dma_start(
                            out=dst[pn * 128:(pn + 1) * 128, seg], in_=sb_[:]),
                        deps=[ev])
                    prev[f"{kind}st"] = st
                    sts.append(st)
                ne = p.op("vector", lambda v, seg=seg, nbank=nbank,
                          nsb=(qn_sb if kind == "q" else kn_sb): v.tensor_copy(
                    nsb[:, seg], ps[nbank][0:8, 0:TT]), deps=[nmm])
                nrm_ev[kind].append(ne)
            for sub in range(4):
                vbuf = sub % 2
                for half in range(2):
                    pv = 6 + half
                    mv = None
                    for c in range(NCH):
                        mvd = []
                        if c == 0:
                            mvd = [lds[-1], wlds[16]] if rt == 0 and sub == 0 else [lds[-1]]
                            if prev.get(f"vev{pv}") is not None:
                                mvd.append(prev[f"vev{pv}"])
                            if rt == 0 and sub == 0 and half == 0:
                                mvd.append(vbc)
                        mv = p.op("tensor", lambda e, c=c, sub=sub, half=half, bi=bi, pv=pv: e.matmul(
                            ps[pv][:, :],
                            ub[bi][:, c * TT + sub * 128: c * TT + (sub + 1) * 128],
                            vw[:, c * 1024 + half * 512: c * 1024 + (half + 1) * 512],
                            start=(c == 0), stop=(c == NCH - 1)),
                            deps=mvd)
                    vvd = [mv, vbc]
                    if prev.get(f"vst{vbuf}") is not None and half == 0:
                        vvd.append(prev[f"vst{vbuf}"])
                    vv = p.op("vector", lambda v, vbuf=vbuf, half=half, pv=pv: v.tensor_tensor(
                        vstage[vbuf][:, half * 512:(half + 1) * 512],
                        ps[pv][:, :], vbb[:, half * 512:(half + 1) * 512], ALU.add),
                        deps=vvd)
                    prev[f"vev{pv}"] = vv
                tglob = (r * 2048 + tt * TT + sub * 128) // 128
                vst = p.dma("sync", f"{tag}vs{vbuf}",
                    lambda e, tglob=tglob, vbuf=vbuf: e.dma_start(
                        out=vdr[tglob, :, :], in_=vstage[vbuf][:]),
                    deps=[prev["vev6"], prev["vev7"]])
                prev[f"vst{vbuf}"] = vst
                v_sts.append(vst)
            ub_reader[bi] = mv

        mx = p.op("vector", lambda v: v.tensor_reduce(
            maxq8[:, 0:1], qn_sb[:, :], AX.X, ALU.max), deps=nrm_ev["q"])
        sqx = p.op("scalar", lambda e: e.activation(
            maxq8[:, 1:2], maxq8[:, 0:1], AF.Sqrt), deps=[mx])
        ngx = p.op("vector", lambda v: v.tensor_scalar(
            maxq8[:, 1:2], maxq8[:, 1:2], -1.0, None, ALU.mult), hard=[sqx])
        mrows_done = []
        prev_mt = None
        for sgi in range(8):
            seg = slice(sgi * TT, (sgi + 1) * TT)
            s1 = p.op("scalar", lambda e, seg=seg: e.activation(
                Mtmp[:], kn_sb[:, seg], AF.Sqrt),
                deps=[ngx, nrm_ev["k"][-1]] + ([prev_mt] if prev_mt else []))
            s2 = p.op("vector", lambda v, seg=seg: v.tensor_scalar(
                Mtmp2[:], Mtmp[:], maxq8[:, 1:2], None, ALU.mult), deps=[s1])
            s3 = p.op("vector", lambda v, seg=seg: v.tensor_tensor(
                Mrows[:, seg], Mtmp2[:], con["iota8neg"][:, seg], ALU.add), hard=[s2])
            prev_mt = s3
            mrows_done.append(s3)

    scope1.__exit__(None, None, None)
    first_tail = [q_sts[-1], q_sts[-2], k_sts[-1], k_sts[-2],
                  v_sts[-1], v_sts[-2], mrows_done[-1],
                  ub_reader[0], ub_reader[1], nrm_ev["q"][-1], nrm_ev["k"][-1]]
    p.fence(first_tail)
    with ctx.scope():
        qtl = [ctx.sb(f"{tag}_qt{i}", [128, 128], BF16) for i in range(2)]
        ktl = [ctx.sb(f"{tag}_kt{i}", [128, TT], BF16) for i in range(2)]
        vtl = [ctx.sb(f"{tag}_vt{i}", [128, 128], BF16) for i in range(2)]
        pbuf = [ctx.sb(f"{tag}_pb{i}", [128, TT], BF16) for i in range(3)]
        rsb = ctx.sb(f"{tag}_rsb", [1, TT], F32)
        rbs = ctx.sb(f"{tag}_rbs", [128, TT], F32)
        ostage = [ctx.sb(f"{tag}_os{i}", [128, TT], BF16) for i in range(2)]
        qt_reader = {0: None, 1: None}
        kt_reader = {0: None, 1: None}
        vt_reader = {0: None, 1: None}
        pb_reader = {0: None, 1: None, 2: None}
        sband = {0: None, 1: None}
        av_user = None
        den_user = None
        prev_cp = None
        ost_reader = {0: None, 1: None}
        first_score_deps = [q_sts[-1], q_sts[-2], k_sts[-1], k_sts[-2],
                           v_sts[-1], v_sts[-2], mrows_done[-1]]
        for hh in range(8):
            for it in range(8):
                jb_lo = max(0, 4 * it - 6)
                jb_hi = 4 * it + 3
                kldd = list(first_score_deps) if hh == 0 and it == 0 else []
                if kt_reader[it % 2] is not None:
                    kldd.append(kt_reader[it % 2])
                kld = p.dma("sync", f"{tag}ktl{it % 2}",
                    lambda e, hh=hh, it=it: e.dma_start(
                        out=ktl[it % 2][:],
                        in_=kdr[hh * 128:(hh + 1) * 128, it * TT:(it + 1) * TT]),
                    deps=kldd)
                avmm = None
                denmm = None
                njb = jb_hi - jb_lo + 1
                for ji, jb in enumerate(range(jb_lo, jb_hi + 1)):
                    qbuf = ji % 2
                    qldd = [qt_reader[qbuf]] if qt_reader[qbuf] else []
                    if hh == 0 and it == 0 and ji < 2:
                        qldd += first_score_deps[:2]
                    qld = p.dma("sync", f"{tag}qtl{qbuf}",
                        lambda e, hh=hh, jb=jb, qbuf=qbuf: e.dma_start(
                            out=qtl[qbuf][:],
                            in_=qdr[hh * 128:(hh + 1) * 128, jb * 128:(jb + 1) * 128]),
                        deps=qldd)
                    vldd = [vt_reader[qbuf]] if vt_reader[qbuf] else []
                    vld = p.dma("sync", f"{tag}vtl{qbuf}",
                        lambda e, hh=hh, jb=jb, qbuf=qbuf: e.dma_start(
                            out=vtl[qbuf][:],
                            in_=vdr[jb, :, hh * 128:(hh + 1) * 128]),
                        deps=vldd)
                    sb_ = ji % 2
                    augd = [mrows_done[it]]
                    if sband[sb_] is not None:
                        augd.append(sband[sb_])
                    p.op("tensor", lambda e, hh=hh, it=it, sb_=sb_: e.matmul(
                        ps[sb_][:, :], con["e8T"][hh][:],
                        Mrows[0:8, it * TT:(it + 1) * TT],
                        start=True, stop=False), deps=augd)
                    smm = p.op("tensor", lambda e, sb_=sb_, qbuf=qbuf, it=it: e.matmul(
                        ps[sb_][:, :], qtl[qbuf][:], ktl[it % 2][:],
                        start=False, stop=True), deps=[qld, kld])
                    qt_reader[qbuf] = smm
                    exd_extra = []
                    if jb >= 4 * it:
                        m = jb - 4 * it
                        mk = p.op("vector", lambda v, m=m, sb_=sb_: v.tensor_tensor(
                            ps[sb_][:, :], ps[sb_][:, :], con["triA"][m][:], ALU.add),
                            deps=[smm])
                        exd_extra = [mk]
                    pbN = ji % 3
                    exd = [smm] + exd_extra
                    if pb_reader[pbN] is not None:
                        exd.append(pb_reader[pbN])
                    ex = p.op("scalar", lambda e, sb_=sb_, pbN=pbN, jb=jb: e.activation(
                        pbuf[pbN][:], ps[sb_][:, :], AF.Exp, scale=con["sc1"][:],
                        bias=con["jcols"][:, jb:jb + 1]), deps=exd)
                    sband[sb_] = ex
                    dend = [ex, sh.ready[1]]
                    if ji == 0 and den_user is not None:
                        dend.append(den_user)
                    denmm = p.op("tensor", lambda e, ji=ji, pbN=pbN, njb=njb: e.matmul(
                        ps[4][0:1, :], sh.ones_b[:], pbuf[pbN][:],
                        start=(ji == 0), stop=(ji == njb - 1)), deps=dend)
                    avd = [ex, vld]
                    if ji == 0 and av_user is not None:
                        avd.append(av_user)
                    avmm = p.op("tensor", lambda e, ji=ji, pbN=pbN, qbuf=qbuf, njb=njb: e.matmul(
                        ps[5][:, :], vtl[qbuf][:], pbuf[pbN][:],
                        start=(ji == 0), stop=(ji == njb - 1)), deps=avd)
                    vt_reader[qbuf] = avmm
                    pb_reader[pbN] = avmm
                kt_reader[it % 2] = smm
                rc = p.op("vector", lambda v: v.reciprocal(rsb[:], ps[4][0:1, :]),
                          deps=[denmm])
                den_user = rc
                bc = p.op("tensor", lambda e: e.matmul(
                    ps[3][:, :], con["onesr64"][0:1, :], rsb[:],
                    start=True, stop=True),
                    deps=[rc] + ([prev_cp] if prev_cp else []) +
                         (nrm_ev["k"][-1:] if hh == 0 and it == 0 else []))
                cp = p.op("vector", lambda v: v.tensor_copy(rbs[:], ps[3][:, :]),
                          deps=[bc])
                prev_cp = cp
                ob = it % 2
                mud = [cp]
                if ost_reader[ob] is not None:
                    mud.append(ost_reader[ob])
                mu = p.op("vector", lambda v, ob=ob: v.tensor_tensor(
                    ostage[ob][:], ps[5][:, :], rbs[:], ALU.mult), deps=mud)
                av_user = mu
                st = p.dma("sync", f"{tag}osx{ob}",
                    lambda e, hh=hh, it=it, ob=ob: e.dma_start(
                        out=odr[hh * 128:(hh + 1) * 128, it * TT:(it + 1) * TT],
                        in_=ostage[ob][:]), deps=[mu])
                ost_reader[ob] = st
                o_sts.append(st)
    p.fence(o_sts[-2:])
    return o_sts


def build_l1attn_launch():
    nc = bass.Bass("TRN2", target_bir_lowering=False)
    stack = contextlib.ExitStack()
    xin = nc.dram_tensor("x", [D, T], F32, kind="ExternalInput").ap()
    W = dict(
        qwT=nc.dram_tensor("qwT", [8, 2048, 128], F32, kind="ExternalInput").ap(),
        kwT=nc.dram_tensor("kwT", [8, 2048, 128], F32, kind="ExternalInput").ap(),
        vwT=nc.dram_tensor("vwT", [2048, 1024], F32, kind="ExternalInput").ap(),
        qb=nc.dram_tensor("qb", [128, 8], F32, kind="ExternalInput").ap(),
        kb=nc.dram_tensor("kb", [128, 8], F32, kind="ExternalInput").ap(),
    )
    owT = nc.dram_tensor("owT", [16, 1024, 128], F32, kind="ExternalInput").ap()
    obi = nc.dram_tensor("ob", [128, 16], F32, kind="ExternalInput").ap()
    xout = nc.dram_tensor("xo", [D, T], F32, kind="ExternalOutput").ap()
    cc_u = nc.dram_tensor("cc_u", [D, T], BF16, kind="Internal").ap()
    u_all = nc.dram_tensor("u_all", [2 * D, T], BF16, kind="Internal").ap()
    dbg_kind = "ExternalOutput" if globals().get("_ATTN_DEBUG") else "Internal"
    qdr = nc.dram_tensor("q_scr", [1024, TFULL], BF16, kind=dbg_kind).ap()
    kdr = nc.dram_tensor("k_scr", [1024, TFULL], BF16, kind=dbg_kind).ap()
    vdr1 = nc.dram_tensor("v_scr", [32, 128, 1024], BF16, kind=dbg_kind).ap()
    odr = nc.dram_tensor("o_scr", [1024, TFULL], BF16, kind=dbg_kind).ap()
    cc2_in = nc.dram_tensor("cc2_in", [2 * D, T], F32, kind="Internal").ap()
    cc2_out = nc.dram_tensor("cc2_out", [D, T], F32, kind="Internal").ap()

    ctx = Ctx(nc, stack)
    sh = Shared(ctx)
    con, crdy = setup_l1attn_consts(ctx, sh, nc)
    ob_cols = ctx.sb("ob_cols", [128, 16], F32)
    obl = ctx.plan.dma("gpsimd", "cst1", lambda e: e.dma_start(
        out=ob_cols[:], in_=obi[:, :]), deps=())
    ug = emit_ugather(ctx, sh, xin, u_all, cc_u, "ug")
    ctx.plan.fence(crdy + [obl])
    scratch = dict(u_all=u_all, qdr=qdr, kdr=kdr, vdr1=vdr1, odr=odr,
                   gather_op=ug["gather"], vb_deps=[crdy[-3]])
    o_sts = emit_l1attn(ctx, sh, con, W, scratch, "a1")
    sts = emit_oproj_exchange(ctx, sh, odr, o_sts[-2:], owT, cc2_in, cc2_out,
                              xin, xout, ob_cols, "a1x")
    p = ctx.plan
    p.op("gpsimd", lambda g: g.nop(), deps=sts[-4:])
    p.emit()
    stack.close()
    return nc


def build_l0ffn_launch(debug=False):
    nc = bass.Bass("TRN2", target_bir_lowering=False)
    stack = contextlib.ExitStack()
    xin = nc.dram_tensor("x", [D, T], F32, kind="ExternalInput").ap()
    wfcT = nc.dram_tensor("wfcT", [64, 2048, 128], F32, kind="ExternalInput").ap()
    wprojT = nc.dram_tensor("wprojT", [16, 8192, 128], F32, kind="ExternalInput").ap()
    xout = nc.dram_tensor("xo", [D, T], F32, kind="ExternalOutput").ap()
    kind = "ExternalOutput" if debug else "Internal"
    tprojRB = nc.dram_tensor("tprojRB", [64, 128, 2048], BF16, kind=kind).ap()
    h_dram = nc.dram_tensor("h_scr", [8192, T], BF16, kind=kind).ap()
    if debug:
        dbg = nc.dram_tensor("dbg", [128, NCH * T], BF16, kind="ExternalOutput").ap()
        dbg2 = nc.dram_tensor("dbg2", [128, 8], F32, kind="ExternalOutput").ap()

    ctx = Ctx(nc, stack)
    sh = Shared(ctx)
    xn_sb = ctx.sb("xn", [128, NCH * T], BF16)
    prep = emit_l0ffn_prep(ctx, sh, wfcT, wprojT, tprojRB, "pp")
    ctx.plan.fence(prep["tail"])
    ln = emit_ln(ctx, sh, xin, xn_sb, "ln", compose_inner=True)
    ctx.plan.fence(ln["tail"])
    sts = emit_l0ffn_main(ctx, sh, prep, xn_sb, ln["tile_done"], xin, xout,
                          wfcT, tprojRB, h_dram, "ff")
    p = ctx.plan
    if debug:
        sd = p.dma("sync", "dbgxn", lambda e: e.dma_start(out=dbg[:, :], in_=xn_sb[:]),
                   deps=ln["tile_done"])

        def dbgcols(v):
            v.tensor_copy(dbg2s[:, 0:1], prep["gfc_col"][:])
            v.tensor_copy(dbg2s[:, 1:2], prep["thr_fc"][:])
            v.tensor_copy(dbg2s[:, 2:3], prep["thr_pj"][:])
            return v.tensor_copy(dbg2s[0:1, 3:4], prep["gph"][0:1, :])
        dbg2s = ctx.sb("dbg2s", [128, 8], F32)
        z = p.op("vector", lambda v: v.memset(dbg2s[:], 0.0))
        dc = p.op("vector", dbgcols, deps=[prep["cc"], z])
        sd2 = p.dma("sync", "dbgc", lambda e: e.dma_start(out=dbg2[:, :], in_=dbg2s[:]),
                    deps=[dc])
        p.op("gpsimd", lambda g: g.nop(), deps=sts[-16:] + [sd, sd2])
    else:
        p.op("gpsimd", lambda g: g.nop(), deps=sts[-16:])
    p.emit()
    stack.close()
    return nc


def build_fused():
    nc = bass.Bass("TRN2", target_bir_lowering=False)
    stack = contextlib.ExitStack()
    A = lambda n, s, dt=F32, k="ExternalInput": nc.dram_tensor(n, s, dt, kind=k).ap()
    x_in = A("x", [D, T])
    wfcT = A("wfcT", [64, 2048, 128])
    wprojT = A("wprojT", [16, 8192, 128])
    guT = A("guT", [128, 2048, 128])
    downRB = A("downRB", [64, 128, 2048])
    W0 = dict(qwT=A("qwT0", [8, 2048, 128]), krT=A("krT", [1, 2048, 128]),
              dnT=A("dnT", [2, 2048, 128]), upkT=A("upkT", [256, 128]),
              upvT=A("upvT", [256, 256]))
    owT0 = A("owT0", [16, 1024, 128])
    W1 = dict(qwT=A("qwT1", [8, 2048, 128]), kwT=A("kwT1", [8, 2048, 128]),
              vwT=A("vwT1", [2048, 1024]), qb=A("qb1", [128, 8]),
              kb=A("kb1", [128, 8]))
    owT1 = A("owT1", [16, 1024, 128])
    obi = A("ob1", [128, 16])
    xo = A("xo", [D, T], F32, "ExternalOutput")
    xi0 = A("xi0", [D, T], F32, "Internal")
    xi1 = A("xi1", [D, T], F32, "Internal")
    h_dram = A("h_scr", [8192, T], BF16, "Internal")
    tprojRB = A("tprojRB", [64, 128, 2048], BF16, "Internal")
    cc_u = A("cc_u", [D, T], BF16, "Internal")
    u_all = A("u_all", [2 * D, T], BF16, "Internal")
    cc2_in = A("cc2_in", [2 * D, T], F32, "Internal")
    cc2_out = A("cc2_out", [D, T], F32, "Internal")
    qp = A("qp_scr", [16, 65, TFULL], BF16, "Internal")
    kp = A("kp_scr", [4, 65, TFULL], BF16, "Internal")
    vdr0 = A("v_scr0", [32, 128, 260], BF16, "Internal")
    odr = A("o_scr", [1024, TFULL], BF16, "Internal")
    qdr = A("q_scr", [1024, TFULL], BF16, "Internal")
    kdr = A("k_scr", [1024, TFULL], BF16, "Internal")
    vdr1 = A("v_scr1", [32, 128, 1024], BF16, "Internal")

    ctx = Ctx(nc, stack)
    sh = Shared(ctx)
    p = ctx.plan
    prep = emit_l0ffn_prep(ctx, sh, wfcT, wprojT, tprojRB, "pp")
    p.fence(prep["tail"])

    def l0ffn(xa, xb, tag):
        with ctx.scope():
            xn_sb = ctx.sb(f"{tag}_xn", [128, NCH * T], BF16)
            ln = emit_ln(ctx, sh, xa, xn_sb, f"{tag}l", compose_inner=True)
            p.fence(ln["tail"])
            sts = emit_l0ffn_main(ctx, sh, prep, xn_sb, ln["tile_done"], xa, xb,
                                  wfcT, tprojRB, h_dram, tag)
        p.fence(sts[-4:])

    def l1ffn(xa, xb, tag):
        with ctx.scope():
            u_sb = ctx.sb(f"{tag}_u", [128, NCH * T], BF16)
            ln = emit_ln(ctx, sh, xa, u_sb, f"{tag}l", compose_inner=False, eps=1e-5)
            p.fence(ln["tail"])
            sts = emit_l1ffn(ctx, sh, u_sb, ln["tile_done"], xa, xb, guT, downRB,
                             h_dram, tag)
        p.fence(sts[-4:])

    def l0attn(xa, xb, tag):
        with ctx.scope():
            con, crdy = setup_l0attn_consts(ctx, sh, nc)
            ug = emit_ugather(ctx, sh, xa, u_all, cc_u, f"{tag}g")
            p.fence(crdy)
            scratch = dict(u_all=u_all, qp=qp, kp=kp, vdr=vdr0, odr=odr,
                           gather_op=ug["gather"])
            o_sts = emit_l0attn(ctx, sh, con, xa, xb, W0, scratch, tag)
            sts = emit_oproj_exchange(ctx, sh, odr, o_sts[-2:], owT0, cc2_in,
                                      cc2_out, xa, xb, None, f"{tag}x")
        p.fence(sts[-4:])

    def l1attn(xa, xb, tag):
        with ctx.scope():
            con, crdy = setup_l1attn_consts(ctx, sh, nc)
            ob_cols = ctx.sb(f"{tag}_obc", [128, 16], F32)
            obl = p.dma("gpsimd", "cst1", lambda e: e.dma_start(
                out=ob_cols[:], in_=obi[:, :]), deps=())
            ug = emit_ugather(ctx, sh, xa, u_all, cc_u, f"{tag}g")
            p.fence(crdy + [obl])
            scratch = dict(u_all=u_all, qdr=qdr, kdr=kdr, vdr1=vdr1, odr=odr,
                           gather_op=ug["gather"], vb_deps=[crdy[-3]])
            o_sts = emit_l1attn(ctx, sh, con, W1, scratch, tag)
            sts = emit_oproj_exchange(ctx, sh, odr, o_sts[-2:], owT1, cc2_in,
                                      cc2_out, xa, xb, ob_cols, f"{tag}x")
        p.fence(sts[-4:])

    l0ffn(x_in, xi0, "fA")
    l0attn(xi0, xi1, "aA")
    l0ffn(xi1, xi0, "fB")
    l1ffn(xi0, xi1, "gA")
    l1attn(xi1, xi0, "bA")
    l1ffn(xi0, xo, "gB")
    p.op("gpsimd", lambda g: g.nop(), deps=())
    p.emit()
    stack.close()
    return nc


_BUILD_CACHE = {}


def _get(name, fn):
    if name not in _BUILD_CACHE:
        _BUILD_CACHE[name] = fn()
    return _BUILD_CACHE[name]


def kernel(**inputs):
    cores = list(range(8))
    prep = [prep_core_inputs(inputs, c) for c in cores]
    nc = _get("fused", build_fused)
    maps = []
    for c in cores:
        d = prep[c]
        maps.append(dict(
            x=d["x"], wfcT=d["wfcT"], wprojT=d["wprojT"], guT=d["guT"],
            downRB=d["downRB"],
            qwT0=d["l0_qwT"], krT=d["l0_krT"], dnT=d["l0_dnT"],
            upkT=d["l0_upkT"], upvT=d["l0_upvT"], owT0=d["l0_owT"],
            qwT1=d["l1_qwT"], kwT1=d["l1_kwT"], vwT1=d["l1_vwT"],
            qb1=d["l1_qb"], kb1=d["l1_kb"], owT1=d["l1_owT"], ob1=d["l1_ob"],
            cos_q=d["cos_q"], sin_q=d["sin_q"], cos_k=d["cos_k"],
            sin_k=d["sin_k"], permT=d["permT"], H4=d["H4"], Hq=d["Hq"],
            IND=d["IND"], tri_mul=d["tri_mul"],
            iota8neg=d["iota8neg"], jcols=d["jcols"], tri_add=d["tri_add"],
            e8T=d["e8T"], E8=d["E8"], vb=d["l1_vb"]))
    res = run_bass_kernel_spmd(nc, maps, core_ids=cores)
    out = np.zeros((4, 4096, 2048), np.float32)
    for c in cores:
        b, h = c // 2, c % 2
        out[b, h * T:(h + 1) * T, :] = np.asarray(res.results[c]["xo"]).T
    return out


def kernel_unfused(**inputs):
    cores = list(range(8))
    prep = [prep_core_inputs(inputs, c) for c in cores]
    x = [prep[c]["x"] for c in cores]

    nc_ffn0 = _get("ffn0", build_l0ffn_launch)
    nc_att0 = _get("att0", build_l0attn_launch)
    nc_ffn1 = _get("ffn1", build_l1ffn_launch)
    nc_att1 = _get("att1", build_l1attn_launch)

    def run(nc, maps):
        res = run_bass_kernel_spmd(nc, maps, core_ids=cores)
        return [res.results[c]["xo"] for c in cores]

    for step, nc in [("f0", nc_ffn0), ("a0", nc_att0), ("f0", nc_ffn0),
                     ("f1", nc_ffn1), ("a1", nc_att1), ("f1", nc_ffn1)]:
        maps = []
        for c in cores:
            d = prep[c]
            if step == "f0":
                maps.append(dict(x=x[c], wfcT=d["wfcT"], wprojT=d["wprojT"]))
            elif step == "a0":
                maps.append(dict(x=x[c], qwT=d["l0_qwT"], krT=d["l0_krT"],
                                 dnT=d["l0_dnT"], upkT=d["l0_upkT"],
                                 upvT=d["l0_upvT"], owT=d["l0_owT"],
                                 cos_q=d["cos_q"], sin_q=d["sin_q"],
                                 cos_k=d["cos_k"], sin_k=d["sin_k"],
                                 permT=d["permT"], H4=d["H4"], Hq=d["Hq"],
                                 IND=d["IND"], tri_mul=d["tri_mul"]))
            elif step == "f1":
                maps.append(dict(x=x[c], guT=d["guT"], downRB=d["downRB"]))
            else:
                maps.append(dict(x=x[c], qwT=d["l1_qwT"], kwT=d["l1_kwT"],
                                 vwT=d["l1_vwT"], qb=d["l1_qb"], kb=d["l1_kb"],
                                 vb=d["l1_vb"], owT=d["l1_owT"], ob=d["l1_ob"],
                                 iota8neg=d["iota8neg"], jcols=d["jcols"],
                                 tri_add=d["tri_add"], e8T=d["e8T"], E8=d["E8"]))
        x = run(nc, maps)

    out = np.zeros((4, 4096, 2048), np.float32)
    for c in cores:
        b, h = c // 2, c % 2
        out[b, h * T:(h + 1) * T, :] = np.asarray(x[c]).T
    return out


# version 06
# speedup vs baseline: 1.1134x; 1.0990x over previous
import contextlib
import math
import numpy as np
import ml_dtypes
import concourse.bass as bass
import concourse.mybir as mybir
from concourse.bass_utils import run_bass_kernel_spmd

F32 = mybir.dt.float32
BF16 = mybir.dt.bfloat16
AF = mybir.ActivationFunctionType
ALU = mybir.AluOpType
AX = mybir.AxisListType

D = 2048
T = 2048
TFULL = 4096
NCH = D // 128
PAIRS = [[0, 1], [2, 3], [4, 5], [6, 7]]


class _Op:
    __slots__ = ("idx", "eng", "fn", "deps", "is_dma", "tag", "marked",
                 "sem", "val", "dma_count", "hard", "is_async")

    def __init__(self, idx, eng, fn, deps, is_dma=False, tag=None):
        self.hard = set()
        self.is_async = False
        self.idx = idx
        self.eng = eng
        self.fn = fn
        self.deps = list(deps)
        self.is_dma = is_dma
        self.tag = tag
        self.marked = False
        self.sem = None
        self.val = None


class Plan:
    ENGINES = ("tensor", "vector", "scalar", "gpsimd", "sync")

    DMA_POOL = 80

    def __init__(self, nc, stack):
        self.nc = nc
        self.stack = stack
        self.ops = []
        self.eng_sem = {e: stack.enter_context(nc.semaphore(f"S_{e}"))
                        for e in self.ENGINES}
        self.dma_pool = []
        self.dma_sems = {}
        self.sem_counts = {}
        self._fence = {e: None for e in self.ENGINES}

    def fence(self, deps):
        deps = [d for d in deps if d is not None]
        for e in self.ENGINES:
            cur = self._fence[e] or []
            self._fence[e] = cur + deps

    def _take_fence(self, eng):
        f = self._fence[eng]
        self._fence[eng] = None
        return f or []

    def op(self, eng, fn, deps=(), hard=(), is_async=False):
        deps = list(deps) + list(hard) + self._take_fence(eng)
        o = _Op(len(self.ops), eng, fn, deps)
        o.hard = set(id(h) for h in hard)
        o.is_async = is_async
        self.ops.append(o)
        for d in deps:
            d.marked = True
        return o

    def dma(self, eng, tag, fn, deps=()):
        deps = list(deps) + self._take_fence(eng)
        o = _Op(len(self.ops), eng, fn, deps, is_dma=True, tag=tag)
        self.ops.append(o)
        for d in deps:
            d.marked = True
        return o

    def barrier_all(self, deps):
        return deps

    def emit(self):
        eng_count = {e: 0 for e in self.ENGINES}
        pool_by_eng = {}
        tagcount_by_eng = {}
        PER_ENG = {"sync": 40, "gpsimd": 30, "scalar": 8, "vector": 8, "tensor": 8}
        for o in self.ops:
            if o.is_dma:
                key = (o.eng, o.tag)
                if key not in self.dma_sems:
                    pool = pool_by_eng.setdefault(o.eng, [])
                    n = tagcount_by_eng.get(o.eng, 0)
                    tagcount_by_eng[o.eng] = n + 1
                    idx = n % PER_ENG[o.eng]
                    while len(pool) <= idx:
                        pool.append(self.stack.enter_context(
                            self.nc.semaphore(f"Dp_{o.eng}{len(pool)}")))
                    self.dma_sems[key] = pool[idx]
                h = self.dma_sems[key]
                cnt = self.sem_counts.get(id(h), 0) + 16
                self.sem_counts[id(h)] = cnt
                o.sem, o.val = h, cnt
            elif o.marked:
                eng_count[o.eng] += 1
                o.sem, o.val = self.eng_sem[o.eng], eng_count[o.eng]

        queues = {e: [] for e in self.ENGINES}
        waited = {e: {} for e in self.ENGINES}

        for o in self.ops:
            q = queues[o.eng]
            w = waited[o.eng]
            for d in o.deps:
                if (d.eng == o.eng and not d.is_dma and not d.is_async
                        and id(d) not in o.hard):
                    continue
                assert d.sem is not None, "dep without completion signal"
                key = id(d.sem)
                if w.get(key, -1) >= d.val:
                    continue
                w[key] = d.val
                q.append(lambda e, s=d.sem, v=d.val: e.wait_ge(s, v))
            if o.is_dma:
                q.append(lambda e, o=o: o.fn(e).then_inc(o.sem, 16))
            elif o.marked:
                q.append(lambda e, o=o: o.fn(e).then_inc(o.sem, 1))
            else:
                q.append(lambda e, o=o: o.fn(e))

        nc = self.nc
        with nc.Block() as block:
            @block.tensor
            def _(t):
                for fn in queues["tensor"]:
                    fn(t)

            @block.vector
            def _(v):
                for fn in queues["vector"]:
                    fn(v)

            @block.scalar
            def _(s):
                for fn in queues["scalar"]:
                    fn(s)

            @block.gpsimd
            def _(g):
                for fn in queues["gpsimd"]:
                    fn(g)

            @block.sync
            def _(sy):
                for fn in queues["sync"]:
                    fn(sy)


class Ctx:

    def __init__(self, nc, stack):
        self.nc = nc
        self.stack = stack
        self.plan = Plan(nc, stack)
        self.ps = [stack.enter_context(nc.psum_tensor(f"ps{i}", [128, 512], F32))
                   for i in range(8)]

    def sb(self, name, shape, dtype):
        return self.stack.enter_context(self.nc.sbuf_tensor(name, shape, dtype))

    @contextlib.contextmanager
    def scope(self):
        old = self.stack
        try:
            with contextlib.ExitStack() as es:
                self.stack = es
                yield
        finally:
            self.stack = old



def _panels(wT, pw=128):
    K, O = wT.shape
    return np.ascontiguousarray(wT.reshape(K, O // pw, pw).transpose(1, 0, 2))


def _rowblocks(wT, rb=128):
    K, O = wT.shape
    return np.ascontiguousarray(wT.reshape(K // rb, rb, O))


def _sigma_perm():
    return np.array(list(range(0, 16)) + list(range(32, 48)) +
                    list(range(16, 32)) + list(range(48, 64)))


def _rope_tables():
    HD = 64
    inv_freq = 1.0 / (10000.0 ** (np.arange(0, HD, 2) / HD))
    pos = np.arange(TFULL, dtype=np.float64)
    ang = pos[None, :] * inv_freq[:, None]
    cos, sin = np.cos(ang), np.sin(ang)
    cos_q = np.concatenate([cos[0:16], cos[0:16], cos[16:32], cos[16:32]], 0)
    sin_q = np.concatenate([sin[0:16], sin[0:16], sin[16:32], sin[16:32]], 0)
    cos_k = np.concatenate([cos[0:16], cos[0:16]], 0)
    sin_k = np.concatenate([sin[0:16], sin[0:16]], 0)
    return (np.tile(cos_q, (2, 1)).astype(np.float32),
            np.tile(sin_q, (2, 1)).astype(np.float32),
            np.tile(cos_k, (4, 1)).astype(np.float32),
            np.tile(sin_k, (4, 1)).astype(np.float32))


def _perm_matrix():
    P = np.zeros((128, 128), np.float32)
    for blk in range(4):
        b = 32 * blk
        for p in range(16):
            P[b + p, b + p + 16] = -1.0
            P[b + p + 16, b + p] = 1.0
    return P


def prep_common_consts():
    c = {}
    mul_masks, add_masks = [], []
    for m in range(4):
        k = np.arange(128)[:, None] + 128 * m
        q = np.arange(512)[None, :]
        keep = (k <= q)
        mul_masks.append(keep.astype(np.float32))
        add_masks.append(np.where(keep, 0.0, -1e30).astype(np.float32))
    c["tri_mul"] = np.stack(mul_masks)
    c["tri_add"] = np.stack(add_masks)
    cq, sq, ck, sk = _rope_tables()
    c["cos_q"], c["sin_q"], c["cos_k"], c["sin_k"] = cq, sq, ck, sk
    c["permT"] = np.ascontiguousarray(_perm_matrix().T)
    Hq = np.zeros((8, 128, 16), np.float32)
    for pn in range(8):
        Hq[pn, 0:64, 2 * pn] = 1.0
        Hq[pn, 64:128, 2 * pn + 1] = 1.0
    c["Hq"] = Hq
    H4 = np.zeros((128, 4), np.float32)
    for g in range(4):
        H4[32 * g:32 * (g + 1), g] = 1.0
    c["H4"] = H4
    IND = np.zeros((4, 16), np.float32)
    for qh in range(16):
        IND[qh // 4, qh] = -1.0
    c["IND"] = IND
    SCALE1 = 1.0 / math.sqrt(128.0)
    c["iota8neg"] = np.tile(-np.arange(TFULL, dtype=np.float64)[None, :] / SCALE1,
                            (8, 1)).astype(np.float32)
    jc = np.arange(128, dtype=np.float32)[:, None] + 128.0 * np.arange(32)[None, :]
    c["jcols"] = np.ascontiguousarray(jc)
    e8T = np.zeros((8, 8, 128), np.float32)
    for hh in range(8):
        e8T[hh, hh, :] = 1.0
    c["e8T"] = e8T
    E8 = np.zeros((8, 128, 8), np.float32)
    for pn in range(8):
        E8[pn, :, pn] = 1.0
    c["E8"] = E8
    return c


def prep_core_inputs(inputs, core):
    b, h = core // 2, core % 2
    d = {}
    x = np.asarray(inputs["L0_attn_hidden_states"])
    d["x"] = np.ascontiguousarray(x[b, h * T:(h + 1) * T, :].T)

    d["wfcT"] = _panels(np.ascontiguousarray(np.asarray(inputs["L0_ffn_w_fc"]).T))
    d["wprojT"] = _panels(np.ascontiguousarray(np.asarray(inputs["L0_ffn_w_proj"]).T))

    sig = _sigma_perm()
    qw = np.asarray(inputs["L0_attn_q_w"])
    qw_my = np.concatenate([qw[qh * 64 + sig, :] for qh in range(16 * h, 16 * (h + 1))], 0)
    d["l0_qwT"] = _panels(np.ascontiguousarray(qw_my.T))
    kr = np.asarray(inputs["L0_attn_kr_w"])
    d["l0_krT"] = _panels(np.ascontiguousarray(kr[128 * h:128 * (h + 1), :].T))
    d["l0_dnT"] = _panels(np.ascontiguousarray(np.asarray(inputs["L0_attn_down_w"]).T))
    upk = np.asarray(inputs["L0_attn_upk_w"])
    d["l0_upkT"] = np.ascontiguousarray(upk[128 * h:128 * (h + 1), :].T)
    upv = np.asarray(inputs["L0_attn_upv_w"])
    d["l0_upvT"] = np.ascontiguousarray(upv[256 * h:256 * (h + 1), :].T)
    ow = np.asarray(inputs["L0_attn_o_w"])
    d["l0_owT"] = _panels(np.ascontiguousarray(ow[:, 1024 * h:1024 * (h + 1)].T))

    for nm, wkey, bkey in [("q", "L1_attn_q_w", "L1_attn_q_b"),
                           ("k", "L1_attn_k_w", "L1_attn_k_b")]:
        w = np.asarray(inputs[wkey])
        d[f"l1_{nm}wT"] = _panels(np.ascontiguousarray(w[1024 * h:1024 * (h + 1), :].T))
        bias = np.asarray(inputs[bkey])
        d[f"l1_{nm}b"] = np.ascontiguousarray(bias[1024 * h:1024 * (h + 1)].reshape(8, 128).T)
    vw = np.asarray(inputs["L1_attn_v_w"])
    d["l1_vwT"] = np.ascontiguousarray(vw[1024 * h:1024 * (h + 1), :].T)
    d["l1_vb"] = np.ascontiguousarray(
        np.asarray(inputs["L1_attn_v_b"])[1024 * h:1024 * (h + 1)][None, :])
    ow1 = np.asarray(inputs["L1_attn_o_w"])
    d["l1_owT"] = _panels(np.ascontiguousarray(ow1[:, 1024 * h:1024 * (h + 1)].T))
    d["l1_ob"] = np.ascontiguousarray(np.asarray(inputs["L1_attn_o_b"]).reshape(16, 128).T)

    d["guT"] = _panels(np.ascontiguousarray(np.asarray(inputs["L1_ffn_w_gate_up"]).T))
    d["downRB"] = _rowblocks(np.ascontiguousarray(np.asarray(inputs["L1_ffn_w_down"]).T))

    d.update({k: np.ascontiguousarray(v) for k, v in prep_common_consts().items()})
    return d



class Shared:

    def __init__(self, ctx):
        p = ctx.plan
        self.ones_f = ctx.sb("c_ones_f", [128, 1], F32)
        self.ones_b = ctx.sb("c_ones_b", [128, 1], BF16)
        self.ones_row_f = ctx.sb("c_onesr_f", [1, 128], F32)
        o1 = p.op("vector", lambda v: v.memset(self.ones_f[:], 1.0))
        o2 = p.op("vector", lambda v: v.memset(self.ones_b[:], 1.0))
        o3 = p.op("vector", lambda v: v.memset(self.ones_row_f[:], 1.0))
        self.ready = [o1, o2, o3]


def emit_ln(ctx, sh, xin_dram, out_sb, tag, compose_inner, eps=1e-5,
            out_dram=None, deps=()):
    p = ctx.plan
    ps = ctx.ps
    TT = 512
    NT = T // TT
    scope = ctx.scope()
    scope.__enter__()
    xtile = [ctx.sb(f"{tag}_xt{i}", [128, NCH * TT], F32) for i in range(2)]
    sq = [ctx.sb(f"{tag}_sq{i}", [128, TT], BF16) for i in range(2)]
    rows = ctx.sb(f"{tag}_rows", [1, 3 * TT], F32)
    srow = ctx.sb(f"{tag}_srow", [1, T], F32)
    msrow = ctx.sb(f"{tag}_msrow", [1, T], F32)
    tmpf = ctx.sb(f"{tag}_tmpf", [128, TT], F32)

    tile_done = []
    prev_apply0 = {}
    mm2_hist = {}
    fix1_hist = {}
    st_ops = []
    for tt in range(NT):
        xb = xtile[tt % 2]
        war = [tile_done[tt - 2]] if tt >= 2 else []
        lds = []
        for c in range(NCH):
            lds.append(p.dma("sync", f"{tag}x{tt % 2}",
                             lambda e, c=c, tt=tt, xb=xb: e.dma_start(
                                 out=xb[:, c * TT:(c + 1) * TT],
                                 in_=xin_dram[c * 128:(c + 1) * 128,
                                              tt * TT:(tt + 1) * TT]),
                             deps=list(deps) + war))
        mm1l = mm2l = None
        for c in range(NCH):
            sqd = [lds[c]]
            if (tt, c - 2) in mm2_hist:
                sqd.append(mm2_hist[(tt, c - 2)])
            elif tt > 0 and c < 2:
                sqd.append(mm2_hist[(tt - 1, NCH - 2 + c)])
            sqo = p.op("scalar", lambda e, c=c, xb=xb, s=sq[c % 2]: e.activation(
                s[:], xb[:, c * TT:(c + 1) * TT], AF.Square), deps=sqd)
            d1 = [lds[c], sh.ready[0]]
            if c == 0 and tt > 0:
                d1.append(fix1_hist[tt - 1])
            mm1l = p.op("tensor", lambda e, c=c, xb=xb: e.matmul(
                ps[0][0:1, :], sh.ones_f[:], xb[:, c * TT:(c + 1) * TT],
                start=(c == 0), stop=(c == NCH - 1)), deps=d1)
            d2 = [sqo, sh.ready[1]]
            if c == 0 and tt > 0:
                d2.append(fix1_hist[tt - 1])
            mm2l = p.op("tensor", lambda e, c=c, s=sq[c % 2]: e.matmul(
                ps[1][0:1, :], sh.ones_b[:], s[:],
                start=(c == 0), stop=(c == NCH - 1)), deps=d2)
            mm2_hist[(tt, c)] = mm2l

        m_ = rows[0:1, 0:TT]
        va_ = rows[0:1, TT:2 * TT]
        tp_ = rows[0:1, 2 * TT:3 * TT]
        fa = p.op("vector", lambda v: (
            v.tensor_scalar(m_, ps[0][0:1, :], 1.0 / D, None, ALU.mult),
            v.tensor_scalar(va_, ps[1][0:1, :], 1.0 / D, None, ALU.mult))[-1],
            deps=[mm1l, mm2l])
        fbb = p.op("vector", lambda v: v.tensor_tensor(tp_, m_, m_, ALU.mult),
                   hard=[fa])
        fcc = p.op("vector", lambda v: v.tensor_tensor(va_, va_, tp_, ALU.subtract),
                   hard=[fbb])
        if compose_inner:
            f1 = p.op("vector", lambda v: v.tensor_scalar(
                va_, va_, 1.0 + 1e-6, 1e-11, ALU.mult, ALU.add), hard=[fcc])
        else:
            f1 = p.op("vector", lambda v: v.tensor_scalar(
                va_, va_, eps, None, ALU.add), hard=[fcc])
        fix1_hist[tt] = f1
        sq1 = p.op("scalar", lambda e: e.activation(
            rows[0:1, 2 * TT:3 * TT], rows[0:1, TT:2 * TT], AF.Sqrt), deps=[f1])
        f2a = p.op("vector", lambda v, tt=tt: v.reciprocal(
            srow[0:1, tt * TT:(tt + 1) * TT], rows[0:1, 2 * TT:3 * TT]), deps=[sq1])
        f2 = p.op("vector", lambda v, tt=tt: v.tensor_tensor(
            msrow[0:1, tt * TT:(tt + 1) * TT], rows[0:1, 0:TT],
            srow[0:1, tt * TT:(tt + 1) * TT], ALU.mult), hard=[f2a])

        bwar = [tile_done[tt - 1]] if tt > 0 else []
        bs = p.op("tensor", lambda e, tt=tt: e.matmul(
            ps[2][:, :], sh.ones_row_f[:], srow[0:1, tt * TT:(tt + 1) * TT],
            start=True, stop=True), deps=[f2, sh.ready[2]] + bwar)
        bms = p.op("tensor", lambda e, tt=tt: e.matmul(
            ps[3][:, :], sh.ones_row_f[:], msrow[0:1, tt * TT:(tt + 1) * TT],
            start=True, stop=True), deps=[f2] + bwar)

        last_apply = None
        for c in range(NCH):
            def apply(v, c=c, tt=tt, xb=xb):
                v.tensor_tensor(tmpf[:], xb[:, c * TT:(c + 1) * TT], ps[2][:, :], ALU.mult)
                return v.tensor_tensor(
                    out_sb[:, c * T + tt * TT: c * T + (tt + 1) * TT],
                    tmpf[:], ps[3][:, :], ALU.subtract)
            ap_op = p.op("vector", apply, deps=([bs, bms] if c == 0 else []))
            if c == 0:
                prev_apply0[tt] = ap_op
            if out_dram is not None:
                st_ops.append(p.dma("sync", f"{tag}st",
                                    lambda e, c=c, tt=tt: e.dma_start(
                                        out=out_dram[c * 128:(c + 1) * 128,
                                                     tt * TT:(tt + 1) * TT],
                                        in_=out_sb[:, c * T + tt * TT: c * T + (tt + 1) * TT]),
                                    deps=[ap_op]))
            last_apply = ap_op
        tile_done.append(last_apply)
    scope.__exit__(None, None, None)
    tail = [tile_done[-1]] + st_ops[-2:]
    return dict(tile_done=tile_done, st_ops=st_ops, tail=tail)


def emit_l0ffn_prep(ctx, sh, wfcT, wprojT, tprojRB, tag):
    p = ctx.plan
    ps = ctx.ps
    thr_fc = ctx.sb(f"{tag}_thrfc", [128, 1], F32)
    thr_pj = ctx.sb(f"{tag}_thrpj", [128, 1], F32)
    gfc_col = ctx.sb(f"{tag}_gfccol", [128, 1], F32)
    gph = ctx.sb(f"{tag}_gph", [1, 1], F32)
    rs_row = ctx.sb(f"{tag}_rsrow", [1, D], F32)
    scope = ctx.scope()
    scope.__enter__()
    gbuf = [ctx.sb(f"{tag}_gb{i}", [128, 4096], F32) for i in range(2)]
    gcols = ctx.sb(f"{tag}_gcols", [128, 100], F32)
    rows = ctx.sb(f"{tag}_grows", [1, 16], F32)

    segs = []
    for i in range(64):
        segs.append((lambda i=i: wfcT[i].rearrange("(c p) o -> p c o", p=128), 16, "fc"))
    for i in range(16):
        segs.append((lambda i=i: wprojT[i].rearrange("(c p) o -> p c o", p=128)[:, 0:32, :], 32, "pj"))
        segs.append((lambda i=i: wprojT[i].rearrange("(c p) o -> p c o", p=128)[:, 32:64, :], 32, "pj"))
    red_ops = []
    fc_slots, pj_slots = [], []
    for si, (apf, nch, kind) in enumerate(segs):
        b = gbuf[si % 2]
        war = [red_ops[si - 2]] if si >= 2 else []
        ld = p.dma("sync", f"{tag}g{si % 2}",
                   lambda e, apf=apf, b=b, nch=nch: e.dma_start(
                       out=b.ap().rearrange("p (c o) -> p c o", o=128)[:, 0:nch, :],
                       in_=apf()),
                   deps=war)
        ro = p.op("vector", lambda v, b=b, nch=nch, sl=si: v.tensor_reduce(
            gcols[:, sl:sl + 1], b[:, 0:nch * 128], AX.X, ALU.add,
            apply_absolute_value=True), deps=[ld])
        red_ops.append(ro)
        (fc_slots if kind == "fc" else pj_slots).append(si)

    def red2(v):
        v.tensor_reduce(gcols[:, 96:97], gcols[:, fc_slots[0]:fc_slots[-1] + 1],
                        AX.X, ALU.add)
        return v.tensor_reduce(gcols[:, 97:98], gcols[:, pj_slots[0]:pj_slots[-1] + 1],
                               AX.X, ALU.add)
    r2 = p.op("vector", red2, hard=[red_ops[-1], red_ops[-2]])
    tot_fc = p.op("tensor", lambda e: e.matmul(
        ps[0][0:1, 0:1], gcols[:, 96:97], sh.ones_f[:], start=True, stop=True),
        deps=[r2, sh.ready[0]])
    tot_pj = p.op("tensor", lambda e: e.matmul(
        ps[4][0:1, 0:1], gcols[:, 97:98], sh.ones_f[:], start=True, stop=True),
        deps=[r2])

    gfa = p.op("vector", lambda v: (
        v.tensor_scalar(rows[0:1, 0:1], ps[0][0:1, 0:1], 1.0 / (8192.0 * 2048.0), None, ALU.mult),
        v.tensor_scalar(rows[0:1, 2:3], ps[4][0:1, 0:1], 1.0 / (8192.0 * 2048.0), None, ALU.mult))[-1],
        deps=[tot_fc, tot_pj])
    gf = p.op("vector", lambda v: (
        v.tensor_scalar(rows[0:1, 1:2], rows[0:1, 0:1], 1e-6, 0.5, ALU.add, ALU.mult),
        v.tensor_scalar(rows[0:1, 3:4], rows[0:1, 2:3], 1e-6, 0.5, ALU.add, ALU.mult),
        v.tensor_scalar(gph[0:1, :], rows[0:1, 2:3], 0.5, None, ALU.mult))[-1],
        hard=[gfa])
    bc = p.op("tensor", lambda e: e.matmul(
        ps[1][:, 0:4], sh.ones_row_f[:], rows[0:1, 0:4], start=True, stop=True),
        deps=[gf, sh.ready[2]])

    def copycols(v):
        v.tensor_copy(gfc_col[:], ps[1][:, 0:1])
        v.tensor_copy(thr_fc[:], ps[1][:, 1:2])
        return v.tensor_copy(thr_pj[:], ps[1][:, 3:4])
    cc = p.op("vector", copycols, deps=[bc])

    pfp = [ctx.sb(f"{tag}_pf{i}", [128, 4096], F32) for i in range(2)]
    ptr = [ctx.sb(f"{tag}_pt{i}", [128, 4096], BF16) for i in range(2)]
    psg = [ctx.sb(f"{tag}_psg{i}", [128, 4096], BF16) for i in range(2)]
    prev = {}
    rs_evacs = []
    last_spill = {0: None, 1: None}
    seg_idx = 0
    for pi in range(16):
        last_mm = None
        for hh in range(2):
            bi = seg_idx % 2
            pm2 = prev.get(seg_idx - 2)
            ld = p.dma("sync", f"{tag}p{bi}",
                       lambda e, pi=pi, hh=hh, bi=bi: e.dma_start(
                           out=pfp[bi].ap().rearrange("p (c o) -> p c o", o=128),
                           in_=wprojT[pi].rearrange("(c p) o -> p c o", p=128)[:, hh * 32:(hh + 1) * 32, :]),
                       deps=[pm2["tn"]] if pm2 else [])
            mkp = p.op("vector", lambda v, bi=bi: v.tensor_scalar(
                ptr[bi][:], pfp[bi][:], thr_pj[:], 0.0, ALU.subtract, ALU.is_gt),
                deps=[ld, cc] + ([pm2["last_t"], pm2["last_sp"]] if pm2 else []))
            mkn = p.op("vector", lambda v, bi=bi: v.tensor_scalar(
                psg[bi][:], pfp[bi][:], thr_pj[:], 0.0, ALU.add, ALU.is_lt),
                deps=[ld] + ([pm2["tn"]] if pm2 else []))
            tn = p.op("vector", lambda v, bi=bi: v.tensor_tensor(
                ptr[bi][:], ptr[bi][:], psg[bi][:], ALU.subtract), deps=[])
            sp = None
            for c in range(32):
                kc = hh * 32 + c
                mmd = []
                if c == 0:
                    mmd = [tn, sh.ready[1]]
                    if kc == 0 and rs_evacs:
                        mmd.append(rs_evacs[-1])
                last_mm = p.op("tensor", lambda e, bi=bi, c=c, kc=kc: e.matmul(
                    ps[2][0:1, 0:128], sh.ones_b[:], ptr[bi][:, c * 128:(c + 1) * 128],
                    start=(kc == 0), stop=(kc == 63)), deps=mmd)
                sp = p.dma("gpsimd", f"{tag}sp{bi}",
                           lambda e, bi=bi, c=c, pi=pi, kc=kc: e.dma_start(
                               out=tprojRB[kc, :, pi * 128:(pi + 1) * 128],
                               in_=ptr[bi][:, c * 128:(c + 1) * 128]),
                           deps=[tn])
            prev[seg_idx] = dict(tn=tn, last_t=last_mm, last_sp=sp)
            last_spill[bi] = sp
            seg_idx += 1
        ev = p.op("vector", lambda v, pi=pi: v.tensor_copy(
            rs_row[0:1, pi * 128:(pi + 1) * 128], ps[2][0:1, 0:128]),
            deps=[last_mm])
        rs_evacs.append(ev)

    scope.__exit__(None, None, None)
    tail = [cc, rs_evacs[-1], last_spill[0], last_spill[1],
            prev[30]["last_t"], prev[31]["last_t"],
            prev[30]["tn"], prev[31]["tn"]]
    return dict(thr_fc=thr_fc, thr_pj=thr_pj, gfc_col=gfc_col, gph=gph,
                rs_row=rs_row, cc=cc, rs_evacs=rs_evacs,
                spill_done=[last_spill[0], last_spill[1]], tail=tail)


def emit_l0ffn_main(ctx, sh, prep, xn_sb, xn_ready, xin, xout, wfcT, tprojRB,
                    h_dram, tag):
    p = ctx.plan
    ps = ctx.ps
    TT = 512
    NT = T // TT

    NBW = 3
    wfp = [ctx.sb(f"{tag}_wf{i}", [128, NCH * 128], F32) for i in range(NBW)]
    ftr = [ctx.sb(f"{tag}_ft{i}", [128, NCH * 128], BF16) for i in range(NBW)]
    fsg = [ctx.sb(f"{tag}_fs{i}", [128, NCH * 128], BF16) for i in range(NBW)]
    hst = [ctx.sb(f"{tag}_hs{i}", [128, TT], BF16) for i in range(2)]
    prevA = {}
    last_st_h = {}
    last_gelu_bank = {4: None, 5: None}
    for oc in range(64):
        bi = oc % NBW
        pm2 = prevA.get(oc - NBW)
        ld = p.dma("sync", f"{tag}wf{bi}",
                   lambda e, oc=oc, bi=bi: e.dma_start(
                       out=wfp[bi].ap().rearrange("p (c o) -> p c o", o=128),
                       in_=wfcT[oc].rearrange("(c p) o -> p c o", p=128)),
                   deps=[pm2["tn"]] if pm2 else [])
        mkp = p.op("vector", lambda v, bi=bi: v.tensor_scalar(
            ftr[bi][:], wfp[bi][:], prep["thr_fc"][:], 0.0, ALU.subtract, ALU.is_gt),
            deps=[ld, prep["cc"]] + ([pm2["last_mm"]] if pm2 else []))
        mkn = p.op("vector", lambda v, bi=bi: v.tensor_scalar(
            fsg[bi][:], wfp[bi][:], prep["thr_fc"][:], 0.0, ALU.add, ALU.is_lt),
            deps=[ld] + ([pm2["tn"]] if pm2 else []))
        tn = p.op("vector", lambda v, bi=bi: v.tensor_tensor(
            ftr[bi][:], ftr[bi][:], fsg[bi][:], ALU.subtract), deps=[])
        last_mm_all = None
        for tt in range(NT):
            pb = 4 + tt % 2
            mmd0 = [tn, xn_ready[tt]]
            if last_gelu_bank[pb] is not None:
                mmd0.append(last_gelu_bank[pb])
            last_mm = None
            for c in range(NCH):
                last_mm = p.op("tensor", lambda e, c=c, tt=tt, bi=bi, pb=pb: e.matmul(
                    ps[pb][:, :], ftr[bi][:, c * 128:(c + 1) * 128],
                    xn_sb[:, c * T + tt * TT: c * T + (tt + 1) * TT],
                    start=(c == 0), stop=(c == NCH - 1)),
                    deps=(mmd0 if c == 0 else []))
            gl_d = [last_mm, prep["cc"]]
            if (oc - 1, tt) in last_st_h:
                gl_d.append(last_st_h[(oc - 1, tt)])
            gl = p.op("scalar", lambda e, tt=tt, pb=pb: e.activation(
                hst[tt % 2][:], ps[pb][:, :], AF.Gelu, scale=prep["gfc_col"][:]),
                deps=gl_d)
            last_gelu_bank[pb] = gl
            sth = p.dma("sync", f"{tag}h{tt % 2}",
                        lambda e, oc=oc, tt=tt: e.dma_start(
                            out=h_dram[oc * 128:(oc + 1) * 128, tt * TT:(tt + 1) * TT],
                            in_=hst[tt % 2][:]),
                        deps=[gl])
            last_st_h[(oc, tt)] = sth
            last_mm_all = last_mm
        prevA[oc] = dict(last_mm=last_mm_all, tn=tn)
    h_done = [last_st_h[(63, tt)] for tt in range(NT)] + \
             [last_st_h[(62, tt)] for tt in range(NT)]

    NBH = 4
    hb = [ctx.sb(f"{tag}_hb{i}", [128, TT], BF16) for i in range(NBH)]
    hsc = [ctx.sb(f"{tag}_hc{i}", [128, TT], BF16) for i in range(NBH)]
    hsq = [ctx.sb(f"{tag}_hq{i}", [128, TT], BF16) for i in range(NBH)]
    wrb = [ctx.sb(f"{tag}_wr{i}", [128, 1024], BF16) for i in range(NBH)]
    bbf = ctx.sb(f"{tag}_bb", [128, TT], BF16)
    rowsB = ctx.sb(f"{tag}_rB", [1, 4 * TT], F32)
    augrow = ctx.sb(f"{tag}_aug", [1, TT], F32)
    xf = [ctx.sb(f"{tag}_xf{i}", [128, TT], F32) for i in range(2)]
    res = [ctx.sb(f"{tag}_rs{i}", [128, TT], F32) for i in range(2)]

    final_sts = []
    last_bank_reader = {j: None for j in range(8)}
    last_bb = None
    last_fix = None
    hb_reader = {}
    ep_hist = {}
    stx_hist = {}
    wrb_reader = {}
    for tb in range(NT):
        mm1 = mm2 = None
        sq_reader = {}
        for kc in range(64):
            bi = kc % NBH
            lddeps = []
            if tb == 0 and kc < NBH:
                lddeps += h_done
            if hb_reader.get(bi) is not None:
                lddeps += (hb_reader[bi] if isinstance(hb_reader[bi], list)
                           else [hb_reader[bi]])
            ldh = p.dma("sync", f"{tag}hb{bi}",
                        lambda e, kc=kc, tb=tb, bi=bi: e.dma_start(
                            out=hb[bi][:],
                            in_=h_dram[kc * 128:(kc + 1) * 128, tb * TT:(tb + 1) * TT]),
                        deps=lddeps)
            sqd = [ldh]
            if sq_reader.get(bi) is not None:
                sqd.append(sq_reader[bi])
            sqo = p.op("scalar", lambda e, bi=bi: e.activation(
                hsq[bi][:], hb[bi][:], AF.Square), deps=sqd)
            d1 = [ldh, sh.ready[1]]
            d2 = [sqo]
            if kc == 0 and last_fix is not None:
                d1.append(last_fix)
                d2.append(last_fix)
            mm1 = p.op("tensor", lambda e, kc=kc, bi=bi: e.matmul(
                ps[0][0:1, :], sh.ones_b[:], hb[bi][:],
                start=(kc == 0), stop=(kc == 63)), deps=d1)
            mm2 = p.op("tensor", lambda e, kc=kc, bi=bi: e.matmul(
                ps[1][0:1, :], sh.ones_b[:], hsq[bi][:],
                start=(kc == 0), stop=(kc == 63)), deps=d2)
            sq_reader[bi] = mm2
            hb_reader[bi] = [mm1, sqo]

        mB_ = rowsB[0:1, 0:TT]
        vaB_ = rowsB[0:1, TT:2 * TT]
        tpB_ = rowsB[0:1, 2 * TT:3 * TT]
        s2B_ = rowsB[0:1, 3 * TT:4 * TT]
        fba = p.op("vector", lambda v: (
            v.tensor_scalar(mB_, ps[0][0:1, :], 1.0 / 8192.0, None, ALU.mult),
            v.tensor_scalar(vaB_, ps[1][0:1, :], 1.0 / 8192.0, None, ALU.mult))[-1],
            deps=[mm1, mm2])
        fbb = p.op("vector", lambda v: v.tensor_tensor(tpB_, mB_, mB_, ALU.mult),
                   hard=[fba])
        fbc = p.op("vector", lambda v: v.tensor_tensor(vaB_, vaB_, tpB_, ALU.subtract),
                   hard=[fbb])
        fb = p.op("vector", lambda v: v.tensor_scalar(vaB_, vaB_, 1e-6, None, ALU.add),
                  hard=[fbc])
        sqB = p.op("scalar", lambda e: e.activation(
            tpB_, vaB_, AF.Sqrt), deps=[fb])
        f2b1 = p.op("vector", lambda v: v.reciprocal(s2B_, tpB_), deps=[sqB])
        f2b2 = p.op("vector", lambda v: v.tensor_scalar(
            s2B_, s2B_, prep["gph"][0:1, 0:1], None, ALU.mult), hard=[f2b1])
        f2b3 = p.op("vector", lambda v: v.tensor_tensor(
            augrow[0:1, :], s2B_, mB_, ALU.mult), hard=[f2b2])
        f2b = p.op("vector", lambda v: v.tensor_scalar(
            augrow[0:1, :], augrow[0:1, :], -1.0, None, ALU.mult), hard=[f2b3])
        last_fix = f2b
        bcb_d = [f2b, sh.ready[2]]
        if last_bb is not None:
            bcb_d.append(last_bb)
        if tb == 0:
            bcb_d += prep["rs_evacs"][-1:]
        bcb = p.op("tensor", lambda e: e.matmul(
            ps[2][:, :], sh.ones_row_f[:], rowsB[0:1, 3 * TT:4 * TT],
            start=True, stop=True), deps=bcb_d)
        bbo = p.op("vector", lambda v: v.tensor_copy(bbf[:], ps[2][:, :]), deps=[bcb])
        last_bb = bbo

        for og in range(2):
            scale_ops = {}
            last_grp_mm = None
            for kc in range(64):
                bi = kc % NBH
                lddeps = [f2b] if (og == 0 and kc < NBH) else []
                if hb_reader.get(bi) is not None:
                    lddeps += (hb_reader[bi] if isinstance(hb_reader[bi], list)
                               else [hb_reader[bi]])
                ldh = p.dma("sync", f"{tag}hb{bi}",
                            lambda e, kc=kc, tb=tb, bi=bi: e.dma_start(
                                out=hb[bi][:],
                                in_=h_dram[kc * 128:(kc + 1) * 128, tb * TT:(tb + 1) * TT]),
                            deps=lddeps)
                scd = [ldh, bbo]
                if kc >= NBH and scale_ops.get(kc - NBH) is not None:
                    scd.append(scale_ops[kc - NBH]["reader"])
                sc = p.op("vector", lambda v, bi=bi: v.tensor_tensor(
                    hsc[bi][:], hb[bi][:], bbf[:], ALU.mult), deps=scd)
                hb_reader[bi] = sc
                ldw_deps = []
                if wrb_reader.get(bi) is not None:
                    ldw_deps.append(wrb_reader[bi])
                if tb == 0 and og == 0 and kc < NBH:
                    ldw_deps += prep["spill_done"]
                ldw = p.dma("gpsimd", f"{tag}wr{bi}",
                            lambda e, kc=kc, og=og, bi=bi: e.dma_start(
                                out=wrb[bi][:],
                                in_=tprojRB[kc, :, og * 1024:(og + 1) * 1024]),
                            deps=ldw_deps)
                lm = None
                for j in range(8):
                    oc = og * 8 + j
                    if kc == 0:
                        augd = [f2b, sh.ready[2]]
                        if last_bank_reader[j] is not None:
                            augd.append(last_bank_reader[j])
                        if tb == 0:
                            augd += h_done
                        p.op("tensor", lambda e, j=j, oc=oc: e.matmul(
                            ps[j][:, :], prep["rs_row"][0:1, oc * 128:(oc + 1) * 128],
                            augrow[0:1, :], start=True, stop=False), deps=augd)
                    lm = p.op("tensor", lambda e, j=j, kc=kc, bi=bi: e.matmul(
                        ps[j][:, :], wrb[bi][:, j * 128:(j + 1) * 128], hsc[bi][:],
                        start=False, stop=(kc == 63)),
                        deps=([sc, ldw] if j == 0 else []))
                scale_ops[kc] = dict(op=sc, reader=lm)
                wrb_reader[bi] = lm
                last_grp_mm = lm
            for j in range(8):
                oc = og * 8 + j
                bi = j % 2
                ldx_d = [ep_hist[bi]] if ep_hist.get(bi) is not None else []
                ldx = p.dma("sync", f"{tag}xi{bi}",
                            lambda e, oc=oc, tb=tb, bi=bi: e.dma_start(
                                out=xf[bi][:],
                                in_=xin[oc * 128:(oc + 1) * 128, tb * TT:(tb + 1) * TT]),
                            deps=ldx_d)
                ep_d = [last_grp_mm, ldx]
                if stx_hist.get(bi) is not None:
                    ep_d.append(stx_hist[bi])
                ep = p.op("vector", lambda v, j=j, bi=bi: v.tensor_tensor(
                    res[bi][:], ps[j][:, :], xf[bi][:], ALU.add), deps=ep_d)
                ep_hist[bi] = ep
                last_bank_reader[j] = ep
                stx = p.dma("sync", f"{tag}xo{bi}",
                            lambda e, oc=oc, tb=tb, bi=bi: e.dma_start(
                                out=xout[oc * 128:(oc + 1) * 128, tb * TT:(tb + 1) * TT],
                                in_=res[bi][:]),
                            deps=[ep])
                stx_hist[bi] = stx
                final_sts.append(stx)
    return final_sts



def emit_l1ffn(ctx, sh, u_sb, u_ready, xin, xout, guT, downRB, h_dram, tag):
    p = ctx.plan
    ps = ctx.ps
    TT = 512
    NT = T // TT

    NBW = 3
    gp = [ctx.sb(f"{tag}_gp{i}", [128, NCH * 128], BF16) for i in range(NBW)]
    up = [ctx.sb(f"{tag}_up{i}", [128, NCH * 128], BF16) for i in range(NBW)]
    sil = ctx.sb(f"{tag}_sil", [128, TT], F32)
    hst = [ctx.sb(f"{tag}_hs{i}", [128, TT], BF16) for i in range(2)]
    prevA = {}
    last_st_h = {}
    last_sil = None
    last_stt_bank = {4: None, 5: None, 6: None, 7: None}
    for oc in range(64):
        bi = oc % NBW
        pm2 = prevA.get(oc - NBW)
        ldg = p.dma("gpsimd", f"{tag}wg{bi}",
                    lambda e, oc=oc, bi=bi: e.dma_start(
                        out=gp[bi].ap().rearrange("p (c o) -> p c o", o=128),
                        in_=guT[oc].rearrange("(c p) o -> p c o", p=128)),
                    deps=[pm2["last_g"]] if pm2 else [])
        ldu = p.dma("gpsimd", f"{tag}wu{bi}",
                    lambda e, oc=oc, bi=bi: e.dma_start(
                        out=up[bi].ap().rearrange("p (c o) -> p c o", o=128),
                        in_=guT[64 + oc].rearrange("(c p) o -> p c o", p=128)),
                    deps=[pm2["last_u"]] if pm2 else [])
        lg = lu = None
        for tt in range(NT):
            pbg = 4 + tt % 2
            pbu = 6 + tt % 2
            mmdg = [ldg, u_ready[tt]]
            if last_stt_bank[pbg] is not None:
                mmdg.append(last_stt_bank[pbg])
            mmdu = [ldu, u_ready[tt]]
            if last_stt_bank[pbu] is not None:
                mmdu.append(last_stt_bank[pbu])
            for c in range(NCH):
                lg = p.op("tensor", lambda e, c=c, tt=tt, bi=bi, pbg=pbg: e.matmul(
                    ps[pbg][:, :], gp[bi][:, c * 128:(c + 1) * 128],
                    u_sb[:, c * T + tt * TT: c * T + (tt + 1) * TT],
                    start=(c == 0), stop=(c == NCH - 1)),
                    deps=(mmdg if c == 0 else []))
            for c in range(NCH):
                lu = p.op("tensor", lambda e, c=c, tt=tt, bi=bi, pbu=pbu: e.matmul(
                    ps[pbu][:, :], up[bi][:, c * 128:(c + 1) * 128],
                    u_sb[:, c * T + tt * TT: c * T + (tt + 1) * TT],
                    start=(c == 0), stop=(c == NCH - 1)),
                    deps=(mmdu if c == 0 else []))
            sl_d = [lg]
            if last_sil is not None:
                sl_d.append(last_sil)
            sl = p.op("scalar", lambda e, pbg=pbg: e.activation(
                sil[:], ps[pbg][:, :], AF.Silu), deps=sl_d)
            stt_d = [sl, lu]
            if (oc - 1, tt) in last_st_h:
                stt_d.append(last_st_h[(oc - 1, tt)])
            so = p.op("vector", lambda v, tt=tt, pbu=pbu: v.scalar_tensor_tensor(
                hst[tt % 2][:], sil[:], 0.5, ps[pbu][:, :], ALU.mult, ALU.mult),
                deps=stt_d)
            last_sil = so
            last_stt_bank[pbg] = so
            last_stt_bank[pbu] = so
            sth = p.dma("sync", f"{tag}h{tt % 2}",
                        lambda e, oc=oc, tt=tt: e.dma_start(
                            out=h_dram[oc * 128:(oc + 1) * 128, tt * TT:(tt + 1) * TT],
                            in_=hst[tt % 2][:]),
                        deps=[so])
            last_st_h[(oc, tt)] = sth
        prevA[oc] = dict(last_g=lg, last_u=lu)
    h_done = [last_st_h[(63, tt)] for tt in range(NT)] + \
             [last_st_h[(62, tt)] for tt in range(NT)]

    NBH = 4
    hb = [ctx.sb(f"{tag}_hb{i}", [128, TT], BF16) for i in range(NBH)]
    wrb = [ctx.sb(f"{tag}_wr{i}", [128, 1024], BF16) for i in range(NBH)]
    xf = [ctx.sb(f"{tag}_xf{i}", [128, TT], F32) for i in range(2)]
    res = [ctx.sb(f"{tag}_rs{i}", [128, TT], F32) for i in range(2)]

    final_sts = []
    last_bank_reader = {j: None for j in range(8)}
    hb_reader = {}
    wrb_reader = {}
    ep_hist = {}
    stx_hist = {}
    for tb in range(NT):
        for og in range(2):
            last_grp_mm = None
            for kc in range(64):
                bi = kc % NBH
                lddeps = list(h_done) if (tb == 0 and og == 0 and kc < NBH) else []
                if hb_reader.get(bi) is not None:
                    lddeps.append(hb_reader[bi])
                ldh = p.dma("sync", f"{tag}hb{bi}",
                            lambda e, kc=kc, tb=tb, bi=bi: e.dma_start(
                                out=hb[bi][:],
                                in_=h_dram[kc * 128:(kc + 1) * 128, tb * TT:(tb + 1) * TT]),
                            deps=lddeps)
                ldw_deps = []
                if wrb_reader.get(bi) is not None:
                    ldw_deps.append(wrb_reader[bi])
                ldw = p.dma("gpsimd", f"{tag}wr{bi}",
                            lambda e, kc=kc, og=og, bi=bi: e.dma_start(
                                out=wrb[bi][:],
                                in_=downRB[kc, :, og * 1024:(og + 1) * 1024]),
                            deps=ldw_deps)
                lm = None
                for j in range(8):
                    mmd = []
                    if kc == 0:
                        mmd = [ldh, ldw]
                        if last_bank_reader[j] is not None:
                            mmd.append(last_bank_reader[j])
                    elif j == 0:
                        mmd = [ldh, ldw]
                    lm = p.op("tensor", lambda e, j=j, kc=kc, bi=bi: e.matmul(
                        ps[j][:, :], wrb[bi][:, j * 128:(j + 1) * 128], hb[bi][:],
                        start=(kc == 0), stop=(kc == 63)), deps=mmd)
                hb_reader[bi] = lm
                wrb_reader[bi] = lm
                last_grp_mm = lm
            for j in range(8):
                oc = og * 8 + j
                bi = j % 2
                ldx_d = [ep_hist[bi]] if ep_hist.get(bi) is not None else []
                ldx = p.dma("sync", f"{tag}xi{bi}",
                            lambda e, oc=oc, tb=tb, bi=bi: e.dma_start(
                                out=xf[bi][:],
                                in_=xin[oc * 128:(oc + 1) * 128, tb * TT:(tb + 1) * TT]),
                            deps=ldx_d)
                ep_d = [last_grp_mm, ldx]
                if stx_hist.get(bi) is not None:
                    ep_d.append(stx_hist[bi])
                ep = p.op("vector", lambda v, j=j, bi=bi: v.tensor_tensor(
                    res[bi][:], ps[j][:, :], xf[bi][:], ALU.add), deps=ep_d)
                ep_hist[bi] = ep
                last_bank_reader[j] = ep
                stx = p.dma("sync", f"{tag}xo{bi}",
                            lambda e, oc=oc, tb=tb, bi=bi: e.dma_start(
                                out=xout[oc * 128:(oc + 1) * 128, tb * TT:(tb + 1) * TT],
                                in_=res[bi][:]),
                            deps=[ep])
                stx_hist[bi] = stx
                final_sts.append(stx)
    return final_sts


def build_l1ffn_launch():
    nc = bass.Bass("TRN2", target_bir_lowering=False)
    stack = contextlib.ExitStack()
    xin = nc.dram_tensor("x", [D, T], F32, kind="ExternalInput").ap()
    guT = nc.dram_tensor("guT", [128, 2048, 128], F32, kind="ExternalInput").ap()
    downRB = nc.dram_tensor("downRB", [64, 128, 2048], F32, kind="ExternalInput").ap()
    xout = nc.dram_tensor("xo", [D, T], F32, kind="ExternalOutput").ap()
    h_dram = nc.dram_tensor("h_scr", [8192, T], BF16, kind="Internal").ap()

    ctx = Ctx(nc, stack)
    sh = Shared(ctx)
    u_sb = ctx.sb("u", [128, NCH * T], BF16)
    ln = emit_ln(ctx, sh, xin, u_sb, "ln", compose_inner=False, eps=1e-5)
    ctx.plan.fence(ln["tail"])
    sts = emit_l1ffn(ctx, sh, u_sb, ln["tile_done"], xin, xout, guT, downRB,
                     h_dram, "sw")
    p = ctx.plan
    p.op("gpsimd", lambda g: g.nop(), deps=sts[-16:])
    p.emit()
    stack.close()
    return nc



def emit_ugather(ctx, sh, xin, u_all, cc_u, tag, deps=()):
    p = ctx.plan
    with ctx.scope():
        u_sb = ctx.sb(f"{tag}_usb", [128, NCH * T], BF16)
        ln = emit_ln(ctx, sh, xin, u_sb, f"{tag}ln", compose_inner=False,
                     eps=1e-5, out_dram=cc_u, deps=deps)
    gather = p.op("gpsimd", lambda g: g.collective_compute(
        "AllGather", ALU.bypass, replica_groups=PAIRS,
        ins=[cc_u.opt()], outs=[u_all.opt()]),
        deps=[ln["st_ops"][-1], ln["tile_done"][-1]], is_async=True)
    p.fence(ln["tail"])
    return dict(gather=gather)


def emit_oproj_exchange(ctx, sh, o_dram, o_ready, owT, cc2_in, cc2_out,
                        xin, xout, ob_cols, tag):
    p = ctx.plan
    ps = ctx.ps
    TT = 512
    final_sts = []
    with ctx.scope():
        own = [ctx.sb(f"{tag}_ow{i}", [128, 8 * 128], BF16) for i in range(16)]
        osb = [ctx.sb(f"{tag}_os{i}", [128, 8 * 512], BF16) for i in range(2)]
        cst = [ctx.sb(f"{tag}_cs{i}", [128, TT], F32) for i in range(2)]
        ow_ready = []
        for oc in range(16):
            ow_ready.append(p.dma("gpsimd", f"{tag}owp{oc % 2}",
                lambda e, oc=oc: e.dma_start(
                    out=own[oc].ap().rearrange("p (c o) -> p c o", o=128),
                    in_=owT[oc].rearrange("(c p) o -> p c o", p=128)),
                deps=()))
        osb_reader = {0: None, 1: None}
        cst_reader = {0: None, 1: None}
        evac_bank = {6: None, 7: None}
        cc_sts = []
        for rt in range(8):
            r, tt = rt // 4, rt % 4
            bi = rt % 2
            lds = []
            lddeps = list(o_ready) if rt < 2 else []
            if osb_reader[bi] is not None:
                lddeps.append(osb_reader[bi])
            for c in range(8):
                lds.append(p.dma("sync", f"{tag}ol{bi}",
                    lambda e, c=c, r=r, tt=tt, bi=bi: e.dma_start(
                        out=osb[bi][:, c * 512:(c + 1) * 512],
                        in_=o_dram[c * 128:(c + 1) * 128,
                                   r * 2048 + tt * TT: r * 2048 + (tt + 1) * TT]),
                    deps=lddeps if c < 2 else []))
            last_mm = None
            for oc in range(16):
                pb = 6 + oc % 2
                mmd = [lds[-1], ow_ready[oc]]
                if evac_bank[pb] is not None:
                    mmd.append(evac_bank[pb])
                for c in range(8):
                    last_mm = p.op("tensor", lambda e, oc=oc, c=c, bi=bi, pb=pb: e.matmul(
                        ps[pb][:, :], own[oc][:, c * 128:(c + 1) * 128],
                        osb[bi][:, c * 512:(c + 1) * 512],
                        start=(c == 0), stop=(c == 7)),
                        deps=(mmd if c == 0 else []))
                cb = oc % 2
                evd = [last_mm]
                if cst_reader[cb] is not None:
                    evd.append(cst_reader[cb])
                ev = p.op("vector", lambda v, cb=cb, pb=pb: v.tensor_copy(
                    cst[cb][:], ps[pb][:, :]), deps=evd)
                evac_bank[pb] = ev
                stc = p.dma("sync", f"{tag}cs{cb}",
                    lambda e, oc=oc, r=r, tt=tt, cb=cb: e.dma_start(
                        out=cc2_in[r * 2048 + oc * 128: r * 2048 + (oc + 1) * 128,
                                   tt * TT:(tt + 1) * TT],
                        in_=cst[cb][:]),
                    deps=[ev])
                cst_reader[cb] = stc
                cc_sts.append(stc)
            osb_reader[bi] = last_mm
        rs = p.op("gpsimd", lambda g: g.collective_compute(
            "ReduceScatter", ALU.add, replica_groups=PAIRS,
            ins=[cc2_in.opt()], outs=[cc2_out.opt()]),
            deps=[cc_sts[-1], cc_sts[-2]], is_async=True)
        xf = [ctx.sb(f"{tag}_xf{i}", [128, TT], F32) for i in range(2)]
        ccb = [ctx.sb(f"{tag}_cc{i}", [128, TT], F32) for i in range(2)]
        res = [ctx.sb(f"{tag}_rs{i}", [128, TT], F32) for i in range(2)]
        ep_hist = {0: None, 1: None}
        stx_hist = {0: None, 1: None}
        for oc in range(16):
            for tt in range(4):
                bi = (oc * 4 + tt) % 2
                lddeps = [rs] if oc == 0 and tt < 2 else []
                if ep_hist[bi] is not None:
                    lddeps.append(ep_hist[bi])
                ldx = p.dma("sync", f"{tag}xi{bi}",
                    lambda e, oc=oc, tt=tt, bi=bi: e.dma_start(
                        out=xf[bi][:],
                        in_=xin[oc * 128:(oc + 1) * 128, tt * TT:(tt + 1) * TT]),
                    deps=lddeps)
                ldc = p.dma("sync", f"{tag}ci{bi}",
                    lambda e, oc=oc, tt=tt, bi=bi: e.dma_start(
                        out=ccb[bi][:],
                        in_=cc2_out[oc * 128:(oc + 1) * 128, tt * TT:(tt + 1) * TT]),
                    deps=[rs] if oc == 0 and tt < 2 else [])
                epd = [ldx, ldc]
                if stx_hist[bi] is not None:
                    epd.append(stx_hist[bi])
                if ob_cols is not None:
                    ep = p.op("vector", lambda v, bi=bi, oc=oc: v.scalar_tensor_tensor(
                        res[bi][:], ccb[bi][:], ob_cols[:, oc:oc + 1], xf[bi][:],
                        ALU.add, ALU.add), deps=epd)
                else:
                    ep = p.op("vector", lambda v, bi=bi: v.tensor_tensor(
                        res[bi][:], ccb[bi][:], xf[bi][:], ALU.add), deps=epd)
                ep_hist[bi] = ep
                stx = p.dma("sync", f"{tag}xo{bi}",
                    lambda e, oc=oc, tt=tt, bi=bi: e.dma_start(
                        out=xout[oc * 128:(oc + 1) * 128, tt * TT:(tt + 1) * TT],
                        in_=res[bi][:]),
                    deps=[ep])
                stx_hist[bi] = stx
                final_sts.append(stx)
    tail = final_sts[-2:] + [ep_hist[0], ep_hist[1]]
    p.fence(tail)
    return final_sts



def emit_l0attn(ctx, sh, con, xin, xout, W, scratch, tag):
    p = ctx.plan
    ps = ctx.ps
    TT = 512
    u_all, qp, kp, vdr = scratch["u_all"], scratch["qp"], scratch["kp"], scratch["vdr"]
    gather = scratch["gather_op"]

    o_sts = []
    with ctx.scope():
        qw = [ctx.sb(f"{tag}_qw{i}", [128, NCH * 128], BF16) for i in range(8)]
        krw = ctx.sb(f"{tag}_krw", [128, NCH * 128], BF16)
        dnw = [ctx.sb(f"{tag}_dnw{i}", [128, NCH * 128], BF16) for i in range(2)]
        upkw = ctx.sb(f"{tag}_upkw", [128, 2 * 128], BF16)
        upvw = ctx.sb(f"{tag}_upvw", [128, 2 * 256], BF16)
        wlds = []
        for pn in range(8):
            wlds.append(p.dma("gpsimd", f"{tag}wq{pn % 2}", lambda e, pn=pn: e.dma_start(
                out=qw[pn].ap().rearrange("p (c o) -> p c o", o=128),
                in_=W["qwT"][pn].rearrange("(c p) o -> p c o", p=128)), deps=()))
        wlds.append(p.dma("gpsimd", f"{tag}wk", lambda e: e.dma_start(
            out=krw.ap().rearrange("p (c o) -> p c o", o=128),
            in_=W["krT"][0].rearrange("(c p) o -> p c o", p=128)), deps=()))
        for j in range(2):
            wlds.append(p.dma("gpsimd", f"{tag}wd{j}", lambda e, j=j: e.dma_start(
                out=dnw[j].ap().rearrange("p (c o) -> p c o", o=128),
                in_=W["dnT"][j].rearrange("(c p) o -> p c o", p=128)), deps=()))
        wlds.append(p.dma("gpsimd", f"{tag}wuk", lambda e: e.dma_start(
            out=upkw.ap().rearrange("p (c o) -> p c o", o=128),
            in_=W["upkT"].rearrange("(c p) o -> p c o", p=128)), deps=()))
        wlds.append(p.dma("gpsimd", f"{tag}wuv", lambda e: e.dma_start(
            out=upvw.ap().rearrange("p (c o) -> p c o", o=256),
            in_=W["upvT"].rearrange("(c p) o -> p c o", p=128)), deps=()))
        wld_done = wlds[-6:] + wlds[:2]

        ub = [ctx.sb(f"{tag}_ub{i}", [128, NCH * TT], BF16) for i in range(2)]
        qraw = ctx.sb(f"{tag}_qraw", [128, TT], BF16)
        qsq = ctx.sb(f"{tag}_qsq", [128, TT], BF16)
        rope1 = ctx.sb(f"{tag}_rp1", [128, TT], BF16)
        ropeo = ctx.sb(f"{tag}_rpo", [128, TT], BF16)
        ckv = ctx.sb(f"{tag}_ckv", [128, 2 * TT], BF16)
        vstage = [ctx.sb(f"{tag}_vst{i}", [128, 260], BF16) for i in range(2)]
        knorm = ctx.sb(f"{tag}_knorm", [4, TFULL], F32)
        maxkq = ctx.sb(f"{tag}_maxkq", [16, 1], F32)
        qn_sqrt = ctx.sb(f"{tag}_qns", [16, TT], F32)
        aug16 = ctx.sb(f"{tag}_aug16", [16, TT], BF16)
        onesrow_b = ctx.sb(f"{tag}_onerow", [1, TFULL], BF16)
        rowtmp = ctx.sb(f"{tag}_rowt", [4, 4], F32)

        oro = p.op("vector", lambda v: v.memset(onesrow_b[:], 1.0))
        ones_sts = [p.dma("sync", f"{tag}kon", lambda e, g=g: e.dma_start(
            out=kp[g, 64:65, :], in_=onesrow_b[:]), deps=[oro]) for g in range(4)]

        ub_reader = {0: None, 1: None}
        kr_sts, kc_sts, v_sts = [], [], []
        last_knmm = None
        kn_ev = []
        prev_rope = None
        prev_vst = {0: None, 1: None}
        prev_vev = None
        prev_ckv = None
        prev_kq = {}
        for rt in range(8):
            r, tt = rt // 4, rt % 4
            bi = rt % 2
            seg = slice(r * 2048 + tt * TT, r * 2048 + (tt + 1) * TT)
            lds = []
            for c in range(NCH):
                dd = [gather] if rt < 2 and c < 2 else []
                if ub_reader[bi] is not None and c == 0:
                    dd.append(ub_reader[bi])
                lds.append(p.dma("sync", f"{tag}u{bi}",
                    lambda e, c=c, r=r, tt=tt, bi=bi: e.dma_start(
                        out=ub[bi][:, c * TT:(c + 1) * TT],
                        in_=u_all[r * 2048 + c * 128: r * 2048 + (c + 1) * 128,
                                  tt * TT:(tt + 1) * TT]),
                    deps=dd))
            mmd = [lds[-1]] + (wld_done if rt == 0 else [])
            if prev_rope is not None:
                mmd.append(prev_rope)
            mm = None
            for c in range(NCH):
                mm = p.op("tensor", lambda e, c=c, bi=bi: e.matmul(
                    ps[0][:, :], krw[:, c * 128:(c + 1) * 128],
                    ub[bi][:, c * TT:(c + 1) * TT],
                    start=(c == 0), stop=(c == NCH - 1)), deps=(mmd if c == 0 else []))
            ev = p.op("vector", lambda v: v.tensor_copy(qraw[:], ps[0][:, :]),
                      deps=[mm, prev_kq.get("krperm")] if prev_kq.get("krperm") else [mm])
            pm = p.op("tensor", lambda e: e.matmul(
                ps[1][:, :], con["permT"][:], qraw[:], start=True, stop=True),
                deps=[ev] + ([prev_kq["krrope2"]] if "krrope2" in prev_kq else []))
            prev_kq["krperm"] = pm
            r1 = p.op("vector", lambda v, seg=seg: v.tensor_tensor(
                rope1[:], qraw[:], con["cos_k"][:, seg], ALU.mult), deps=[])
            r2 = p.op("vector", lambda v, seg=seg: v.tensor_tensor(
                ropeo[:], ps[1][:, :], con["sin_k"][:, seg], ALU.mult),
                deps=[pm] + kc_sts[-4:] + ([prev_kq["sq2"]] if "sq2" in prev_kq else []))
            r3 = p.op("vector", lambda v: v.tensor_tensor(
                ropeo[:], ropeo[:], rope1[:], ALU.add), deps=[])
            prev_kq["krrope2"] = r3
            sq1 = p.op("scalar", lambda e: e.activation(qsq[:], ropeo[:], AF.Square),
                       deps=[r3] + ([prev_kq["knmm1"]] if "knmm1" in prev_kq else []))
            knmm1 = p.op("tensor", lambda e: e.matmul(
                ps[2][0:4, 0:TT], con["H4"][:], qsq[:], start=True, stop=False),
                deps=[sq1] + (kn_ev[-1:] if kn_ev else []))
            prev_kq["knmm1"] = knmm1
            for g in range(4):
                kr_sts.append(p.dma("sync", f"{tag}krs{g % 2}",
                    lambda e, g=g, seg=seg: e.dma_start(
                        out=kp[g, 0:32, seg], in_=ropeo[32 * g:32 * (g + 1), :]),
                    deps=[r3]))
            prev_rope = r3

            mmc = None
            for j in range(2):
                for c in range(NCH):
                    mmc = p.op("tensor", lambda e, j=j, c=c, bi=bi: e.matmul(
                        ps[3][:, :],
                        dnw[j][:, c * 128:(c + 1) * 128],
                        ub[bi][:, c * TT:(c + 1) * TT],
                        start=(c == 0), stop=(c == NCH - 1)),
                        deps=([lds[-1], prev_ckv] if c == 0 and prev_ckv else ([lds[-1]] if c == 0 else [])))
                evc = p.op("vector", lambda v, j=j: v.tensor_copy(
                    ckv[:, j * TT:(j + 1) * TT], ps[3][:, :]), deps=[mmc])
                prev_ckv = evc
            mk = None
            for j in range(2):
                mk = p.op("tensor", lambda e, j=j: e.matmul(
                    ps[4][:, :], upkw[:, j * 128:(j + 1) * 128],
                    ckv[:, j * TT:(j + 1) * TT],
                    start=(j == 0), stop=(j == 1)),
                    deps=[prev_ckv] + ([prev_kq["kcev"]] if "kcev" in prev_kq else []))
            evk = p.op("vector", lambda v: v.tensor_copy(ropeo[:], ps[4][:, :]),
                       deps=[mk, sq1] + kr_sts[-4:])
            prev_kq["kcev"] = evk
            sq2 = p.op("scalar", lambda e: e.activation(qsq[:], ropeo[:], AF.Square),
                       deps=[evk, knmm1])
            prev_kq["sq2"] = sq2
            knmm2 = p.op("tensor", lambda e: e.matmul(
                ps[2][0:4, 0:TT], con["H4"][:], qsq[:], start=False, stop=True),
                deps=[sq2])
            kne = p.op("vector", lambda v, seg=seg: v.tensor_copy(
                knorm[:, seg], ps[2][0:4, 0:TT]), deps=[knmm2])
            kn_ev.append(kne)
            for g in range(4):
                kc_sts.append(p.dma("sync", f"{tag}kcs{g % 2}",
                    lambda e, g=g, seg=seg: e.dma_start(
                        out=kp[g, 32:64, seg], in_=ropeo[32 * g:32 * (g + 1), :]),
                    deps=[evk]))
            for sub in range(4):
                vb = sub % 2
                mv = None
                for j in range(2):
                    mv = p.op("tensor", lambda e, j=j, sub=sub: e.matmul(
                        ps[5][0:128, 0:256],
                        ckv[:, j * TT + sub * 128: j * TT + (sub + 1) * 128],
                        upvw[:, j * 256:(j + 1) * 256],
                        start=(j == 0), stop=(j == 1)),
                        deps=([prev_ckv] if sub == 0 and j == 0 else []) +
                             ([prev_vev] if j == 0 and prev_vev is not None else []) +
                             ([prev_vst[vb]] if j == 0 and prev_vst[vb] else []))
                def vevac(v, vb=vb):
                    for g in range(4):
                        v.tensor_copy(vstage[vb][:, 65 * g:65 * g + 64],
                                      ps[5][:, 64 * g:64 * (g + 1)])
                    return v.memset(vstage[vb][:, 64:260:65], 1.0)
                vev = p.op("vector", vevac, deps=[mv])
                prev_vst[vb] = vev
                prev_vev = vev
                tglob = (r * 2048 + tt * TT + sub * 128) // 128
                v_sts.append(p.dma("sync", f"{tag}vs{vb}",
                    lambda e, tglob=tglob, vb=vb: e.dma_start(
                        out=vdr[tglob, :, :], in_=vstage[vb][:]), deps=[vev]))
            ub_reader[bi] = mmc

        mx = p.op("vector", lambda v: v.tensor_reduce(
            rowtmp[0:4, 0:1], knorm[:, :], AX.X, ALU.max), deps=kn_ev)
        sqx = p.op("scalar", lambda e: e.activation(
            rowtmp[0:4, 1:2], rowtmp[0:4, 0:1], AF.Sqrt), deps=[mx])
        mi = p.op("tensor", lambda e: e.matmul(
            ps[2][0:16, 0:1], con["IND"][:], rowtmp[0:4, 1:2], start=True, stop=True),
            deps=[sqx, kn_ev[-1]])
        mie = p.op("vector", lambda v: v.tensor_copy(maxkq[:], ps[2][0:16, 0:1]),
                   deps=[mi])

        q_sts = []
        prev_q = {}
        qn_last = None
        for rt in range(8):
            r, tt = rt // 4, rt % 4
            bi = rt % 2
            seg = slice(r * 2048 + tt * TT, r * 2048 + (tt + 1) * TT)
            lds = []
            for c in range(NCH):
                dd = []
                if ub_reader[bi] is not None and c == 0:
                    dd.append(ub_reader[bi])
                lds.append(p.dma("sync", f"{tag}u{bi}",
                    lambda e, c=c, r=r, tt=tt, bi=bi: e.dma_start(
                        out=ub[bi][:, c * TT:(c + 1) * TT],
                        in_=u_all[r * 2048 + c * 128: r * 2048 + (c + 1) * 128,
                                  tt * TT:(tt + 1) * TT]),
                    deps=dd))
            last_pn_mm = None
            qnmm = None
            for pn in range(8):
                mmd = [lds[-1]]
                if "ev" in prev_q:
                    mmd.append(prev_q["ev"])
                mm = None
                for c in range(NCH):
                    mm = p.op("tensor", lambda e, c=c, bi=bi, pn=pn: e.matmul(
                        ps[0][:, :], qw[pn][:, c * 128:(c + 1) * 128],
                        ub[bi][:, c * TT:(c + 1) * TT],
                        start=(c == 0), stop=(c == NCH - 1)), deps=(mmd if c == 0 else []))
                ev = p.op("vector", lambda v: v.tensor_copy(qraw[:], ps[0][:, :]),
                          deps=[mm] + ([prev_q["pm"]] if "pm" in prev_q else []) +
                               ([prev_q["sq"]] if "sq" in prev_q else []))
                prev_q["ev"] = ev
                pm = p.op("tensor", lambda e: e.matmul(
                    ps[1][:, :], con["permT"][:], qraw[:], start=True, stop=True),
                    deps=[ev] + ([prev_q["r2"]] if "r2" in prev_q else []))
                prev_q["pm"] = pm
                r1 = p.op("vector", lambda v, seg=seg: v.tensor_tensor(
                    rope1[:], qraw[:], con["cos_q"][:, seg], ALU.mult), deps=[])
                r2 = p.op("vector", lambda v, seg=seg: v.tensor_tensor(
                    ropeo[:], ps[1][:, :], con["sin_q"][:, seg], ALU.mult),
                    deps=[pm] + q_sts[-2:])
                prev_q["r2"] = r2
                r3 = p.op("vector", lambda v: v.tensor_tensor(
                    ropeo[:], ropeo[:], rope1[:], ALU.add), deps=[])
                sq = p.op("scalar", lambda e: e.activation(qsq[:], qraw[:], AF.Square),
                          deps=[r1, ev] + ([qnmm] if qnmm else []))
                prev_q["sq"] = sq
                qnd = [sq]
                if pn == 0:
                    qnd.append(qn_last if qn_last is not None else mie)
                qnmm = p.op("tensor", lambda e, pn=pn: e.matmul(
                    ps[2][0:16, 0:TT], con["Hq"][pn][:], qsq[:],
                    start=(pn == 0), stop=(pn == 7)), deps=qnd)
                for half in range(2):
                    qh = 2 * pn + half
                    q_sts.append(p.dma("sync", f"{tag}qs{qh % 2}",
                        lambda e, qh=qh, half=half, seg=seg: e.dma_start(
                            out=qp[qh, 0:64, seg],
                            in_=ropeo[64 * half:64 * (half + 1), :]),
                        deps=[r3]))
            sqa = p.op("scalar", lambda e: e.activation(
                qn_sqrt[:], ps[2][0:16, 0:TT], AF.Sqrt),
                deps=[qnmm] + ([qn_last] if qn_last else []))
            aug = p.op("vector", lambda v: v.tensor_scalar(
                aug16[:], qn_sqrt[:], maxkq[:], None, ALU.mult),
                deps=[sqa, mie])
            qn_last = aug
            q_sts.append(p.dma("sync", f"{tag}qa",
                lambda e, seg=seg: e.dma_start(
                    out=qp[:, 64, seg], in_=aug16[:]), deps=[aug]))
            ub_reader[bi] = mm
        proj_done = [q_sts[-1], q_sts[-2], q_sts[-3],
                     kr_sts[-1], kr_sts[-2], kc_sts[-1], kc_sts[-2],
                     v_sts[-1], v_sts[-2]] + ones_sts

    p.fence([proj_done[0], proj_done[1]] + proj_done)
    o_dram = scratch["odr"]
    with ctx.scope():
        vsb = ctx.sb(f"{tag}_vsb", [128, 32 * 260], BF16)
        ksb = ctx.sb(f"{tag}_ksb", [65, TFULL], BF16)
        qsb = [ctx.sb(f"{tag}_qsb{i}", [65, TT], BF16) for i in range(3)]
        pbuf = [ctx.sb(f"{tag}_pb{i}", [128, TT], BF16) for i in range(3)]
        rsb = ctx.sb(f"{tag}_rsb", [1, TT], F32)
        rbs = ctx.sb(f"{tag}_rbs", [64, TT], F32)
        ostage = [ctx.sb(f"{tag}_ost{i}", [64, TT], BF16) for i in range(2)]
        vlds = [p.dma("sync", f"{tag}vl{ch % 2}", lambda e, ch=ch: e.dma_start(
            out=vsb[:, ch * 260:(ch + 1) * 260], in_=vdr[ch, :, :]),
            deps=proj_done if ch < 2 else []) for ch in range(32)]
        ksb_reader = None
        qsb_reader = {}
        pb_reader = {0: None, 1: None, 2: None}
        sband = {0: None, 1: None}
        avband = None
        rb_user = None
        prev_cp = None
        ost_reader = {0: None, 1: None}
        for g in range(4):
            kld = p.dma("sync", f"{tag}kl", lambda e, g=g: e.dma_start(
                out=ksb[:], in_=kp[g, :, :]),
                deps=([ksb_reader] if ksb_reader else []) +
                     (proj_done if g == 0 else []))
            for lq in range(4):
                qh = 4 * g + lq
                for qt in range(8):
                    qb = qt % 3
                    qld = p.dma("sync", f"{tag}ql{qb}", lambda e, qh=qh, qt=qt, qb=qb: e.dma_start(
                        out=qsb[qb][:], in_=qp[qh, :, qt * TT:(qt + 1) * TT]),
                        deps=[qsb_reader[qb]] if qsb_reader.get(qb) else [])
                    nkb = 4 * (qt + 1)
                    avmm = None
                    for kb in range(nkb):
                        sb = kb % 2
                        smd = [kld, qld]
                        if sband[sb] is not None:
                            smd.append(sband[sb])
                        smm = p.op("tensor", lambda e, kb=kb, sb=sb, qb=qb: e.matmul(
                            ps[sb][:, :], ksb[:, kb * 128:(kb + 1) * 128], qsb[qb][:],
                            start=True, stop=True), deps=smd)
                        pbN = kb % 3
                        exd = [smm]
                        if pb_reader[pbN] is not None:
                            exd.append(pb_reader[pbN])
                        ex = p.op("scalar", lambda e, sb=sb, pbN=pbN: e.activation(
                            pbuf[pbN][:], ps[sb][:, :], AF.Exp, scale=con["eighth"][:]),
                            deps=exd)
                        sband[sb] = ex
                        mdep = [ex]
                        if kb >= 4 * qt:
                            m = kb - 4 * qt
                            mk = p.op("vector", lambda v, m=m, pbN=pbN: v.tensor_tensor(
                                pbuf[pbN][:], pbuf[pbN][:], con["tri"][m][:], ALU.mult),
                                deps=[ex])
                            mdep = [mk]
                        avd = mdep + [vlds[kb]]
                        if kb == 0:
                            if avband is not None:
                                avd.append(avband)
                        avmm = p.op("tensor", lambda e, kb=kb, g=g, pbN=pbN: e.matmul(
                            ps[4][0:65, :], vsb[:, kb * 260 + 65 * g: kb * 260 + 65 * (g + 1)],
                            pbuf[pbN][:], start=(kb == 0), stop=(kb == nkb - 1)),
                            deps=avd)
                        pb_reader[pbN] = avmm
                    qsb_reader[qb] = avmm
                    rc = p.op("vector", lambda v: v.reciprocal(rsb[:], ps[4][64:65, :]),
                              deps=[avmm] + ([rb_user] if rb_user else []))
                    bc = p.op("tensor", lambda e: e.matmul(
                        ps[5][0:64, 0:TT], con["onesr64"][:], rsb[:],
                        start=True, stop=True), deps=[rc] + ([prev_cp] if prev_cp else []))
                    cp = p.op("vector", lambda v: v.tensor_copy(rbs[:], ps[5][0:64, 0:TT]),
                              deps=[bc])
                    prev_cp = cp
                    ob = qt % 2
                    mu_d = [cp]
                    if ost_reader[ob] is not None:
                        mu_d.append(ost_reader[ob])
                    mu = p.op("vector", lambda v, ob=ob: v.tensor_tensor(
                        ostage[ob][:], ps[4][0:64, 0:TT], rbs[:], ALU.mult), deps=mu_d)
                    rb_user = mu
                    avband = mu
                    st = p.dma("sync", f"{tag}os{ob}", lambda e, qh=qh, qt=qt, ob=ob: e.dma_start(
                        out=o_dram[qh * 64:(qh + 1) * 64, qt * TT:(qt + 1) * TT],
                        in_=ostage[ob][:]), deps=[mu])
                    ost_reader[ob] = st
                    o_sts.append(st)
            ksb_reader = avmm
    p.fence(o_sts[-2:])
    return o_sts



def setup_l0attn_consts(ctx, sh, nc):
    p = ctx.plan
    ins = {}
    for nm, shape in [("cos_q", [128, TFULL]), ("sin_q", [128, TFULL]),
                      ("cos_k", [128, TFULL]), ("sin_k", [128, TFULL]),
                      ("permT", [128, 128]), ("H4", [128, 4]),
                      ("Hq", [8, 128, 16]), ("IND", [4, 16]),
                      ("tri_mul", [4, 128, 512])]:
        ins[nm] = nc.dram_tensor(nm, shape, F32, kind="ExternalInput").ap()
    con = {}
    rdy = []
    for nm in ["cos_q", "sin_q", "cos_k", "sin_k"]:
        con[nm] = ctx.sb(f"c_{nm}", [128, TFULL], BF16)
        rdy.append(p.dma("gpsimd", "cst", lambda e, nm=nm: e.dma_start(
            out=con[nm][:], in_=ins[nm][:, :]), deps=()))
    con["permT"] = ctx.sb("c_permT", [128, 128], BF16)
    rdy.append(p.dma("gpsimd", "cst", lambda e: e.dma_start(
        out=con["permT"][:], in_=ins["permT"][:, :]), deps=()))
    con["H4"] = ctx.sb("c_H4", [128, 4], BF16)
    rdy.append(p.dma("gpsimd", "cst", lambda e: e.dma_start(
        out=con["H4"][:], in_=ins["H4"][:, :]), deps=()))
    con["Hq"] = []
    for pn in range(8):
        t = ctx.sb(f"c_Hq{pn}", [128, 16], BF16)
        con["Hq"].append(t)
        rdy.append(p.dma("gpsimd", "cst", lambda e, pn=pn, t=t: e.dma_start(
            out=t[:], in_=ins["Hq"][pn, :, :]), deps=()))
    con["IND"] = ctx.sb("c_IND", [4, 16], F32)
    rdy.append(p.dma("gpsimd", "cst", lambda e: e.dma_start(
        out=con["IND"][:], in_=ins["IND"][:, :]), deps=()))
    con["tri"] = []
    for m in range(4):
        t = ctx.sb(f"c_tri{m}", [128, 512], BF16)
        con["tri"].append(t)
        rdy.append(p.dma("gpsimd", "cst", lambda e, m=m, t=t: e.dma_start(
            out=t[:], in_=ins["tri_mul"][m, :, :]), deps=()))
    con["onesr64"] = ctx.sb("c_or64", [1, 64], F32)
    rdy.append(p.op("vector", lambda v: v.memset(con["onesr64"][:], 1.0)))
    con["eighth"] = ctx.sb("c_eighth", [128, 1], F32)
    rdy.append(p.op("vector", lambda v: v.memset(con["eighth"][:], 0.125)))
    return con, rdy


def build_l0attn_launch():
    nc = bass.Bass("TRN2", target_bir_lowering=False)
    stack = contextlib.ExitStack()
    xin = nc.dram_tensor("x", [D, T], F32, kind="ExternalInput").ap()
    W = dict(
        qwT=nc.dram_tensor("qwT", [8, 2048, 128], F32, kind="ExternalInput").ap(),
        krT=nc.dram_tensor("krT", [1, 2048, 128], F32, kind="ExternalInput").ap(),
        dnT=nc.dram_tensor("dnT", [2, 2048, 128], F32, kind="ExternalInput").ap(),
        upkT=nc.dram_tensor("upkT", [256, 128], F32, kind="ExternalInput").ap(),
        upvT=nc.dram_tensor("upvT", [256, 256], F32, kind="ExternalInput").ap(),
    )
    owT = nc.dram_tensor("owT", [16, 1024, 128], F32, kind="ExternalInput").ap()
    xout = nc.dram_tensor("xo", [D, T], F32, kind="ExternalOutput").ap()
    cc_u = nc.dram_tensor("cc_u", [D, T], BF16, kind="Internal").ap()
    u_all = nc.dram_tensor("u_all", [2 * D, T], BF16, kind="Internal").ap()
    dbg_kind = "ExternalOutput" if globals().get("_ATTN_DEBUG") else "Internal"
    qp = nc.dram_tensor("qp_scr", [16, 65, TFULL], BF16, kind=dbg_kind).ap()
    kp = nc.dram_tensor("kp_scr", [4, 65, TFULL], BF16, kind=dbg_kind).ap()
    vdr = nc.dram_tensor("v_scr", [32, 128, 260], BF16, kind=dbg_kind).ap()
    odr = nc.dram_tensor("o_scr", [1024, TFULL], BF16, kind=dbg_kind).ap()
    cc2_in = nc.dram_tensor("cc2_in", [2 * D, T], F32, kind="Internal").ap()
    cc2_out = nc.dram_tensor("cc2_out", [D, T], F32, kind="Internal").ap()

    ctx = Ctx(nc, stack)
    sh = Shared(ctx)
    con, crdy = setup_l0attn_consts(ctx, sh, nc)
    ug = emit_ugather(ctx, sh, xin, u_all, cc_u, "ug")
    ctx.plan.fence(crdy)
    scratch = dict(u_all=u_all, qp=qp, kp=kp, vdr=vdr, odr=odr,
                   gather_op=ug["gather"])
    o_sts = emit_l0attn(ctx, sh, con, xin, xout, W, scratch, "a0")
    sts = emit_oproj_exchange(ctx, sh, odr, o_sts[-2:], owT, cc2_in, cc2_out,
                              xin, xout, None, "a0x")
    p = ctx.plan
    p.op("gpsimd", lambda g: g.nop(), deps=sts[-4:])
    p.emit()
    stack.close()
    return nc


def setup_l1attn_consts(ctx, sh, nc):
    p = ctx.plan
    ins = {}
    for nm, shape in [("iota8neg", [8, TFULL]), ("jcols", [128, 32]),
                      ("tri_add", [4, 128, 512]), ("e8T", [8, 8, 128]),
                      ("E8", [8, 128, 8])]:
        ins[nm] = nc.dram_tensor(nm, shape, F32, kind="ExternalInput").ap()
    vbi = nc.dram_tensor("vb", [1, 1024], F32, kind="ExternalInput").ap()
    con = {}
    rdy = []
    con["iota8neg"] = ctx.sb("c_io8", [8, TFULL], BF16)
    rdy.append(p.dma("gpsimd", "cst1", lambda e: e.dma_start(
        out=con["iota8neg"][:], in_=ins["iota8neg"][:, :]), deps=()))
    con["jcols"] = ctx.sb("c_jc", [128, 32], F32)
    rdy.append(p.dma("gpsimd", "cst1", lambda e: e.dma_start(
        out=con["jcols"][:], in_=ins["jcols"][:, :]), deps=()))
    con["triA"] = []
    for m in range(4):
        t = ctx.sb(f"c_triA{m}", [128, 512], F32)
        con["triA"].append(t)
        rdy.append(p.dma("gpsimd", "cst1", lambda e, m=m, t=t: e.dma_start(
            out=t[:], in_=ins["tri_add"][m, :, :]), deps=()))
    con["e8T"] = []
    for hh in range(8):
        t = ctx.sb(f"c_e8T{hh}", [8, 128], BF16)
        con["e8T"].append(t)
        rdy.append(p.dma("gpsimd", "cst1", lambda e, hh=hh, t=t: e.dma_start(
            out=t[:], in_=ins["e8T"][hh, :, :]), deps=()))
    con["E8"] = []
    for pn in range(8):
        t = ctx.sb(f"c_E8{pn}", [128, 8], BF16)
        con["E8"].append(t)
        rdy.append(p.dma("gpsimd", "cst1", lambda e, pn=pn, t=t: e.dma_start(
            out=t[:], in_=ins["E8"][pn, :, :]), deps=()))
    con["vb_row"] = ctx.sb("c_vbr", [1, 1024], F32)
    rdy.append(p.dma("gpsimd", "cst1", lambda e: e.dma_start(
        out=con["vb_row"][:], in_=vbi[:, :]), deps=()))
    con["sc1"] = ctx.sb("c_sc1", [128, 1], F32)
    rdy.append(p.op("vector", lambda v: v.memset(con["sc1"][:], 1.0 / math.sqrt(128.0))))
    con["onesr64"] = ctx.sb("c_or64b", [1, 128], F32)
    rdy.append(p.op("vector", lambda v: v.memset(con["onesr64"][:], 1.0)))
    return con, rdy


def emit_l1attn(ctx, sh, con, W, scratch, tag):
    p = ctx.plan
    ps = ctx.ps
    TT = 512
    u_all = scratch["u_all"]
    qdr, kdr, vdr, odr = scratch["qdr"], scratch["kdr"], scratch["vdr1"], scratch["odr"]
    gather = scratch["gather_op"]
    o_sts = []
    qn_sb = ctx.sb(f"{tag}_qn", [8, TFULL], BF16)
    kn_sb = ctx.sb(f"{tag}_kn", [8, TFULL], BF16)
    maxq8 = ctx.sb(f"{tag}_mq8", [8, 2], F32)
    Mrows = ctx.sb(f"{tag}_Mr", [8, TFULL], BF16)
    Mtmp = ctx.sb(f"{tag}_Mt", [8, TT], F32)
    Mtmp2 = ctx.sb(f"{tag}_Mt2", [8, TT], BF16)
    scope1 = ctx.scope()
    scope1.__enter__()
    if True:
        qw = [ctx.sb(f"{tag}_qw{i}", [128, NCH * 128], BF16) for i in range(8)]
        kw = [ctx.sb(f"{tag}_kw{i}", [128, NCH * 128], BF16) for i in range(8)]
        vw = ctx.sb(f"{tag}_vw", [128, NCH * 1024], BF16)
        qb = ctx.sb(f"{tag}_qb", [128, 8], F32)
        kb = ctx.sb(f"{tag}_kb", [128, 8], F32)
        vbb = ctx.sb(f"{tag}_vbb", [128, 1024], F32)
        wlds = []
        for pn in range(8):
            wlds.append(p.dma("gpsimd", f"{tag}wq{pn % 2}", lambda e, pn=pn: e.dma_start(
                out=qw[pn].ap().rearrange("p (c o) -> p c o", o=128),
                in_=W["qwT"][pn].rearrange("(c p) o -> p c o", p=128)), deps=()))
            wlds.append(p.dma("gpsimd", f"{tag}wk{pn % 2}", lambda e, pn=pn: e.dma_start(
                out=kw[pn].ap().rearrange("p (c o) -> p c o", o=128),
                in_=W["kwT"][pn].rearrange("(c p) o -> p c o", p=128)), deps=()))
        wlds.append(p.dma("gpsimd", f"{tag}wv", lambda e: e.dma_start(
            out=vw.ap().rearrange("p (c o) -> p c o", o=1024),
            in_=W["vwT"].rearrange("(c p) o -> p c o", p=128)), deps=()))
        wlds.append(p.dma("gpsimd", f"{tag}wb", lambda e: e.dma_start(
            out=qb[:], in_=W["qb"][:, :]), deps=()))
        wlds.append(p.dma("gpsimd", f"{tag}wb", lambda e: e.dma_start(
            out=kb[:], in_=W["kb"][:, :]), deps=()))
        vb1 = p.op("tensor", lambda e: e.matmul(
            ps[6][:, :], con["onesr64"][0:1, :], con["vb_row"][0:1, 0:512],
            start=True, stop=True), deps=scratch["vb_deps"] + [sh.ready[2]])
        vb2 = p.op("tensor", lambda e: e.matmul(
            ps[7][:, :], con["onesr64"][0:1, :], con["vb_row"][0:1, 512:1024],
            start=True, stop=True), deps=scratch["vb_deps"])
        vbc = p.op("vector", lambda v: (
            v.tensor_copy(vbb[:, 0:512], ps[6][:, :]),
            v.tensor_copy(vbb[:, 512:1024], ps[7][:, :]))[-1], deps=[vb1, vb2])

        ub = [ctx.sb(f"{tag}_ub{i}", [128, NCH * TT], BF16) for i in range(2)]
        stage = [ctx.sb(f"{tag}_st{i}", [128, TT], BF16) for i in range(2)]
        sqb = [ctx.sb(f"{tag}_sq{i}", [128, TT], BF16) for i in range(2)]
        vstage = [ctx.sb(f"{tag}_vs{i}", [128, 1024], BF16) for i in range(2)]

        ub_reader = {0: None, 1: None}
        q_sts, k_sts, v_sts = [], [], []
        prev = {}
        nrm_ev = {"q": [], "k": []}
        for rt in range(8):
            r, tt = rt // 4, rt % 4
            bi = rt % 2
            seg = slice(r * 2048 + tt * TT, r * 2048 + (tt + 1) * TT)
            lds = []
            for c in range(NCH):
                dd = [gather] if rt < 2 and c < 2 else []
                if ub_reader[bi] is not None and c == 0:
                    dd.append(ub_reader[bi])
                lds.append(p.dma("sync", f"{tag}u{bi}",
                    lambda e, c=c, r=r, tt=tt, bi=bi: e.dma_start(
                        out=ub[bi][:, c * TT:(c + 1) * TT],
                        in_=u_all[r * 2048 + c * 128: r * 2048 + (c + 1) * 128,
                                  tt * TT:(tt + 1) * TT]),
                    deps=dd))
            last_mm = None
            for kind, wws, bcol, dst, sts, nbank in (
                    ("q", qw, qb, qdr, q_sts, 2), ("k", kw, kb, kdr, k_sts, 3)):
                pbk = 0 if kind == "q" else 1
                nmm = None
                for pn in range(8):
                    mmd = [lds[-1]] + (wlds[:2] if rt == 0 and pn == 0 else [])
                    if f"{kind}ev" in prev:
                        mmd.append(prev[f"{kind}ev"])
                    mm = None
                    for c in range(NCH):
                        mm = p.op("tensor", lambda e, c=c, bi=bi, pn=pn, wws=wws, pbk=pbk: e.matmul(
                            ps[pbk][:, :],
                            wws[pn][:, c * 128:(c + 1) * 128],
                            ub[bi][:, c * TT:(c + 1) * TT],
                            start=(c == 0), stop=(c == NCH - 1)),
                            deps=(mmd if c == 0 else []))
                    sb_ = stage[0 if kind == "q" else 1]
                    sq_ = sqb[0 if kind == "q" else 1]
                    evd = [mm]
                    if f"{kind}st" in prev:
                        evd.append(prev[f"{kind}st"])
                    if f"{kind}sq" in prev:
                        evd.append(prev[f"{kind}sq"])
                    ev = p.op("vector", lambda v, sb_=sb_, pn=pn, bcol=bcol, pbk=pbk: v.tensor_scalar(
                        sb_[:], ps[pbk][:, :],
                        bcol[:, pn:pn + 1], None, ALU.add), deps=evd)
                    prev[f"{kind}ev"] = ev
                    sqo = p.op("scalar", lambda e, sb_=sb_, sq_=sq_: e.activation(
                        sq_[:], sb_[:], AF.Square),
                        deps=[ev] + ([nmm] if nmm else []))
                    prev[f"{kind}sq"] = sqo
                    nd = [sqo]
                    if pn == 0:
                        if nrm_ev[kind]:
                            nd.append(nrm_ev[kind][-1])
                    nmm = p.op("tensor", lambda e, pn=pn, sq_=sq_, nbank=nbank: e.matmul(
                        ps[nbank][0:8, 0:TT], con["E8"][pn][:], sq_[:],
                        start=(pn == 0), stop=(pn == 7)), deps=nd)
                    st = p.dma("sync", f"{tag}{kind}s{pn % 2}",
                        lambda e, pn=pn, seg=seg, sb_=sb_, dst=dst: e.dma_start(
                            out=dst[pn * 128:(pn + 1) * 128, seg], in_=sb_[:]),
                        deps=[ev])
                    prev[f"{kind}st"] = st
                    sts.append(st)
                ne = p.op("vector", lambda v, seg=seg, nbank=nbank,
                          nsb=(qn_sb if kind == "q" else kn_sb): v.tensor_copy(
                    nsb[:, seg], ps[nbank][0:8, 0:TT]), deps=[nmm])
                nrm_ev[kind].append(ne)
            for sub in range(4):
                vbuf = sub % 2
                for half in range(2):
                    pv = 6 + half
                    mv = None
                    for c in range(NCH):
                        mvd = []
                        if c == 0:
                            mvd = [lds[-1], wlds[16]] if rt == 0 and sub == 0 else [lds[-1]]
                            if prev.get(f"vev{pv}") is not None:
                                mvd.append(prev[f"vev{pv}"])
                            if rt == 0 and sub == 0 and half == 0:
                                mvd.append(vbc)
                        mv = p.op("tensor", lambda e, c=c, sub=sub, half=half, bi=bi, pv=pv: e.matmul(
                            ps[pv][:, :],
                            ub[bi][:, c * TT + sub * 128: c * TT + (sub + 1) * 128],
                            vw[:, c * 1024 + half * 512: c * 1024 + (half + 1) * 512],
                            start=(c == 0), stop=(c == NCH - 1)),
                            deps=mvd)
                    vvd = [mv, vbc]
                    if prev.get(f"vst{vbuf}") is not None and half == 0:
                        vvd.append(prev[f"vst{vbuf}"])
                    vv = p.op("vector", lambda v, vbuf=vbuf, half=half, pv=pv: v.tensor_tensor(
                        vstage[vbuf][:, half * 512:(half + 1) * 512],
                        ps[pv][:, :], vbb[:, half * 512:(half + 1) * 512], ALU.add),
                        deps=vvd)
                    prev[f"vev{pv}"] = vv
                tglob = (r * 2048 + tt * TT + sub * 128) // 128
                vst = p.dma("sync", f"{tag}vs{vbuf}",
                    lambda e, tglob=tglob, vbuf=vbuf: e.dma_start(
                        out=vdr[tglob, :, :], in_=vstage[vbuf][:]),
                    deps=[prev["vev6"], prev["vev7"]])
                prev[f"vst{vbuf}"] = vst
                v_sts.append(vst)
            ub_reader[bi] = mv

        mx = p.op("vector", lambda v: v.tensor_reduce(
            maxq8[:, 0:1], qn_sb[:, :], AX.X, ALU.max), deps=nrm_ev["q"])
        sqx = p.op("scalar", lambda e: e.activation(
            maxq8[:, 1:2], maxq8[:, 0:1], AF.Sqrt), deps=[mx])
        ngx = p.op("vector", lambda v: v.tensor_scalar(
            maxq8[:, 1:2], maxq8[:, 1:2], -1.0, None, ALU.mult), hard=[sqx])
        mrows_done = []
        prev_mt = None
        for sgi in range(8):
            seg = slice(sgi * TT, (sgi + 1) * TT)
            s1 = p.op("scalar", lambda e, seg=seg: e.activation(
                Mtmp[:], kn_sb[:, seg], AF.Sqrt),
                deps=[ngx, nrm_ev["k"][-1]] + ([prev_mt] if prev_mt else []))
            s2 = p.op("vector", lambda v, seg=seg: v.tensor_scalar(
                Mtmp2[:], Mtmp[:], maxq8[:, 1:2], None, ALU.mult), deps=[s1])
            s3 = p.op("vector", lambda v, seg=seg: v.tensor_tensor(
                Mrows[:, seg], Mtmp2[:], con["iota8neg"][:, seg], ALU.add), hard=[s2])
            prev_mt = s3
            mrows_done.append(s3)

    scope1.__exit__(None, None, None)
    first_tail = [q_sts[-1], q_sts[-2], k_sts[-1], k_sts[-2],
                  v_sts[-1], v_sts[-2], mrows_done[-1],
                  ub_reader[0], ub_reader[1], nrm_ev["q"][-1], nrm_ev["k"][-1]]
    p.fence(first_tail)
    with ctx.scope():
        NBQ = 3
        qtl = [ctx.sb(f"{tag}_qt{i}", [128, 128], BF16) for i in range(NBQ)]
        ktl = [ctx.sb(f"{tag}_kt{i}", [128, TT], BF16) for i in range(2)]
        vtl = [ctx.sb(f"{tag}_vt{i}", [128, 128], BF16) for i in range(NBQ)]
        pbuf = [ctx.sb(f"{tag}_pb{i}", [128, TT], BF16) for i in range(3)]
        rsb = ctx.sb(f"{tag}_rsb", [1, TT], F32)
        rbs = ctx.sb(f"{tag}_rbs", [128, TT], F32)
        ostage = [ctx.sb(f"{tag}_os{i}", [128, TT], BF16) for i in range(2)]
        qt_reader = {}
        kt_reader = {0: None, 1: None}
        vt_reader = {}
        pb_reader = {0: None, 1: None, 2: None}
        sband = {0: None, 1: None}
        av_user = None
        den_user = None
        prev_cp = None
        ost_reader = {0: None, 1: None}
        first_score_deps = [q_sts[-1], q_sts[-2], k_sts[-1], k_sts[-2],
                           v_sts[-1], v_sts[-2], mrows_done[-1]]
        for hh in range(8):
            for it in range(8):
                jb_lo = max(0, 4 * it - 6)
                jb_hi = 4 * it + 3
                kldd = list(first_score_deps) if hh == 0 and it == 0 else []
                if kt_reader[it % 2] is not None:
                    kldd.append(kt_reader[it % 2])
                kld = p.dma("sync", f"{tag}ktl{it % 2}",
                    lambda e, hh=hh, it=it: e.dma_start(
                        out=ktl[it % 2][:],
                        in_=kdr[hh * 128:(hh + 1) * 128, it * TT:(it + 1) * TT]),
                    deps=kldd)
                avmm = None
                denmm = None
                njb = jb_hi - jb_lo + 1
                for ji, jb in enumerate(range(jb_lo, jb_hi + 1)):
                    qbuf = ji % NBQ
                    qldd = [qt_reader[qbuf]] if qt_reader.get(qbuf) else []
                    if hh == 0 and it == 0 and ji < 2:
                        qldd += first_score_deps[:2]
                    qld = p.dma("sync", f"{tag}qtl{qbuf}",
                        lambda e, hh=hh, jb=jb, qbuf=qbuf: e.dma_start(
                            out=qtl[qbuf][:],
                            in_=qdr[hh * 128:(hh + 1) * 128, jb * 128:(jb + 1) * 128]),
                        deps=qldd)
                    vldd = [vt_reader[qbuf]] if vt_reader.get(qbuf) else []
                    vld = p.dma("sync", f"{tag}vtl{qbuf}",
                        lambda e, hh=hh, jb=jb, qbuf=qbuf: e.dma_start(
                            out=vtl[qbuf][:],
                            in_=vdr[jb, :, hh * 128:(hh + 1) * 128]),
                        deps=vldd)
                    sb_ = ji % 2
                    augd = [mrows_done[it]]
                    if sband[sb_] is not None:
                        augd.append(sband[sb_])
                    p.op("tensor", lambda e, hh=hh, it=it, sb_=sb_: e.matmul(
                        ps[sb_][:, :], con["e8T"][hh][:],
                        Mrows[0:8, it * TT:(it + 1) * TT],
                        start=True, stop=False), deps=augd)
                    smm = p.op("tensor", lambda e, sb_=sb_, qbuf=qbuf, it=it: e.matmul(
                        ps[sb_][:, :], qtl[qbuf][:], ktl[it % 2][:],
                        start=False, stop=True), deps=[qld, kld])
                    qt_reader[qbuf] = smm
                    exd_extra = []
                    if jb >= 4 * it:
                        m = jb - 4 * it
                        mk = p.op("vector", lambda v, m=m, sb_=sb_: v.tensor_tensor(
                            ps[sb_][:, :], ps[sb_][:, :], con["triA"][m][:], ALU.add),
                            deps=[smm])
                        exd_extra = [mk]
                    pbN = ji % 3
                    exd = [smm] + exd_extra
                    if pb_reader[pbN] is not None:
                        exd.append(pb_reader[pbN])
                    ex = p.op("scalar", lambda e, sb_=sb_, pbN=pbN, jb=jb: e.activation(
                        pbuf[pbN][:], ps[sb_][:, :], AF.Exp, scale=con["sc1"][:],
                        bias=con["jcols"][:, jb:jb + 1]), deps=exd)
                    sband[sb_] = ex
                    dend = [ex, sh.ready[1]]
                    if ji == 0 and den_user is not None:
                        dend.append(den_user)
                    denmm = p.op("tensor", lambda e, ji=ji, pbN=pbN, njb=njb: e.matmul(
                        ps[4][0:1, :], sh.ones_b[:], pbuf[pbN][:],
                        start=(ji == 0), stop=(ji == njb - 1)), deps=dend)
                    avd = [ex, vld]
                    if ji == 0 and av_user is not None:
                        avd.append(av_user)
                    avmm = p.op("tensor", lambda e, ji=ji, pbN=pbN, qbuf=qbuf, njb=njb: e.matmul(
                        ps[5][:, :], vtl[qbuf][:], pbuf[pbN][:],
                        start=(ji == 0), stop=(ji == njb - 1)), deps=avd)
                    vt_reader[qbuf] = avmm
                    pb_reader[pbN] = avmm
                kt_reader[it % 2] = smm
                rc = p.op("vector", lambda v: v.reciprocal(rsb[:], ps[4][0:1, :]),
                          deps=[denmm])
                den_user = rc
                bc = p.op("tensor", lambda e: e.matmul(
                    ps[3][:, :], con["onesr64"][0:1, :], rsb[:],
                    start=True, stop=True),
                    deps=[rc] + ([prev_cp] if prev_cp else []) +
                         (nrm_ev["k"][-1:] if hh == 0 and it == 0 else []))
                cp = p.op("vector", lambda v: v.tensor_copy(rbs[:], ps[3][:, :]),
                          deps=[bc])
                prev_cp = cp
                ob = it % 2
                mud = [cp]
                if ost_reader[ob] is not None:
                    mud.append(ost_reader[ob])
                mu = p.op("vector", lambda v, ob=ob: v.tensor_tensor(
                    ostage[ob][:], ps[5][:, :], rbs[:], ALU.mult), deps=mud)
                av_user = mu
                st = p.dma("sync", f"{tag}osx{ob}",
                    lambda e, hh=hh, it=it, ob=ob: e.dma_start(
                        out=odr[hh * 128:(hh + 1) * 128, it * TT:(it + 1) * TT],
                        in_=ostage[ob][:]), deps=[mu])
                ost_reader[ob] = st
                o_sts.append(st)
    p.fence(o_sts[-2:])
    return o_sts


def build_l1attn_launch():
    nc = bass.Bass("TRN2", target_bir_lowering=False)
    stack = contextlib.ExitStack()
    xin = nc.dram_tensor("x", [D, T], F32, kind="ExternalInput").ap()
    W = dict(
        qwT=nc.dram_tensor("qwT", [8, 2048, 128], F32, kind="ExternalInput").ap(),
        kwT=nc.dram_tensor("kwT", [8, 2048, 128], F32, kind="ExternalInput").ap(),
        vwT=nc.dram_tensor("vwT", [2048, 1024], F32, kind="ExternalInput").ap(),
        qb=nc.dram_tensor("qb", [128, 8], F32, kind="ExternalInput").ap(),
        kb=nc.dram_tensor("kb", [128, 8], F32, kind="ExternalInput").ap(),
    )
    owT = nc.dram_tensor("owT", [16, 1024, 128], F32, kind="ExternalInput").ap()
    obi = nc.dram_tensor("ob", [128, 16], F32, kind="ExternalInput").ap()
    xout = nc.dram_tensor("xo", [D, T], F32, kind="ExternalOutput").ap()
    cc_u = nc.dram_tensor("cc_u", [D, T], BF16, kind="Internal").ap()
    u_all = nc.dram_tensor("u_all", [2 * D, T], BF16, kind="Internal").ap()
    dbg_kind = "ExternalOutput" if globals().get("_ATTN_DEBUG") else "Internal"
    qdr = nc.dram_tensor("q_scr", [1024, TFULL], BF16, kind=dbg_kind).ap()
    kdr = nc.dram_tensor("k_scr", [1024, TFULL], BF16, kind=dbg_kind).ap()
    vdr1 = nc.dram_tensor("v_scr", [32, 128, 1024], BF16, kind=dbg_kind).ap()
    odr = nc.dram_tensor("o_scr", [1024, TFULL], BF16, kind=dbg_kind).ap()
    cc2_in = nc.dram_tensor("cc2_in", [2 * D, T], F32, kind="Internal").ap()
    cc2_out = nc.dram_tensor("cc2_out", [D, T], F32, kind="Internal").ap()

    ctx = Ctx(nc, stack)
    sh = Shared(ctx)
    con, crdy = setup_l1attn_consts(ctx, sh, nc)
    ob_cols = ctx.sb("ob_cols", [128, 16], F32)
    obl = ctx.plan.dma("gpsimd", "cst1", lambda e: e.dma_start(
        out=ob_cols[:], in_=obi[:, :]), deps=())
    ug = emit_ugather(ctx, sh, xin, u_all, cc_u, "ug")
    ctx.plan.fence(crdy + [obl])
    scratch = dict(u_all=u_all, qdr=qdr, kdr=kdr, vdr1=vdr1, odr=odr,
                   gather_op=ug["gather"], vb_deps=[crdy[-3]])
    o_sts = emit_l1attn(ctx, sh, con, W, scratch, "a1")
    sts = emit_oproj_exchange(ctx, sh, odr, o_sts[-2:], owT, cc2_in, cc2_out,
                              xin, xout, ob_cols, "a1x")
    p = ctx.plan
    p.op("gpsimd", lambda g: g.nop(), deps=sts[-4:])
    p.emit()
    stack.close()
    return nc


def build_l0ffn_launch(debug=False):
    nc = bass.Bass("TRN2", target_bir_lowering=False)
    stack = contextlib.ExitStack()
    xin = nc.dram_tensor("x", [D, T], F32, kind="ExternalInput").ap()
    wfcT = nc.dram_tensor("wfcT", [64, 2048, 128], F32, kind="ExternalInput").ap()
    wprojT = nc.dram_tensor("wprojT", [16, 8192, 128], F32, kind="ExternalInput").ap()
    xout = nc.dram_tensor("xo", [D, T], F32, kind="ExternalOutput").ap()
    kind = "ExternalOutput" if debug else "Internal"
    tprojRB = nc.dram_tensor("tprojRB", [64, 128, 2048], BF16, kind=kind).ap()
    h_dram = nc.dram_tensor("h_scr", [8192, T], BF16, kind=kind).ap()
    if debug:
        dbg = nc.dram_tensor("dbg", [128, NCH * T], BF16, kind="ExternalOutput").ap()
        dbg2 = nc.dram_tensor("dbg2", [128, 8], F32, kind="ExternalOutput").ap()

    ctx = Ctx(nc, stack)
    sh = Shared(ctx)
    xn_sb = ctx.sb("xn", [128, NCH * T], BF16)
    prep = emit_l0ffn_prep(ctx, sh, wfcT, wprojT, tprojRB, "pp")
    ctx.plan.fence(prep["tail"])
    ln = emit_ln(ctx, sh, xin, xn_sb, "ln", compose_inner=True)
    ctx.plan.fence(ln["tail"])
    sts = emit_l0ffn_main(ctx, sh, prep, xn_sb, ln["tile_done"], xin, xout,
                          wfcT, tprojRB, h_dram, "ff")
    p = ctx.plan
    if debug:
        sd = p.dma("sync", "dbgxn", lambda e: e.dma_start(out=dbg[:, :], in_=xn_sb[:]),
                   deps=ln["tile_done"])

        def dbgcols(v):
            v.tensor_copy(dbg2s[:, 0:1], prep["gfc_col"][:])
            v.tensor_copy(dbg2s[:, 1:2], prep["thr_fc"][:])
            v.tensor_copy(dbg2s[:, 2:3], prep["thr_pj"][:])
            return v.tensor_copy(dbg2s[0:1, 3:4], prep["gph"][0:1, :])
        dbg2s = ctx.sb("dbg2s", [128, 8], F32)
        z = p.op("vector", lambda v: v.memset(dbg2s[:], 0.0))
        dc = p.op("vector", dbgcols, deps=[prep["cc"], z])
        sd2 = p.dma("sync", "dbgc", lambda e: e.dma_start(out=dbg2[:, :], in_=dbg2s[:]),
                    deps=[dc])
        p.op("gpsimd", lambda g: g.nop(), deps=sts[-16:] + [sd, sd2])
    else:
        p.op("gpsimd", lambda g: g.nop(), deps=sts[-16:])
    p.emit()
    stack.close()
    return nc


def build_fused():
    nc = bass.Bass("TRN2", target_bir_lowering=False)
    stack = contextlib.ExitStack()
    A = lambda n, s, dt=F32, k="ExternalInput": nc.dram_tensor(n, s, dt, kind=k).ap()
    x_in = A("x", [D, T])
    wfcT = A("wfcT", [64, 2048, 128])
    wprojT = A("wprojT", [16, 8192, 128])
    guT = A("guT", [128, 2048, 128])
    downRB = A("downRB", [64, 128, 2048])
    W0 = dict(qwT=A("qwT0", [8, 2048, 128]), krT=A("krT", [1, 2048, 128]),
              dnT=A("dnT", [2, 2048, 128]), upkT=A("upkT", [256, 128]),
              upvT=A("upvT", [256, 256]))
    owT0 = A("owT0", [16, 1024, 128])
    W1 = dict(qwT=A("qwT1", [8, 2048, 128]), kwT=A("kwT1", [8, 2048, 128]),
              vwT=A("vwT1", [2048, 1024]), qb=A("qb1", [128, 8]),
              kb=A("kb1", [128, 8]))
    owT1 = A("owT1", [16, 1024, 128])
    obi = A("ob1", [128, 16])
    xo = A("xo", [D, T], F32, "ExternalOutput")
    xi0 = A("xi0", [D, T], F32, "Internal")
    xi1 = A("xi1", [D, T], F32, "Internal")
    h_dram = A("h_scr", [8192, T], BF16, "Internal")
    tprojRB = A("tprojRB", [64, 128, 2048], BF16, "Internal")
    cc_u = A("cc_u", [D, T], BF16, "Internal")
    u_all = A("u_all", [2 * D, T], BF16, "Internal")
    cc2_in = A("cc2_in", [2 * D, T], F32, "Internal")
    cc2_out = A("cc2_out", [D, T], F32, "Internal")
    qp = A("qp_scr", [16, 65, TFULL], BF16, "Internal")
    kp = A("kp_scr", [4, 65, TFULL], BF16, "Internal")
    vdr0 = A("v_scr0", [32, 128, 260], BF16, "Internal")
    odr = A("o_scr", [1024, TFULL], BF16, "Internal")
    qdr = A("q_scr", [1024, TFULL], BF16, "Internal")
    kdr = A("k_scr", [1024, TFULL], BF16, "Internal")
    vdr1 = A("v_scr1", [32, 128, 1024], BF16, "Internal")

    ctx = Ctx(nc, stack)
    sh = Shared(ctx)
    p = ctx.plan
    prep = emit_l0ffn_prep(ctx, sh, wfcT, wprojT, tprojRB, "pp")
    p.fence(prep["tail"])

    def l0ffn(xa, xb, tag):
        with ctx.scope():
            xn_sb = ctx.sb(f"{tag}_xn", [128, NCH * T], BF16)
            ln = emit_ln(ctx, sh, xa, xn_sb, f"{tag}l", compose_inner=True)
            p.fence(ln["tail"])
            sts = emit_l0ffn_main(ctx, sh, prep, xn_sb, ln["tile_done"], xa, xb,
                                  wfcT, tprojRB, h_dram, tag)
        p.fence(sts[-4:])

    def l1ffn(xa, xb, tag):
        with ctx.scope():
            u_sb = ctx.sb(f"{tag}_u", [128, NCH * T], BF16)
            ln = emit_ln(ctx, sh, xa, u_sb, f"{tag}l", compose_inner=False, eps=1e-5)
            p.fence(ln["tail"])
            sts = emit_l1ffn(ctx, sh, u_sb, ln["tile_done"], xa, xb, guT, downRB,
                             h_dram, tag)
        p.fence(sts[-4:])

    def l0attn(xa, xb, tag):
        with ctx.scope():
            con, crdy = setup_l0attn_consts(ctx, sh, nc)
            ug = emit_ugather(ctx, sh, xa, u_all, cc_u, f"{tag}g")
            p.fence(crdy)
            scratch = dict(u_all=u_all, qp=qp, kp=kp, vdr=vdr0, odr=odr,
                           gather_op=ug["gather"])
            o_sts = emit_l0attn(ctx, sh, con, xa, xb, W0, scratch, tag)
            sts = emit_oproj_exchange(ctx, sh, odr, o_sts[-2:], owT0, cc2_in,
                                      cc2_out, xa, xb, None, f"{tag}x")
        p.fence(sts[-4:])

    def l1attn(xa, xb, tag):
        with ctx.scope():
            con, crdy = setup_l1attn_consts(ctx, sh, nc)
            ob_cols = ctx.sb(f"{tag}_obc", [128, 16], F32)
            obl = p.dma("gpsimd", "cst1", lambda e: e.dma_start(
                out=ob_cols[:], in_=obi[:, :]), deps=())
            ug = emit_ugather(ctx, sh, xa, u_all, cc_u, f"{tag}g")
            p.fence(crdy + [obl])
            scratch = dict(u_all=u_all, qdr=qdr, kdr=kdr, vdr1=vdr1, odr=odr,
                           gather_op=ug["gather"], vb_deps=[crdy[-3]])
            o_sts = emit_l1attn(ctx, sh, con, W1, scratch, tag)
            sts = emit_oproj_exchange(ctx, sh, odr, o_sts[-2:], owT1, cc2_in,
                                      cc2_out, xa, xb, ob_cols, f"{tag}x")
        p.fence(sts[-4:])

    l0ffn(x_in, xi0, "fA")
    l0attn(xi0, xi1, "aA")
    l0ffn(xi1, xi0, "fB")
    l1ffn(xi0, xi1, "gA")
    l1attn(xi1, xi0, "bA")
    l1ffn(xi0, xo, "gB")
    p.op("gpsimd", lambda g: g.nop(), deps=())
    p.emit()
    stack.close()
    return nc


_BUILD_CACHE = {}


def _get(name, fn):
    if name not in _BUILD_CACHE:
        _BUILD_CACHE[name] = fn()
    return _BUILD_CACHE[name]


def kernel(**inputs):
    cores = list(range(8))
    prep = [prep_core_inputs(inputs, c) for c in cores]
    nc = _get("fused", build_fused)
    maps = []
    for c in cores:
        d = prep[c]
        maps.append(dict(
            x=d["x"], wfcT=d["wfcT"], wprojT=d["wprojT"], guT=d["guT"],
            downRB=d["downRB"],
            qwT0=d["l0_qwT"], krT=d["l0_krT"], dnT=d["l0_dnT"],
            upkT=d["l0_upkT"], upvT=d["l0_upvT"], owT0=d["l0_owT"],
            qwT1=d["l1_qwT"], kwT1=d["l1_kwT"], vwT1=d["l1_vwT"],
            qb1=d["l1_qb"], kb1=d["l1_kb"], owT1=d["l1_owT"], ob1=d["l1_ob"],
            cos_q=d["cos_q"], sin_q=d["sin_q"], cos_k=d["cos_k"],
            sin_k=d["sin_k"], permT=d["permT"], H4=d["H4"], Hq=d["Hq"],
            IND=d["IND"], tri_mul=d["tri_mul"],
            iota8neg=d["iota8neg"], jcols=d["jcols"], tri_add=d["tri_add"],
            e8T=d["e8T"], E8=d["E8"], vb=d["l1_vb"]))
    res = run_bass_kernel_spmd(nc, maps, core_ids=cores)
    out = np.zeros((4, 4096, 2048), np.float32)
    for c in cores:
        b, h = c // 2, c % 2
        out[b, h * T:(h + 1) * T, :] = np.asarray(res.results[c]["xo"]).T
    return out


def kernel_unfused(**inputs):
    cores = list(range(8))
    prep = [prep_core_inputs(inputs, c) for c in cores]
    x = [prep[c]["x"] for c in cores]

    nc_ffn0 = _get("ffn0", build_l0ffn_launch)
    nc_att0 = _get("att0", build_l0attn_launch)
    nc_ffn1 = _get("ffn1", build_l1ffn_launch)
    nc_att1 = _get("att1", build_l1attn_launch)

    def run(nc, maps):
        res = run_bass_kernel_spmd(nc, maps, core_ids=cores)
        return [res.results[c]["xo"] for c in cores]

    for step, nc in [("f0", nc_ffn0), ("a0", nc_att0), ("f0", nc_ffn0),
                     ("f1", nc_ffn1), ("a1", nc_att1), ("f1", nc_ffn1)]:
        maps = []
        for c in cores:
            d = prep[c]
            if step == "f0":
                maps.append(dict(x=x[c], wfcT=d["wfcT"], wprojT=d["wprojT"]))
            elif step == "a0":
                maps.append(dict(x=x[c], qwT=d["l0_qwT"], krT=d["l0_krT"],
                                 dnT=d["l0_dnT"], upkT=d["l0_upkT"],
                                 upvT=d["l0_upvT"], owT=d["l0_owT"],
                                 cos_q=d["cos_q"], sin_q=d["sin_q"],
                                 cos_k=d["cos_k"], sin_k=d["sin_k"],
                                 permT=d["permT"], H4=d["H4"], Hq=d["Hq"],
                                 IND=d["IND"], tri_mul=d["tri_mul"]))
            elif step == "f1":
                maps.append(dict(x=x[c], guT=d["guT"], downRB=d["downRB"]))
            else:
                maps.append(dict(x=x[c], qwT=d["l1_qwT"], kwT=d["l1_kwT"],
                                 vwT=d["l1_vwT"], qb=d["l1_qb"], kb=d["l1_kb"],
                                 vb=d["l1_vb"], owT=d["l1_owT"], ob=d["l1_ob"],
                                 iota8neg=d["iota8neg"], jcols=d["jcols"],
                                 tri_add=d["tri_add"], e8T=d["e8T"], E8=d["E8"]))
        x = run(nc, maps)

    out = np.zeros((4, 4096, 2048), np.float32)
    for c in cores:
        b, h = c // 2, c % 2
        out[b, h * T:(h + 1) * T, :] = np.asarray(x[c]).T
    return out


# version 08
# speedup vs baseline: 1.1277x; 1.0128x over previous
import contextlib
import math
import numpy as np
import ml_dtypes
import concourse.bass as bass
import concourse.mybir as mybir
from concourse.bass_utils import run_bass_kernel_spmd

F32 = mybir.dt.float32
BF16 = mybir.dt.bfloat16
AF = mybir.ActivationFunctionType
ALU = mybir.AluOpType
AX = mybir.AxisListType

D = 2048
T = 2048
TFULL = 4096
NCH = D // 128
PAIRS = [[0, 1], [2, 3], [4, 5], [6, 7]]


class _Op:
    __slots__ = ("idx", "eng", "fn", "deps", "is_dma", "tag", "marked",
                 "sem", "val", "dma_count", "hard", "is_async")

    def __init__(self, idx, eng, fn, deps, is_dma=False, tag=None):
        self.hard = set()
        self.is_async = False
        self.idx = idx
        self.eng = eng
        self.fn = fn
        self.deps = list(deps)
        self.is_dma = is_dma
        self.tag = tag
        self.marked = False
        self.sem = None
        self.val = None


class Plan:
    ENGINES = ("tensor", "vector", "scalar", "gpsimd", "sync")

    DMA_POOL = 80

    def __init__(self, nc, stack):
        self.nc = nc
        self.stack = stack
        self.ops = []
        self.eng_sem = {e: stack.enter_context(nc.semaphore(f"S_{e}"))
                        for e in self.ENGINES}
        self.dma_pool = []
        self.dma_sems = {}
        self.sem_counts = {}
        self._fence = {e: None for e in self.ENGINES}

    def fence(self, deps):
        deps = [d for d in deps if d is not None]
        for e in self.ENGINES:
            cur = self._fence[e] or []
            self._fence[e] = cur + deps

    def _take_fence(self, eng):
        f = self._fence[eng]
        self._fence[eng] = None
        return f or []

    def op(self, eng, fn, deps=(), hard=(), is_async=False):
        deps = list(deps) + list(hard) + self._take_fence(eng)
        o = _Op(len(self.ops), eng, fn, deps)
        o.hard = set(id(h) for h in hard)
        o.is_async = is_async
        self.ops.append(o)
        for d in deps:
            d.marked = True
        return o

    def dma(self, eng, tag, fn, deps=()):
        deps = list(deps) + self._take_fence(eng)
        o = _Op(len(self.ops), eng, fn, deps, is_dma=True, tag=tag)
        self.ops.append(o)
        for d in deps:
            d.marked = True
        return o

    def barrier_all(self, deps):
        return deps

    def emit(self):
        eng_count = {e: 0 for e in self.ENGINES}
        pool_by_eng = {}
        tagcount_by_eng = {}
        PER_ENG = {"sync": 40, "gpsimd": 30, "scalar": 8, "vector": 8, "tensor": 8}
        for o in self.ops:
            if o.is_dma:
                key = (o.eng, o.tag)
                if key not in self.dma_sems:
                    pool = pool_by_eng.setdefault(o.eng, [])
                    n = tagcount_by_eng.get(o.eng, 0)
                    tagcount_by_eng[o.eng] = n + 1
                    idx = n % PER_ENG[o.eng]
                    while len(pool) <= idx:
                        pool.append(self.stack.enter_context(
                            self.nc.semaphore(f"Dp_{o.eng}{len(pool)}")))
                    self.dma_sems[key] = pool[idx]
                h = self.dma_sems[key]
                cnt = self.sem_counts.get(id(h), 0) + 16
                self.sem_counts[id(h)] = cnt
                o.sem, o.val = h, cnt
            elif o.marked:
                eng_count[o.eng] += 1
                o.sem, o.val = self.eng_sem[o.eng], eng_count[o.eng]

        queues = {e: [] for e in self.ENGINES}
        waited = {e: {} for e in self.ENGINES}

        for o in self.ops:
            q = queues[o.eng]
            w = waited[o.eng]
            for d in o.deps:
                if (d.eng == o.eng and not d.is_dma and not d.is_async
                        and id(d) not in o.hard
                        and o.eng in ("tensor", "gpsimd", "sync")):
                    continue
                assert d.sem is not None, "dep without completion signal"
                key = id(d.sem)
                if w.get(key, -1) >= d.val:
                    continue
                w[key] = d.val
                q.append(lambda e, s=d.sem, v=d.val: e.wait_ge(s, v))
            if o.is_dma:
                q.append(lambda e, o=o: o.fn(e).then_inc(o.sem, 16))
            elif o.marked:
                q.append(lambda e, o=o: o.fn(e).then_inc(o.sem, 1))
            else:
                q.append(lambda e, o=o: o.fn(e))

        nc = self.nc
        with nc.Block() as block:
            @block.tensor
            def _(t):
                for fn in queues["tensor"]:
                    fn(t)

            @block.vector
            def _(v):
                for fn in queues["vector"]:
                    fn(v)

            @block.scalar
            def _(s):
                for fn in queues["scalar"]:
                    fn(s)

            @block.gpsimd
            def _(g):
                for fn in queues["gpsimd"]:
                    fn(g)

            @block.sync
            def _(sy):
                for fn in queues["sync"]:
                    fn(sy)


class Ctx:

    def __init__(self, nc, stack):
        self.nc = nc
        self.stack = stack
        self.plan = Plan(nc, stack)
        self.ps = [stack.enter_context(nc.psum_tensor(f"ps{i}", [128, 512], F32))
                   for i in range(8)]

    def sb(self, name, shape, dtype):
        return self.stack.enter_context(self.nc.sbuf_tensor(name, shape, dtype))

    @contextlib.contextmanager
    def scope(self):
        old = self.stack
        try:
            with contextlib.ExitStack() as es:
                self.stack = es
                yield
        finally:
            self.stack = old



def _panels(wT, pw=128):
    K, O = wT.shape
    return np.ascontiguousarray(wT.reshape(K, O // pw, pw).transpose(1, 0, 2))


def _rowblocks(wT, rb=128):
    K, O = wT.shape
    return np.ascontiguousarray(wT.reshape(K // rb, rb, O))


def _sigma_perm():
    return np.array(list(range(0, 16)) + list(range(32, 48)) +
                    list(range(16, 32)) + list(range(48, 64)))


def _rope_tables():
    HD = 64
    inv_freq = 1.0 / (10000.0 ** (np.arange(0, HD, 2) / HD))
    pos = np.arange(TFULL, dtype=np.float64)
    ang = pos[None, :] * inv_freq[:, None]
    cos, sin = np.cos(ang), np.sin(ang)
    cos_q = np.concatenate([cos[0:16], cos[0:16], cos[16:32], cos[16:32]], 0)
    sin_q = np.concatenate([sin[0:16], sin[0:16], sin[16:32], sin[16:32]], 0)
    cos_k = np.concatenate([cos[0:16], cos[0:16]], 0)
    sin_k = np.concatenate([sin[0:16], sin[0:16]], 0)
    return (np.tile(cos_q, (2, 1)).astype(np.float32),
            np.tile(sin_q, (2, 1)).astype(np.float32),
            np.tile(cos_k, (4, 1)).astype(np.float32),
            np.tile(sin_k, (4, 1)).astype(np.float32))


def _perm_matrix():
    P = np.zeros((128, 128), np.float32)
    for blk in range(4):
        b = 32 * blk
        for p in range(16):
            P[b + p, b + p + 16] = -1.0
            P[b + p + 16, b + p] = 1.0
    return P


def prep_common_consts():
    c = {}
    mul_masks, add_masks = [], []
    for m in range(4):
        k = np.arange(128)[:, None] + 128 * m
        q = np.arange(512)[None, :]
        keep = (k <= q)
        mul_masks.append(keep.astype(np.float32))
        add_masks.append(np.where(keep, 0.0, -1e30).astype(np.float32))
    c["tri_mul"] = np.stack(mul_masks)
    c["tri_add"] = np.stack(add_masks)
    cq, sq, ck, sk = _rope_tables()
    c["cos_q"], c["sin_q"], c["cos_k"], c["sin_k"] = cq, sq, ck, sk
    c["permT"] = np.ascontiguousarray(_perm_matrix().T)
    Hq = np.zeros((8, 128, 16), np.float32)
    for pn in range(8):
        Hq[pn, 0:64, 2 * pn] = 1.0
        Hq[pn, 64:128, 2 * pn + 1] = 1.0
    c["Hq"] = Hq
    H4 = np.zeros((128, 4), np.float32)
    for g in range(4):
        H4[32 * g:32 * (g + 1), g] = 1.0
    c["H4"] = H4
    IND = np.zeros((4, 16), np.float32)
    for qh in range(16):
        IND[qh // 4, qh] = -1.0
    c["IND"] = IND
    SCALE1 = 1.0 / math.sqrt(128.0)
    c["iota8neg"] = np.tile(-np.arange(TFULL, dtype=np.float64)[None, :] / SCALE1,
                            (8, 1)).astype(np.float32)
    jc = np.arange(128, dtype=np.float32)[:, None] + 128.0 * np.arange(32)[None, :]
    c["jcols"] = np.ascontiguousarray(jc)
    e8T = np.zeros((8, 8, 128), np.float32)
    for hh in range(8):
        e8T[hh, hh, :] = 1.0
    c["e8T"] = e8T
    E8 = np.zeros((8, 128, 8), np.float32)
    for pn in range(8):
        E8[pn, :, pn] = 1.0
    c["E8"] = E8
    return c


def prep_core_inputs(inputs, core):
    b, h = core // 2, core % 2
    d = {}
    x = np.asarray(inputs["L0_attn_hidden_states"])
    d["x"] = np.ascontiguousarray(x[b, h * T:(h + 1) * T, :].T)

    d["wfcT"] = _panels(np.ascontiguousarray(np.asarray(inputs["L0_ffn_w_fc"]).T))
    d["wprojT"] = _panels(np.ascontiguousarray(np.asarray(inputs["L0_ffn_w_proj"]).T))

    sig = _sigma_perm()
    qw = np.asarray(inputs["L0_attn_q_w"])
    qw_my = np.concatenate([qw[qh * 64 + sig, :] for qh in range(16 * h, 16 * (h + 1))], 0)
    d["l0_qwT"] = _panels(np.ascontiguousarray(qw_my.T))
    kr = np.asarray(inputs["L0_attn_kr_w"])
    d["l0_krT"] = _panels(np.ascontiguousarray(kr[128 * h:128 * (h + 1), :].T))
    d["l0_dnT"] = _panels(np.ascontiguousarray(np.asarray(inputs["L0_attn_down_w"]).T))
    upk = np.asarray(inputs["L0_attn_upk_w"])
    d["l0_upkT"] = np.ascontiguousarray(upk[128 * h:128 * (h + 1), :].T)
    upv = np.asarray(inputs["L0_attn_upv_w"])
    d["l0_upvT"] = np.ascontiguousarray(upv[256 * h:256 * (h + 1), :].T)
    ow = np.asarray(inputs["L0_attn_o_w"])
    d["l0_owT"] = _panels(np.ascontiguousarray(ow[:, 1024 * h:1024 * (h + 1)].T))

    for nm, wkey, bkey in [("q", "L1_attn_q_w", "L1_attn_q_b"),
                           ("k", "L1_attn_k_w", "L1_attn_k_b")]:
        w = np.asarray(inputs[wkey])
        d[f"l1_{nm}wT"] = _panels(np.ascontiguousarray(w[1024 * h:1024 * (h + 1), :].T))
        bias = np.asarray(inputs[bkey])
        d[f"l1_{nm}b"] = np.ascontiguousarray(bias[1024 * h:1024 * (h + 1)].reshape(8, 128).T)
    vw = np.asarray(inputs["L1_attn_v_w"])
    d["l1_vwT"] = np.ascontiguousarray(vw[1024 * h:1024 * (h + 1), :].T)
    d["l1_vb"] = np.ascontiguousarray(
        np.asarray(inputs["L1_attn_v_b"])[1024 * h:1024 * (h + 1)][None, :])
    ow1 = np.asarray(inputs["L1_attn_o_w"])
    d["l1_owT"] = _panels(np.ascontiguousarray(ow1[:, 1024 * h:1024 * (h + 1)].T))
    d["l1_ob"] = np.ascontiguousarray(np.asarray(inputs["L1_attn_o_b"]).reshape(16, 128).T)

    d["guT"] = _panels(np.ascontiguousarray(np.asarray(inputs["L1_ffn_w_gate_up"]).T))
    d["downRB"] = _rowblocks(np.ascontiguousarray(np.asarray(inputs["L1_ffn_w_down"]).T))

    d.update({k: np.ascontiguousarray(v) for k, v in prep_common_consts().items()})
    return d



class Shared:

    def __init__(self, ctx):
        p = ctx.plan
        self.ones_f = ctx.sb("c_ones_f", [128, 1], F32)
        self.ones_b = ctx.sb("c_ones_b", [128, 1], BF16)
        self.ones_row_f = ctx.sb("c_onesr_f", [1, 128], F32)
        o1 = p.op("vector", lambda v: v.memset(self.ones_f[:], 1.0))
        o2 = p.op("vector", lambda v: v.memset(self.ones_b[:], 1.0))
        o3 = p.op("vector", lambda v: v.memset(self.ones_row_f[:], 1.0))
        self.ready = [o1, o2, o3]


def emit_ln(ctx, sh, xin_dram, out_sb, tag, compose_inner, eps=1e-5,
            out_dram=None, deps=()):
    p = ctx.plan
    ps = ctx.ps
    TT = 512
    NT = T // TT
    scope = ctx.scope()
    scope.__enter__()
    xtile = [ctx.sb(f"{tag}_xt{i}", [128, NCH * TT], F32) for i in range(2)]
    sq = [ctx.sb(f"{tag}_sq{i}", [128, TT], BF16) for i in range(2)]
    rows = ctx.sb(f"{tag}_rows", [1, 3 * TT], F32)
    srow = ctx.sb(f"{tag}_srow", [1, T], F32)
    msrow = ctx.sb(f"{tag}_msrow", [1, T], F32)
    tmpf = ctx.sb(f"{tag}_tmpf", [128, TT], F32)

    tile_done = []
    prev_apply0 = {}
    mm2_hist = {}
    fix1_hist = {}
    st_ops = []
    for tt in range(NT):
        xb = xtile[tt % 2]
        war = [tile_done[tt - 2]] if tt >= 2 else []
        lds = []
        for c in range(NCH):
            lds.append(p.dma("sync", f"{tag}x{tt % 2}",
                             lambda e, c=c, tt=tt, xb=xb: e.dma_start(
                                 out=xb[:, c * TT:(c + 1) * TT],
                                 in_=xin_dram[c * 128:(c + 1) * 128,
                                              tt * TT:(tt + 1) * TT]),
                             deps=list(deps) + war))
        mm1l = mm2l = None
        for c in range(NCH):
            sqd = [lds[c]]
            if (tt, c - 2) in mm2_hist:
                sqd.append(mm2_hist[(tt, c - 2)])
            elif tt > 0 and c < 2:
                sqd.append(mm2_hist[(tt - 1, NCH - 2 + c)])
            sqo = p.op("scalar", lambda e, c=c, xb=xb, s=sq[c % 2]: e.activation(
                s[:], xb[:, c * TT:(c + 1) * TT], AF.Square), deps=sqd)
            d1 = [lds[c], sh.ready[0]]
            if c == 0 and tt > 0:
                d1.append(fix1_hist[tt - 1])
            mm1l = p.op("tensor", lambda e, c=c, xb=xb: e.matmul(
                ps[0][0:1, :], sh.ones_f[:], xb[:, c * TT:(c + 1) * TT],
                start=(c == 0), stop=(c == NCH - 1)), deps=d1)
            d2 = [sqo, sh.ready[1]]
            if c == 0 and tt > 0:
                d2.append(fix1_hist[tt - 1])
            mm2l = p.op("tensor", lambda e, c=c, s=sq[c % 2]: e.matmul(
                ps[1][0:1, :], sh.ones_b[:], s[:],
                start=(c == 0), stop=(c == NCH - 1)), deps=d2)
            mm2_hist[(tt, c)] = mm2l

        m_ = rows[0:1, 0:TT]
        va_ = rows[0:1, TT:2 * TT]
        tp_ = rows[0:1, 2 * TT:3 * TT]
        fa = p.op("vector", lambda v: (
            v.tensor_scalar(m_, ps[0][0:1, :], 1.0 / D, None, ALU.mult),
            v.tensor_scalar(va_, ps[1][0:1, :], 1.0 / D, None, ALU.mult))[-1],
            deps=[mm1l, mm2l])
        fbb = p.op("vector", lambda v: v.tensor_tensor(tp_, m_, m_, ALU.mult),
                   hard=[fa])
        fcc = p.op("vector", lambda v: v.tensor_tensor(va_, va_, tp_, ALU.subtract),
                   hard=[fbb])
        if compose_inner:
            f1 = p.op("vector", lambda v: v.tensor_scalar(
                va_, va_, 1.0 + 1e-6, 1e-11, ALU.mult, ALU.add), hard=[fcc])
        else:
            f1 = p.op("vector", lambda v: v.tensor_scalar(
                va_, va_, eps, None, ALU.add), hard=[fcc])
        fix1_hist[tt] = f1
        sq1 = p.op("scalar", lambda e: e.activation(
            rows[0:1, 2 * TT:3 * TT], rows[0:1, TT:2 * TT], AF.Sqrt), deps=[f1])
        f2a = p.op("vector", lambda v, tt=tt: v.reciprocal(
            srow[0:1, tt * TT:(tt + 1) * TT], rows[0:1, 2 * TT:3 * TT]), deps=[sq1])
        f2 = p.op("vector", lambda v, tt=tt: v.tensor_tensor(
            msrow[0:1, tt * TT:(tt + 1) * TT], rows[0:1, 0:TT],
            srow[0:1, tt * TT:(tt + 1) * TT], ALU.mult), hard=[f2a])

        bwar = [tile_done[tt - 1]] if tt > 0 else []
        bs = p.op("tensor", lambda e, tt=tt: e.matmul(
            ps[2][:, :], sh.ones_row_f[:], srow[0:1, tt * TT:(tt + 1) * TT],
            start=True, stop=True), deps=[f2, sh.ready[2]] + bwar)
        bms = p.op("tensor", lambda e, tt=tt: e.matmul(
            ps[3][:, :], sh.ones_row_f[:], msrow[0:1, tt * TT:(tt + 1) * TT],
            start=True, stop=True), deps=[f2] + bwar)

        last_apply = None
        for c in range(NCH):
            def apply(v, c=c, tt=tt, xb=xb):
                v.tensor_tensor(tmpf[:], xb[:, c * TT:(c + 1) * TT], ps[2][:, :], ALU.mult)
                return v.tensor_tensor(
                    out_sb[:, c * T + tt * TT: c * T + (tt + 1) * TT],
                    tmpf[:], ps[3][:, :], ALU.subtract)
            ap_op = p.op("vector", apply, deps=([bs, bms] if c == 0 else []))
            if c == 0:
                prev_apply0[tt] = ap_op
            if out_dram is not None:
                st_ops.append(p.dma("sync", f"{tag}st",
                                    lambda e, c=c, tt=tt: e.dma_start(
                                        out=out_dram[c * 128:(c + 1) * 128,
                                                     tt * TT:(tt + 1) * TT],
                                        in_=out_sb[:, c * T + tt * TT: c * T + (tt + 1) * TT]),
                                    deps=[ap_op]))
            last_apply = ap_op
        tile_done.append(last_apply)
    scope.__exit__(None, None, None)
    tail = [tile_done[-1]] + st_ops[-2:]
    return dict(tile_done=tile_done, st_ops=st_ops, tail=tail)


def emit_l0ffn_prep(ctx, sh, wfcT, wprojT, tprojRB, tag):
    p = ctx.plan
    ps = ctx.ps
    thr_fc = ctx.sb(f"{tag}_thrfc", [128, 1], F32)
    thr_pj = ctx.sb(f"{tag}_thrpj", [128, 1], F32)
    gfc_col = ctx.sb(f"{tag}_gfccol", [128, 1], F32)
    gph = ctx.sb(f"{tag}_gph", [1, 1], F32)
    rs_row = ctx.sb(f"{tag}_rsrow", [1, D], F32)
    scope = ctx.scope()
    scope.__enter__()
    gbuf = [ctx.sb(f"{tag}_gb{i}", [128, 4096], F32) for i in range(2)]
    gcols = ctx.sb(f"{tag}_gcols", [128, 100], F32)
    rows = ctx.sb(f"{tag}_grows", [1, 16], F32)

    segs = []
    for i in range(64):
        segs.append((lambda i=i: wfcT[i].rearrange("(c p) o -> p c o", p=128), 16, "fc"))
    for i in range(16):
        segs.append((lambda i=i: wprojT[i].rearrange("(c p) o -> p c o", p=128)[:, 0:32, :], 32, "pj"))
        segs.append((lambda i=i: wprojT[i].rearrange("(c p) o -> p c o", p=128)[:, 32:64, :], 32, "pj"))
    red_ops = []
    fc_slots, pj_slots = [], []
    for si, (apf, nch, kind) in enumerate(segs):
        b = gbuf[si % 2]
        war = [red_ops[si - 2]] if si >= 2 else []
        ld = p.dma("sync", f"{tag}g{si % 2}",
                   lambda e, apf=apf, b=b, nch=nch: e.dma_start(
                       out=b.ap().rearrange("p (c o) -> p c o", o=128)[:, 0:nch, :],
                       in_=apf()),
                   deps=war)
        ro = p.op("vector", lambda v, b=b, nch=nch, sl=si: v.tensor_reduce(
            gcols[:, sl:sl + 1], b[:, 0:nch * 128], AX.X, ALU.add,
            apply_absolute_value=True), deps=[ld])
        red_ops.append(ro)
        (fc_slots if kind == "fc" else pj_slots).append(si)

    def red2(v):
        v.tensor_reduce(gcols[:, 96:97], gcols[:, fc_slots[0]:fc_slots[-1] + 1],
                        AX.X, ALU.add)
        return v.tensor_reduce(gcols[:, 97:98], gcols[:, pj_slots[0]:pj_slots[-1] + 1],
                               AX.X, ALU.add)
    r2 = p.op("vector", red2, hard=[red_ops[-1], red_ops[-2]])
    tot_fc = p.op("tensor", lambda e: e.matmul(
        ps[0][0:1, 0:1], gcols[:, 96:97], sh.ones_f[:], start=True, stop=True),
        deps=[r2, sh.ready[0]])
    tot_pj = p.op("tensor", lambda e: e.matmul(
        ps[4][0:1, 0:1], gcols[:, 97:98], sh.ones_f[:], start=True, stop=True),
        deps=[r2])

    gfa = p.op("vector", lambda v: (
        v.tensor_scalar(rows[0:1, 0:1], ps[0][0:1, 0:1], 1.0 / (8192.0 * 2048.0), None, ALU.mult),
        v.tensor_scalar(rows[0:1, 2:3], ps[4][0:1, 0:1], 1.0 / (8192.0 * 2048.0), None, ALU.mult))[-1],
        deps=[tot_fc, tot_pj])
    gf = p.op("vector", lambda v: (
        v.tensor_scalar(rows[0:1, 1:2], rows[0:1, 0:1], 1e-6, 0.5, ALU.add, ALU.mult),
        v.tensor_scalar(rows[0:1, 3:4], rows[0:1, 2:3], 1e-6, 0.5, ALU.add, ALU.mult),
        v.tensor_scalar(gph[0:1, :], rows[0:1, 2:3], 0.5, None, ALU.mult))[-1],
        hard=[gfa])
    bc = p.op("tensor", lambda e: e.matmul(
        ps[1][:, 0:4], sh.ones_row_f[:], rows[0:1, 0:4], start=True, stop=True),
        deps=[gf, sh.ready[2]])

    def copycols(v):
        v.tensor_copy(gfc_col[:], ps[1][:, 0:1])
        v.tensor_copy(thr_fc[:], ps[1][:, 1:2])
        return v.tensor_copy(thr_pj[:], ps[1][:, 3:4])
    cc = p.op("vector", copycols, deps=[bc])

    pfp = [ctx.sb(f"{tag}_pf{i}", [128, 4096], F32) for i in range(2)]
    ptr = [ctx.sb(f"{tag}_pt{i}", [128, 4096], BF16) for i in range(2)]
    psg = [ctx.sb(f"{tag}_psg{i}", [128, 4096], BF16) for i in range(2)]
    prev = {}
    rs_evacs = []
    last_spill = {0: None, 1: None}
    seg_idx = 0
    for pi in range(16):
        last_mm = None
        for hh in range(2):
            bi = seg_idx % 2
            pm2 = prev.get(seg_idx - 2)
            ld = p.dma("sync", f"{tag}p{bi}",
                       lambda e, pi=pi, hh=hh, bi=bi: e.dma_start(
                           out=pfp[bi].ap().rearrange("p (c o) -> p c o", o=128),
                           in_=wprojT[pi].rearrange("(c p) o -> p c o", p=128)[:, hh * 32:(hh + 1) * 32, :]),
                       deps=[pm2["tn"]] if pm2 else [])
            mkp = p.op("vector", lambda v, bi=bi: v.tensor_scalar(
                ptr[bi][:], pfp[bi][:], thr_pj[:], 0.0, ALU.subtract, ALU.is_gt),
                deps=[ld, cc] + ([pm2["last_t"], pm2["last_sp"]] if pm2 else []))
            mkn = p.op("vector", lambda v, bi=bi: v.tensor_scalar(
                psg[bi][:], pfp[bi][:], thr_pj[:], 0.0, ALU.add, ALU.is_lt),
                deps=[ld] + ([pm2["tn"]] if pm2 else []))
            tn = p.op("vector", lambda v, bi=bi: v.tensor_tensor(
                ptr[bi][:], ptr[bi][:], psg[bi][:], ALU.subtract), deps=[])
            sp = None
            for c in range(32):
                kc = hh * 32 + c
                mmd = []
                if c == 0:
                    mmd = [tn, sh.ready[1]]
                    if kc == 0 and rs_evacs:
                        mmd.append(rs_evacs[-1])
                last_mm = p.op("tensor", lambda e, bi=bi, c=c, kc=kc: e.matmul(
                    ps[2][0:1, 0:128], sh.ones_b[:], ptr[bi][:, c * 128:(c + 1) * 128],
                    start=(kc == 0), stop=(kc == 63)), deps=mmd)
                sp = p.dma("gpsimd", f"{tag}sp{bi}",
                           lambda e, bi=bi, c=c, pi=pi, kc=kc: e.dma_start(
                               out=tprojRB[kc, :, pi * 128:(pi + 1) * 128],
                               in_=ptr[bi][:, c * 128:(c + 1) * 128]),
                           deps=[tn])
            prev[seg_idx] = dict(tn=tn, last_t=last_mm, last_sp=sp)
            last_spill[bi] = sp
            seg_idx += 1
        ev = p.op("vector", lambda v, pi=pi: v.tensor_copy(
            rs_row[0:1, pi * 128:(pi + 1) * 128], ps[2][0:1, 0:128]),
            deps=[last_mm])
        rs_evacs.append(ev)

    scope.__exit__(None, None, None)
    tail = [cc, rs_evacs[-1], last_spill[0], last_spill[1],
            prev[30]["last_t"], prev[31]["last_t"],
            prev[30]["tn"], prev[31]["tn"]]
    return dict(thr_fc=thr_fc, thr_pj=thr_pj, gfc_col=gfc_col, gph=gph,
                rs_row=rs_row, cc=cc, rs_evacs=rs_evacs,
                spill_done=[last_spill[0], last_spill[1]], tail=tail)


def emit_l0ffn_main(ctx, sh, prep, xn_sb, xn_ready, xin, xout, wfcT, tprojRB,
                    h_dram, tag):
    p = ctx.plan
    ps = ctx.ps
    TT = 512
    NT = T // TT

    NBW = 3
    wfp = [ctx.sb(f"{tag}_wf{i}", [128, NCH * 128], F32) for i in range(NBW)]
    ftr = [ctx.sb(f"{tag}_ft{i}", [128, NCH * 128], BF16) for i in range(NBW)]
    fsg = [ctx.sb(f"{tag}_fs{i}", [128, NCH * 128], BF16) for i in range(NBW)]
    hst = [ctx.sb(f"{tag}_hs{i}", [128, TT], BF16) for i in range(2)]
    prevA = {}
    last_st_h = {}
    last_gelu_bank = {4: None, 5: None}
    for oc in range(64):
        bi = oc % NBW
        pm2 = prevA.get(oc - NBW)
        ld = p.dma("sync", f"{tag}wf{bi}",
                   lambda e, oc=oc, bi=bi: e.dma_start(
                       out=wfp[bi].ap().rearrange("p (c o) -> p c o", o=128),
                       in_=wfcT[oc].rearrange("(c p) o -> p c o", p=128)),
                   deps=[pm2["tn"]] if pm2 else [])
        mkp = p.op("vector", lambda v, bi=bi: v.tensor_scalar(
            ftr[bi][:], wfp[bi][:], prep["thr_fc"][:], 0.0, ALU.subtract, ALU.is_gt),
            deps=[ld, prep["cc"]] + ([pm2["last_mm"]] if pm2 else []))
        mkn = p.op("vector", lambda v, bi=bi: v.tensor_scalar(
            fsg[bi][:], wfp[bi][:], prep["thr_fc"][:], 0.0, ALU.add, ALU.is_lt),
            deps=[ld] + ([pm2["tn"]] if pm2 else []))
        tn = p.op("vector", lambda v, bi=bi: v.tensor_tensor(
            ftr[bi][:], ftr[bi][:], fsg[bi][:], ALU.subtract), deps=[])
        last_mm_all = None
        for tt in range(NT):
            pb = 4 + tt % 2
            mmd0 = [tn, xn_ready[tt]]
            if last_gelu_bank[pb] is not None:
                mmd0.append(last_gelu_bank[pb])
            last_mm = None
            for c in range(NCH):
                last_mm = p.op("tensor", lambda e, c=c, tt=tt, bi=bi, pb=pb: e.matmul(
                    ps[pb][:, :], ftr[bi][:, c * 128:(c + 1) * 128],
                    xn_sb[:, c * T + tt * TT: c * T + (tt + 1) * TT],
                    start=(c == 0), stop=(c == NCH - 1)),
                    deps=(mmd0 if c == 0 else []))
            gl_d = [last_mm, prep["cc"]]
            if (oc - 1, tt) in last_st_h:
                gl_d.append(last_st_h[(oc - 1, tt)])
            gl = p.op("scalar", lambda e, tt=tt, pb=pb: e.activation(
                hst[tt % 2][:], ps[pb][:, :], AF.Gelu, scale=prep["gfc_col"][:]),
                deps=gl_d)
            last_gelu_bank[pb] = gl
            sth = p.dma("sync", f"{tag}h{tt % 2}",
                        lambda e, oc=oc, tt=tt: e.dma_start(
                            out=h_dram[oc * 128:(oc + 1) * 128, tt * TT:(tt + 1) * TT],
                            in_=hst[tt % 2][:]),
                        deps=[gl])
            last_st_h[(oc, tt)] = sth
            last_mm_all = last_mm
        prevA[oc] = dict(last_mm=last_mm_all, tn=tn)
    h_done = [last_st_h[(63, tt)] for tt in range(NT)] + \
             [last_st_h[(62, tt)] for tt in range(NT)]

    NBH = 4
    hb = [ctx.sb(f"{tag}_hb{i}", [128, TT], BF16) for i in range(NBH)]
    hsc = [ctx.sb(f"{tag}_hc{i}", [128, TT], BF16) for i in range(NBH)]
    hsq = [ctx.sb(f"{tag}_hq{i}", [128, TT], BF16) for i in range(NBH)]
    wrb = [ctx.sb(f"{tag}_wr{i}", [128, 1024], BF16) for i in range(NBH)]
    bbf = ctx.sb(f"{tag}_bb", [128, TT], BF16)
    rowsB = ctx.sb(f"{tag}_rB", [1, 4 * TT], F32)
    augrow = ctx.sb(f"{tag}_aug", [1, TT], F32)
    xf = [ctx.sb(f"{tag}_xf{i}", [128, TT], F32) for i in range(2)]
    res = [ctx.sb(f"{tag}_rs{i}", [128, TT], F32) for i in range(2)]

    final_sts = []
    last_bank_reader = {j: None for j in range(8)}
    last_bb = None
    last_fix = None
    hb_reader = {}
    ep_hist = {}
    stx_hist = {}
    wrb_reader = {}
    for tb in range(NT):
        mm1 = mm2 = None
        sq_reader = {}
        for kc in range(64):
            bi = kc % NBH
            lddeps = []
            if tb == 0 and kc < NBH:
                lddeps += h_done
            if hb_reader.get(bi) is not None:
                lddeps += (hb_reader[bi] if isinstance(hb_reader[bi], list)
                           else [hb_reader[bi]])
            ldh = p.dma("sync", f"{tag}hb{bi}",
                        lambda e, kc=kc, tb=tb, bi=bi: e.dma_start(
                            out=hb[bi][:],
                            in_=h_dram[kc * 128:(kc + 1) * 128, tb * TT:(tb + 1) * TT]),
                        deps=lddeps)
            sqd = [ldh]
            if sq_reader.get(bi) is not None:
                sqd.append(sq_reader[bi])
            sqo = p.op("scalar", lambda e, bi=bi: e.activation(
                hsq[bi][:], hb[bi][:], AF.Square), deps=sqd)
            d1 = [ldh, sh.ready[1]]
            d2 = [sqo]
            if kc == 0 and last_fix is not None:
                d1.append(last_fix)
                d2.append(last_fix)
            mm1 = p.op("tensor", lambda e, kc=kc, bi=bi: e.matmul(
                ps[0][0:1, :], sh.ones_b[:], hb[bi][:],
                start=(kc == 0), stop=(kc == 63)), deps=d1)
            mm2 = p.op("tensor", lambda e, kc=kc, bi=bi: e.matmul(
                ps[1][0:1, :], sh.ones_b[:], hsq[bi][:],
                start=(kc == 0), stop=(kc == 63)), deps=d2)
            sq_reader[bi] = mm2
            hb_reader[bi] = [mm1, sqo]

        mB_ = rowsB[0:1, 0:TT]
        vaB_ = rowsB[0:1, TT:2 * TT]
        tpB_ = rowsB[0:1, 2 * TT:3 * TT]
        s2B_ = rowsB[0:1, 3 * TT:4 * TT]
        fba = p.op("vector", lambda v: (
            v.tensor_scalar(mB_, ps[0][0:1, :], 1.0 / 8192.0, None, ALU.mult),
            v.tensor_scalar(vaB_, ps[1][0:1, :], 1.0 / 8192.0, None, ALU.mult))[-1],
            deps=[mm1, mm2])
        fbb = p.op("vector", lambda v: v.tensor_tensor(tpB_, mB_, mB_, ALU.mult),
                   hard=[fba])
        fbc = p.op("vector", lambda v: v.tensor_tensor(vaB_, vaB_, tpB_, ALU.subtract),
                   hard=[fbb])
        fb = p.op("vector", lambda v: v.tensor_scalar(vaB_, vaB_, 1e-6, None, ALU.add),
                  hard=[fbc])
        sqB = p.op("scalar", lambda e: e.activation(
            tpB_, vaB_, AF.Sqrt), deps=[fb])
        f2b1 = p.op("vector", lambda v: v.reciprocal(s2B_, tpB_), deps=[sqB])
        f2b2 = p.op("vector", lambda v: v.tensor_scalar(
            s2B_, s2B_, prep["gph"][0:1, 0:1], None, ALU.mult), hard=[f2b1])
        f2b3 = p.op("vector", lambda v: v.tensor_tensor(
            augrow[0:1, :], s2B_, mB_, ALU.mult), hard=[f2b2])
        f2b = p.op("vector", lambda v: v.tensor_scalar(
            augrow[0:1, :], augrow[0:1, :], -1.0, None, ALU.mult), hard=[f2b3])
        last_fix = f2b
        bcb_d = [f2b, sh.ready[2]]
        if last_bb is not None:
            bcb_d.append(last_bb)
        if tb == 0:
            bcb_d += prep["rs_evacs"][-1:]
        bcb = p.op("tensor", lambda e: e.matmul(
            ps[2][:, :], sh.ones_row_f[:], rowsB[0:1, 3 * TT:4 * TT],
            start=True, stop=True), deps=bcb_d)
        bbo = p.op("vector", lambda v: v.tensor_copy(bbf[:], ps[2][:, :]), deps=[bcb])
        last_bb = bbo

        for og in range(2):
            scale_ops = {}
            last_grp_mm = None
            for kc in range(64):
                bi = kc % NBH
                lddeps = [f2b] if (og == 0 and kc < NBH) else []
                if hb_reader.get(bi) is not None:
                    lddeps += (hb_reader[bi] if isinstance(hb_reader[bi], list)
                               else [hb_reader[bi]])
                ldh = p.dma("sync", f"{tag}hb{bi}",
                            lambda e, kc=kc, tb=tb, bi=bi: e.dma_start(
                                out=hb[bi][:],
                                in_=h_dram[kc * 128:(kc + 1) * 128, tb * TT:(tb + 1) * TT]),
                            deps=lddeps)
                scd = [ldh, bbo]
                if kc >= NBH and scale_ops.get(kc - NBH) is not None:
                    scd.append(scale_ops[kc - NBH]["reader"])
                sc = p.op("vector", lambda v, bi=bi: v.tensor_tensor(
                    hsc[bi][:], hb[bi][:], bbf[:], ALU.mult), deps=scd)
                hb_reader[bi] = sc
                ldw_deps = []
                if wrb_reader.get(bi) is not None:
                    ldw_deps.append(wrb_reader[bi])
                if tb == 0 and og == 0 and kc < NBH:
                    ldw_deps += prep["spill_done"]
                ldw = p.dma("gpsimd", f"{tag}wr{bi}",
                            lambda e, kc=kc, og=og, bi=bi: e.dma_start(
                                out=wrb[bi][:],
                                in_=tprojRB[kc, :, og * 1024:(og + 1) * 1024]),
                            deps=ldw_deps)
                lm = None
                for j in range(8):
                    oc = og * 8 + j
                    if kc == 0:
                        augd = [f2b, sh.ready[2]]
                        if last_bank_reader[j] is not None:
                            augd.append(last_bank_reader[j])
                        if tb == 0:
                            augd += h_done
                        p.op("tensor", lambda e, j=j, oc=oc: e.matmul(
                            ps[j][:, :], prep["rs_row"][0:1, oc * 128:(oc + 1) * 128],
                            augrow[0:1, :], start=True, stop=False), deps=augd)
                    lm = p.op("tensor", lambda e, j=j, kc=kc, bi=bi: e.matmul(
                        ps[j][:, :], wrb[bi][:, j * 128:(j + 1) * 128], hsc[bi][:],
                        start=False, stop=(kc == 63)),
                        deps=([sc, ldw] if j == 0 else []))
                scale_ops[kc] = dict(op=sc, reader=lm)
                wrb_reader[bi] = lm
                last_grp_mm = lm
            for j in range(8):
                oc = og * 8 + j
                bi = j % 2
                ldx_d = [ep_hist[bi]] if ep_hist.get(bi) is not None else []
                ldx = p.dma("sync", f"{tag}xi{bi}",
                            lambda e, oc=oc, tb=tb, bi=bi: e.dma_start(
                                out=xf[bi][:],
                                in_=xin[oc * 128:(oc + 1) * 128, tb * TT:(tb + 1) * TT]),
                            deps=ldx_d)
                ep_d = [last_grp_mm, ldx]
                if stx_hist.get(bi) is not None:
                    ep_d.append(stx_hist[bi])
                ep = p.op("vector", lambda v, j=j, bi=bi: v.tensor_tensor(
                    res[bi][:], ps[j][:, :], xf[bi][:], ALU.add), deps=ep_d)
                ep_hist[bi] = ep
                last_bank_reader[j] = ep
                stx = p.dma("sync", f"{tag}xo{bi}",
                            lambda e, oc=oc, tb=tb, bi=bi: e.dma_start(
                                out=xout[oc * 128:(oc + 1) * 128, tb * TT:(tb + 1) * TT],
                                in_=res[bi][:]),
                            deps=[ep])
                stx_hist[bi] = stx
                final_sts.append(stx)
    return final_sts



def emit_l1ffn(ctx, sh, u_sb, u_ready, xin, xout, guT, downRB, h_dram, tag):
    p = ctx.plan
    ps = ctx.ps
    TT = 512
    NT = T // TT

    NBW = 3
    gp = [ctx.sb(f"{tag}_gp{i}", [128, NCH * 128], BF16) for i in range(NBW)]
    up = [ctx.sb(f"{tag}_up{i}", [128, NCH * 128], BF16) for i in range(NBW)]
    sil = ctx.sb(f"{tag}_sil", [128, TT], F32)
    hst = [ctx.sb(f"{tag}_hs{i}", [128, TT], BF16) for i in range(2)]
    prevA = {}
    last_st_h = {}
    last_sil = None
    last_stt_bank = {4: None, 5: None, 6: None, 7: None}
    for oc in range(64):
        bi = oc % NBW
        pm2 = prevA.get(oc - NBW)
        ldg = p.dma("gpsimd", f"{tag}wg{bi}",
                    lambda e, oc=oc, bi=bi: e.dma_start(
                        out=gp[bi].ap().rearrange("p (c o) -> p c o", o=128),
                        in_=guT[oc].rearrange("(c p) o -> p c o", p=128)),
                    deps=[pm2["last_g"]] if pm2 else [])
        ldu = p.dma("gpsimd", f"{tag}wu{bi}",
                    lambda e, oc=oc, bi=bi: e.dma_start(
                        out=up[bi].ap().rearrange("p (c o) -> p c o", o=128),
                        in_=guT[64 + oc].rearrange("(c p) o -> p c o", p=128)),
                    deps=[pm2["last_u"]] if pm2 else [])
        lg = lu = None
        for tt in range(NT):
            pbg = 4 + tt % 2
            pbu = 6 + tt % 2
            mmdg = [ldg, u_ready[tt]]
            if last_stt_bank[pbg] is not None:
                mmdg.append(last_stt_bank[pbg])
            mmdu = [ldu, u_ready[tt]]
            if last_stt_bank[pbu] is not None:
                mmdu.append(last_stt_bank[pbu])
            for c in range(NCH):
                lg = p.op("tensor", lambda e, c=c, tt=tt, bi=bi, pbg=pbg: e.matmul(
                    ps[pbg][:, :], gp[bi][:, c * 128:(c + 1) * 128],
                    u_sb[:, c * T + tt * TT: c * T + (tt + 1) * TT],
                    start=(c == 0), stop=(c == NCH - 1)),
                    deps=(mmdg if c == 0 else []))
            for c in range(NCH):
                lu = p.op("tensor", lambda e, c=c, tt=tt, bi=bi, pbu=pbu: e.matmul(
                    ps[pbu][:, :], up[bi][:, c * 128:(c + 1) * 128],
                    u_sb[:, c * T + tt * TT: c * T + (tt + 1) * TT],
                    start=(c == 0), stop=(c == NCH - 1)),
                    deps=(mmdu if c == 0 else []))
            sl_d = [lg]
            if last_sil is not None:
                sl_d.append(last_sil)
            sl = p.op("scalar", lambda e, pbg=pbg: e.activation(
                sil[:], ps[pbg][:, :], AF.Silu), deps=sl_d)
            stt_d = [sl, lu]
            if (oc - 1, tt) in last_st_h:
                stt_d.append(last_st_h[(oc - 1, tt)])
            so = p.op("vector", lambda v, tt=tt, pbu=pbu: v.scalar_tensor_tensor(
                hst[tt % 2][:], sil[:], 0.5, ps[pbu][:, :], ALU.mult, ALU.mult),
                deps=stt_d)
            last_sil = so
            last_stt_bank[pbg] = so
            last_stt_bank[pbu] = so
            sth = p.dma("sync", f"{tag}h{tt % 2}",
                        lambda e, oc=oc, tt=tt: e.dma_start(
                            out=h_dram[oc * 128:(oc + 1) * 128, tt * TT:(tt + 1) * TT],
                            in_=hst[tt % 2][:]),
                        deps=[so])
            last_st_h[(oc, tt)] = sth
        prevA[oc] = dict(last_g=lg, last_u=lu)
    h_done = [last_st_h[(63, tt)] for tt in range(NT)] + \
             [last_st_h[(62, tt)] for tt in range(NT)]

    NBH = 4
    hb = [ctx.sb(f"{tag}_hb{i}", [128, TT], BF16) for i in range(NBH)]
    wrb = [ctx.sb(f"{tag}_wr{i}", [128, 1024], BF16) for i in range(NBH)]
    xf = [ctx.sb(f"{tag}_xf{i}", [128, TT], F32) for i in range(2)]
    res = [ctx.sb(f"{tag}_rs{i}", [128, TT], F32) for i in range(2)]

    final_sts = []
    last_bank_reader = {j: None for j in range(8)}
    hb_reader = {}
    wrb_reader = {}
    ep_hist = {}
    stx_hist = {}
    for tb in range(NT):
        for og in range(2):
            last_grp_mm = None
            for kc in range(64):
                bi = kc % NBH
                lddeps = list(h_done) if (tb == 0 and og == 0 and kc < NBH) else []
                if hb_reader.get(bi) is not None:
                    lddeps.append(hb_reader[bi])
                ldh = p.dma("sync", f"{tag}hb{bi}",
                            lambda e, kc=kc, tb=tb, bi=bi: e.dma_start(
                                out=hb[bi][:],
                                in_=h_dram[kc * 128:(kc + 1) * 128, tb * TT:(tb + 1) * TT]),
                            deps=lddeps)
                ldw_deps = []
                if wrb_reader.get(bi) is not None:
                    ldw_deps.append(wrb_reader[bi])
                ldw = p.dma("gpsimd", f"{tag}wr{bi}",
                            lambda e, kc=kc, og=og, bi=bi: e.dma_start(
                                out=wrb[bi][:],
                                in_=downRB[kc, :, og * 1024:(og + 1) * 1024]),
                            deps=ldw_deps)
                lm = None
                for j in range(8):
                    mmd = []
                    if kc == 0:
                        mmd = [ldh, ldw]
                        if last_bank_reader[j] is not None:
                            mmd.append(last_bank_reader[j])
                    elif j == 0:
                        mmd = [ldh, ldw]
                    lm = p.op("tensor", lambda e, j=j, kc=kc, bi=bi: e.matmul(
                        ps[j][:, :], wrb[bi][:, j * 128:(j + 1) * 128], hb[bi][:],
                        start=(kc == 0), stop=(kc == 63)), deps=mmd)
                hb_reader[bi] = lm
                wrb_reader[bi] = lm
                last_grp_mm = lm
            for j in range(8):
                oc = og * 8 + j
                bi = j % 2
                ldx_d = [ep_hist[bi]] if ep_hist.get(bi) is not None else []
                ldx = p.dma("sync", f"{tag}xi{bi}",
                            lambda e, oc=oc, tb=tb, bi=bi: e.dma_start(
                                out=xf[bi][:],
                                in_=xin[oc * 128:(oc + 1) * 128, tb * TT:(tb + 1) * TT]),
                            deps=ldx_d)
                ep_d = [last_grp_mm, ldx]
                if stx_hist.get(bi) is not None:
                    ep_d.append(stx_hist[bi])
                ep = p.op("vector", lambda v, j=j, bi=bi: v.tensor_tensor(
                    res[bi][:], ps[j][:, :], xf[bi][:], ALU.add), deps=ep_d)
                ep_hist[bi] = ep
                last_bank_reader[j] = ep
                stx = p.dma("sync", f"{tag}xo{bi}",
                            lambda e, oc=oc, tb=tb, bi=bi: e.dma_start(
                                out=xout[oc * 128:(oc + 1) * 128, tb * TT:(tb + 1) * TT],
                                in_=res[bi][:]),
                            deps=[ep])
                stx_hist[bi] = stx
                final_sts.append(stx)
    return final_sts


def build_l1ffn_launch():
    nc = bass.Bass("TRN2", target_bir_lowering=False)
    stack = contextlib.ExitStack()
    xin = nc.dram_tensor("x", [D, T], F32, kind="ExternalInput").ap()
    guT = nc.dram_tensor("guT", [128, 2048, 128], F32, kind="ExternalInput").ap()
    downRB = nc.dram_tensor("downRB", [64, 128, 2048], F32, kind="ExternalInput").ap()
    xout = nc.dram_tensor("xo", [D, T], F32, kind="ExternalOutput").ap()
    h_dram = nc.dram_tensor("h_scr", [8192, T], BF16, kind="Internal").ap()

    ctx = Ctx(nc, stack)
    sh = Shared(ctx)
    u_sb = ctx.sb("u", [128, NCH * T], BF16)
    ln = emit_ln(ctx, sh, xin, u_sb, "ln", compose_inner=False, eps=1e-5)
    ctx.plan.fence(ln["tail"])
    sts = emit_l1ffn(ctx, sh, u_sb, ln["tile_done"], xin, xout, guT, downRB,
                     h_dram, "sw")
    p = ctx.plan
    p.op("gpsimd", lambda g: g.nop(), deps=sts[-16:])
    p.emit()
    stack.close()
    return nc



def emit_ugather(ctx, sh, xin, u_all, cc_u, tag, deps=()):
    p = ctx.plan
    with ctx.scope():
        u_sb = ctx.sb(f"{tag}_usb", [128, NCH * T], BF16)
        ln = emit_ln(ctx, sh, xin, u_sb, f"{tag}ln", compose_inner=False,
                     eps=1e-5, out_dram=cc_u, deps=deps)
    gather = p.op("gpsimd", lambda g: g.collective_compute(
        "AllGather", ALU.bypass, replica_groups=PAIRS,
        ins=[cc_u.opt()], outs=[u_all.opt()]),
        deps=[ln["st_ops"][-1], ln["tile_done"][-1]], is_async=True)
    p.fence(ln["tail"])
    return dict(gather=gather)


def emit_oproj_exchange(ctx, sh, o_dram, o_ready, owT, cc2_in, cc2_out,
                        xin, xout, ob_cols, tag):
    p = ctx.plan
    ps = ctx.ps
    TT = 512
    final_sts = []
    with ctx.scope():
        own = [ctx.sb(f"{tag}_ow{i}", [128, 8 * 128], BF16) for i in range(16)]
        osb = [ctx.sb(f"{tag}_os{i}", [128, 8 * 512], BF16) for i in range(2)]
        cst = [ctx.sb(f"{tag}_cs{i}", [128, TT], F32) for i in range(2)]
        ow_ready = []
        for oc in range(16):
            ow_ready.append(p.dma("gpsimd", f"{tag}owp{oc % 2}",
                lambda e, oc=oc: e.dma_start(
                    out=own[oc].ap().rearrange("p (c o) -> p c o", o=128),
                    in_=owT[oc].rearrange("(c p) o -> p c o", p=128)),
                deps=()))
        osb_reader = {0: None, 1: None}
        cst_reader = {0: None, 1: None}
        evac_bank = {6: None, 7: None}
        cc_sts = []
        for rt in range(8):
            r, tt = rt // 4, rt % 4
            bi = rt % 2
            lds = []
            lddeps = list(o_ready) if rt < 2 else []
            if osb_reader[bi] is not None:
                lddeps.append(osb_reader[bi])
            for c in range(8):
                lds.append(p.dma("sync", f"{tag}ol{bi}",
                    lambda e, c=c, r=r, tt=tt, bi=bi: e.dma_start(
                        out=osb[bi][:, c * 512:(c + 1) * 512],
                        in_=o_dram[c * 128:(c + 1) * 128,
                                   r * 2048 + tt * TT: r * 2048 + (tt + 1) * TT]),
                    deps=lddeps if c < 2 else []))
            last_mm = None
            for oc in range(16):
                pb = 6 + oc % 2
                mmd = [lds[-1], ow_ready[oc]]
                if evac_bank[pb] is not None:
                    mmd.append(evac_bank[pb])
                for c in range(8):
                    last_mm = p.op("tensor", lambda e, oc=oc, c=c, bi=bi, pb=pb: e.matmul(
                        ps[pb][:, :], own[oc][:, c * 128:(c + 1) * 128],
                        osb[bi][:, c * 512:(c + 1) * 512],
                        start=(c == 0), stop=(c == 7)),
                        deps=(mmd if c == 0 else []))
                cb = oc % 2
                evd = [last_mm]
                if cst_reader[cb] is not None:
                    evd.append(cst_reader[cb])
                ev = p.op("vector", lambda v, cb=cb, pb=pb: v.tensor_copy(
                    cst[cb][:], ps[pb][:, :]), deps=evd)
                evac_bank[pb] = ev
                stc = p.dma("sync", f"{tag}cs{cb}",
                    lambda e, oc=oc, r=r, tt=tt, cb=cb: e.dma_start(
                        out=cc2_in[r * 2048 + oc * 128: r * 2048 + (oc + 1) * 128,
                                   tt * TT:(tt + 1) * TT],
                        in_=cst[cb][:]),
                    deps=[ev])
                cst_reader[cb] = stc
                cc_sts.append(stc)
            osb_reader[bi] = last_mm
        rs = p.op("gpsimd", lambda g: g.collective_compute(
            "ReduceScatter", ALU.add, replica_groups=PAIRS,
            ins=[cc2_in.opt()], outs=[cc2_out.opt()]),
            deps=[cc_sts[-1], cc_sts[-2]], is_async=True)
        xf = [ctx.sb(f"{tag}_xf{i}", [128, TT], F32) for i in range(2)]
        ccb = [ctx.sb(f"{tag}_cc{i}", [128, TT], F32) for i in range(2)]
        res = [ctx.sb(f"{tag}_rs{i}", [128, TT], F32) for i in range(2)]
        ep_hist = {0: None, 1: None}
        stx_hist = {0: None, 1: None}
        for oc in range(16):
            for tt in range(4):
                bi = (oc * 4 + tt) % 2
                lddeps = [rs] if oc == 0 and tt < 2 else []
                if ep_hist[bi] is not None:
                    lddeps.append(ep_hist[bi])
                ldx = p.dma("sync", f"{tag}xi{bi}",
                    lambda e, oc=oc, tt=tt, bi=bi: e.dma_start(
                        out=xf[bi][:],
                        in_=xin[oc * 128:(oc + 1) * 128, tt * TT:(tt + 1) * TT]),
                    deps=lddeps)
                ldc = p.dma("sync", f"{tag}ci{bi}",
                    lambda e, oc=oc, tt=tt, bi=bi: e.dma_start(
                        out=ccb[bi][:],
                        in_=cc2_out[oc * 128:(oc + 1) * 128, tt * TT:(tt + 1) * TT]),
                    deps=[rs] if oc == 0 and tt < 2 else [])
                epd = [ldx, ldc]
                if stx_hist[bi] is not None:
                    epd.append(stx_hist[bi])
                if ob_cols is not None:
                    ep = p.op("vector", lambda v, bi=bi, oc=oc: v.scalar_tensor_tensor(
                        res[bi][:], ccb[bi][:], ob_cols[:, oc:oc + 1], xf[bi][:],
                        ALU.add, ALU.add), deps=epd)
                else:
                    ep = p.op("vector", lambda v, bi=bi: v.tensor_tensor(
                        res[bi][:], ccb[bi][:], xf[bi][:], ALU.add), deps=epd)
                ep_hist[bi] = ep
                stx = p.dma("sync", f"{tag}xo{bi}",
                    lambda e, oc=oc, tt=tt, bi=bi: e.dma_start(
                        out=xout[oc * 128:(oc + 1) * 128, tt * TT:(tt + 1) * TT],
                        in_=res[bi][:]),
                    deps=[ep])
                stx_hist[bi] = stx
                final_sts.append(stx)
    tail = final_sts[-2:] + [ep_hist[0], ep_hist[1]]
    p.fence(tail)
    return final_sts



def emit_l0attn(ctx, sh, con, xin, xout, W, scratch, tag):
    p = ctx.plan
    ps = ctx.ps
    TT = 512
    u_all, qp, kp, vdr = scratch["u_all"], scratch["qp"], scratch["kp"], scratch["vdr"]
    gather = scratch["gather_op"]

    o_sts = []
    with ctx.scope():
        qw = [ctx.sb(f"{tag}_qw{i}", [128, NCH * 128], BF16) for i in range(8)]
        krw = ctx.sb(f"{tag}_krw", [128, NCH * 128], BF16)
        dnw = [ctx.sb(f"{tag}_dnw{i}", [128, NCH * 128], BF16) for i in range(2)]
        upkw = ctx.sb(f"{tag}_upkw", [128, 2 * 128], BF16)
        upvw = ctx.sb(f"{tag}_upvw", [128, 2 * 256], BF16)
        wlds = []
        for pn in range(8):
            wlds.append(p.dma("gpsimd", f"{tag}wq{pn % 2}", lambda e, pn=pn: e.dma_start(
                out=qw[pn].ap().rearrange("p (c o) -> p c o", o=128),
                in_=W["qwT"][pn].rearrange("(c p) o -> p c o", p=128)), deps=()))
        wlds.append(p.dma("gpsimd", f"{tag}wk", lambda e: e.dma_start(
            out=krw.ap().rearrange("p (c o) -> p c o", o=128),
            in_=W["krT"][0].rearrange("(c p) o -> p c o", p=128)), deps=()))
        for j in range(2):
            wlds.append(p.dma("gpsimd", f"{tag}wd{j}", lambda e, j=j: e.dma_start(
                out=dnw[j].ap().rearrange("p (c o) -> p c o", o=128),
                in_=W["dnT"][j].rearrange("(c p) o -> p c o", p=128)), deps=()))
        wlds.append(p.dma("gpsimd", f"{tag}wuk", lambda e: e.dma_start(
            out=upkw.ap().rearrange("p (c o) -> p c o", o=128),
            in_=W["upkT"].rearrange("(c p) o -> p c o", p=128)), deps=()))
        wlds.append(p.dma("gpsimd", f"{tag}wuv", lambda e: e.dma_start(
            out=upvw.ap().rearrange("p (c o) -> p c o", o=256),
            in_=W["upvT"].rearrange("(c p) o -> p c o", p=128)), deps=()))
        wld_done = wlds[-6:] + wlds[:2]

        ub = [ctx.sb(f"{tag}_ub{i}", [128, NCH * TT], BF16) for i in range(2)]
        qraw = ctx.sb(f"{tag}_qraw", [128, TT], BF16)
        qsq = ctx.sb(f"{tag}_qsq", [128, TT], BF16)
        rope1 = ctx.sb(f"{tag}_rp1", [128, TT], BF16)
        ropeo = ctx.sb(f"{tag}_rpo", [128, TT], BF16)
        ckv = ctx.sb(f"{tag}_ckv", [128, 2 * TT], BF16)
        vstage = [ctx.sb(f"{tag}_vst{i}", [128, 260], BF16) for i in range(2)]
        knorm = ctx.sb(f"{tag}_knorm", [4, TFULL], F32)
        maxkq = ctx.sb(f"{tag}_maxkq", [16, 1], F32)
        qn_sqrt = ctx.sb(f"{tag}_qns", [16, TT], F32)
        aug16 = ctx.sb(f"{tag}_aug16", [16, TT], BF16)
        onesrow_b = ctx.sb(f"{tag}_onerow", [1, TFULL], BF16)
        rowtmp = ctx.sb(f"{tag}_rowt", [4, 4], F32)

        oro = p.op("vector", lambda v: v.memset(onesrow_b[:], 1.0))
        ones_sts = [p.dma("sync", f"{tag}kon", lambda e, g=g: e.dma_start(
            out=kp[g, 64:65, :], in_=onesrow_b[:]), deps=[oro]) for g in range(4)]

        ub_reader = {0: None, 1: None}
        kr_sts, kc_sts, v_sts = [], [], []
        last_knmm = None
        kn_ev = []
        prev_rope = None
        prev_vst = {0: None, 1: None}
        prev_vev = None
        prev_ckv = None
        prev_kq = {}
        for rt in range(8):
            r, tt = rt // 4, rt % 4
            bi = rt % 2
            seg = slice(r * 2048 + tt * TT, r * 2048 + (tt + 1) * TT)
            lds = []
            for c in range(NCH):
                dd = [gather] if rt < 2 and c < 2 else []
                if ub_reader[bi] is not None and c == 0:
                    dd.append(ub_reader[bi])
                lds.append(p.dma("sync", f"{tag}u{bi}",
                    lambda e, c=c, r=r, tt=tt, bi=bi: e.dma_start(
                        out=ub[bi][:, c * TT:(c + 1) * TT],
                        in_=u_all[r * 2048 + c * 128: r * 2048 + (c + 1) * 128,
                                  tt * TT:(tt + 1) * TT]),
                    deps=dd))
            mmd = [lds[-1]] + (wld_done if rt == 0 else [])
            if prev_rope is not None:
                mmd.append(prev_rope)
            mm = None
            for c in range(NCH):
                mm = p.op("tensor", lambda e, c=c, bi=bi: e.matmul(
                    ps[0][:, :], krw[:, c * 128:(c + 1) * 128],
                    ub[bi][:, c * TT:(c + 1) * TT],
                    start=(c == 0), stop=(c == NCH - 1)), deps=(mmd if c == 0 else []))
            ev = p.op("vector", lambda v: v.tensor_copy(qraw[:], ps[0][:, :]),
                      deps=[mm, prev_kq.get("krperm")] if prev_kq.get("krperm") else [mm])
            pm = p.op("tensor", lambda e: e.matmul(
                ps[1][:, :], con["permT"][:], qraw[:], start=True, stop=True),
                deps=[ev] + ([prev_kq["krrope2"]] if "krrope2" in prev_kq else []))
            prev_kq["krperm"] = pm
            r1 = p.op("vector", lambda v, seg=seg: v.tensor_tensor(
                rope1[:], qraw[:], con["cos_k"][:, seg], ALU.mult), deps=[])
            r2 = p.op("vector", lambda v, seg=seg: v.tensor_tensor(
                ropeo[:], ps[1][:, :], con["sin_k"][:, seg], ALU.mult),
                deps=[pm] + kc_sts[-4:] + ([prev_kq["sq2"]] if "sq2" in prev_kq else []))
            r3 = p.op("vector", lambda v: v.tensor_tensor(
                ropeo[:], ropeo[:], rope1[:], ALU.add), deps=[])
            prev_kq["krrope2"] = r3
            sq1 = p.op("scalar", lambda e: e.activation(qsq[:], ropeo[:], AF.Square),
                       deps=[r3] + ([prev_kq["knmm1"]] if "knmm1" in prev_kq else []))
            knmm1 = p.op("tensor", lambda e: e.matmul(
                ps[2][0:4, 0:TT], con["H4"][:], qsq[:], start=True, stop=False),
                deps=[sq1] + (kn_ev[-1:] if kn_ev else []))
            prev_kq["knmm1"] = knmm1
            for g in range(4):
                kr_sts.append(p.dma("sync", f"{tag}krs{g % 2}",
                    lambda e, g=g, seg=seg: e.dma_start(
                        out=kp[g, 0:32, seg], in_=ropeo[32 * g:32 * (g + 1), :]),
                    deps=[r3]))
            prev_rope = r3

            mmc = None
            for j in range(2):
                for c in range(NCH):
                    mmc = p.op("tensor", lambda e, j=j, c=c, bi=bi: e.matmul(
                        ps[3][:, :],
                        dnw[j][:, c * 128:(c + 1) * 128],
                        ub[bi][:, c * TT:(c + 1) * TT],
                        start=(c == 0), stop=(c == NCH - 1)),
                        deps=([lds[-1], prev_ckv] if c == 0 and prev_ckv else ([lds[-1]] if c == 0 else [])))
                evc = p.op("vector", lambda v, j=j: v.tensor_copy(
                    ckv[:, j * TT:(j + 1) * TT], ps[3][:, :]), deps=[mmc])
                prev_ckv = evc
            mk = None
            for j in range(2):
                mk = p.op("tensor", lambda e, j=j: e.matmul(
                    ps[4][:, :], upkw[:, j * 128:(j + 1) * 128],
                    ckv[:, j * TT:(j + 1) * TT],
                    start=(j == 0), stop=(j == 1)),
                    deps=[prev_ckv] + ([prev_kq["kcev"]] if "kcev" in prev_kq else []))
            evk = p.op("vector", lambda v: v.tensor_copy(ropeo[:], ps[4][:, :]),
                       deps=[mk, sq1] + kr_sts[-4:])
            prev_kq["kcev"] = evk
            sq2 = p.op("scalar", lambda e: e.activation(qsq[:], ropeo[:], AF.Square),
                       deps=[evk, knmm1])
            prev_kq["sq2"] = sq2
            knmm2 = p.op("tensor", lambda e: e.matmul(
                ps[2][0:4, 0:TT], con["H4"][:], qsq[:], start=False, stop=True),
                deps=[sq2])
            kne = p.op("vector", lambda v, seg=seg: v.tensor_copy(
                knorm[:, seg], ps[2][0:4, 0:TT]), deps=[knmm2])
            kn_ev.append(kne)
            for g in range(4):
                kc_sts.append(p.dma("sync", f"{tag}kcs{g % 2}",
                    lambda e, g=g, seg=seg: e.dma_start(
                        out=kp[g, 32:64, seg], in_=ropeo[32 * g:32 * (g + 1), :]),
                    deps=[evk]))
            for sub in range(4):
                vb = sub % 2
                mv = None
                for j in range(2):
                    mv = p.op("tensor", lambda e, j=j, sub=sub: e.matmul(
                        ps[5][0:128, 0:256],
                        ckv[:, j * TT + sub * 128: j * TT + (sub + 1) * 128],
                        upvw[:, j * 256:(j + 1) * 256],
                        start=(j == 0), stop=(j == 1)),
                        deps=([prev_ckv] if sub == 0 and j == 0 else []) +
                             ([prev_vev] if j == 0 and prev_vev is not None else []) +
                             ([prev_vst[vb]] if j == 0 and prev_vst[vb] else []))
                def vevac(v, vb=vb):
                    for g in range(4):
                        v.tensor_copy(vstage[vb][:, 65 * g:65 * g + 64],
                                      ps[5][:, 64 * g:64 * (g + 1)])
                    return v.memset(vstage[vb][:, 64:260:65], 1.0)
                vev = p.op("vector", vevac, deps=[mv])
                prev_vst[vb] = vev
                prev_vev = vev
                tglob = (r * 2048 + tt * TT + sub * 128) // 128
                v_sts.append(p.dma("sync", f"{tag}vs{vb}",
                    lambda e, tglob=tglob, vb=vb: e.dma_start(
                        out=vdr[tglob, :, :], in_=vstage[vb][:]), deps=[vev]))
            ub_reader[bi] = mmc

        mx = p.op("vector", lambda v: v.tensor_reduce(
            rowtmp[0:4, 0:1], knorm[:, :], AX.X, ALU.max), deps=kn_ev)
        sqx = p.op("scalar", lambda e: e.activation(
            rowtmp[0:4, 1:2], rowtmp[0:4, 0:1], AF.Sqrt), deps=[mx])
        mi = p.op("tensor", lambda e: e.matmul(
            ps[2][0:16, 0:1], con["IND"][:], rowtmp[0:4, 1:2], start=True, stop=True),
            deps=[sqx, kn_ev[-1]])
        mie = p.op("vector", lambda v: v.tensor_copy(maxkq[:], ps[2][0:16, 0:1]),
                   deps=[mi])

        q_sts = []
        prev_q = {}
        qn_last = None
        for rt in range(8):
            r, tt = rt // 4, rt % 4
            bi = rt % 2
            seg = slice(r * 2048 + tt * TT, r * 2048 + (tt + 1) * TT)
            lds = []
            for c in range(NCH):
                dd = []
                if ub_reader[bi] is not None and c == 0:
                    dd.append(ub_reader[bi])
                lds.append(p.dma("sync", f"{tag}u{bi}",
                    lambda e, c=c, r=r, tt=tt, bi=bi: e.dma_start(
                        out=ub[bi][:, c * TT:(c + 1) * TT],
                        in_=u_all[r * 2048 + c * 128: r * 2048 + (c + 1) * 128,
                                  tt * TT:(tt + 1) * TT]),
                    deps=dd))
            last_pn_mm = None
            qnmm = None
            for pn in range(8):
                mmd = [lds[-1]]
                if "ev" in prev_q:
                    mmd.append(prev_q["ev"])
                mm = None
                for c in range(NCH):
                    mm = p.op("tensor", lambda e, c=c, bi=bi, pn=pn: e.matmul(
                        ps[0][:, :], qw[pn][:, c * 128:(c + 1) * 128],
                        ub[bi][:, c * TT:(c + 1) * TT],
                        start=(c == 0), stop=(c == NCH - 1)), deps=(mmd if c == 0 else []))
                ev = p.op("vector", lambda v: v.tensor_copy(qraw[:], ps[0][:, :]),
                          deps=[mm] + ([prev_q["pm"]] if "pm" in prev_q else []) +
                               ([prev_q["sq"]] if "sq" in prev_q else []))
                prev_q["ev"] = ev
                pm = p.op("tensor", lambda e: e.matmul(
                    ps[1][:, :], con["permT"][:], qraw[:], start=True, stop=True),
                    deps=[ev] + ([prev_q["r2"]] if "r2" in prev_q else []))
                prev_q["pm"] = pm
                r1 = p.op("vector", lambda v, seg=seg: v.tensor_tensor(
                    rope1[:], qraw[:], con["cos_q"][:, seg], ALU.mult), deps=[])
                r2 = p.op("vector", lambda v, seg=seg: v.tensor_tensor(
                    ropeo[:], ps[1][:, :], con["sin_q"][:, seg], ALU.mult),
                    deps=[pm] + q_sts[-2:])
                prev_q["r2"] = r2
                r3 = p.op("vector", lambda v: v.tensor_tensor(
                    ropeo[:], ropeo[:], rope1[:], ALU.add), deps=[])
                sq = p.op("scalar", lambda e: e.activation(qsq[:], qraw[:], AF.Square),
                          deps=[r1, ev] + ([qnmm] if qnmm else []))
                prev_q["sq"] = sq
                qnd = [sq]
                if pn == 0:
                    qnd.append(qn_last if qn_last is not None else mie)
                qnmm = p.op("tensor", lambda e, pn=pn: e.matmul(
                    ps[2][0:16, 0:TT], con["Hq"][pn][:], qsq[:],
                    start=(pn == 0), stop=(pn == 7)), deps=qnd)
                for half in range(2):
                    qh = 2 * pn + half
                    q_sts.append(p.dma("sync", f"{tag}qs{qh % 2}",
                        lambda e, qh=qh, half=half, seg=seg: e.dma_start(
                            out=qp[qh, 0:64, seg],
                            in_=ropeo[64 * half:64 * (half + 1), :]),
                        deps=[r3]))
            sqa = p.op("scalar", lambda e: e.activation(
                qn_sqrt[:], ps[2][0:16, 0:TT], AF.Sqrt),
                deps=[qnmm] + ([qn_last] if qn_last else []))
            aug = p.op("vector", lambda v: v.tensor_scalar(
                aug16[:], qn_sqrt[:], maxkq[:], None, ALU.mult),
                deps=[sqa, mie])
            qn_last = aug
            q_sts.append(p.dma("sync", f"{tag}qa",
                lambda e, seg=seg: e.dma_start(
                    out=qp[:, 64, seg], in_=aug16[:]), deps=[aug]))
            ub_reader[bi] = mm
        proj_done = [q_sts[-1], q_sts[-2], q_sts[-3],
                     kr_sts[-1], kr_sts[-2], kc_sts[-1], kc_sts[-2],
                     v_sts[-1], v_sts[-2]] + ones_sts

    p.fence([proj_done[0], proj_done[1]] + proj_done)
    o_dram = scratch["odr"]
    with ctx.scope():
        vsb = ctx.sb(f"{tag}_vsb", [128, 32 * 260], BF16)
        ksb = ctx.sb(f"{tag}_ksb", [65, TFULL], BF16)
        qsb = [ctx.sb(f"{tag}_qsb{i}", [65, TT], BF16) for i in range(3)]
        pbuf = [ctx.sb(f"{tag}_pb{i}", [128, TT], BF16) for i in range(3)]
        rsb = ctx.sb(f"{tag}_rsb", [1, TT], F32)
        rbs = ctx.sb(f"{tag}_rbs", [64, TT], F32)
        ostage = [ctx.sb(f"{tag}_ost{i}", [64, TT], BF16) for i in range(2)]
        vlds = [p.dma("sync", f"{tag}vl{ch % 2}", lambda e, ch=ch: e.dma_start(
            out=vsb[:, ch * 260:(ch + 1) * 260], in_=vdr[ch, :, :]),
            deps=proj_done if ch < 2 else []) for ch in range(32)]
        ksb_reader = None
        qsb_reader = {}
        pb_reader = {0: None, 1: None, 2: None}
        sband = {0: None, 1: None}
        avband = None
        rb_user = None
        prev_cp = None
        ost_reader = {0: None, 1: None}
        for g in range(4):
            kld = p.dma("sync", f"{tag}kl", lambda e, g=g: e.dma_start(
                out=ksb[:], in_=kp[g, :, :]),
                deps=([ksb_reader] if ksb_reader else []) +
                     (proj_done if g == 0 else []))
            for lq in range(4):
                qh = 4 * g + lq
                for qt in range(8):
                    qb = qt % 3
                    qld = p.dma("sync", f"{tag}ql{qb}", lambda e, qh=qh, qt=qt, qb=qb: e.dma_start(
                        out=qsb[qb][:], in_=qp[qh, :, qt * TT:(qt + 1) * TT]),
                        deps=[qsb_reader[qb]] if qsb_reader.get(qb) else [])
                    nkb = 4 * (qt + 1)
                    avmm = None
                    for kb in range(nkb):
                        sb = kb % 2
                        smd = [kld, qld]
                        if sband[sb] is not None:
                            smd.append(sband[sb])
                        smm = p.op("tensor", lambda e, kb=kb, sb=sb, qb=qb: e.matmul(
                            ps[sb][:, :], ksb[:, kb * 128:(kb + 1) * 128], qsb[qb][:],
                            start=True, stop=True), deps=smd)
                        pbN = kb % 3
                        exd = [smm]
                        if pb_reader[pbN] is not None:
                            exd.append(pb_reader[pbN])
                        ex = p.op("scalar", lambda e, sb=sb, pbN=pbN: e.activation(
                            pbuf[pbN][:], ps[sb][:, :], AF.Exp, scale=con["eighth"][:]),
                            deps=exd)
                        sband[sb] = ex
                        mdep = [ex]
                        if kb >= 4 * qt:
                            m = kb - 4 * qt
                            mk = p.op("vector", lambda v, m=m, pbN=pbN: v.tensor_tensor(
                                pbuf[pbN][:], pbuf[pbN][:], con["tri"][m][:], ALU.mult),
                                deps=[ex])
                            mdep = [mk]
                        avd = mdep + [vlds[kb]]
                        if kb == 0:
                            if avband is not None:
                                avd.append(avband)
                        avmm = p.op("tensor", lambda e, kb=kb, g=g, pbN=pbN: e.matmul(
                            ps[4][0:65, :], vsb[:, kb * 260 + 65 * g: kb * 260 + 65 * (g + 1)],
                            pbuf[pbN][:], start=(kb == 0), stop=(kb == nkb - 1)),
                            deps=avd)
                        pb_reader[pbN] = avmm
                    qsb_reader[qb] = avmm
                    rc = p.op("vector", lambda v: v.reciprocal(rsb[:], ps[4][64:65, :]),
                              deps=[avmm] + ([rb_user] if rb_user else []))
                    bc = p.op("tensor", lambda e: e.matmul(
                        ps[5][0:64, 0:TT], con["onesr64"][:], rsb[:],
                        start=True, stop=True), deps=[rc] + ([prev_cp] if prev_cp else []))
                    cp = p.op("vector", lambda v: v.tensor_copy(rbs[:], ps[5][0:64, 0:TT]),
                              deps=[bc])
                    prev_cp = cp
                    ob = qt % 2
                    mu_d = [cp]
                    if ost_reader[ob] is not None:
                        mu_d.append(ost_reader[ob])
                    mu = p.op("vector", lambda v, ob=ob: v.tensor_tensor(
                        ostage[ob][:], ps[4][0:64, 0:TT], rbs[:], ALU.mult), deps=mu_d)
                    rb_user = mu
                    avband = mu
                    st = p.dma("sync", f"{tag}os{ob}", lambda e, qh=qh, qt=qt, ob=ob: e.dma_start(
                        out=o_dram[qh * 64:(qh + 1) * 64, qt * TT:(qt + 1) * TT],
                        in_=ostage[ob][:]), deps=[mu])
                    ost_reader[ob] = st
                    o_sts.append(st)
            ksb_reader = avmm
    p.fence(o_sts[-2:])
    return o_sts



def setup_l0attn_consts(ctx, sh, nc):
    p = ctx.plan
    ins = {}
    for nm, shape in [("cos_q", [128, TFULL]), ("sin_q", [128, TFULL]),
                      ("cos_k", [128, TFULL]), ("sin_k", [128, TFULL]),
                      ("permT", [128, 128]), ("H4", [128, 4]),
                      ("Hq", [8, 128, 16]), ("IND", [4, 16]),
                      ("tri_mul", [4, 128, 512])]:
        ins[nm] = nc.dram_tensor(nm, shape, F32, kind="ExternalInput").ap()
    con = {}
    rdy = []
    for nm in ["cos_q", "sin_q", "cos_k", "sin_k"]:
        con[nm] = ctx.sb(f"c_{nm}", [128, TFULL], BF16)
        rdy.append(p.dma("gpsimd", "cst", lambda e, nm=nm: e.dma_start(
            out=con[nm][:], in_=ins[nm][:, :]), deps=()))
    con["permT"] = ctx.sb("c_permT", [128, 128], BF16)
    rdy.append(p.dma("gpsimd", "cst", lambda e: e.dma_start(
        out=con["permT"][:], in_=ins["permT"][:, :]), deps=()))
    con["H4"] = ctx.sb("c_H4", [128, 4], BF16)
    rdy.append(p.dma("gpsimd", "cst", lambda e: e.dma_start(
        out=con["H4"][:], in_=ins["H4"][:, :]), deps=()))
    con["Hq"] = []
    for pn in range(8):
        t = ctx.sb(f"c_Hq{pn}", [128, 16], BF16)
        con["Hq"].append(t)
        rdy.append(p.dma("gpsimd", "cst", lambda e, pn=pn, t=t: e.dma_start(
            out=t[:], in_=ins["Hq"][pn, :, :]), deps=()))
    con["IND"] = ctx.sb("c_IND", [4, 16], F32)
    rdy.append(p.dma("gpsimd", "cst", lambda e: e.dma_start(
        out=con["IND"][:], in_=ins["IND"][:, :]), deps=()))
    con["tri"] = []
    for m in range(4):
        t = ctx.sb(f"c_tri{m}", [128, 512], BF16)
        con["tri"].append(t)
        rdy.append(p.dma("gpsimd", "cst", lambda e, m=m, t=t: e.dma_start(
            out=t[:], in_=ins["tri_mul"][m, :, :]), deps=()))
    con["onesr64"] = ctx.sb("c_or64", [1, 64], F32)
    rdy.append(p.op("vector", lambda v: v.memset(con["onesr64"][:], 1.0)))
    con["eighth"] = ctx.sb("c_eighth", [128, 1], F32)
    rdy.append(p.op("vector", lambda v: v.memset(con["eighth"][:], 0.125)))
    return con, rdy


def build_l0attn_launch():
    nc = bass.Bass("TRN2", target_bir_lowering=False)
    stack = contextlib.ExitStack()
    xin = nc.dram_tensor("x", [D, T], F32, kind="ExternalInput").ap()
    W = dict(
        qwT=nc.dram_tensor("qwT", [8, 2048, 128], F32, kind="ExternalInput").ap(),
        krT=nc.dram_tensor("krT", [1, 2048, 128], F32, kind="ExternalInput").ap(),
        dnT=nc.dram_tensor("dnT", [2, 2048, 128], F32, kind="ExternalInput").ap(),
        upkT=nc.dram_tensor("upkT", [256, 128], F32, kind="ExternalInput").ap(),
        upvT=nc.dram_tensor("upvT", [256, 256], F32, kind="ExternalInput").ap(),
    )
    owT = nc.dram_tensor("owT", [16, 1024, 128], F32, kind="ExternalInput").ap()
    xout = nc.dram_tensor("xo", [D, T], F32, kind="ExternalOutput").ap()
    cc_u = nc.dram_tensor("cc_u", [D, T], BF16, kind="Internal").ap()
    u_all = nc.dram_tensor("u_all", [2 * D, T], BF16, kind="Internal").ap()
    dbg_kind = "ExternalOutput" if globals().get("_ATTN_DEBUG") else "Internal"
    qp = nc.dram_tensor("qp_scr", [16, 65, TFULL], BF16, kind=dbg_kind).ap()
    kp = nc.dram_tensor("kp_scr", [4, 65, TFULL], BF16, kind=dbg_kind).ap()
    vdr = nc.dram_tensor("v_scr", [32, 128, 260], BF16, kind=dbg_kind).ap()
    odr = nc.dram_tensor("o_scr", [1024, TFULL], BF16, kind=dbg_kind).ap()
    cc2_in = nc.dram_tensor("cc2_in", [2 * D, T], F32, kind="Internal").ap()
    cc2_out = nc.dram_tensor("cc2_out", [D, T], F32, kind="Internal").ap()

    ctx = Ctx(nc, stack)
    sh = Shared(ctx)
    con, crdy = setup_l0attn_consts(ctx, sh, nc)
    ug = emit_ugather(ctx, sh, xin, u_all, cc_u, "ug")
    ctx.plan.fence(crdy)
    scratch = dict(u_all=u_all, qp=qp, kp=kp, vdr=vdr, odr=odr,
                   gather_op=ug["gather"])
    o_sts = emit_l0attn(ctx, sh, con, xin, xout, W, scratch, "a0")
    sts = emit_oproj_exchange(ctx, sh, odr, o_sts[-2:], owT, cc2_in, cc2_out,
                              xin, xout, None, "a0x")
    p = ctx.plan
    p.op("gpsimd", lambda g: g.nop(), deps=sts[-4:])
    p.emit()
    stack.close()
    return nc


def setup_l1attn_consts(ctx, sh, nc):
    p = ctx.plan
    ins = {}
    for nm, shape in [("iota8neg", [8, TFULL]), ("jcols", [128, 32]),
                      ("tri_add", [4, 128, 512]), ("e8T", [8, 8, 128]),
                      ("E8", [8, 128, 8])]:
        ins[nm] = nc.dram_tensor(nm, shape, F32, kind="ExternalInput").ap()
    vbi = nc.dram_tensor("vb", [1, 1024], F32, kind="ExternalInput").ap()
    con = {}
    rdy = []
    con["iota8neg"] = ctx.sb("c_io8", [8, TFULL], BF16)
    rdy.append(p.dma("gpsimd", "cst1", lambda e: e.dma_start(
        out=con["iota8neg"][:], in_=ins["iota8neg"][:, :]), deps=()))
    con["jcols"] = ctx.sb("c_jc", [128, 32], F32)
    rdy.append(p.dma("gpsimd", "cst1", lambda e: e.dma_start(
        out=con["jcols"][:], in_=ins["jcols"][:, :]), deps=()))
    con["triA"] = []
    for m in range(4):
        t = ctx.sb(f"c_triA{m}", [128, 512], F32)
        con["triA"].append(t)
        rdy.append(p.dma("gpsimd", "cst1", lambda e, m=m, t=t: e.dma_start(
            out=t[:], in_=ins["tri_add"][m, :, :]), deps=()))
    con["e8T"] = []
    for hh in range(8):
        t = ctx.sb(f"c_e8T{hh}", [8, 128], BF16)
        con["e8T"].append(t)
        rdy.append(p.dma("gpsimd", "cst1", lambda e, hh=hh, t=t: e.dma_start(
            out=t[:], in_=ins["e8T"][hh, :, :]), deps=()))
    con["E8"] = []
    for pn in range(8):
        t = ctx.sb(f"c_E8{pn}", [128, 8], BF16)
        con["E8"].append(t)
        rdy.append(p.dma("gpsimd", "cst1", lambda e, pn=pn, t=t: e.dma_start(
            out=t[:], in_=ins["E8"][pn, :, :]), deps=()))
    con["vb_row"] = ctx.sb("c_vbr", [1, 1024], F32)
    rdy.append(p.dma("gpsimd", "cst1", lambda e: e.dma_start(
        out=con["vb_row"][:], in_=vbi[:, :]), deps=()))
    con["sc1"] = ctx.sb("c_sc1", [128, 1], F32)
    rdy.append(p.op("vector", lambda v: v.memset(con["sc1"][:], 1.0 / math.sqrt(128.0))))
    con["onesr64"] = ctx.sb("c_or64b", [1, 128], F32)
    rdy.append(p.op("vector", lambda v: v.memset(con["onesr64"][:], 1.0)))
    return con, rdy


def emit_l1attn(ctx, sh, con, W, scratch, tag):
    p = ctx.plan
    ps = ctx.ps
    TT = 512
    u_all = scratch["u_all"]
    qdr, kdr, vdr, odr = scratch["qdr"], scratch["kdr"], scratch["vdr1"], scratch["odr"]
    gather = scratch["gather_op"]
    o_sts = []
    qn_sb = ctx.sb(f"{tag}_qn", [8, TFULL], BF16)
    kn_sb = ctx.sb(f"{tag}_kn", [8, TFULL], BF16)
    maxq8 = ctx.sb(f"{tag}_mq8", [8, 2], F32)
    Mrows = ctx.sb(f"{tag}_Mr", [8, TFULL], BF16)
    Mtmp = ctx.sb(f"{tag}_Mt", [8, TT], F32)
    Mtmp2 = ctx.sb(f"{tag}_Mt2", [8, TT], BF16)
    scope1 = ctx.scope()
    scope1.__enter__()
    if True:
        qw = [ctx.sb(f"{tag}_qw{i}", [128, NCH * 128], BF16) for i in range(8)]
        kw = [ctx.sb(f"{tag}_kw{i}", [128, NCH * 128], BF16) for i in range(8)]
        vw = ctx.sb(f"{tag}_vw", [128, NCH * 1024], BF16)
        qb = ctx.sb(f"{tag}_qb", [128, 8], F32)
        kb = ctx.sb(f"{tag}_kb", [128, 8], F32)
        vbb = ctx.sb(f"{tag}_vbb", [128, 1024], F32)
        wlds = []
        for pn in range(8):
            wlds.append(p.dma("gpsimd", f"{tag}wq{pn % 2}", lambda e, pn=pn: e.dma_start(
                out=qw[pn].ap().rearrange("p (c o) -> p c o", o=128),
                in_=W["qwT"][pn].rearrange("(c p) o -> p c o", p=128)), deps=()))
            wlds.append(p.dma("gpsimd", f"{tag}wk{pn % 2}", lambda e, pn=pn: e.dma_start(
                out=kw[pn].ap().rearrange("p (c o) -> p c o", o=128),
                in_=W["kwT"][pn].rearrange("(c p) o -> p c o", p=128)), deps=()))
        wlds.append(p.dma("gpsimd", f"{tag}wv", lambda e: e.dma_start(
            out=vw.ap().rearrange("p (c o) -> p c o", o=1024),
            in_=W["vwT"].rearrange("(c p) o -> p c o", p=128)), deps=()))
        wlds.append(p.dma("gpsimd", f"{tag}wb", lambda e: e.dma_start(
            out=qb[:], in_=W["qb"][:, :]), deps=()))
        wlds.append(p.dma("gpsimd", f"{tag}wb", lambda e: e.dma_start(
            out=kb[:], in_=W["kb"][:, :]), deps=()))
        vb1 = p.op("tensor", lambda e: e.matmul(
            ps[6][:, :], con["onesr64"][0:1, :], con["vb_row"][0:1, 0:512],
            start=True, stop=True), deps=scratch["vb_deps"] + [sh.ready[2]])
        vb2 = p.op("tensor", lambda e: e.matmul(
            ps[7][:, :], con["onesr64"][0:1, :], con["vb_row"][0:1, 512:1024],
            start=True, stop=True), deps=scratch["vb_deps"])
        vbc = p.op("vector", lambda v: (
            v.tensor_copy(vbb[:, 0:512], ps[6][:, :]),
            v.tensor_copy(vbb[:, 512:1024], ps[7][:, :]))[-1], deps=[vb1, vb2])

        ub = [ctx.sb(f"{tag}_ub{i}", [128, NCH * TT], BF16) for i in range(2)]
        stage = [ctx.sb(f"{tag}_st{i}", [128, TT], BF16) for i in range(2)]
        sqb = [ctx.sb(f"{tag}_sq{i}", [128, TT], BF16) for i in range(2)]
        vstage = [ctx.sb(f"{tag}_vs{i}", [128, 1024], BF16) for i in range(2)]

        ub_reader = {0: None, 1: None}
        q_sts, k_sts, v_sts = [], [], []
        prev = {}
        nrm_ev = {"q": [], "k": []}
        for rt in range(8):
            r, tt = rt // 4, rt % 4
            bi = rt % 2
            seg = slice(r * 2048 + tt * TT, r * 2048 + (tt + 1) * TT)
            lds = []
            for c in range(NCH):
                dd = [gather] if rt < 2 and c < 2 else []
                if ub_reader[bi] is not None and c == 0:
                    dd.append(ub_reader[bi])
                lds.append(p.dma("sync", f"{tag}u{bi}",
                    lambda e, c=c, r=r, tt=tt, bi=bi: e.dma_start(
                        out=ub[bi][:, c * TT:(c + 1) * TT],
                        in_=u_all[r * 2048 + c * 128: r * 2048 + (c + 1) * 128,
                                  tt * TT:(tt + 1) * TT]),
                    deps=dd))
            last_mm = None
            for kind, wws, bcol, dst, sts, nbank in (
                    ("q", qw, qb, qdr, q_sts, 2), ("k", kw, kb, kdr, k_sts, 3)):
                pbk = 0 if kind == "q" else 1
                nmm = None
                for pn in range(8):
                    mmd = [lds[-1]] + (wlds[:2] if rt == 0 and pn == 0 else [])
                    if f"{kind}ev" in prev:
                        mmd.append(prev[f"{kind}ev"])
                    mm = None
                    for c in range(NCH):
                        mm = p.op("tensor", lambda e, c=c, bi=bi, pn=pn, wws=wws, pbk=pbk: e.matmul(
                            ps[pbk][:, :],
                            wws[pn][:, c * 128:(c + 1) * 128],
                            ub[bi][:, c * TT:(c + 1) * TT],
                            start=(c == 0), stop=(c == NCH - 1)),
                            deps=(mmd if c == 0 else []))
                    sb_ = stage[0 if kind == "q" else 1]
                    sq_ = sqb[0 if kind == "q" else 1]
                    evd = [mm]
                    if f"{kind}st" in prev:
                        evd.append(prev[f"{kind}st"])
                    if f"{kind}sq" in prev:
                        evd.append(prev[f"{kind}sq"])
                    ev = p.op("vector", lambda v, sb_=sb_, pn=pn, bcol=bcol, pbk=pbk: v.tensor_scalar(
                        sb_[:], ps[pbk][:, :],
                        bcol[:, pn:pn + 1], None, ALU.add), deps=evd)
                    prev[f"{kind}ev"] = ev
                    sqo = p.op("scalar", lambda e, sb_=sb_, sq_=sq_: e.activation(
                        sq_[:], sb_[:], AF.Square),
                        deps=[ev] + ([nmm] if nmm else []))
                    prev[f"{kind}sq"] = sqo
                    nd = [sqo]
                    if pn == 0:
                        if nrm_ev[kind]:
                            nd.append(nrm_ev[kind][-1])
                    nmm = p.op("tensor", lambda e, pn=pn, sq_=sq_, nbank=nbank: e.matmul(
                        ps[nbank][0:8, 0:TT], con["E8"][pn][:], sq_[:],
                        start=(pn == 0), stop=(pn == 7)), deps=nd)
                    st = p.dma("sync", f"{tag}{kind}s{pn % 2}",
                        lambda e, pn=pn, seg=seg, sb_=sb_, dst=dst: e.dma_start(
                            out=dst[pn * 128:(pn + 1) * 128, seg], in_=sb_[:]),
                        deps=[ev])
                    prev[f"{kind}st"] = st
                    sts.append(st)
                ne = p.op("vector", lambda v, seg=seg, nbank=nbank,
                          nsb=(qn_sb if kind == "q" else kn_sb): v.tensor_copy(
                    nsb[:, seg], ps[nbank][0:8, 0:TT]), deps=[nmm])
                nrm_ev[kind].append(ne)
            for sub in range(4):
                vbuf = sub % 2
                for half in range(2):
                    pv = 6 + half
                    mv = None
                    for c in range(NCH):
                        mvd = []
                        if c == 0:
                            mvd = [lds[-1], wlds[16]] if rt == 0 and sub == 0 else [lds[-1]]
                            if prev.get(f"vev{pv}") is not None:
                                mvd.append(prev[f"vev{pv}"])
                            if rt == 0 and sub == 0 and half == 0:
                                mvd.append(vbc)
                        mv = p.op("tensor", lambda e, c=c, sub=sub, half=half, bi=bi, pv=pv: e.matmul(
                            ps[pv][:, :],
                            ub[bi][:, c * TT + sub * 128: c * TT + (sub + 1) * 128],
                            vw[:, c * 1024 + half * 512: c * 1024 + (half + 1) * 512],
                            start=(c == 0), stop=(c == NCH - 1)),
                            deps=mvd)
                    vvd = [mv, vbc]
                    if prev.get(f"vst{vbuf}") is not None and half == 0:
                        vvd.append(prev[f"vst{vbuf}"])
                    vv = p.op("vector", lambda v, vbuf=vbuf, half=half, pv=pv: v.tensor_tensor(
                        vstage[vbuf][:, half * 512:(half + 1) * 512],
                        ps[pv][:, :], vbb[:, half * 512:(half + 1) * 512], ALU.add),
                        deps=vvd)
                    prev[f"vev{pv}"] = vv
                tglob = (r * 2048 + tt * TT + sub * 128) // 128
                vst = p.dma("sync", f"{tag}vs{vbuf}",
                    lambda e, tglob=tglob, vbuf=vbuf: e.dma_start(
                        out=vdr[tglob, :, :], in_=vstage[vbuf][:]),
                    deps=[prev["vev6"], prev["vev7"]])
                prev[f"vst{vbuf}"] = vst
                v_sts.append(vst)
            ub_reader[bi] = mv

        mx = p.op("vector", lambda v: v.tensor_reduce(
            maxq8[:, 0:1], qn_sb[:, :], AX.X, ALU.max), deps=nrm_ev["q"])
        sqx = p.op("scalar", lambda e: e.activation(
            maxq8[:, 1:2], maxq8[:, 0:1], AF.Sqrt), deps=[mx])
        ngx = p.op("vector", lambda v: v.tensor_scalar(
            maxq8[:, 1:2], maxq8[:, 1:2], -1.0, None, ALU.mult), hard=[sqx])
        mrows_done = []
        prev_mt = None
        for sgi in range(8):
            seg = slice(sgi * TT, (sgi + 1) * TT)
            s1 = p.op("scalar", lambda e, seg=seg: e.activation(
                Mtmp[:], kn_sb[:, seg], AF.Sqrt),
                deps=[ngx, nrm_ev["k"][-1]] + ([prev_mt] if prev_mt else []))
            s2 = p.op("vector", lambda v, seg=seg: v.tensor_scalar(
                Mtmp2[:], Mtmp[:], maxq8[:, 1:2], None, ALU.mult), deps=[s1])
            s3 = p.op("vector", lambda v, seg=seg: v.tensor_tensor(
                Mrows[:, seg], Mtmp2[:], con["iota8neg"][:, seg], ALU.add), hard=[s2])
            prev_mt = s3
            mrows_done.append(s3)

    scope1.__exit__(None, None, None)
    first_tail = [q_sts[-1], q_sts[-2], k_sts[-1], k_sts[-2],
                  v_sts[-1], v_sts[-2], mrows_done[-1],
                  ub_reader[0], ub_reader[1], nrm_ev["q"][-1], nrm_ev["k"][-1]]
    p.fence(first_tail)
    with ctx.scope():
        NBQ = 3
        qtl = [ctx.sb(f"{tag}_qt{i}", [128, 128], BF16) for i in range(NBQ)]
        ktl = [ctx.sb(f"{tag}_kt{i}", [128, TT], BF16) for i in range(2)]
        vtl = [ctx.sb(f"{tag}_vt{i}", [128, 128], BF16) for i in range(NBQ)]
        pbuf = [ctx.sb(f"{tag}_pb{i}", [128, TT], BF16) for i in range(3)]
        rsb = ctx.sb(f"{tag}_rsb", [1, TT], F32)
        rbs = ctx.sb(f"{tag}_rbs", [128, TT], F32)
        ostage = [ctx.sb(f"{tag}_os{i}", [128, TT], BF16) for i in range(2)]
        qt_reader = {}
        kt_reader = {0: None, 1: None}
        vt_reader = {}
        pb_reader = {0: None, 1: None, 2: None}
        sband = {0: None, 1: None}
        av_user = None
        den_user = None
        prev_cp = None
        ost_reader = {0: None, 1: None}
        first_score_deps = [q_sts[-1], q_sts[-2], k_sts[-1], k_sts[-2],
                           v_sts[-1], v_sts[-2], mrows_done[-1]]
        for hh in range(8):
            for it in range(8):
                jb_lo = max(0, 4 * it - 6)
                jb_hi = 4 * it + 3
                kldd = list(first_score_deps) if hh == 0 and it == 0 else []
                if kt_reader[it % 2] is not None:
                    kldd.append(kt_reader[it % 2])
                kld = p.dma("sync", f"{tag}ktl{it % 2}",
                    lambda e, hh=hh, it=it: e.dma_start(
                        out=ktl[it % 2][:],
                        in_=kdr[hh * 128:(hh + 1) * 128, it * TT:(it + 1) * TT]),
                    deps=kldd)
                avmm = None
                denmm = None
                njb = jb_hi - jb_lo + 1
                for ji, jb in enumerate(range(jb_lo, jb_hi + 1)):
                    qbuf = ji % NBQ
                    qldd = [qt_reader[qbuf]] if qt_reader.get(qbuf) else []
                    if hh == 0 and it == 0 and ji < 2:
                        qldd += first_score_deps[:2]
                    qld = p.dma("sync", f"{tag}qtl{qbuf}",
                        lambda e, hh=hh, jb=jb, qbuf=qbuf: e.dma_start(
                            out=qtl[qbuf][:],
                            in_=qdr[hh * 128:(hh + 1) * 128, jb * 128:(jb + 1) * 128]),
                        deps=qldd)
                    vldd = [vt_reader[qbuf]] if vt_reader.get(qbuf) else []
                    vld = p.dma("sync", f"{tag}vtl{qbuf}",
                        lambda e, hh=hh, jb=jb, qbuf=qbuf: e.dma_start(
                            out=vtl[qbuf][:],
                            in_=vdr[jb, :, hh * 128:(hh + 1) * 128]),
                        deps=vldd)
                    sb_ = ji % 2
                    augd = [mrows_done[it]]
                    if sband[sb_] is not None:
                        augd.append(sband[sb_])
                    p.op("tensor", lambda e, hh=hh, it=it, sb_=sb_: e.matmul(
                        ps[sb_][:, :], con["e8T"][hh][:],
                        Mrows[0:8, it * TT:(it + 1) * TT],
                        start=True, stop=False), deps=augd)
                    smm = p.op("tensor", lambda e, sb_=sb_, qbuf=qbuf, it=it: e.matmul(
                        ps[sb_][:, :], qtl[qbuf][:], ktl[it % 2][:],
                        start=False, stop=True), deps=[qld, kld])
                    qt_reader[qbuf] = smm
                    exd_extra = []
                    if jb >= 4 * it:
                        m = jb - 4 * it
                        mk = p.op("vector", lambda v, m=m, sb_=sb_: v.tensor_tensor(
                            ps[sb_][:, :], ps[sb_][:, :], con["triA"][m][:], ALU.add),
                            deps=[smm])
                        exd_extra = [mk]
                    pbN = ji % 3
                    exd = [smm] + exd_extra
                    if pb_reader[pbN] is not None:
                        exd.append(pb_reader[pbN])
                    ex = p.op("scalar", lambda e, sb_=sb_, pbN=pbN, jb=jb: e.activation(
                        pbuf[pbN][:], ps[sb_][:, :], AF.Exp, scale=con["sc1"][:],
                        bias=con["jcols"][:, jb:jb + 1]), deps=exd)
                    sband[sb_] = ex
                    dend = [ex, sh.ready[1]]
                    if ji == 0 and den_user is not None:
                        dend.append(den_user)
                    denmm = p.op("tensor", lambda e, ji=ji, pbN=pbN, njb=njb: e.matmul(
                        ps[4][0:1, :], sh.ones_b[:], pbuf[pbN][:],
                        start=(ji == 0), stop=(ji == njb - 1)), deps=dend)
                    avd = [ex, vld]
                    if ji == 0 and av_user is not None:
                        avd.append(av_user)
                    avmm = p.op("tensor", lambda e, ji=ji, pbN=pbN, qbuf=qbuf, njb=njb: e.matmul(
                        ps[5][:, :], vtl[qbuf][:], pbuf[pbN][:],
                        start=(ji == 0), stop=(ji == njb - 1)), deps=avd)
                    vt_reader[qbuf] = avmm
                    pb_reader[pbN] = avmm
                kt_reader[it % 2] = smm
                rc = p.op("vector", lambda v: v.reciprocal(rsb[:], ps[4][0:1, :]),
                          deps=[denmm])
                den_user = rc
                bc = p.op("tensor", lambda e: e.matmul(
                    ps[3][:, :], con["onesr64"][0:1, :], rsb[:],
                    start=True, stop=True),
                    deps=[rc] + ([prev_cp] if prev_cp else []) +
                         (nrm_ev["k"][-1:] if hh == 0 and it == 0 else []))
                cp = p.op("vector", lambda v: v.tensor_copy(rbs[:], ps[3][:, :]),
                          deps=[bc])
                prev_cp = cp
                ob = it % 2
                mud = [cp]
                if ost_reader[ob] is not None:
                    mud.append(ost_reader[ob])
                mu = p.op("vector", lambda v, ob=ob: v.tensor_tensor(
                    ostage[ob][:], ps[5][:, :], rbs[:], ALU.mult), deps=mud)
                av_user = mu
                st = p.dma("sync", f"{tag}osx{ob}",
                    lambda e, hh=hh, it=it, ob=ob: e.dma_start(
                        out=odr[hh * 128:(hh + 1) * 128, it * TT:(it + 1) * TT],
                        in_=ostage[ob][:]), deps=[mu])
                ost_reader[ob] = st
                o_sts.append(st)
    p.fence(o_sts[-2:])
    return o_sts


def build_l1attn_launch():
    nc = bass.Bass("TRN2", target_bir_lowering=False)
    stack = contextlib.ExitStack()
    xin = nc.dram_tensor("x", [D, T], F32, kind="ExternalInput").ap()
    W = dict(
        qwT=nc.dram_tensor("qwT", [8, 2048, 128], F32, kind="ExternalInput").ap(),
        kwT=nc.dram_tensor("kwT", [8, 2048, 128], F32, kind="ExternalInput").ap(),
        vwT=nc.dram_tensor("vwT", [2048, 1024], F32, kind="ExternalInput").ap(),
        qb=nc.dram_tensor("qb", [128, 8], F32, kind="ExternalInput").ap(),
        kb=nc.dram_tensor("kb", [128, 8], F32, kind="ExternalInput").ap(),
    )
    owT = nc.dram_tensor("owT", [16, 1024, 128], F32, kind="ExternalInput").ap()
    obi = nc.dram_tensor("ob", [128, 16], F32, kind="ExternalInput").ap()
    xout = nc.dram_tensor("xo", [D, T], F32, kind="ExternalOutput").ap()
    cc_u = nc.dram_tensor("cc_u", [D, T], BF16, kind="Internal").ap()
    u_all = nc.dram_tensor("u_all", [2 * D, T], BF16, kind="Internal").ap()
    dbg_kind = "ExternalOutput" if globals().get("_ATTN_DEBUG") else "Internal"
    qdr = nc.dram_tensor("q_scr", [1024, TFULL], BF16, kind=dbg_kind).ap()
    kdr = nc.dram_tensor("k_scr", [1024, TFULL], BF16, kind=dbg_kind).ap()
    vdr1 = nc.dram_tensor("v_scr", [32, 128, 1024], BF16, kind=dbg_kind).ap()
    odr = nc.dram_tensor("o_scr", [1024, TFULL], BF16, kind=dbg_kind).ap()
    cc2_in = nc.dram_tensor("cc2_in", [2 * D, T], F32, kind="Internal").ap()
    cc2_out = nc.dram_tensor("cc2_out", [D, T], F32, kind="Internal").ap()

    ctx = Ctx(nc, stack)
    sh = Shared(ctx)
    con, crdy = setup_l1attn_consts(ctx, sh, nc)
    ob_cols = ctx.sb("ob_cols", [128, 16], F32)
    obl = ctx.plan.dma("gpsimd", "cst1", lambda e: e.dma_start(
        out=ob_cols[:], in_=obi[:, :]), deps=())
    ug = emit_ugather(ctx, sh, xin, u_all, cc_u, "ug")
    ctx.plan.fence(crdy + [obl])
    scratch = dict(u_all=u_all, qdr=qdr, kdr=kdr, vdr1=vdr1, odr=odr,
                   gather_op=ug["gather"], vb_deps=[crdy[-3]])
    o_sts = emit_l1attn(ctx, sh, con, W, scratch, "a1")
    sts = emit_oproj_exchange(ctx, sh, odr, o_sts[-2:], owT, cc2_in, cc2_out,
                              xin, xout, ob_cols, "a1x")
    p = ctx.plan
    p.op("gpsimd", lambda g: g.nop(), deps=sts[-4:])
    p.emit()
    stack.close()
    return nc


def build_l0ffn_launch(debug=False):
    nc = bass.Bass("TRN2", target_bir_lowering=False)
    stack = contextlib.ExitStack()
    xin = nc.dram_tensor("x", [D, T], F32, kind="ExternalInput").ap()
    wfcT = nc.dram_tensor("wfcT", [64, 2048, 128], F32, kind="ExternalInput").ap()
    wprojT = nc.dram_tensor("wprojT", [16, 8192, 128], F32, kind="ExternalInput").ap()
    xout = nc.dram_tensor("xo", [D, T], F32, kind="ExternalOutput").ap()
    kind = "ExternalOutput" if debug else "Internal"
    tprojRB = nc.dram_tensor("tprojRB", [64, 128, 2048], BF16, kind=kind).ap()
    h_dram = nc.dram_tensor("h_scr", [8192, T], BF16, kind=kind).ap()
    if debug:
        dbg = nc.dram_tensor("dbg", [128, NCH * T], BF16, kind="ExternalOutput").ap()
        dbg2 = nc.dram_tensor("dbg2", [128, 8], F32, kind="ExternalOutput").ap()

    ctx = Ctx(nc, stack)
    sh = Shared(ctx)
    xn_sb = ctx.sb("xn", [128, NCH * T], BF16)
    prep = emit_l0ffn_prep(ctx, sh, wfcT, wprojT, tprojRB, "pp")
    ctx.plan.fence(prep["tail"])
    ln = emit_ln(ctx, sh, xin, xn_sb, "ln", compose_inner=True)
    ctx.plan.fence(ln["tail"])
    sts = emit_l0ffn_main(ctx, sh, prep, xn_sb, ln["tile_done"], xin, xout,
                          wfcT, tprojRB, h_dram, "ff")
    p = ctx.plan
    if debug:
        sd = p.dma("sync", "dbgxn", lambda e: e.dma_start(out=dbg[:, :], in_=xn_sb[:]),
                   deps=ln["tile_done"])

        def dbgcols(v):
            v.tensor_copy(dbg2s[:, 0:1], prep["gfc_col"][:])
            v.tensor_copy(dbg2s[:, 1:2], prep["thr_fc"][:])
            v.tensor_copy(dbg2s[:, 2:3], prep["thr_pj"][:])
            return v.tensor_copy(dbg2s[0:1, 3:4], prep["gph"][0:1, :])
        dbg2s = ctx.sb("dbg2s", [128, 8], F32)
        z = p.op("vector", lambda v: v.memset(dbg2s[:], 0.0))
        dc = p.op("vector", dbgcols, deps=[prep["cc"], z])
        sd2 = p.dma("sync", "dbgc", lambda e: e.dma_start(out=dbg2[:, :], in_=dbg2s[:]),
                    deps=[dc])
        p.op("gpsimd", lambda g: g.nop(), deps=sts[-16:] + [sd, sd2])
    else:
        p.op("gpsimd", lambda g: g.nop(), deps=sts[-16:])
    p.emit()
    stack.close()
    return nc


def build_fused():
    nc = bass.Bass("TRN2", target_bir_lowering=False)
    stack = contextlib.ExitStack()
    A = lambda n, s, dt=F32, k="ExternalInput": nc.dram_tensor(n, s, dt, kind=k).ap()
    x_in = A("x", [D, T])
    wfcT = A("wfcT", [64, 2048, 128])
    wprojT = A("wprojT", [16, 8192, 128])
    guT = A("guT", [128, 2048, 128])
    downRB = A("downRB", [64, 128, 2048])
    W0 = dict(qwT=A("qwT0", [8, 2048, 128]), krT=A("krT", [1, 2048, 128]),
              dnT=A("dnT", [2, 2048, 128]), upkT=A("upkT", [256, 128]),
              upvT=A("upvT", [256, 256]))
    owT0 = A("owT0", [16, 1024, 128])
    W1 = dict(qwT=A("qwT1", [8, 2048, 128]), kwT=A("kwT1", [8, 2048, 128]),
              vwT=A("vwT1", [2048, 1024]), qb=A("qb1", [128, 8]),
              kb=A("kb1", [128, 8]))
    owT1 = A("owT1", [16, 1024, 128])
    obi = A("ob1", [128, 16])
    xo = A("xo", [D, T], F32, "ExternalOutput")
    xi0 = A("xi0", [D, T], F32, "Internal")
    xi1 = A("xi1", [D, T], F32, "Internal")
    h_dram = A("h_scr", [8192, T], BF16, "Internal")
    tprojRB = A("tprojRB", [64, 128, 2048], BF16, "Internal")
    cc_u = A("cc_u", [D, T], BF16, "Internal")
    u_all = A("u_all", [2 * D, T], BF16, "Internal")
    cc2_in = A("cc2_in", [2 * D, T], F32, "Internal")
    cc2_out = A("cc2_out", [D, T], F32, "Internal")
    qp = A("qp_scr", [16, 65, TFULL], BF16, "Internal")
    kp = A("kp_scr", [4, 65, TFULL], BF16, "Internal")
    vdr0 = A("v_scr0", [32, 128, 260], BF16, "Internal")
    odr = A("o_scr", [1024, TFULL], BF16, "Internal")
    qdr = A("q_scr", [1024, TFULL], BF16, "Internal")
    kdr = A("k_scr", [1024, TFULL], BF16, "Internal")
    vdr1 = A("v_scr1", [32, 128, 1024], BF16, "Internal")

    ctx = Ctx(nc, stack)
    sh = Shared(ctx)
    p = ctx.plan
    prep = emit_l0ffn_prep(ctx, sh, wfcT, wprojT, tprojRB, "pp")
    p.fence(prep["tail"])

    def l0ffn(xa, xb, tag):
        with ctx.scope():
            xn_sb = ctx.sb(f"{tag}_xn", [128, NCH * T], BF16)
            ln = emit_ln(ctx, sh, xa, xn_sb, f"{tag}l", compose_inner=True)
            p.fence(ln["tail"])
            sts = emit_l0ffn_main(ctx, sh, prep, xn_sb, ln["tile_done"], xa, xb,
                                  wfcT, tprojRB, h_dram, tag)
        p.fence(sts[-4:])

    def l1ffn(xa, xb, tag):
        with ctx.scope():
            u_sb = ctx.sb(f"{tag}_u", [128, NCH * T], BF16)
            ln = emit_ln(ctx, sh, xa, u_sb, f"{tag}l", compose_inner=False, eps=1e-5)
            p.fence(ln["tail"])
            sts = emit_l1ffn(ctx, sh, u_sb, ln["tile_done"], xa, xb, guT, downRB,
                             h_dram, tag)
        p.fence(sts[-4:])

    def l0attn(xa, xb, tag):
        with ctx.scope():
            con, crdy = setup_l0attn_consts(ctx, sh, nc)
            ug = emit_ugather(ctx, sh, xa, u_all, cc_u, f"{tag}g")
            p.fence(crdy)
            scratch = dict(u_all=u_all, qp=qp, kp=kp, vdr=vdr0, odr=odr,
                           gather_op=ug["gather"])
            o_sts = emit_l0attn(ctx, sh, con, xa, xb, W0, scratch, tag)
            sts = emit_oproj_exchange(ctx, sh, odr, o_sts[-2:], owT0, cc2_in,
                                      cc2_out, xa, xb, None, f"{tag}x")
        p.fence(sts[-4:])

    def l1attn(xa, xb, tag):
        with ctx.scope():
            con, crdy = setup_l1attn_consts(ctx, sh, nc)
            ob_cols = ctx.sb(f"{tag}_obc", [128, 16], F32)
            obl = p.dma("gpsimd", "cst1", lambda e: e.dma_start(
                out=ob_cols[:], in_=obi[:, :]), deps=())
            ug = emit_ugather(ctx, sh, xa, u_all, cc_u, f"{tag}g")
            p.fence(crdy + [obl])
            scratch = dict(u_all=u_all, qdr=qdr, kdr=kdr, vdr1=vdr1, odr=odr,
                           gather_op=ug["gather"], vb_deps=[crdy[-3]])
            o_sts = emit_l1attn(ctx, sh, con, W1, scratch, tag)
            sts = emit_oproj_exchange(ctx, sh, odr, o_sts[-2:], owT1, cc2_in,
                                      cc2_out, xa, xb, ob_cols, f"{tag}x")
        p.fence(sts[-4:])

    l0ffn(x_in, xi0, "fA")
    l0attn(xi0, xi1, "aA")
    l0ffn(xi1, xi0, "fB")
    l1ffn(xi0, xi1, "gA")
    l1attn(xi1, xi0, "bA")
    l1ffn(xi0, xo, "gB")
    p.op("gpsimd", lambda g: g.nop(), deps=())
    p.emit()
    stack.close()
    return nc


_BUILD_CACHE = {}


def _get(name, fn):
    if name not in _BUILD_CACHE:
        _BUILD_CACHE[name] = fn()
    return _BUILD_CACHE[name]


def kernel(**inputs):
    cores = list(range(8))
    prep = [prep_core_inputs(inputs, c) for c in cores]
    nc = _get("fused", build_fused)
    maps = []
    for c in cores:
        d = prep[c]
        maps.append(dict(
            x=d["x"], wfcT=d["wfcT"], wprojT=d["wprojT"], guT=d["guT"],
            downRB=d["downRB"],
            qwT0=d["l0_qwT"], krT=d["l0_krT"], dnT=d["l0_dnT"],
            upkT=d["l0_upkT"], upvT=d["l0_upvT"], owT0=d["l0_owT"],
            qwT1=d["l1_qwT"], kwT1=d["l1_kwT"], vwT1=d["l1_vwT"],
            qb1=d["l1_qb"], kb1=d["l1_kb"], owT1=d["l1_owT"], ob1=d["l1_ob"],
            cos_q=d["cos_q"], sin_q=d["sin_q"], cos_k=d["cos_k"],
            sin_k=d["sin_k"], permT=d["permT"], H4=d["H4"], Hq=d["Hq"],
            IND=d["IND"], tri_mul=d["tri_mul"],
            iota8neg=d["iota8neg"], jcols=d["jcols"], tri_add=d["tri_add"],
            e8T=d["e8T"], E8=d["E8"], vb=d["l1_vb"]))
    res = run_bass_kernel_spmd(nc, maps, core_ids=cores)
    out = np.zeros((4, 4096, 2048), np.float32)
    for c in cores:
        b, h = c // 2, c % 2
        out[b, h * T:(h + 1) * T, :] = np.asarray(res.results[c]["xo"]).T
    return out


def kernel_unfused(**inputs):
    cores = list(range(8))
    prep = [prep_core_inputs(inputs, c) for c in cores]
    x = [prep[c]["x"] for c in cores]

    nc_ffn0 = _get("ffn0", build_l0ffn_launch)
    nc_att0 = _get("att0", build_l0attn_launch)
    nc_ffn1 = _get("ffn1", build_l1ffn_launch)
    nc_att1 = _get("att1", build_l1attn_launch)

    def run(nc, maps):
        res = run_bass_kernel_spmd(nc, maps, core_ids=cores)
        return [res.results[c]["xo"] for c in cores]

    for step, nc in [("f0", nc_ffn0), ("a0", nc_att0), ("f0", nc_ffn0),
                     ("f1", nc_ffn1), ("a1", nc_att1), ("f1", nc_ffn1)]:
        maps = []
        for c in cores:
            d = prep[c]
            if step == "f0":
                maps.append(dict(x=x[c], wfcT=d["wfcT"], wprojT=d["wprojT"]))
            elif step == "a0":
                maps.append(dict(x=x[c], qwT=d["l0_qwT"], krT=d["l0_krT"],
                                 dnT=d["l0_dnT"], upkT=d["l0_upkT"],
                                 upvT=d["l0_upvT"], owT=d["l0_owT"],
                                 cos_q=d["cos_q"], sin_q=d["sin_q"],
                                 cos_k=d["cos_k"], sin_k=d["sin_k"],
                                 permT=d["permT"], H4=d["H4"], Hq=d["Hq"],
                                 IND=d["IND"], tri_mul=d["tri_mul"]))
            elif step == "f1":
                maps.append(dict(x=x[c], guT=d["guT"], downRB=d["downRB"]))
            else:
                maps.append(dict(x=x[c], qwT=d["l1_qwT"], kwT=d["l1_kwT"],
                                 vwT=d["l1_vwT"], qb=d["l1_qb"], kb=d["l1_kb"],
                                 vb=d["l1_vb"], owT=d["l1_owT"], ob=d["l1_ob"],
                                 iota8neg=d["iota8neg"], jcols=d["jcols"],
                                 tri_add=d["tri_add"], e8T=d["e8T"], E8=d["E8"]))
        x = run(nc, maps)

    out = np.zeros((4, 4096, 2048), np.float32)
    for c in cores:
        b, h = c // 2, c % 2
        out[b, h * T:(h + 1) * T, :] = np.asarray(x[c]).T
    return out
